# version 03
# baseline (speedup 1.0000x reference)
import numpy as np
from contextlib import ExitStack

import concourse.bass as bass
import concourse.mybir as mybir
from concourse.bass_utils import run_bass_kernel_spmd

F32 = mybir.dt.float32
BF16 = mybir.dt.bfloat16
AF = mybir.ActivationFunctionType
ALU = mybir.AluOpType
AX = mybir.AxisListType
AP = bass.AP

B, S, E, H, HD, NE = 4, 4096, 1024, 16, 64, 8
NT = 16
TOK = NT * 128
NC = 8
LN_EPS = 1e-5

ENGS = ("sync", "gpsimd", "tensor", "vector", "scalar")
DEBUG_TAPS = False


class Sem:
    __slots__ = ("h", "n", "by")

    def __init__(self, st, nc, name):
        self.h = st.enter_context(nc.semaphore(name))
        self.n = 0
        self.by = {e: 0 for e in ENGS}


class Prog:

    def __init__(self, nc):
        self.nc = nc
        self.seq = {e: [] for e in ENGS}

    def op(self, eng, emit, after=(), inc=None, inc_by=1, wafter=()):
        w = []
        for a in after:
            s, v = (a if isinstance(a, tuple) else (a, a.n))
            if v > 0 and s.by[eng] < v:
                w.append((s, v))
        for a in wafter:
            s, v = (a if isinstance(a, tuple) else (a, a.n))
            if v > 0:
                w.append((s, v))
        if inc is not None:
            inc.n += inc_by
            inc.by[eng] += inc_by
        self.seq[eng].append((w, emit, inc, inc_by))

    def run(self):
        with self.nc.Block() as blk:
            for eng in ENGS:
                ops = self.seq[eng]
                if not ops:
                    continue

                def body(engine, ops=ops):
                    seen = {}
                    for (w, emit, inc, inc_by) in ops:
                        for (s, v) in w:
                            if seen.get(id(s), -1) < v:
                                engine.wait_ge(s.h, v)
                                seen[id(s)] = v
                        inst = emit(engine)
                        if inc is not None:
                            inst.then_inc(inc.h, inc_by)
                getattr(blk, eng)(body)


def build_program():
    nc = bass.Bass("TRN2", target_bir_lowering=False,
                   detect_race_conditions=False)

    def I(name, shape, dt=F32):
        return nc.dram_tensor(name, shape, dt, kind="ExternalInput").ap()

    x0f = I("x0f", [128, NC, S])
    x0fo = I("x0fo", [128, NC, TOK])
    x0tm = I("x0tm", [TOK, E])
    wqkv1 = I("wqkv1", [128, 8, NC, 384])
    wout1 = I("wout1", [64, H, E])
    wqkv2 = I("wqkv2", [128, 8, NC, 384])
    wout2 = I("wout2", [64, H, E])
    wexp = I("wexp", [NE, 128, NC, E])
    bqk = I("bqk", [128, 16])
    bvrow = I("bvrow", [1, E])
    borow = I("borow", [1, E])
    gwt = I("gwt", [128, NC, 8])
    gbrow = I("gbrow", [1, 8])
    berow = I("berow", [1, NE * E])
    ln1g = I("ln1g", [1, E]); ln1b = I("ln1b", [1, E])
    ln2g = I("ln2g", [1, E]); ln2b = I("ln2b", [1, E])
    cost = I("cost", [TOK, 512]); sint = I("sint", [TOK, 512])
    m0in = I("m0in", [128, 128]); m1in = I("m1in", [128, 128])
    identf = I("identf", [128, 128])

    OUT = nc.dram_tensor("OUT", [TOK, E], F32, kind="ExternalOutput").ap()

    x1d = nc.dram_tensor("x1d", [TOK, E], F32, kind="Internal").ap()
    y1d = nc.dram_tensor("y1d", [TOK, E], F32, kind="Internal").ap()
    k2loc = nc.dram_tensor("k2loc", [1024, TOK], BF16, kind="Internal")
    v2loc = nc.dram_tensor("v2loc", [TOK, 8 * 130], BF16, kind="Internal")
    k2all = nc.dram_tensor("k2all", [2048, TOK], BF16, kind="Internal")
    v2all = nc.dram_tensor("v2all", [2 * TOK, 8 * 130], BF16, kind="Internal")
    arin = nc.dram_tensor("arin", [1, 1], F32, kind="Internal")
    arout = nc.dram_tensor("arout", [1, 1], F32, kind="Internal")
    RG = [[0, 1], [2, 3], [4, 5], [6, 7]]

    with ExitStack() as st0:
        EE = st0.enter_context
        dL = Sem(st0, nc, "dL")
        dG = Sem(st0, nc, "dG")
        cc1 = Sem(st0, nc, "cc1"); cc2 = Sem(st0, nc, "cc2"); cc3 = Sem(st0, nc, "cc3")
        sPJ = Sem(st0, nc, "sPJ")
        sEV = Sem(st0, nc, "sEV")
        sS = Sem(st0, nc, "sS")
        sE = Sem(st0, nc, "sE")
        sM = Sem(st0, nc, "sM")
        sP = Sem(st0, nc, "sP")
        sDV = Sem(st0, nc, "sDV")
        sA = Sem(st0, nc, "sA")
        sB = Sem(st0, nc, "sB")
        sTR = Sem(st0, nc, "sTR")
        sC = Sem(st0, nc, "sC")
        sV = Sem(st0, nc, "sV")

        ident = EE(nc.sbuf_tensor("ident", [128, 128], BF16))
        m0 = EE(nc.sbuf_tensor("m0", [128, 128], BF16))
        m1 = EE(nc.sbuf_tensor("m1", [128, 128], BF16))
        ones1 = EE(nc.sbuf_tensor("ones1", [1, 128], BF16))
        ones1f = EE(nc.sbuf_tensor("ones1f", [1, 128], F32))
        ones128 = EE(nc.sbuf_tensor("ones128", [128, 1], BF16))
        bqk_t = EE(nc.sbuf_tensor("bqk_t", [128, 16], F32))
        bv_bf = EE(nc.sbuf_tensor("bv_bf", [1, E], BF16))
        bo_bf = EE(nc.sbuf_tensor("bo_bf", [1, E], BF16))
        gb_bf = EE(nc.sbuf_tensor("gb_bf", [1, 8], BF16))
        gwt_bf = EE(nc.sbuf_tensor("gwt_bf", [128, NC, 8], BF16))
        negm = EE(nc.sbuf_tensor("negm", [128, 2], F32))
        negm2 = EE(nc.sbuf_tensor("negm2", [128, 8], F32))
        stt = EE(nc.sbuf_tensor("stt", [128, 16], F32))
        nrm = EE(nc.sbuf_tensor("nrm", [1, 24], F32))
        nrmbf = EE(nc.sbuf_tensor("nrmbf", [1, 2], BF16))
        recb = EE(nc.sbuf_tensor("recb", [128, 8], F32))

        evcnt = {}

        with ExitStack() as st:
            stage = st.enter_context(nc.sbuf_tensor("stage", [128, 128], F32))
            rstage = st.enter_context(nc.sbuf_tensor("rstage", [1, E], F32))
            ph = Prog(nc)
            for (src, dst) in ((identf, ident), (m0in, m0), (m1in, m1)):
                ph.op("sync", lambda s, src=src: s.dma_start(stage[:, :], src),
                      after=[sEV], inc=dL, inc_by=16)
                ph.op("vector", lambda v, dst=dst: v.tensor_copy(dst[:, :], stage[:, :]),
                      after=[dL], inc=sEV)
            ph.op("sync", lambda s: s.dma_start(bqk_t[:, :], bqk), inc=dL, inc_by=16)
            for (src, dst, w_) in ((bvrow, bv_bf, E), (borow, bo_bf, E), (gbrow, gb_bf, 8)):
                ph.op("sync", lambda s, src=src, w_=w_: s.dma_start(rstage[:, 0:w_], src),
                      after=[sEV], inc=dL, inc_by=16)
                ph.op("vector", lambda v, dst=dst, w_=w_: v.tensor_copy(dst[:, 0:w_], rstage[:, 0:w_]),
                      after=[dL], inc=sEV)
            ph.op("gpsimd", lambda g: g.dma_start(gwt_bf[:, :, :], gwt), inc=dG, inc_by=16)
            ph.op("vector", lambda v: v.memset(ones1[:, :], 1.0), after=[dG, dL])
            ph.op("vector", lambda v: v.memset(ones1f[:, :], 1.0))
            ph.op("vector", lambda v: v.memset(ones128[:, :], 1.0))
            ph.run()

        def emit_norm_bound(ph, pk0, src, n512, sqb, acc_col, init):
            first = init
            for j in range(n512):
                ph.op("vector", lambda v, j=j: v.tensor_mul(
                    sqb[:, :], src[:, 512 * j:512 * j + 512],
                    src[:, 512 * j:512 * j + 512]),
                    after=[(sPJ, sPJ.n)], inc=sC)
                ph.op("tensor", lambda t: t.matmul(
                    pk0[0:1, 0:512], ones128[:, :], sqb[:, :], start=True, stop=True),
                    after=[(sC, sC.n), (sEV, evcnt.get(id(pk0), 0))], inc=sPJ)
                ph.op("vector", lambda v: v.reduce_max(
                    nrm[0:1, 2:3], pk0[0:1, 0:512], axis=AX.X), after=[(sPJ, sPJ.n)],
                    inc=sV)
                if first:
                    ph.op("vector", lambda v: v.tensor_copy(
                        nrm[0:1, acc_col:acc_col + 1], nrm[0:1, 2:3]), inc=sV,
                        wafter=[(sV, sV.n)])
                    first = False
                else:
                    ph.op("vector", lambda v: v.tensor_tensor(
                        nrm[0:1, acc_col:acc_col + 1],
                        nrm[0:1, acc_col:acc_col + 1], nrm[0:1, 2:3], ALU.max), inc=sV,
                        wafter=[(sV, sV.n)])
                ph.op("vector", lambda v: v.nop(), wafter=[(sV, sV.n)], inc=sEV)
                evcnt[id(pk0)] = sEV.n

        def emit_make_negm(ph, pk1, qcol, kcol, dst, dst_col):
            ph.op("vector", lambda v: v.tensor_mul(
                nrm[0:1, 4:5], nrm[0:1, qcol:qcol + 1], nrm[0:1, kcol:kcol + 1]),
                wafter=[(sV, sV.n)], inc=sA)
            ph.op("scalar", lambda s: s.activation(nrm[0:1, 5:6], nrm[0:1, 4:5], AF.Ln),
                  after=[(sA, sA.n)])
            ph.op("scalar", lambda s: s.activation(nrm[0:1, 6:7], nrm[0:1, 5:6], AF.Exp,
                                                   scale=0.5), inc=sB)
            ph.op("vector", lambda v: v.tensor_scalar(
                nrm[0:1, 7:8], nrm[0:1, 6:7], -1.0, -1.0, ALU.mult, ALU.add),
                after=[(sB, sB.n)], inc=sV)
            ph.op("vector", lambda v: v.tensor_copy(nrmbf[0:1, 0:1], nrm[0:1, 7:8]),
                  wafter=[(sV, sV.n)], inc=sC)
            ph.op("tensor", lambda t: t.matmul(
                pk1[:, 0:1], ones1[:, :], nrmbf[0:1, 0:1], start=True, stop=True),
                after=[(sC, sC.n), (sEV, evcnt.get(id(pk1), 0))], inc=sPJ)
            ph.op("vector", lambda v: v.tensor_copy(
                dst[:, dst_col:dst_col + 1], pk1[:, 0:1]), after=[(sPJ, sPJ.n)], inc=sEV)
            evcnt[id(pk1)] = sEV.n
            evcnt[("nm", id(dst), dst_col)] = sEV.n

        expcnt = {}

        def emit_attention(ph, is2, hp, Q, K, V, nmten, nmcol, ctx_sb, Sb, ctxf,
                           psb, pk, recb_bf, rbsb, ready):
            for h2 in range(2):
                h = 2 * hp + h2
                for qc in range(2):
                    nkb = 32 if not is2 else (16 if qc == 0 else 32)
                    divwait = (sDV, sDV.n)
                    for kb in range(nkb):
                        bi = kb % 2
                        Sx = Sb[bi]
                        scw = [(sE, expcnt.get(id(Sx), 0))] + ready
                        for jj in range(2):
                            ph.op("tensor", lambda t, kb=kb, jj=jj, Sx=Sx,
                                  h2=h2, qc=qc: t.matmul(
                                Sx[:, 512 * jj:512 * jj + 512],
                                K[64 * h2:64 * h2 + 64, 128 * kb:128 * kb + 128],
                                Q[64 * h2:64 * h2 + 64,
                                  1024 * qc + 512 * jj:1024 * qc + 512 * jj + 512],
                                start=True, stop=True),
                                after=scw, inc=sS)
                        ph.op("scalar", lambda s, bi=bi, Sx=Sx: s.activation(
                            psb[bi][:, :], Sx[:, :], AF.Exp,
                            bias=nmten[:, nmcol:nmcol + 1], scale=1.0),
                            after=[(sS, sS.n), (sP, sP.n),
                                   (sEV, evcnt.get(("nm", id(nmten), nmcol), 0))],
                            inc=sE)
                        expcnt[id(Sx)] = sE.n
                        expv = sE.n
                        if is2:
                            j0 = (kb // 2) if qc == 0 else max(0, (kb - 16) // 2)
                            rr = (kb - 2 * j0) if qc == 0 else (kb - 16 - 2 * j0)
                        else:
                            j0, rr = 0, -1
                        if is2 and rr in (0, 1):
                            mk = m0 if rr == 0 else m1
                            ph.op("vector", lambda v, bi=bi, j0=j0, mk=mk: v.tensor_mul(
                                psb[bi][:, 128 * j0:128 * j0 + 128],
                                psb[bi][:, 128 * j0:128 * j0 + 128], mk[:, :]),
                                after=[(sE, expv)], inc=sM)
                            if j0 > 0:
                                ph.op("vector", lambda v, bi=bi, j0=j0: v.memset(
                                    psb[bi][:, 0:128 * j0], 0.0), inc=sM)
                        mwait = (sM, sM.n)
                        for i in range(2):
                            ph.op("tensor", lambda t, kb=kb, bi=bi, i=i, h2=h2,
                                  nkb=nkb: t.matmul(
                                ctxf[0:65, 512 * i:512 * i + 512],
                                V[:, 130 * kb + 65 * h2:130 * kb + 65 * h2 + 65],
                                psb[bi][:, 512 * i:512 * i + 512],
                                start=(kb == 0), stop=(kb == nkb - 1)),
                                after=[(sE, expv), mwait, divwait] + ready,
                                inc=(sP if i == 1 else None))
                    ph.op("vector", lambda v: v.reciprocal(
                        rbsb[0:1, :], ctxf[64:65, 0:1024]),
                        after=[(sP, sP.n)], inc=sV)
                    ph.op("vector", lambda v: v.tensor_copy(
                        recb_bf[0:1, :], rbsb[0:1, :]),
                        wafter=[(sV, sV.n)], inc=sV)
                    recf = (sV, sV.n)
                    for i in range(2):
                        bank = pk[i]
                        ph.op("tensor", lambda t, i=i, bank=bank: t.matmul(
                            bank[0:64, :], ones1[0:1, 0:64],
                            recb_bf[0:1, 512 * i:512 * i + 512],
                            start=True, stop=True),
                            after=[recf, (sEV, evcnt.get(id(bank), 0))], inc=sPJ)
                    bcv = (sPJ, sPJ.n)
                    for i in range(2):
                        bank = pk[i]
                        ph.op("vector", lambda v, i=i, bank=bank: v.tensor_copy(
                            rbsb[0:64, 512 * i:512 * i + 512], bank[0:64, :]),
                            after=[bcv])
                    for i in range(2):
                        ph.op("vector", lambda v, i=i, h=h, qc=qc: v.tensor_mul(
                            ctx_sb[0:64, h, 1024 * qc + 512 * i:1024 * qc + 512 * i + 512],
                            ctxf[0:64, 512 * i:512 * i + 512],
                            rbsb[0:64, 512 * i:512 * i + 512]),
                            inc=(sDV if i == 1 else sEV))
                    evcnt[id(pk[0])] = sEV.n
                    evcnt[id(pk[1])] = sEV.n

        def emit_ln(ph, t_in, t_out, scr, gamma=None, beta=None, outwait=None,
                    inc=None):
            ph.op("vector", lambda v: v.scalar_tensor_tensor(
                scr[:, :], t_in, 1.0, t_in, ALU.mult, ALU.mult, accum_out=stt[:, 1:2]),
                inc=sV)
            ph.op("vector", lambda v: v.reduce_sum(stt[:, 0:1], t_in, axis=AX.X),
                  inc=sV)
            ph.op("vector", lambda v: v.tensor_scalar(stt[:, 2:3], stt[:, 0:1], 1.0 / E, None, ALU.mult),
                  wafter=[(sV, sV.n)], inc=sV)
            ph.op("vector", lambda v: v.tensor_scalar(stt[:, 3:4], stt[:, 1:2], 1.0 / E, None, ALU.mult),
                  inc=sV)
            ph.op("vector", lambda v: v.tensor_mul(stt[:, 4:5], stt[:, 2:3], stt[:, 2:3]),
                  wafter=[(sV, sV.n)], inc=sV)
            ph.op("vector", lambda v: v.tensor_sub(stt[:, 5:6], stt[:, 3:4], stt[:, 4:5]),
                  wafter=[(sV, sV.n)], inc=sV)
            ph.op("vector", lambda v: v.tensor_scalar(stt[:, 6:7], stt[:, 5:6], LN_EPS, None, ALU.add),
                  wafter=[(sV, sV.n)], inc=sA)
            ph.op("scalar", lambda s: s.activation(stt[:, 7:8], stt[:, 6:7], AF.Ln),
                  after=[(sA, sA.n)])
            ph.op("scalar", lambda s: s.activation(stt[:, 8:9], stt[:, 7:8], AF.Exp, scale=-0.5),
                  inc=sB)
            w = [(sB, sB.n)] + ([outwait] if outwait else [])
            lastop = ph.op if gamma is not None else None
            ph.op("vector", lambda v: v.tensor_scalar(
                t_out, t_in, stt[:, 2:3], stt[:, 8:9], ALU.subtract, ALU.mult), after=w,
                inc=(None if gamma is not None else inc))
            if gamma is not None:
                ph.op("vector", lambda v: v.tensor_mul(t_out, t_out, gamma))
                ph.op("vector", lambda v: v.tensor_add(t_out, t_out, beta), inc=inc)

        def emit_drain(ph):
            ph.op("sync", lambda s: s.nop(), after=[(dL, dL.n)])
            ph.op("gpsimd", lambda g: g.nop(), after=[(dG, dG.n)])

        def emit_qkv(ph, is2, hp, wbuf, wq_ready, Q, K, V, pk, qsrc, qwaits,
                     ksrc, vsrc, kwid, kvfree=None, bufwait=None):
            kvw = [kvfree] if kvfree else []
            bw = [bufwait] if bufwait else []
            for qb in range(4):
                bank = pk[qb % 2]
                for c in range(NC):
                    ph.op("tensor", lambda t, c=c, qb=qb, bank=bank: t.matmul(
                        bank[:, :], wbuf[:, c, 0:128],
                        qsrc[:, c, 512 * qb:512 * qb + 512],
                        start=(c == 0), stop=(c == NC - 1)),
                        after=[wq_ready] + qwaits + [(sEV, evcnt.get(id(bank), 0))],
                        inc=(sPJ if c == NC - 1 else None))
                if is2:
                    ph.op("vector", lambda v, qb=qb, bank=bank: v.tensor_scalar(
                        Q[:, 512 * qb:512 * qb + 512], bank[:, :],
                        bqk_t[:, hp:hp + 1], 0.125, ALU.add, ALU.mult),
                        after=[(sPJ, sPJ.n)] + bw, inc=sEV)
                else:
                    ph.op("vector", lambda v, qb=qb, bank=bank: v.tensor_scalar(
                        Q[:, 512 * qb:512 * qb + 512], bank[:, :], 0.125, None, ALU.mult),
                        after=[(sPJ, sPJ.n)] + bw, inc=sEV)
                evcnt[id(bank)] = sEV.n
            for kb in range(kwid // 512):
                bank = pk[kb % 2]
                for c in range(NC):
                    ph.op("tensor", lambda t, c=c, kb=kb, bank=bank: t.matmul(
                        bank[:, :], wbuf[:, c, 128:256],
                        ksrc[:, c, 512 * kb:512 * kb + 512],
                        start=(c == 0), stop=(c == NC - 1)),
                        after=[(sEV, evcnt.get(id(bank), 0))],
                        inc=(sPJ if c == NC - 1 else None))
                if is2:
                    ph.op("vector", lambda v, kb=kb, bank=bank: v.tensor_scalar(
                        K[:, 512 * kb:512 * kb + 512], bank[:, :],
                        bqk_t[:, 8 + hp:9 + hp], None, ALU.add),
                        after=[(sPJ, sPJ.n)] + kvw + bw, inc=sEV)
                else:
                    ph.op("vector", lambda v, kb=kb, bank=bank: v.tensor_copy(
                        K[:, 512 * kb:512 * kb + 512], bank[:, :]),
                        after=[(sPJ, sPJ.n)] + bw, inc=sEV)
                evcnt[id(bank)] = sEV.n
            nvt = kwid // 128
            ph.op("vector", lambda v, nvt=nvt: v.memset(
                AP(V, 64, [[nvt * 130, 128], [130, nvt], [65, 2]]), 1.0),
                after=kvw + bw, inc=sEV)
            vones = sEV.n
            for tt in range(nvt):
                bank = pk[tt % 2]
                for c in range(NC):
                    ph.op("tensor", lambda t, c=c, tt=tt, bank=bank: t.matmul(
                        bank[:, 0:128], vsrc[:, c, 128 * tt:128 * tt + 128],
                        wbuf[:, c, 256:384],
                        start=(c == 0), stop=(False if is2 else c == NC - 1)),
                        after=[(sEV, evcnt.get(id(bank), 0))],
                        inc=(sPJ if (not is2 and c == NC - 1) else None))
                if is2:
                    ph.op("tensor", lambda t, bank=bank: t.matmul(
                        bank[:, 0:128], ones1[:, :],
                        bv_bf[0:1, 128 * hp:128 * hp + 128],
                        start=False, stop=True), inc=sPJ)
                ph.op("vector", lambda v, tt=tt, bank=bank, nvt=nvt: v.tensor_copy(
                    AP(V, 130 * tt, [[nvt * 130, 128], [65, 2], [1, 64]]),
                    AP(bank.ap().tensor, 0, [[512, 128], [64, 2], [1, 64]])),
                    after=[(sPJ, sPJ.n), (sEV, vones)] + bw, inc=sEV)
                evcnt[id(bank)] = sEV.n

        ctx_scope = ExitStack()
        ctx_sb = ctx_scope.enter_context(nc.sbuf_tensor("ctx_sb", [64, H, TOK], BF16))
        recb_bf = ctx_scope.enter_context(nc.sbuf_tensor("recb_bf", [1, 1024], BF16))
        rbsb = ctx_scope.enter_context(nc.sbuf_tensor("rbsb", [64, 1024], F32))

        with ExitStack() as st:
            x0bf = st.enter_context(nc.sbuf_tensor("x0bf", [128, NC, S], BF16))
            x0obf = st.enter_context(nc.sbuf_tensor("x0obf", [128, NC, TOK], BF16))
            wst = st.enter_context(nc.sbuf_tensor("wst", [128, NC, 384], BF16))
            Qp = st.enter_context(nc.sbuf_tensor("Qp", [128, TOK], BF16))
            Kp = st.enter_context(nc.sbuf_tensor("Kp", [128, S], BF16))
            Vp = st.enter_context(nc.sbuf_tensor("Vp", [128, 32 * 130], BF16))
            psb = [st.enter_context(nc.sbuf_tensor(f"psb{i}", [128, 1024], BF16)) for i in range(2)]
            sqb = st.enter_context(nc.sbuf_tensor("sqb", [128, 512], BF16))
            S0 = st.enter_context(nc.psum_tensor("S0", [128, 1024], F32))
            S1 = st.enter_context(nc.psum_tensor("S1", [128, 1024], F32))
            ctxf = st.enter_context(nc.psum_tensor("ctxf", [128, 1024], F32))
            pk = [st.enter_context(nc.psum_tensor(f"pk{i}", [128, 512], F32)) for i in range(2)]

            ph = Prog(nc)
            for c in range(NC):
                ph.op("gpsimd", lambda g, c=c: g.dma_start(x0bf[:, c, :], x0f[:, c, :]),
                      inc=dG, inc_by=16)
                ph.op("gpsimd", lambda g, c=c: g.dma_start(x0obf[:, c, :], x0fo[:, c, :]),
                      inc=dG, inc_by=16)
            x0rdy = (dG, dG.n)
            for hp in range(8):
                ph.op("gpsimd", lambda g, hp=hp: g.dma_start(
                    wst[:, :, :], wqkv1[:, hp, :, :]),
                    after=[(sPJ, sPJ.n)], inc=dG, inc_by=16)
                wb = wst
                wq_ready = (dG, dG.n)
                emit_qkv(ph, False, hp, wb, wq_ready, Qp, Kp,
                         Vp, pk, x0obf, [x0rdy], x0bf, x0bf, S,
                         bufwait=(sDV, sDV.n))
                emit_norm_bound(ph, pk[0], Qp, 4, sqb, 0, True)
                emit_norm_bound(ph, pk[0], Kp, 8, sqb, 3, True)
                emit_make_negm(ph, pk[1], 0, 3, negm, hp % 2)
                ready = [(sEV, sEV.n)]
                emit_attention(ph, False, hp, Qp, Kp, Vp,
                               negm, hp % 2, ctx_sb, [S0, S1], ctxf, psb, pk,
                               recb_bf, rbsb, ready)
            emit_drain(ph)
            ph.run()

        with ExitStack() as st:
            wo1h = st.enter_context(nc.sbuf_tensor("wo1h", [64, H, E], BF16))
            x0t = [st.enter_context(nc.sbuf_tensor(f"x0t{i}", [128, E], F32)) for i in range(2)]
            tres = st.enter_context(nc.sbuf_tensor("tres", [128, E], F32))
            xout = [st.enter_context(nc.sbuf_tensor(f"xout{i}", [128, E], F32)) for i in range(2)]
            scr = st.enter_context(nc.sbuf_tensor("scr", [128, E], F32))
            po = [st.enter_context(nc.psum_tensor(f"po{i}", [128, 512], F32)) for i in range(2)]

            ph = Prog(nc)
            if DEBUG_TAPS:
                dc1 = nc.dram_tensor("dbgctx1", [64, H * TOK], BF16,
                                     kind="ExternalOutput").ap()
                ph.op("sync", lambda s: s.dma_start(
                    dc1, AP(ctx_sb, 0, [[H * TOK, 64], [1, H * TOK]])),
                    after=[(sDV, sDV.n)], inc=dL, inc_by=16)
            ph.op("gpsimd", lambda g: g.dma_start(wo1h[:, :, :], wout1), inc=dG, inc_by=16)
            wordy = (dG, dG.n)
            divdone = (sDV, sDV.n)
            x0used = {}
            xoutfree = {}
            for t in range(NT):
                ph.op("sync", lambda s, t=t: s.dma_start(
                    x0t[t % 2][:, :], x0tm[128 * t:128 * t + 128, :]),
                    after=[(sEV, x0used.get(t % 2, 0))], inc=dL, inc_by=16)
                x0rdy2 = (dL, dL.n)
                for jj in range(2):
                    bank = po[jj]
                    for h in range(H):
                        ph.op("tensor", lambda tt, t=t, h=h, jj=jj, bank=bank: tt.matmul(
                            bank[:, :], ctx_sb[0:64, h, 128 * t:128 * t + 128],
                            wo1h[0:64, h, 512 * jj:512 * jj + 512],
                            start=(h == 0), stop=(h == H - 1)),
                            after=[wordy, divdone, (sEV, evcnt.get(id(bank), 0))],
                            inc=(sPJ if h == H - 1 else None))
                ph.op("vector", lambda v, t=t: v.tensor_add(
                    tres[:, 0:512], po[0][:, :], x0t[t % 2][:, 0:512]),
                    after=[(sPJ, sPJ.n), x0rdy2])
                ph.op("vector", lambda v, t=t: v.tensor_add(
                    tres[:, 512:1024], po[1][:, :], x0t[t % 2][:, 512:1024]), inc=sEV)
                evcnt[id(po[0])] = sEV.n
                evcnt[id(po[1])] = sEV.n
                x0used[t % 2] = sEV.n
                emit_ln(ph, tres[:, :], xout[t % 2][:, :], scr,
                        outwait=(dL, xoutfree.get(t % 2, 0)), inc=sC)
                xo_ok = (sC, sC.n)
                ph.op("sync", lambda s, t=t: s.dma_start(
                    x1d[128 * t:128 * t + 128, :], xout[t % 2][:, :]),
                    after=[xo_ok], inc=dL, inc_by=16)
                xoutfree[t % 2] = dL.n
            emit_drain(ph)
            ph.run()

        midq = ExitStack()
        Q2 = midq.enter_context(nc.sbuf_tensor("Q2", [128, 8, TOK], BF16))

        with ExitStack() as st:
            ropefm = st.enter_context(nc.sbuf_tensor("ropefm", [128, NC, TOK], BF16))
            x1fm = st.enter_context(nc.sbuf_tensor("x1fm", [128, NC, TOK], BF16))
            x1t = [st.enter_context(nc.sbuf_tensor(f"x1t{i}", [128, E], F32)) for i in range(2)]
            cs = st.enter_context(nc.sbuf_tensor("cs", [128, E], F32)); cs = [cs, cs]
            tmp512 = st.enter_context(nc.sbuf_tensor("tmp512", [128, 512], F32))
            rope_tm = st.enter_context(nc.sbuf_tensor("rope_tm", [128, E], F32))
            ropebf = st.enter_context(nc.sbuf_tensor("ropebf", [128, E], BF16))
            x1bf = ropebf
            wst2 = st.enter_context(nc.sbuf_tensor("wst2", [128, NC, 384], BF16))
            K2p = st.enter_context(nc.sbuf_tensor("K2p", [128, TOK], BF16))
            V2p = st.enter_context(nc.sbuf_tensor("V2p", [128, NT * 130], BF16))
            sqb2 = None
            tp = [st.enter_context(nc.psum_tensor(f"tp2{i}", [128, 128], BF16)) for i in range(2)]
            pk = [st.enter_context(nc.psum_tensor(f"pk2{i}", [128, 512], F32)) for i in range(2)]

            ph = Prog(nc)
            tilefree = {}
            for t in range(NT):
                ph.op("sync", lambda s, t=t: s.dma_start(
                    x1t[t % 2][:, :], x1d[128 * t:128 * t + 128, :]),
                    after=[(sC, tilefree.get(t % 2, 0))], inc=dL, inc_by=16)
                ph.op("sync", lambda s, t=t: s.dma_start(
                    cs[t % 2][:, 0:512], cost[128 * t:128 * t + 128, :]),
                    after=[(sC, tilefree.get("cs", 0))], inc=dL, inc_by=16)
                ph.op("sync", lambda s, t=t: s.dma_start(
                    cs[t % 2][:, 512:1024], sint[128 * t:128 * t + 128, :]),
                    inc=dL, inc_by=16)
                ld = (dL, dL.n)
                xt = x1t[t % 2]
                ct = cs[t % 2][:, 0:512]
                sn = cs[t % 2][:, 512:1024]
                xe = AP(xt, 0, [[E, 128], [2, 512]])
                xo = AP(xt, 1, [[E, 128], [2, 512]])
                re_ = AP(rope_tm, 0, [[E, 128], [2, 512]])
                ro_ = AP(rope_tm, 1, [[E, 128], [2, 512]])
                ph.op("vector", lambda v, xe=xe, ct=ct, re_=re_: v.tensor_mul(re_, xe, ct),
                      after=[ld])
                ph.op("vector", lambda v, xo=xo, sn=sn: v.tensor_mul(tmp512[:, :], xo, sn))
                ph.op("vector", lambda v, re_=re_: v.tensor_sub(re_, re_, tmp512[:, :]))
                ph.op("vector", lambda v, xo=xo, ct=ct: v.tensor_mul(tmp512[:, :], xo, ct))
                ph.op("vector", lambda v, xe=xe, sn=sn, ro_=ro_: v.tensor_mul(ro_, xe, sn))
                ph.op("vector", lambda v, ro_=ro_: v.tensor_add(ro_, ro_, tmp512[:, :]))
                ph.op("vector", lambda v: v.tensor_copy(ropebf[:, :], rope_tm[:, :]),
                      inc=sC)
                tilefree["cs"] = sC.n
                cast_ok = (sC, sC.n)
                for c in range(NC):
                    k = (t * NC + c) % 2
                    ph.op("tensor", lambda tt, c=c, k=k: tt.transpose(
                        tp[k][:, :], ropebf[:, 128 * c:128 * c + 128], ident[:, :]),
                        after=[cast_ok, (sEV, evcnt.get(id(tp[k]), 0))], inc=sTR)
                    ph.op("vector", lambda v, t=t, c=c, k=k: v.tensor_copy(
                        ropefm[:, c, 128 * t:128 * t + 128], tp[k][:, :]),
                        after=[(sTR, sTR.n)], inc=sEV)
                    evcnt[id(tp[k])] = sEV.n
                ph.op("vector", lambda v, xt=xt: v.tensor_copy(x1bf[:, :], xt[:, :]),
                      after=[(sTR, sTR.n)], inc=sC)
                tilefree[t % 2] = sC.n
                cast_ok = (sC, sC.n)
                for c in range(NC):
                    k = (t * NC + c) % 2
                    ph.op("tensor", lambda tt, c=c, k=k: tt.transpose(
                        tp[k][:, :], x1bf[:, 128 * c:128 * c + 128], ident[:, :]),
                        after=[cast_ok, (sEV, evcnt.get(id(tp[k]), 0))], inc=sTR)
                    ph.op("vector", lambda v, t=t, c=c, k=k: v.tensor_copy(
                        x1fm[:, c, 128 * t:128 * t + 128], tp[k][:, :]),
                        after=[(sTR, sTR.n)], inc=sEV)
                    evcnt[id(tp[k])] = sEV.n
                ph.op("vector", lambda v: v.nop(), after=[(sTR, sTR.n)], inc=sC)
                tilefree[t % 2] = sC.n
            trdone = (sEV, sEV.n)
            kv2st = 0
            for hp in range(8):
                ph.op("gpsimd", lambda g, hp=hp: g.dma_start(
                    wst2[:, :, :], wqkv2[:, hp, :, :]),
                    after=[(sPJ, sPJ.n)], inc=dG, inc_by=16)
                wb = wst2
                wq_ready = (dG, dG.n)
                Qsl = AP(Q2, hp * TOK, [[8 * TOK, 128], [1, TOK]])
                emit_qkv(ph, True, hp, wb, wq_ready, Qsl, K2p, V2p, pk,
                         ropefm, [trdone], ropefm, x1fm, TOK,
                         kvfree=(dL, kv2st))
                emit_norm_bound(ph, pk[0], Qsl, 4, ropebf[:, 0:512], 8 + hp, True)
                emit_norm_bound(ph, pk[0], K2p, 4, ropebf[:, 0:512], 1, hp == 0)
                ph.op("sync", lambda s, hp=hp: s.dma_start(
                    k2loc.ap()[128 * hp:128 * hp + 128, :], K2p[:, :]),
                    after=[(sEV, sEV.n)], inc=dL, inc_by=16)
                ph.op("sync", lambda s, hp=hp: s.dma_start(
                    v2loc.ap().rearrange("(t p) f -> p t f", p=128)[:, :, 130 * hp:130 * hp + 130],
                    AP(V2p, 0, [[NT * 130, 128], [130, NT], [1, 130]])),
                    after=[(sEV, sEV.n)], inc=dL, inc_by=16)
                kv2st = dL.n
            ph.op("sync", lambda s: s.dma_start(arin.ap(), nrm[0:1, 1:2]),
                  after=[(sEV, sEV.n)], inc=dL, inc_by=16)
            ph.op("gpsimd", lambda g: g.collective_compute(
                "AllGather", ALU.bypass, RG, [k2loc.ap().opt()], [k2all.ap().opt()]),
                after=[(dL, dL.n)], inc=cc1)
            ph.op("gpsimd", lambda g: g.collective_compute(
                "AllGather", ALU.bypass, RG, [v2loc.ap().opt()], [v2all.ap().opt()]),
                inc=cc2)
            ph.op("gpsimd", lambda g: g.collective_compute(
                "AllReduce", ALU.max, RG, [arin.ap().opt()], [arout.ap().opt()]),
                inc=cc3)
            ph.op("sync", lambda s: s.dma_start(nrm[0:1, 1:2], arout.ap()),
                  after=[(cc3, cc3.n)], inc=dL, inc_by=16)
            ph.op("vector", lambda v: v.nop(), after=[(dL, dL.n)], inc=sC)
            for hp in range(8):
                emit_make_negm(ph, pk[1], 8 + hp, 1, negm2, hp)
            emit_drain(ph)
            ph.run()

        with ExitStack() as st:
            Kb = [st.enter_context(nc.sbuf_tensor(f"Kb{i}", [128, S], BF16)) for i in range(2)]
            Vb = [st.enter_context(nc.sbuf_tensor(f"Vb{i}", [128, 32 * 130], BF16)) for i in range(2)]
            psb = [st.enter_context(nc.sbuf_tensor(f"psb2{i}", [128, 1024], BF16)) for i in range(2)]
            S0 = st.enter_context(nc.psum_tensor("S0b", [128, 1024], F32))
            S1 = st.enter_context(nc.psum_tensor("S1b", [128, 1024], F32))
            ctxf = st.enter_context(nc.psum_tensor("ctxf2", [128, 1024], F32))
            pk = [st.enter_context(nc.psum_tensor(f"pk3{i}", [128, 512], F32)) for i in range(2)]

            ph = Prog(nc)
            kread = {}
            for hp in range(8):
                bufK, bufV = Kb[hp % 2], Vb[hp % 2]
                usedone = (sP, kread.get(hp % 2, 0))
                for pbit in range(2):
                    ph.op("sync", lambda s, hp=hp, pbit=pbit, bufK=bufK: s.dma_start(
                        AP(bufK, 128 * pbit, [[S, 128], [256, 16], [1, 128]]),
                        k2all.ap()[1024 * pbit + 128 * hp:1024 * pbit + 128 * hp + 128, :]
                        .rearrange("p (t f) -> p t f", f=128)),
                        after=[(cc1, cc1.n), usedone], inc=dL, inc_by=16)
                    ph.op("sync", lambda s, hp=hp, pbit=pbit, bufV=bufV: s.dma_start(
                        AP(bufV, 130 * pbit, [[32 * 130, 128], [260, 16], [1, 130]]),
                        v2all.ap()[2048 * pbit:2048 * pbit + 2048, 130 * hp:130 * hp + 130]
                        .rearrange("(t p) f -> p t f", p=128)),
                        after=[(cc2, cc2.n), usedone], inc=dL, inc_by=16)
                kv_ok = (dL, dL.n)
                Qsl = AP(Q2, hp * TOK, [[8 * TOK, 128], [1, TOK]])
                emit_attention(ph, True, hp, Qsl, bufK, bufV, negm2, hp,
                               ctx_sb, [S0, S1], ctxf, psb, pk, recb_bf, rbsb,
                               ready=[kv_ok])
                kread[hp % 2] = sP.n
            if DEBUG_TAPS:
                dq = nc.dram_tensor("dbgQ2", [128, 8 * TOK], BF16,
                                    kind="ExternalOutput").ap()
                dc = nc.dram_tensor("dbgctx2", [64, H * TOK], BF16,
                                    kind="ExternalOutput").ap()
                dx1 = nc.dram_tensor("dbgx1", [TOK, E], F32,
                                     kind="ExternalOutput").ap()
                dnm = nc.dram_tensor("dbgnm", [128, 8], F32,
                                     kind="ExternalOutput").ap()
                ph.op("sync", lambda s: s.dma_start(
                    dq, AP(Q2, 0, [[8 * TOK, 128], [1, 8 * TOK]])),
                    after=[(sDV, sDV.n)], inc=dL, inc_by=16)
                ph.op("sync", lambda s: s.dma_start(
                    dc, AP(ctx_sb, 0, [[H * TOK, 64], [1, H * TOK]])),
                    after=[(sDV, sDV.n)], inc=dL, inc_by=16)
                ph.op("sync", lambda s: s.dma_start(dx1, x1d),
                      inc=dL, inc_by=16)
                ph.op("sync", lambda s: s.dma_start(
                    dnm, negm2[:, :]), inc=dL, inc_by=16)
            emit_drain(ph)
            ph.run()
        midq.close()

        p45 = ExitStack()
        y1fm = p45.enter_context(nc.sbuf_tensor("y1fm", [128, NC, TOK], BF16, side="right"))
        g1 = p45.enter_context(nc.sbuf_tensor("g1t", [128, E], F32, side="right"))
        b1 = p45.enter_context(nc.sbuf_tensor("b1t", [128, E], F32, side="right"))
        g2 = p45.enter_context(nc.sbuf_tensor("g2t", [128, E], F32, side="right"))
        b2 = p45.enter_context(nc.sbuf_tensor("b2t", [128, E], F32, side="right"))
        w_all = p45.enter_context(nc.sbuf_tensor("w_all", [128, NT, 8], F32, side="right"))

        bob = p45.enter_context(nc.sbuf_tensor("bob", [128, E], F32, side="right"))
        with ExitStack() as st:
            wo2h = st.enter_context(nc.sbuf_tensor("wo2h", [64, H, E], BF16))
            x1t = [st.enter_context(nc.sbuf_tensor(f"p4x1t{i}", [128, E], F32)) for i in range(2)]
            tres = st.enter_context(nc.sbuf_tensor("p4tres", [128, E], F32))
            yout = [st.enter_context(nc.sbuf_tensor(f"yout{i}", [128, E], F32)) for i in range(2)]
            ybf = st.enter_context(nc.sbuf_tensor("ybf", [128, E], BF16))
            scr = st.enter_context(nc.sbuf_tensor("p4scr", [128, E], F32))
            rstage = st.enter_context(nc.sbuf_tensor("p4rst", [1, E], F32))
            tp = [st.enter_context(nc.psum_tensor(f"tp4{i}", [128, 128], BF16)) for i in range(2)]
            po = [st.enter_context(nc.psum_tensor(f"po4{i}", [128, 512], F32)) for i in range(2)]
            pg = st.enter_context(nc.psum_tensor("pg4", [128, 512], F32))

            ph = Prog(nc)
            ph.op("gpsimd", lambda g: g.dma_start(wo2h[:, :, :], wout2), inc=dG, inc_by=16)
            wordy = (dG, dG.n)
            rowused = 0
            for (row, dst) in ((ln1g, g1), (ln1b, b1), (ln2g, g2), (ln2b, b2), (borow, bob)):
                ph.op("sync", lambda s, row=row: s.dma_start(rstage[:, :], row),
                      after=[(sEV, rowused)], inc=dL, inc_by=16)
                rowrdy = (dL, dL.n)
                for jj in range(2):
                    ph.op("tensor", lambda t, jj=jj: t.matmul(
                        pg[:, :], ones1f[:, :], rstage[0:1, 512 * jj:512 * jj + 512],
                        start=True, stop=True),
                        after=[rowrdy, (sEV, evcnt.get(id(pg), 0))], inc=sPJ)
                    ph.op("vector", lambda v, dst=dst, jj=jj: v.tensor_copy(
                        dst[:, 512 * jj:512 * jj + 512], pg[:, :]),
                        after=[(sPJ, sPJ.n)], inc=sEV)
                    evcnt[id(pg)] = sEV.n
                rowused = sEV.n
            divdone = (sDV, sDV.n)
            x1used = {}
            youtfree = {}
            ybffree = 0
            for t in range(NT):
                ph.op("sync", lambda s, t=t: s.dma_start(
                    x1t[t % 2][:, :], x1d[128 * t:128 * t + 128, :]),
                    after=[(sEV, x1used.get(t % 2, 0))], inc=dL, inc_by=16)
                x1rdy = (dL, dL.n)
                for jj in range(2):
                    bank = po[jj]
                    for h in range(H):
                        ph.op("tensor", lambda tt, t=t, h=h, jj=jj, bank=bank: tt.matmul(
                            bank[:, :], ctx_sb[0:64, h, 128 * t:128 * t + 128],
                            wo2h[0:64, h, 512 * jj:512 * jj + 512],
                            start=(h == 0), stop=(h == H - 1)),
                            after=[wordy, divdone, (sEV, evcnt.get(id(bank), 0))],
                            inc=(sPJ if h == H - 1 else None))
                ph.op("vector", lambda v, t=t: v.tensor_add(
                    tres[:, 0:512], po[0][:, :], x1t[t % 2][:, 0:512]),
                    after=[(sPJ, sPJ.n), x1rdy])
                ph.op("vector", lambda v, t=t: v.tensor_add(
                    tres[:, 512:1024], po[1][:, :], x1t[t % 2][:, 512:1024]))
                ph.op("vector", lambda v: v.tensor_add(
                    tres[:, :], tres[:, :], bob[:, :]), inc=sEV)
                evcnt[id(po[0])] = sEV.n
                evcnt[id(po[1])] = sEV.n
                x1used[t % 2] = sEV.n
                emit_ln(ph, tres[:, :], yout[t % 2][:, :], scr, g1[:, :], b1[:, :],
                        outwait=(dL, youtfree.get(t % 2, 0)))
                ph.op("vector", lambda v, t=t: v.tensor_copy(ybf[:, :], yout[t % 2][:, :]),
                      after=[(sTR, ybffree)], inc=sC)
                ybf_ok = (sC, sC.n)
                ph.op("sync", lambda s, t=t: s.dma_start(
                    y1d[128 * t:128 * t + 128, :], yout[t % 2][:, :]),
                    after=[ybf_ok], inc=dL, inc_by=16)
                youtfree[t % 2] = dL.n
                for c in range(NC):
                    k = (t * NC + c) % 2
                    ph.op("tensor", lambda tt, t=t, c=c, k=k: tt.transpose(
                        tp[k][:, :], ybf[:, 128 * c:128 * c + 128], ident[:, :]),
                        after=[ybf_ok, (sEV, evcnt.get(id(tp[k]), 0))], inc=sTR)
                    ph.op("vector", lambda v, t=t, c=c, k=k: v.tensor_copy(
                        y1fm[:, c, 128 * t:128 * t + 128], tp[k][:, :]),
                        after=[(sTR, sTR.n)], inc=sEV)
                    evcnt[id(tp[k])] = sEV.n
                ybffree = sTR.n
            emit_drain(ph)
            ph.run()
        ctx_scope.close()

        with ExitStack() as st:
            acc = st.enter_context(nc.sbuf_tensor("acc", [128, NT, E], F32))
            Webf = [st.enter_context(nc.sbuf_tensor(f"Webf{i}", [128, NC, E], BF16)) for i in range(2)]
            be_bf = st.enter_context(nc.sbuf_tensor("be_bf", [1, NE * E], BF16))
            gex = st.enter_context(nc.sbuf_tensor("gex", [128, 8], F32))
            gsc = st.enter_context(nc.sbuf_tensor("gsc", [128, 8], F32))
            pgt = st.enter_context(nc.psum_tensor("pgt", [128, 8], F32))
            pe_ = [st.enter_context(nc.psum_tensor(f"pe{i}", [128, 512], F32)) for i in range(4)]

            ph = Prog(nc)
            ph.op("gpsimd", lambda g: g.dma_start(be_bf[:, :], berow), inc=dG, inc_by=16)
            berdy = (dG, dG.n)
            for t in range(NT):
                for c in range(NC):
                    ph.op("tensor", lambda tt, t=t, c=c: tt.matmul(
                        pgt[:, :], y1fm[:, c, 128 * t:128 * t + 128], gwt_bf[:, c, :],
                        start=(c == 0), stop=False),
                        after=[(sEV, evcnt.get(id(pgt), 0))])
                ph.op("tensor", lambda tt: tt.matmul(
                    pgt[:, :], ones1[:, :], gb_bf[0:1, :], start=False, stop=True),
                    inc=sPJ)
                ph.op("vector", lambda v: v.reduce_max(stt[:, 9:10], pgt[:, :], axis=AX.X),
                      after=[(sPJ, sPJ.n)], inc=sV)
                ph.op("vector", lambda v: v.tensor_scalar(
                    stt[:, 10:11], stt[:, 9:10], -1.0, None, ALU.mult), inc=sA,
                    wafter=[(sV, sV.n)])
                ph.op("scalar", lambda s: s.activation(
                    gex[:, :], pgt[:, :], AF.Exp, bias=stt[:, 10:11],
                    accum_out=stt[:, 11:12]), after=[(sA, sA.n)], inc=sB)
                ph.op("vector", lambda v: v.nop(), after=[(sB, sB.n)], inc=sEV)
                evcnt[id(pgt)] = sEV.n
                ph.op("vector", lambda v: v.reciprocal(stt[:, 12:13], stt[:, 11:12]),
                      inc=sV)
                ph.op("vector", lambda v: v.reduce_max(stt[:, 13:14], gex[:, :], axis=AX.X),
                      inc=sV)
                ph.op("vector", lambda v: v.tensor_scalar(
                    gsc[:, :], gex[:, :], stt[:, 13:14], -1e30, ALU.is_equal, ALU.mult),
                    wafter=[(sV, sV.n)], inc=sV)
                ph.op("vector", lambda v: v.tensor_add(gsc[:, :], gsc[:, :], gex[:, :]),
                      wafter=[(sV, sV.n)], inc=sV)
                ph.op("vector", lambda v: v.reduce_max(stt[:, 14:15], gsc[:, :], axis=AX.X),
                      wafter=[(sV, sV.n)], inc=sV)
                ph.op("vector", lambda v: v.tensor_scalar(
                    gsc[:, :], gex[:, :], stt[:, 14:15], None, ALU.is_ge),
                    wafter=[(sV, sV.n)], inc=sV)
                ph.op("vector", lambda v: v.tensor_mul(gsc[:, :], gsc[:, :], gex[:, :]),
                      wafter=[(sV, sV.n)], inc=sV)
                ph.op("vector", lambda v, t=t: v.tensor_scalar(
                    w_all[:, t, :], gsc[:, :], stt[:, 12:13], None, ALU.mult),
                    wafter=[(sV, sV.n)], inc=sC)
            wrdy = (sC, sC.n)
            ph.op("gpsimd", lambda g: g.dma_start(Webf[0][:, :, :], wexp[0, :, :, :]),
                  inc=dG, inc_by=16)
            for e in range(NE):
                wb = Webf[e % 2]
                werdy = (dG, dG.n)
                if e < NE - 1:
                    ph.op("gpsimd", lambda g, e=e: g.dma_start(
                        Webf[(e + 1) % 2][:, :, :], wexp[e + 1, :, :, :]),
                        after=[(sPJ, sPJ.n)], inc=dG, inc_by=16)
                for t in range(NT):
                    bks = (pe_[2 * (t % 2)], pe_[2 * (t % 2) + 1])
                    for c in range(NC):
                        for jj in range(2):
                            ph.op("tensor", lambda tt, t=t, c=c, jj=jj, wb=wb,
                                  bks=bks: tt.matmul(
                                      bks[jj][:, :], y1fm[:, c, 128 * t:128 * t + 128],
                                      wb[:, c, 512 * jj:512 * jj + 512],
                                      start=(c == 0), stop=False),
                                  after=[werdy, (sEV, evcnt.get(id(bks[jj]), 0))])
                    for jj in range(2):
                        ph.op("tensor", lambda tt, e=e, jj=jj, bks=bks: tt.matmul(
                            bks[jj][:, :], ones1[:, :],
                            be_bf[0:1, E * e + 512 * jj:E * e + 512 * jj + 512],
                            start=False, stop=True), after=[berdy], inc=sPJ)
                    pjv = (sPJ, sPJ.n)
                    for jj in range(2):
                        bank = bks[jj]
                        if e == 0:
                            ph.op("vector", lambda v, t=t, jj=jj, bank=bank: v.tensor_scalar(
                                acc[:, t, 512 * jj:512 * jj + 512], bank[:, :],
                                w_all[:, t, 0:1], None, ALU.mult),
                                after=[pjv, wrdy], inc=sEV)
                        else:
                            ph.op("vector", lambda v, t=t, jj=jj, e=e, bank=bank:
                                  v.scalar_tensor_tensor(
                                      acc[:, t, 512 * jj:512 * jj + 512], bank[:, :],
                                      w_all[:, t, e:e + 1],
                                      acc[:, t, 512 * jj:512 * jj + 512],
                                      ALU.mult, ALU.add),
                                  after=[pjv], inc=sEV)
                        evcnt[id(bank)] = sEV.n
            mdone = (sEV, sEV.n)
            emit_drain(ph)
            ph.run()

            with ExitStack() as st6:
                yt = [st6.enter_context(nc.sbuf_tensor(f"p6y{i}", [128, E], F32)) for i in range(2)]
                xt = [st6.enter_context(nc.sbuf_tensor(f"p6x{i}", [128, E], F32)) for i in range(2)]
                t2 = st6.enter_context(nc.sbuf_tensor("p6t2", [128, E], F32))
                f_ = st6.enter_context(nc.sbuf_tensor("p6f", [128, E], F32))
                t3 = t2
                oout = [st6.enter_context(nc.sbuf_tensor(f"p6o{i}", [128, E], F32)) for i in range(2)]
                scr6 = st6.enter_context(nc.sbuf_tensor("p6scr", [128, E], F32))

                ph = Prog(nc)
                tileused = {}
                ofree = {}
                for t in range(NT):
                    ph.op("sync", lambda s, t=t: s.dma_start(
                        yt[t % 2][:, :], y1d[128 * t:128 * t + 128, :]),
                        after=[(sC, tileused.get(t % 2, 0))], inc=dL, inc_by=16)
                    ph.op("sync", lambda s, t=t: s.dma_start(
                        xt[t % 2][:, :], x1d[128 * t:128 * t + 128, :]),
                        inc=dL, inc_by=16)
                    ld = (dL, dL.n)
                    ph.op("vector", lambda v, t=t: v.tensor_add(
                        t2[:, :], yt[t % 2][:, :], acc[:, t, :]), after=[ld])
                    emit_ln(ph, t2[:, :], f_[:, :], scr6, g2[:, :], b2[:, :])
                    ph.op("vector", lambda v, t=t: v.tensor_add(
                        t3[:, :], xt[t % 2][:, :], f_[:, :]), inc=sC)
                    tileused[t % 2] = sC.n
                    emit_ln(ph, t3[:, :], oout[t % 2][:, :], scr6,
                            outwait=(dL, ofree.get(t % 2, 0)), inc=sC)
                    oo_ok = (sC, sC.n)
                    ph.op("sync", lambda s, t=t: s.dma_start(
                        OUT[128 * t:128 * t + 128, :], oout[t % 2][:, :]),
                        after=[oo_ok], inc=dL, inc_by=16)
                    ofree[t % 2] = dL.n
                ph.op("sync", lambda s: s.nop(), after=[(dL, dL.n)])
                ph.op("gpsimd", lambda g: g.nop(), after=[(dG, dG.n)])
                ph.run()

    return nc


_CACHE = {}


def _host_inputs(inputs):
    x0 = np.asarray(inputs["L0_attn_x"], np.float32)
    w_qkv = np.asarray(inputs["L0_attn_w_qkv"], np.float32)
    w_out = np.asarray(inputs["L0_attn_w_out"], np.float32)
    ipw = np.asarray(inputs["L0_ffn_in_proj_w"], np.float32)
    ipb = np.asarray(inputs["L0_ffn_in_proj_b"], np.float32)
    opw = np.asarray(inputs["L0_ffn_out_proj_w"], np.float32)
    opb = np.asarray(inputs["L0_ffn_out_proj_b"], np.float32)
    gw = np.asarray(inputs["L0_ffn_gate_w"], np.float32)
    gb = np.asarray(inputs["L0_ffn_gate_b"], np.float32)
    ew = np.asarray(inputs["L0_ffn_expert_w"], np.float32)
    eb = np.asarray(inputs["L0_ffn_expert_b"], np.float32)
    l1g = np.asarray(inputs["L0_ffn_ln1_w"], np.float32)
    l1b = np.asarray(inputs["L0_ffn_ln1_b"], np.float32)
    l2g = np.asarray(inputs["L0_ffn_ln2_w"], np.float32)
    l2b = np.asarray(inputs["L0_ffn_ln2_b"], np.float32)

    def pairs_layout(Wq, Wk, Wv):
        blocks = []
        for hp in range(8):
            s = slice(128 * hp, 128 * hp + 128)
            blocks.append(np.concatenate([Wq[:, s], Wk[:, s], Wv[:, s]], axis=1))
        arr = np.stack(blocks, axis=0)
        arr = arr.reshape(8, 8, 128, 384).transpose(2, 0, 1, 3)
        return np.ascontiguousarray(arr)

    wqkv1 = pairs_layout(w_qkv[:, 0:E], w_qkv[:, E:2 * E], w_qkv[:, 2 * E:3 * E])
    wqkv2 = pairs_layout(ipw[0:E].T, ipw[E:2 * E].T, ipw[2 * E:3 * E].T)
    wout1 = np.ascontiguousarray(w_out.reshape(H, 64, E).transpose(1, 0, 2))
    wout2 = np.ascontiguousarray(np.ascontiguousarray(opw.T).reshape(H, 64, E).transpose(1, 0, 2))
    wexp = np.ascontiguousarray(
        ew.transpose(0, 2, 1).reshape(NE, 8, 128, E).transpose(0, 2, 1, 3))
    bq = ipb[0:E]; bk = ipb[E:2 * E]; bv = ipb[2 * E:3 * E]
    bqk_h = np.ascontiguousarray(
        np.concatenate([bq.reshape(8, 128).T, bk.reshape(8, 128).T], axis=1))
    gwt_h = np.ascontiguousarray(gw.T.reshape(8, 128, 8).transpose(1, 0, 2))

    inv = (1.0 / 10000.0 ** (np.arange(0, HD, 2, dtype=np.float32) / HD)).astype(np.float32)
    tri = np.triu(np.ones((128, 128), np.float32))

    common = dict(
        wqkv1=wqkv1, wout1=wout1, wqkv2=wqkv2, wout2=wout2, wexp=wexp,
        bqk=bqk_h,
        bvrow=np.ascontiguousarray(bv[None, :]),
        borow=np.ascontiguousarray(opb[None, :]),
        gwt=gwt_h, gbrow=np.ascontiguousarray(gb[None, :]),
        berow=np.ascontiguousarray(eb.reshape(1, NE * E)),
        ln1g=np.ascontiguousarray(l1g[None, :]), ln1b=np.ascontiguousarray(l1b[None, :]),
        ln2g=np.ascontiguousarray(l2g[None, :]), ln2b=np.ascontiguousarray(l2b[None, :]),
        identf=np.eye(128, dtype=np.float32),
    )

    in_maps = []
    for core in range(8):
        b, par = core // 2, core % 2
        xb = x0[b]
        xT = np.ascontiguousarray(xb.T)
        x0f_ = np.ascontiguousarray(xT.reshape(8, 128, S).transpose(1, 0, 2))
        own = xb.reshape(32, 128, E)[par::2]
        x0tm_ = np.ascontiguousarray(own.reshape(TOK, E))
        x0fo_ = np.ascontiguousarray(
            x0f_.reshape(128, 8, 32, 128)[:, :, par::2, :].reshape(128, 8, TOK))
        pos = np.arange(S).reshape(32, 128)[par::2].reshape(TOK).astype(np.float32)
        fr = pos[:, None] * inv[None, :]
        cost_ = np.ascontiguousarray(np.tile(np.cos(fr).astype(np.float32), (1, H)))
        sint_ = np.ascontiguousarray(np.tile(np.sin(fr).astype(np.float32), (1, H)))
        m0_ = tri if par == 0 else np.ones((128, 128), np.float32)
        m1_ = np.zeros((128, 128), np.float32) if par == 0 else tri
        im = dict(common)
        im.update(x0f=x0f_, x0fo=x0fo_, x0tm=x0tm_, cost=cost_, sint=sint_,
                  m0in=m0_, m1in=m1_)
        in_maps.append(im)
    return in_maps


def kernel(**inputs):
    in_maps = _host_inputs(inputs)
    if "nc" not in _CACHE:
        _CACHE["nc"] = build_program()
    nc = _CACHE["nc"]
    res = run_bass_kernel_spmd(nc, in_maps, core_ids=list(range(8)))
    out = np.empty((B, S, E), np.float32)
    for core in range(8):
        b, par = core // 2, core % 2
        o = np.asarray(res.results[core]["OUT"], np.float32).reshape(NT, 128, E)
        out.reshape(B, 32, 128, E)[b, par::2] = o
    return out


# version 04
# speedup vs baseline: 1.3805x; 1.3805x over previous
import numpy as np
from contextlib import ExitStack

import concourse.bass as bass
import concourse.mybir as mybir
from concourse.bass_utils import run_bass_kernel_spmd

F32 = mybir.dt.float32
BF16 = mybir.dt.bfloat16
AF = mybir.ActivationFunctionType
ALU = mybir.AluOpType
AX = mybir.AxisListType
AP = bass.AP

B, S, E, H, HD, NE = 4, 4096, 1024, 16, 64, 8
NT = 16
TOK = NT * 128
NC = 8
LN_EPS = 1e-5

ENGS = ("sync", "gpsimd", "tensor", "vector", "scalar")
DEBUG_TAPS = False


class Sem:
    __slots__ = ("h", "n", "by")

    def __init__(self, st, nc, name):
        self.h = st.enter_context(nc.semaphore(name))
        self.n = 0
        self.by = {e: 0 for e in ENGS}


class Prog:

    def __init__(self, nc):
        self.nc = nc
        self.seq = {e: [] for e in ENGS}

    def op(self, eng, emit, after=(), inc=None, inc_by=1, wafter=()):
        w = []
        for a in after:
            s, v = (a if isinstance(a, tuple) else (a, a.n))
            if v > 0 and s.by[eng] < v:
                w.append((s, v))
        for a in wafter:
            s, v = (a if isinstance(a, tuple) else (a, a.n))
            if v > 0:
                w.append((s, v))
        if inc is not None:
            inc.n += inc_by
            inc.by[eng] += inc_by
        self.seq[eng].append((w, emit, inc, inc_by))

    def run(self):
        with self.nc.Block() as blk:
            for eng in ENGS:
                ops = self.seq[eng]
                if not ops:
                    continue

                def body(engine, ops=ops):
                    seen = {}
                    for (w, emit, inc, inc_by) in ops:
                        for (s, v) in w:
                            if seen.get(id(s), -1) < v:
                                engine.wait_ge(s.h, v)
                                seen[id(s)] = v
                        inst = emit(engine)
                        if inc is not None:
                            inst.then_inc(inc.h, inc_by)
                getattr(blk, eng)(body)


def build_program():
    nc = bass.Bass("TRN2", target_bir_lowering=False,
                   detect_race_conditions=False)

    def I(name, shape, dt=F32):
        return nc.dram_tensor(name, shape, dt, kind="ExternalInput").ap()

    x0f = I("x0f", [128, NC, S])
    x0fo = I("x0fo", [128, NC, TOK])
    x0tm = I("x0tm", [TOK, E])
    wqkv1 = I("wqkv1", [128, 8, NC, 384])
    wout1 = I("wout1", [64, H, E])
    wqkv2 = I("wqkv2", [128, 8, NC, 384])
    wout2 = I("wout2", [64, H, E])
    wexp = I("wexp", [NE, 128, NC, E])
    bqk = I("bqk", [128, 16])
    bvrow = I("bvrow", [1, E])
    borow = I("borow", [1, E])
    gwt = I("gwt", [128, NC, 8])
    gbrow = I("gbrow", [1, 8])
    berow = I("berow", [1, NE * E])
    ln1g = I("ln1g", [1, E]); ln1b = I("ln1b", [1, E])
    ln2g = I("ln2g", [1, E]); ln2b = I("ln2b", [1, E])
    cost = I("cost", [TOK, 512]); sint = I("sint", [TOK, 512])
    m0in = I("m0in", [128, 128]); m1in = I("m1in", [128, 128])
    identf = I("identf", [128, 128])

    OUT = nc.dram_tensor("OUT", [TOK, E], F32, kind="ExternalOutput").ap()

    x1d = nc.dram_tensor("x1d", [TOK, E], F32, kind="Internal").ap()
    y1d = nc.dram_tensor("y1d", [TOK, E], F32, kind="Internal").ap()
    k2loc = nc.dram_tensor("k2loc", [1024, TOK], BF16, kind="Internal")
    v2loc = nc.dram_tensor("v2loc", [TOK, 8 * 130], BF16, kind="Internal")
    k2all = nc.dram_tensor("k2all", [2048, TOK], BF16, kind="Internal")
    v2all = nc.dram_tensor("v2all", [2 * TOK, 8 * 130], BF16, kind="Internal")
    arin = nc.dram_tensor("arin", [1, 1], F32, kind="Internal")
    arout = nc.dram_tensor("arout", [1, 1], F32, kind="Internal")
    RG = [[0, 1], [2, 3], [4, 5], [6, 7]]

    with ExitStack() as st0:
        EE = st0.enter_context
        dL = Sem(st0, nc, "dL")
        dG = Sem(st0, nc, "dG")
        cc1 = Sem(st0, nc, "cc1"); cc2 = Sem(st0, nc, "cc2"); cc3 = Sem(st0, nc, "cc3")
        sPJ = Sem(st0, nc, "sPJ")
        sEV = Sem(st0, nc, "sEV")
        sS = Sem(st0, nc, "sS")
        sE = Sem(st0, nc, "sE")
        sM = Sem(st0, nc, "sM")
        sP = Sem(st0, nc, "sP")
        sDV = Sem(st0, nc, "sDV")
        sA = Sem(st0, nc, "sA")
        sB = Sem(st0, nc, "sB")
        sTR = Sem(st0, nc, "sTR")
        sC = Sem(st0, nc, "sC")
        sV = Sem(st0, nc, "sV")

        ident = EE(nc.sbuf_tensor("ident", [128, 128], BF16))
        m0 = EE(nc.sbuf_tensor("m0", [128, 128], BF16))
        m1 = EE(nc.sbuf_tensor("m1", [128, 128], BF16))
        ones1 = EE(nc.sbuf_tensor("ones1", [1, 128], BF16))
        ones1f = EE(nc.sbuf_tensor("ones1f", [1, 128], F32))
        ones128 = EE(nc.sbuf_tensor("ones128", [128, 1], BF16))
        bqk_t = EE(nc.sbuf_tensor("bqk_t", [128, 16], F32))
        bv_bf = EE(nc.sbuf_tensor("bv_bf", [1, E], BF16))
        bo_bf = EE(nc.sbuf_tensor("bo_bf", [1, E], BF16))
        gb_bf = EE(nc.sbuf_tensor("gb_bf", [1, 8], BF16))
        gwt_bf = EE(nc.sbuf_tensor("gwt_bf", [128, NC, 8], BF16))
        negm = EE(nc.sbuf_tensor("negm", [128, 2], F32))
        negm2 = EE(nc.sbuf_tensor("negm2", [128, 8], F32))
        stt = EE(nc.sbuf_tensor("stt", [128, 16], F32))
        nrm = EE(nc.sbuf_tensor("nrm", [1, 24], F32))
        nrmbf = EE(nc.sbuf_tensor("nrmbf", [1, 2], BF16))
        recb = EE(nc.sbuf_tensor("recb", [128, 8], F32))

        evcnt = {}

        with ExitStack() as st:
            stage = st.enter_context(nc.sbuf_tensor("stage", [128, 128], F32))
            rstage = st.enter_context(nc.sbuf_tensor("rstage", [1, E], F32))
            ph = Prog(nc)
            for (src, dst) in ((identf, ident), (m0in, m0), (m1in, m1)):
                ph.op("sync", lambda s, src=src: s.dma_start(stage[:, :], src),
                      after=[sEV], inc=dL, inc_by=16)
                ph.op("vector", lambda v, dst=dst: v.tensor_copy(dst[:, :], stage[:, :]),
                      after=[dL], inc=sEV)
            ph.op("sync", lambda s: s.dma_start(bqk_t[:, :], bqk), inc=dL, inc_by=16)
            for (src, dst, w_) in ((bvrow, bv_bf, E), (borow, bo_bf, E), (gbrow, gb_bf, 8)):
                ph.op("sync", lambda s, src=src, w_=w_: s.dma_start(rstage[:, 0:w_], src),
                      after=[sEV], inc=dL, inc_by=16)
                ph.op("vector", lambda v, dst=dst, w_=w_: v.tensor_copy(dst[:, 0:w_], rstage[:, 0:w_]),
                      after=[dL], inc=sEV)
            ph.op("gpsimd", lambda g: g.dma_start(gwt_bf[:, :, :], gwt), inc=dG, inc_by=16)
            ph.op("vector", lambda v: v.memset(ones1[:, :], 1.0), after=[dG, dL])
            ph.op("vector", lambda v: v.memset(ones1f[:, :], 1.0))
            ph.op("vector", lambda v: v.memset(ones128[:, :], 1.0))
            ph.run()

        def emit_norm_bound(ph, pk0, src, n512, sqb, acc_col, init):
            first = init
            for j in range(n512):
                ph.op("vector", lambda v, j=j: v.tensor_mul(
                    sqb[:, :], src[:, 512 * j:512 * j + 512],
                    src[:, 512 * j:512 * j + 512]),
                    after=[(sPJ, sPJ.n)], inc=sC)
                ph.op("tensor", lambda t: t.matmul(
                    pk0[0:1, 0:512], ones128[:, :], sqb[:, :], start=True, stop=True),
                    after=[(sC, sC.n), (sEV, evcnt.get(id(pk0), 0))], inc=sPJ)
                ph.op("vector", lambda v: v.reduce_max(
                    nrm[0:1, 2:3], pk0[0:1, 0:512], axis=AX.X), after=[(sPJ, sPJ.n)],
                    inc=sV)
                if first:
                    ph.op("vector", lambda v: v.tensor_copy(
                        nrm[0:1, acc_col:acc_col + 1], nrm[0:1, 2:3]), inc=sV,
                        wafter=[(sV, sV.n)])
                    first = False
                else:
                    ph.op("vector", lambda v: v.tensor_tensor(
                        nrm[0:1, acc_col:acc_col + 1],
                        nrm[0:1, acc_col:acc_col + 1], nrm[0:1, 2:3], ALU.max), inc=sV,
                        wafter=[(sV, sV.n)])
                ph.op("vector", lambda v: v.nop(), wafter=[(sV, sV.n)], inc=sEV)
                evcnt[id(pk0)] = sEV.n

        def emit_make_negm(ph, pk1, qcol, kcol, dst, dst_col):
            ph.op("vector", lambda v: v.tensor_mul(
                nrm[0:1, 4:5], nrm[0:1, qcol:qcol + 1], nrm[0:1, kcol:kcol + 1]),
                wafter=[(sV, sV.n)], inc=sA)
            ph.op("scalar", lambda s: s.activation(nrm[0:1, 5:6], nrm[0:1, 4:5], AF.Ln),
                  after=[(sA, sA.n)])
            ph.op("scalar", lambda s: s.activation(nrm[0:1, 6:7], nrm[0:1, 5:6], AF.Exp,
                                                   scale=0.5), inc=sB)
            ph.op("vector", lambda v: v.tensor_scalar(
                nrm[0:1, 7:8], nrm[0:1, 6:7], -1.0, -1.0, ALU.mult, ALU.add),
                after=[(sB, sB.n)], inc=sV)
            ph.op("vector", lambda v: v.tensor_copy(nrmbf[0:1, 0:1], nrm[0:1, 7:8]),
                  wafter=[(sV, sV.n)], inc=sC)
            ph.op("tensor", lambda t: t.matmul(
                pk1[:, 0:1], ones1[:, :], nrmbf[0:1, 0:1], start=True, stop=True),
                after=[(sC, sC.n), (sEV, evcnt.get(id(pk1), 0))], inc=sPJ)
            ph.op("vector", lambda v: v.tensor_copy(
                dst[:, dst_col:dst_col + 1], pk1[:, 0:1]), after=[(sPJ, sPJ.n)], inc=sEV)
            evcnt[id(pk1)] = sEV.n
            evcnt[("nm", id(dst), dst_col)] = sEV.n

        expcnt = {}

        def emit_attention(ph, is2, hp, Q, K, V, nmten, nmcol, ctx_sb, Sb, ctxf,
                           psb, pk, recb_bf, rbsb, ready):
            for h2 in range(2):
                h = 2 * hp + h2
                for qc in range(2):
                    nkb = 32 if not is2 else (16 if qc == 0 else 32)
                    divwait = (sDV, sDV.n)
                    pend = []

                    def flush_pv():
                        (kb, bi, expv, mwait) = pend.pop(0)
                        for i in range(2):
                            ph.op("tensor", lambda t, kb=kb, bi=bi, i=i, h2=h2,
                                  nkb=nkb: t.matmul(
                                ctxf[0:65, 512 * i:512 * i + 512],
                                V[:, 130 * kb + 65 * h2:130 * kb + 65 * h2 + 65],
                                psb[bi][:, 512 * i:512 * i + 512],
                                start=(kb == 0), stop=(kb == nkb - 1)),
                                after=[(sE, expv), mwait, divwait] + ready,
                                inc=(sP if i == 1 else None))
                    for kb in range(nkb):
                        bi = kb % 2
                        Sx = Sb[bi]
                        scw = [(sE, expcnt.get(id(Sx), 0))] + ready
                        for jj in range(2):
                            ph.op("tensor", lambda t, kb=kb, jj=jj, Sx=Sx,
                                  h2=h2, qc=qc: t.matmul(
                                Sx[:, 512 * jj:512 * jj + 512],
                                K[64 * h2:64 * h2 + 64, 128 * kb:128 * kb + 128],
                                Q[64 * h2:64 * h2 + 64,
                                  1024 * qc + 512 * jj:1024 * qc + 512 * jj + 512],
                                start=True, stop=True),
                                after=scw, inc=sS)
                        ph.op("scalar", lambda s, bi=bi, Sx=Sx: s.activation(
                            psb[bi][:, :], Sx[:, :], AF.Exp,
                            bias=nmten[:, nmcol:nmcol + 1], scale=1.0),
                            after=[(sS, sS.n), (sP, sP.n),
                                   (sEV, evcnt.get(("nm", id(nmten), nmcol), 0))],
                            inc=sE)
                        expcnt[id(Sx)] = sE.n
                        expv = sE.n
                        if is2:
                            j0 = (kb // 2) if qc == 0 else max(0, (kb - 16) // 2)
                            rr = (kb - 2 * j0) if qc == 0 else (kb - 16 - 2 * j0)
                        else:
                            j0, rr = 0, -1
                        if is2 and rr in (0, 1):
                            mk = m0 if rr == 0 else m1
                            ph.op("vector", lambda v, bi=bi, j0=j0, mk=mk: v.tensor_mul(
                                psb[bi][:, 128 * j0:128 * j0 + 128],
                                psb[bi][:, 128 * j0:128 * j0 + 128], mk[:, :]),
                                after=[(sE, expv)], inc=sM)
                            if j0 > 0:
                                ph.op("vector", lambda v, bi=bi, j0=j0: v.memset(
                                    psb[bi][:, 0:128 * j0], 0.0), inc=sM)
                        mwait = (sM, sM.n)
                        pend.append((kb, bi, expv, mwait))
                        if len(pend) > 1:
                            flush_pv()
                    flush_pv()
                    ph.op("vector", lambda v: v.reciprocal(
                        rbsb[0:1, :], ctxf[64:65, 0:1024]),
                        after=[(sP, sP.n)], inc=sV)
                    ph.op("vector", lambda v: v.tensor_copy(
                        recb_bf[0:1, :], rbsb[0:1, :]),
                        wafter=[(sV, sV.n)], inc=sV)
                    recf = (sV, sV.n)
                    for i in range(2):
                        bank = pk[i]
                        ph.op("tensor", lambda t, i=i, bank=bank: t.matmul(
                            bank[0:64, :], ones1[0:1, 0:64],
                            recb_bf[0:1, 512 * i:512 * i + 512],
                            start=True, stop=True),
                            after=[recf, (sEV, evcnt.get(id(bank), 0))], inc=sPJ)
                    bcv = (sPJ, sPJ.n)
                    for i in range(2):
                        bank = pk[i]
                        ph.op("vector", lambda v, i=i, bank=bank: v.tensor_copy(
                            rbsb[0:64, 512 * i:512 * i + 512], bank[0:64, :]),
                            after=[bcv])
                    for i in range(2):
                        ph.op("vector", lambda v, i=i, h=h, qc=qc: v.tensor_mul(
                            ctx_sb[0:64, h, 1024 * qc + 512 * i:1024 * qc + 512 * i + 512],
                            ctxf[0:64, 512 * i:512 * i + 512],
                            rbsb[0:64, 512 * i:512 * i + 512]),
                            inc=(sDV if i == 1 else sEV))
                    evcnt[id(pk[0])] = sEV.n
                    evcnt[id(pk[1])] = sEV.n

        def emit_ln(ph, t_in, t_out, scr, gamma=None, beta=None, outwait=None,
                    inc=None):
            ph.op("vector", lambda v: v.scalar_tensor_tensor(
                scr[:, :], t_in, 1.0, t_in, ALU.mult, ALU.mult, accum_out=stt[:, 1:2]),
                inc=sV)
            ph.op("vector", lambda v: v.reduce_sum(stt[:, 0:1], t_in, axis=AX.X),
                  inc=sV)
            ph.op("vector", lambda v: v.tensor_scalar(stt[:, 2:3], stt[:, 0:1], 1.0 / E, None, ALU.mult),
                  wafter=[(sV, sV.n)], inc=sV)
            ph.op("vector", lambda v: v.tensor_scalar(stt[:, 3:4], stt[:, 1:2], 1.0 / E, None, ALU.mult),
                  inc=sV)
            ph.op("vector", lambda v: v.tensor_mul(stt[:, 4:5], stt[:, 2:3], stt[:, 2:3]),
                  wafter=[(sV, sV.n)], inc=sV)
            ph.op("vector", lambda v: v.tensor_sub(stt[:, 5:6], stt[:, 3:4], stt[:, 4:5]),
                  wafter=[(sV, sV.n)], inc=sV)
            ph.op("vector", lambda v: v.tensor_scalar(stt[:, 6:7], stt[:, 5:6], LN_EPS, None, ALU.add),
                  wafter=[(sV, sV.n)], inc=sA)
            ph.op("scalar", lambda s: s.activation(stt[:, 7:8], stt[:, 6:7], AF.Ln),
                  after=[(sA, sA.n)])
            ph.op("scalar", lambda s: s.activation(stt[:, 8:9], stt[:, 7:8], AF.Exp, scale=-0.5),
                  inc=sB)
            w = [(sB, sB.n)] + ([outwait] if outwait else [])
            lastop = ph.op if gamma is not None else None
            ph.op("vector", lambda v: v.tensor_scalar(
                t_out, t_in, stt[:, 2:3], stt[:, 8:9], ALU.subtract, ALU.mult), after=w,
                inc=(None if gamma is not None else inc))
            if gamma is not None:
                ph.op("vector", lambda v: v.tensor_mul(t_out, t_out, gamma))
                ph.op("vector", lambda v: v.tensor_add(t_out, t_out, beta), inc=inc)

        def emit_drain(ph):
            ph.op("sync", lambda s: s.nop(), after=[(dL, dL.n)])
            ph.op("gpsimd", lambda g: g.nop(), after=[(dG, dG.n)])

        def emit_qkv(ph, is2, hp, wbuf, wq_ready, Q, K, V, pk, qsrc, qwaits,
                     ksrc, vsrc, kwid, kvfree=None, bufwait=None):
            kvw = [kvfree] if kvfree else []
            bw = [bufwait] if bufwait else []
            for qb in range(4):
                bank = pk[qb % 2]
                for c in range(NC):
                    ph.op("tensor", lambda t, c=c, qb=qb, bank=bank: t.matmul(
                        bank[:, :], wbuf[:, c, 0:128],
                        qsrc[:, c, 512 * qb:512 * qb + 512],
                        start=(c == 0), stop=(c == NC - 1)),
                        after=[wq_ready] + qwaits + [(sEV, evcnt.get(id(bank), 0))],
                        inc=(sPJ if c == NC - 1 else None))
                if is2:
                    ph.op("vector", lambda v, qb=qb, bank=bank: v.tensor_scalar(
                        Q[:, 512 * qb:512 * qb + 512], bank[:, :],
                        bqk_t[:, hp:hp + 1], 0.125, ALU.add, ALU.mult),
                        after=[(sPJ, sPJ.n)] + bw, inc=sEV)
                else:
                    ph.op("vector", lambda v, qb=qb, bank=bank: v.tensor_scalar(
                        Q[:, 512 * qb:512 * qb + 512], bank[:, :], 0.125, None, ALU.mult),
                        after=[(sPJ, sPJ.n)] + bw, inc=sEV)
                evcnt[id(bank)] = sEV.n
            for kb in range(kwid // 512):
                bank = pk[kb % 2]
                for c in range(NC):
                    ph.op("tensor", lambda t, c=c, kb=kb, bank=bank: t.matmul(
                        bank[:, :], wbuf[:, c, 128:256],
                        ksrc[:, c, 512 * kb:512 * kb + 512],
                        start=(c == 0), stop=(c == NC - 1)),
                        after=[(sEV, evcnt.get(id(bank), 0))],
                        inc=(sPJ if c == NC - 1 else None))
                if is2:
                    ph.op("vector", lambda v, kb=kb, bank=bank: v.tensor_scalar(
                        K[:, 512 * kb:512 * kb + 512], bank[:, :],
                        bqk_t[:, 8 + hp:9 + hp], None, ALU.add),
                        after=[(sPJ, sPJ.n)] + kvw + bw, inc=sEV)
                else:
                    ph.op("vector", lambda v, kb=kb, bank=bank: v.tensor_copy(
                        K[:, 512 * kb:512 * kb + 512], bank[:, :]),
                        after=[(sPJ, sPJ.n)] + bw, inc=sEV)
                evcnt[id(bank)] = sEV.n
            nvt = kwid // 128
            ph.op("vector", lambda v, nvt=nvt: v.memset(
                AP(V, 64, [[nvt * 130, 128], [130, nvt], [65, 2]]), 1.0),
                after=kvw + bw, inc=sEV)
            vones = sEV.n
            for tt in range(nvt):
                bank = pk[tt % 2]
                for c in range(NC):
                    ph.op("tensor", lambda t, c=c, tt=tt, bank=bank: t.matmul(
                        bank[:, 0:128], vsrc[:, c, 128 * tt:128 * tt + 128],
                        wbuf[:, c, 256:384],
                        start=(c == 0), stop=(False if is2 else c == NC - 1)),
                        after=[(sEV, evcnt.get(id(bank), 0))],
                        inc=(sPJ if (not is2 and c == NC - 1) else None))
                if is2:
                    ph.op("tensor", lambda t, bank=bank: t.matmul(
                        bank[:, 0:128], ones1[:, :],
                        bv_bf[0:1, 128 * hp:128 * hp + 128],
                        start=False, stop=True), inc=sPJ)
                ph.op("vector", lambda v, tt=tt, bank=bank, nvt=nvt: v.tensor_copy(
                    AP(V, 130 * tt, [[nvt * 130, 128], [65, 2], [1, 64]]),
                    AP(bank.ap().tensor, 0, [[512, 128], [64, 2], [1, 64]])),
                    after=[(sPJ, sPJ.n), (sEV, vones)] + bw, inc=sEV)
                evcnt[id(bank)] = sEV.n

        ctx_scope = ExitStack()
        ctx_sb = ctx_scope.enter_context(nc.sbuf_tensor("ctx_sb", [64, H, TOK], BF16))
        recb_bf = ctx_scope.enter_context(nc.sbuf_tensor("recb_bf", [1, 1024], BF16))
        rbsb = ctx_scope.enter_context(nc.sbuf_tensor("rbsb", [64, 1024], F32))

        with ExitStack() as st:
            x0bf = st.enter_context(nc.sbuf_tensor("x0bf", [128, NC, S], BF16))
            x0obf = st.enter_context(nc.sbuf_tensor("x0obf", [128, NC, TOK], BF16))
            wst = st.enter_context(nc.sbuf_tensor("wst", [128, NC, 384], BF16))
            Qp = st.enter_context(nc.sbuf_tensor("Qp", [128, TOK], BF16))
            Kp = st.enter_context(nc.sbuf_tensor("Kp", [128, S], BF16))
            Vp = st.enter_context(nc.sbuf_tensor("Vp", [128, 32 * 130], BF16))
            psb = [st.enter_context(nc.sbuf_tensor(f"psb{i}", [128, 1024], BF16)) for i in range(2)]
            sqb = st.enter_context(nc.sbuf_tensor("sqb", [128, 512], BF16))
            S0 = st.enter_context(nc.psum_tensor("S0", [128, 1024], F32))
            S1 = st.enter_context(nc.psum_tensor("S1", [128, 1024], F32))
            ctxf = st.enter_context(nc.psum_tensor("ctxf", [128, 1024], F32))
            pk = [st.enter_context(nc.psum_tensor(f"pk{i}", [128, 512], F32)) for i in range(2)]

            ph = Prog(nc)
            for c in range(NC):
                ph.op("gpsimd", lambda g, c=c: g.dma_start(x0bf[:, c, :], x0f[:, c, :]),
                      inc=dG, inc_by=16)
                ph.op("gpsimd", lambda g, c=c: g.dma_start(x0obf[:, c, :], x0fo[:, c, :]),
                      inc=dG, inc_by=16)
            x0rdy = (dG, dG.n)
            for hp in range(8):
                ph.op("gpsimd", lambda g, hp=hp: g.dma_start(
                    wst[:, :, :], wqkv1[:, hp, :, :]),
                    after=[(sPJ, sPJ.n)], inc=dG, inc_by=16)
                wb = wst
                wq_ready = (dG, dG.n)
                emit_qkv(ph, False, hp, wb, wq_ready, Qp, Kp,
                         Vp, pk, x0obf, [x0rdy], x0bf, x0bf, S,
                         bufwait=(sDV, sDV.n))
                emit_norm_bound(ph, pk[0], Qp, 4, sqb, 0, True)
                emit_norm_bound(ph, pk[0], Kp, 8, sqb, 3, True)
                emit_make_negm(ph, pk[1], 0, 3, negm, hp % 2)
                ready = [(sEV, sEV.n)]
                emit_attention(ph, False, hp, Qp, Kp, Vp,
                               negm, hp % 2, ctx_sb, [S0, S1], ctxf, psb, pk,
                               recb_bf, rbsb, ready)
            emit_drain(ph)
            ph.run()

        with ExitStack() as st:
            wo1h = st.enter_context(nc.sbuf_tensor("wo1h", [64, H, E], BF16))
            x0t = [st.enter_context(nc.sbuf_tensor(f"x0t{i}", [128, E], F32)) for i in range(2)]
            tres = st.enter_context(nc.sbuf_tensor("tres", [128, E], F32))
            xout = [st.enter_context(nc.sbuf_tensor(f"xout{i}", [128, E], F32)) for i in range(2)]
            scr = st.enter_context(nc.sbuf_tensor("scr", [128, E], F32))
            po = [st.enter_context(nc.psum_tensor(f"po{i}", [128, 512], F32)) for i in range(2)]

            ph = Prog(nc)
            if DEBUG_TAPS:
                dc1 = nc.dram_tensor("dbgctx1", [64, H * TOK], BF16,
                                     kind="ExternalOutput").ap()
                ph.op("sync", lambda s: s.dma_start(
                    dc1, AP(ctx_sb, 0, [[H * TOK, 64], [1, H * TOK]])),
                    after=[(sDV, sDV.n)], inc=dL, inc_by=16)
            ph.op("gpsimd", lambda g: g.dma_start(wo1h[:, :, :], wout1), inc=dG, inc_by=16)
            wordy = (dG, dG.n)
            divdone = (sDV, sDV.n)
            x0used = {}
            xoutfree = {}
            for t in range(NT):
                ph.op("sync", lambda s, t=t: s.dma_start(
                    x0t[t % 2][:, :], x0tm[128 * t:128 * t + 128, :]),
                    after=[(sEV, x0used.get(t % 2, 0))], inc=dL, inc_by=16)
                x0rdy2 = (dL, dL.n)
                for jj in range(2):
                    bank = po[jj]
                    for h in range(H):
                        ph.op("tensor", lambda tt, t=t, h=h, jj=jj, bank=bank: tt.matmul(
                            bank[:, :], ctx_sb[0:64, h, 128 * t:128 * t + 128],
                            wo1h[0:64, h, 512 * jj:512 * jj + 512],
                            start=(h == 0), stop=(h == H - 1)),
                            after=[wordy, divdone, (sEV, evcnt.get(id(bank), 0))],
                            inc=(sPJ if h == H - 1 else None))
                ph.op("vector", lambda v, t=t: v.tensor_add(
                    tres[:, 0:512], po[0][:, :], x0t[t % 2][:, 0:512]),
                    after=[(sPJ, sPJ.n), x0rdy2])
                ph.op("vector", lambda v, t=t: v.tensor_add(
                    tres[:, 512:1024], po[1][:, :], x0t[t % 2][:, 512:1024]), inc=sEV)
                evcnt[id(po[0])] = sEV.n
                evcnt[id(po[1])] = sEV.n
                x0used[t % 2] = sEV.n
                emit_ln(ph, tres[:, :], xout[t % 2][:, :], scr,
                        outwait=(dL, xoutfree.get(t % 2, 0)), inc=sC)
                xo_ok = (sC, sC.n)
                ph.op("sync", lambda s, t=t: s.dma_start(
                    x1d[128 * t:128 * t + 128, :], xout[t % 2][:, :]),
                    after=[xo_ok], inc=dL, inc_by=16)
                xoutfree[t % 2] = dL.n
            emit_drain(ph)
            ph.run()

        midq = ExitStack()
        Q2 = midq.enter_context(nc.sbuf_tensor("Q2", [128, 8, TOK], BF16))

        with ExitStack() as st:
            ropefm = st.enter_context(nc.sbuf_tensor("ropefm", [128, NC, TOK], BF16))
            x1fm = st.enter_context(nc.sbuf_tensor("x1fm", [128, NC, TOK], BF16))
            x1t = [st.enter_context(nc.sbuf_tensor(f"x1t{i}", [128, E], F32)) for i in range(2)]
            cs = st.enter_context(nc.sbuf_tensor("cs", [128, E], F32)); cs = [cs, cs]
            tmp512 = st.enter_context(nc.sbuf_tensor("tmp512", [128, 512], F32))
            rope_tm = st.enter_context(nc.sbuf_tensor("rope_tm", [128, E], F32))
            ropebf = st.enter_context(nc.sbuf_tensor("ropebf", [128, E], BF16))
            x1bf = ropebf
            wst2 = st.enter_context(nc.sbuf_tensor("wst2", [128, NC, 384], BF16))
            K2p = st.enter_context(nc.sbuf_tensor("K2p", [128, TOK], BF16))
            V2p = st.enter_context(nc.sbuf_tensor("V2p", [128, NT * 130], BF16))
            sqb2 = None
            tp = [st.enter_context(nc.psum_tensor(f"tp2{i}", [128, 128], BF16)) for i in range(2)]
            pk = [st.enter_context(nc.psum_tensor(f"pk2{i}", [128, 512], F32)) for i in range(2)]

            ph = Prog(nc)
            tilefree = {}
            for t in range(NT):
                ph.op("sync", lambda s, t=t: s.dma_start(
                    x1t[t % 2][:, :], x1d[128 * t:128 * t + 128, :]),
                    after=[(sC, tilefree.get(t % 2, 0))], inc=dL, inc_by=16)
                ph.op("sync", lambda s, t=t: s.dma_start(
                    cs[t % 2][:, 0:512], cost[128 * t:128 * t + 128, :]),
                    after=[(sC, tilefree.get("cs", 0))], inc=dL, inc_by=16)
                ph.op("sync", lambda s, t=t: s.dma_start(
                    cs[t % 2][:, 512:1024], sint[128 * t:128 * t + 128, :]),
                    inc=dL, inc_by=16)
                ld = (dL, dL.n)
                xt = x1t[t % 2]
                ct = cs[t % 2][:, 0:512]
                sn = cs[t % 2][:, 512:1024]
                xe = AP(xt, 0, [[E, 128], [2, 512]])
                xo = AP(xt, 1, [[E, 128], [2, 512]])
                re_ = AP(rope_tm, 0, [[E, 128], [2, 512]])
                ro_ = AP(rope_tm, 1, [[E, 128], [2, 512]])
                ph.op("vector", lambda v, xe=xe, ct=ct, re_=re_: v.tensor_mul(re_, xe, ct),
                      after=[ld])
                ph.op("vector", lambda v, xo=xo, sn=sn: v.tensor_mul(tmp512[:, :], xo, sn))
                ph.op("vector", lambda v, re_=re_: v.tensor_sub(re_, re_, tmp512[:, :]))
                ph.op("vector", lambda v, xo=xo, ct=ct: v.tensor_mul(tmp512[:, :], xo, ct))
                ph.op("vector", lambda v, xe=xe, sn=sn, ro_=ro_: v.tensor_mul(ro_, xe, sn))
                ph.op("vector", lambda v, ro_=ro_: v.tensor_add(ro_, ro_, tmp512[:, :]))
                ph.op("vector", lambda v: v.tensor_copy(ropebf[:, :], rope_tm[:, :]),
                      inc=sC)
                tilefree["cs"] = sC.n
                cast_ok = (sC, sC.n)
                for c in range(NC):
                    k = (t * NC + c) % 2
                    ph.op("tensor", lambda tt, c=c, k=k: tt.transpose(
                        tp[k][:, :], ropebf[:, 128 * c:128 * c + 128], ident[:, :]),
                        after=[cast_ok, (sEV, evcnt.get(id(tp[k]), 0))], inc=sTR)
                    ph.op("vector", lambda v, t=t, c=c, k=k: v.tensor_copy(
                        ropefm[:, c, 128 * t:128 * t + 128], tp[k][:, :]),
                        after=[(sTR, sTR.n)], inc=sEV)
                    evcnt[id(tp[k])] = sEV.n
                ph.op("vector", lambda v, xt=xt: v.tensor_copy(x1bf[:, :], xt[:, :]),
                      after=[(sTR, sTR.n)], inc=sC)
                tilefree[t % 2] = sC.n
                cast_ok = (sC, sC.n)
                for c in range(NC):
                    k = (t * NC + c) % 2
                    ph.op("tensor", lambda tt, c=c, k=k: tt.transpose(
                        tp[k][:, :], x1bf[:, 128 * c:128 * c + 128], ident[:, :]),
                        after=[cast_ok, (sEV, evcnt.get(id(tp[k]), 0))], inc=sTR)
                    ph.op("vector", lambda v, t=t, c=c, k=k: v.tensor_copy(
                        x1fm[:, c, 128 * t:128 * t + 128], tp[k][:, :]),
                        after=[(sTR, sTR.n)], inc=sEV)
                    evcnt[id(tp[k])] = sEV.n
                ph.op("vector", lambda v: v.nop(), after=[(sTR, sTR.n)], inc=sC)
                tilefree[t % 2] = sC.n
            trdone = (sEV, sEV.n)
            kv2st = 0
            for hp in range(8):
                ph.op("gpsimd", lambda g, hp=hp: g.dma_start(
                    wst2[:, :, :], wqkv2[:, hp, :, :]),
                    after=[(sPJ, sPJ.n)], inc=dG, inc_by=16)
                wb = wst2
                wq_ready = (dG, dG.n)
                Qsl = AP(Q2, hp * TOK, [[8 * TOK, 128], [1, TOK]])
                emit_qkv(ph, True, hp, wb, wq_ready, Qsl, K2p, V2p, pk,
                         ropefm, [trdone], ropefm, x1fm, TOK,
                         kvfree=(dL, kv2st))
                emit_norm_bound(ph, pk[0], Qsl, 4, ropebf[:, 0:512], 8 + hp, True)
                emit_norm_bound(ph, pk[0], K2p, 4, ropebf[:, 0:512], 1, hp == 0)
                ph.op("sync", lambda s, hp=hp: s.dma_start(
                    k2loc.ap()[128 * hp:128 * hp + 128, :], K2p[:, :]),
                    after=[(sEV, sEV.n)], inc=dL, inc_by=16)
                ph.op("sync", lambda s, hp=hp: s.dma_start(
                    v2loc.ap().rearrange("(t p) f -> p t f", p=128)[:, :, 130 * hp:130 * hp + 130],
                    AP(V2p, 0, [[NT * 130, 128], [130, NT], [1, 130]])),
                    after=[(sEV, sEV.n)], inc=dL, inc_by=16)
                kv2st = dL.n
            ph.op("sync", lambda s: s.dma_start(arin.ap(), nrm[0:1, 1:2]),
                  after=[(sEV, sEV.n)], inc=dL, inc_by=16)
            ph.op("gpsimd", lambda g: g.collective_compute(
                "AllGather", ALU.bypass, RG, [k2loc.ap().opt()], [k2all.ap().opt()]),
                after=[(dL, dL.n)], inc=cc1)
            ph.op("gpsimd", lambda g: g.collective_compute(
                "AllGather", ALU.bypass, RG, [v2loc.ap().opt()], [v2all.ap().opt()]),
                inc=cc2)
            ph.op("gpsimd", lambda g: g.collective_compute(
                "AllReduce", ALU.max, RG, [arin.ap().opt()], [arout.ap().opt()]),
                inc=cc3)
            ph.op("sync", lambda s: s.dma_start(nrm[0:1, 1:2], arout.ap()),
                  after=[(cc3, cc3.n)], inc=dL, inc_by=16)
            ph.op("vector", lambda v: v.nop(), after=[(dL, dL.n)], inc=sC)
            for hp in range(8):
                emit_make_negm(ph, pk[1], 8 + hp, 1, negm2, hp)
            emit_drain(ph)
            ph.run()

        with ExitStack() as st:
            Kb = [st.enter_context(nc.sbuf_tensor(f"Kb{i}", [128, S], BF16)) for i in range(2)]
            Vb = [st.enter_context(nc.sbuf_tensor(f"Vb{i}", [128, 32 * 130], BF16)) for i in range(2)]
            psb = [st.enter_context(nc.sbuf_tensor(f"psb2{i}", [128, 1024], BF16)) for i in range(2)]
            S0 = st.enter_context(nc.psum_tensor("S0b", [128, 1024], F32))
            S1 = st.enter_context(nc.psum_tensor("S1b", [128, 1024], F32))
            ctxf = st.enter_context(nc.psum_tensor("ctxf2", [128, 1024], F32))
            pk = [st.enter_context(nc.psum_tensor(f"pk3{i}", [128, 512], F32)) for i in range(2)]

            ph = Prog(nc)
            kread = {}
            for hp in range(8):
                bufK, bufV = Kb[hp % 2], Vb[hp % 2]
                usedone = (sP, kread.get(hp % 2, 0))
                for pbit in range(2):
                    ph.op("sync", lambda s, hp=hp, pbit=pbit, bufK=bufK: s.dma_start(
                        AP(bufK, 128 * pbit, [[S, 128], [256, 16], [1, 128]]),
                        k2all.ap()[1024 * pbit + 128 * hp:1024 * pbit + 128 * hp + 128, :]
                        .rearrange("p (t f) -> p t f", f=128)),
                        after=[(cc1, cc1.n), usedone], inc=dL, inc_by=16)
                    ph.op("sync", lambda s, hp=hp, pbit=pbit, bufV=bufV: s.dma_start(
                        AP(bufV, 130 * pbit, [[32 * 130, 128], [260, 16], [1, 130]]),
                        v2all.ap()[2048 * pbit:2048 * pbit + 2048, 130 * hp:130 * hp + 130]
                        .rearrange("(t p) f -> p t f", p=128)),
                        after=[(cc2, cc2.n), usedone], inc=dL, inc_by=16)
                kv_ok = (dL, dL.n)
                Qsl = AP(Q2, hp * TOK, [[8 * TOK, 128], [1, TOK]])
                emit_attention(ph, True, hp, Qsl, bufK, bufV, negm2, hp,
                               ctx_sb, [S0, S1], ctxf, psb, pk, recb_bf, rbsb,
                               ready=[kv_ok])
                kread[hp % 2] = sP.n
            if DEBUG_TAPS:
                dq = nc.dram_tensor("dbgQ2", [128, 8 * TOK], BF16,
                                    kind="ExternalOutput").ap()
                dc = nc.dram_tensor("dbgctx2", [64, H * TOK], BF16,
                                    kind="ExternalOutput").ap()
                dx1 = nc.dram_tensor("dbgx1", [TOK, E], F32,
                                     kind="ExternalOutput").ap()
                dnm = nc.dram_tensor("dbgnm", [128, 8], F32,
                                     kind="ExternalOutput").ap()
                ph.op("sync", lambda s: s.dma_start(
                    dq, AP(Q2, 0, [[8 * TOK, 128], [1, 8 * TOK]])),
                    after=[(sDV, sDV.n)], inc=dL, inc_by=16)
                ph.op("sync", lambda s: s.dma_start(
                    dc, AP(ctx_sb, 0, [[H * TOK, 64], [1, H * TOK]])),
                    after=[(sDV, sDV.n)], inc=dL, inc_by=16)
                ph.op("sync", lambda s: s.dma_start(dx1, x1d),
                      inc=dL, inc_by=16)
                ph.op("sync", lambda s: s.dma_start(
                    dnm, negm2[:, :]), inc=dL, inc_by=16)
            emit_drain(ph)
            ph.run()
        midq.close()

        p45 = ExitStack()
        y1fm = p45.enter_context(nc.sbuf_tensor("y1fm", [128, NC, TOK], BF16, side="right"))
        g1 = p45.enter_context(nc.sbuf_tensor("g1t", [128, E], F32, side="right"))
        b1 = p45.enter_context(nc.sbuf_tensor("b1t", [128, E], F32, side="right"))
        g2 = p45.enter_context(nc.sbuf_tensor("g2t", [128, E], F32, side="right"))
        b2 = p45.enter_context(nc.sbuf_tensor("b2t", [128, E], F32, side="right"))
        w_all = p45.enter_context(nc.sbuf_tensor("w_all", [128, NT, 8], F32, side="right"))

        bob = p45.enter_context(nc.sbuf_tensor("bob", [128, E], F32, side="right"))
        with ExitStack() as st:
            wo2h = st.enter_context(nc.sbuf_tensor("wo2h", [64, H, E], BF16))
            x1t = [st.enter_context(nc.sbuf_tensor(f"p4x1t{i}", [128, E], F32)) for i in range(2)]
            tres = st.enter_context(nc.sbuf_tensor("p4tres", [128, E], F32))
            yout = [st.enter_context(nc.sbuf_tensor(f"yout{i}", [128, E], F32)) for i in range(2)]
            ybf = st.enter_context(nc.sbuf_tensor("ybf", [128, E], BF16))
            scr = st.enter_context(nc.sbuf_tensor("p4scr", [128, E], F32))
            rstage = st.enter_context(nc.sbuf_tensor("p4rst", [1, E], F32))
            tp = [st.enter_context(nc.psum_tensor(f"tp4{i}", [128, 128], BF16)) for i in range(2)]
            po = [st.enter_context(nc.psum_tensor(f"po4{i}", [128, 512], F32)) for i in range(2)]
            pg = st.enter_context(nc.psum_tensor("pg4", [128, 512], F32))

            ph = Prog(nc)
            ph.op("gpsimd", lambda g: g.dma_start(wo2h[:, :, :], wout2), inc=dG, inc_by=16)
            wordy = (dG, dG.n)
            rowused = 0
            for (row, dst) in ((ln1g, g1), (ln1b, b1), (ln2g, g2), (ln2b, b2), (borow, bob)):
                ph.op("sync", lambda s, row=row: s.dma_start(rstage[:, :], row),
                      after=[(sEV, rowused)], inc=dL, inc_by=16)
                rowrdy = (dL, dL.n)
                for jj in range(2):
                    ph.op("tensor", lambda t, jj=jj: t.matmul(
                        pg[:, :], ones1f[:, :], rstage[0:1, 512 * jj:512 * jj + 512],
                        start=True, stop=True),
                        after=[rowrdy, (sEV, evcnt.get(id(pg), 0))], inc=sPJ)
                    ph.op("vector", lambda v, dst=dst, jj=jj: v.tensor_copy(
                        dst[:, 512 * jj:512 * jj + 512], pg[:, :]),
                        after=[(sPJ, sPJ.n)], inc=sEV)
                    evcnt[id(pg)] = sEV.n
                rowused = sEV.n
            divdone = (sDV, sDV.n)
            x1used = {}
            youtfree = {}
            ybffree = 0
            for t in range(NT):
                ph.op("sync", lambda s, t=t: s.dma_start(
                    x1t[t % 2][:, :], x1d[128 * t:128 * t + 128, :]),
                    after=[(sEV, x1used.get(t % 2, 0))], inc=dL, inc_by=16)
                x1rdy = (dL, dL.n)
                for jj in range(2):
                    bank = po[jj]
                    for h in range(H):
                        ph.op("tensor", lambda tt, t=t, h=h, jj=jj, bank=bank: tt.matmul(
                            bank[:, :], ctx_sb[0:64, h, 128 * t:128 * t + 128],
                            wo2h[0:64, h, 512 * jj:512 * jj + 512],
                            start=(h == 0), stop=(h == H - 1)),
                            after=[wordy, divdone, (sEV, evcnt.get(id(bank), 0))],
                            inc=(sPJ if h == H - 1 else None))
                ph.op("vector", lambda v, t=t: v.tensor_add(
                    tres[:, 0:512], po[0][:, :], x1t[t % 2][:, 0:512]),
                    after=[(sPJ, sPJ.n), x1rdy])
                ph.op("vector", lambda v, t=t: v.tensor_add(
                    tres[:, 512:1024], po[1][:, :], x1t[t % 2][:, 512:1024]))
                ph.op("vector", lambda v: v.tensor_add(
                    tres[:, :], tres[:, :], bob[:, :]), inc=sEV)
                evcnt[id(po[0])] = sEV.n
                evcnt[id(po[1])] = sEV.n
                x1used[t % 2] = sEV.n
                emit_ln(ph, tres[:, :], yout[t % 2][:, :], scr, g1[:, :], b1[:, :],
                        outwait=(dL, youtfree.get(t % 2, 0)))
                ph.op("vector", lambda v, t=t: v.tensor_copy(ybf[:, :], yout[t % 2][:, :]),
                      after=[(sTR, ybffree)], inc=sC)
                ybf_ok = (sC, sC.n)
                ph.op("sync", lambda s, t=t: s.dma_start(
                    y1d[128 * t:128 * t + 128, :], yout[t % 2][:, :]),
                    after=[ybf_ok], inc=dL, inc_by=16)
                youtfree[t % 2] = dL.n
                for c in range(NC):
                    k = (t * NC + c) % 2
                    ph.op("tensor", lambda tt, t=t, c=c, k=k: tt.transpose(
                        tp[k][:, :], ybf[:, 128 * c:128 * c + 128], ident[:, :]),
                        after=[ybf_ok, (sEV, evcnt.get(id(tp[k]), 0))], inc=sTR)
                    ph.op("vector", lambda v, t=t, c=c, k=k: v.tensor_copy(
                        y1fm[:, c, 128 * t:128 * t + 128], tp[k][:, :]),
                        after=[(sTR, sTR.n)], inc=sEV)
                    evcnt[id(tp[k])] = sEV.n
                ybffree = sTR.n
            emit_drain(ph)
            ph.run()
        ctx_scope.close()

        with ExitStack() as st:
            acc = st.enter_context(nc.sbuf_tensor("acc", [128, NT, E], F32))
            Webf = [st.enter_context(nc.sbuf_tensor(f"Webf{i}", [128, NC, E], BF16)) for i in range(2)]
            be_bf = st.enter_context(nc.sbuf_tensor("be_bf", [1, NE * E], BF16))
            gex = st.enter_context(nc.sbuf_tensor("gex", [128, 8], F32))
            gsc = st.enter_context(nc.sbuf_tensor("gsc", [128, 8], F32))
            pgt = st.enter_context(nc.psum_tensor("pgt", [128, 8], F32))
            pe_ = [st.enter_context(nc.psum_tensor(f"pe{i}", [128, 512], F32)) for i in range(4)]

            ph = Prog(nc)
            ph.op("gpsimd", lambda g: g.dma_start(be_bf[:, :], berow), inc=dG, inc_by=16)
            berdy = (dG, dG.n)
            for t in range(NT):
                for c in range(NC):
                    ph.op("tensor", lambda tt, t=t, c=c: tt.matmul(
                        pgt[:, :], y1fm[:, c, 128 * t:128 * t + 128], gwt_bf[:, c, :],
                        start=(c == 0), stop=False),
                        after=[(sEV, evcnt.get(id(pgt), 0))])
                ph.op("tensor", lambda tt: tt.matmul(
                    pgt[:, :], ones1[:, :], gb_bf[0:1, :], start=False, stop=True),
                    inc=sPJ)
                ph.op("vector", lambda v: v.reduce_max(stt[:, 9:10], pgt[:, :], axis=AX.X),
                      after=[(sPJ, sPJ.n)], inc=sV)
                ph.op("vector", lambda v: v.tensor_scalar(
                    stt[:, 10:11], stt[:, 9:10], -1.0, None, ALU.mult), inc=sA,
                    wafter=[(sV, sV.n)])
                ph.op("scalar", lambda s: s.activation(
                    gex[:, :], pgt[:, :], AF.Exp, bias=stt[:, 10:11],
                    accum_out=stt[:, 11:12]), after=[(sA, sA.n)], inc=sB)
                ph.op("vector", lambda v: v.nop(), after=[(sB, sB.n)], inc=sEV)
                evcnt[id(pgt)] = sEV.n
                ph.op("vector", lambda v: v.reciprocal(stt[:, 12:13], stt[:, 11:12]),
                      inc=sV)
                ph.op("vector", lambda v: v.reduce_max(stt[:, 13:14], gex[:, :], axis=AX.X),
                      inc=sV)
                ph.op("vector", lambda v: v.tensor_scalar(
                    gsc[:, :], gex[:, :], stt[:, 13:14], -1e30, ALU.is_equal, ALU.mult),
                    wafter=[(sV, sV.n)], inc=sV)
                ph.op("vector", lambda v: v.tensor_add(gsc[:, :], gsc[:, :], gex[:, :]),
                      wafter=[(sV, sV.n)], inc=sV)
                ph.op("vector", lambda v: v.reduce_max(stt[:, 14:15], gsc[:, :], axis=AX.X),
                      wafter=[(sV, sV.n)], inc=sV)
                ph.op("vector", lambda v: v.tensor_scalar(
                    gsc[:, :], gex[:, :], stt[:, 14:15], None, ALU.is_ge),
                    wafter=[(sV, sV.n)], inc=sV)
                ph.op("vector", lambda v: v.tensor_mul(gsc[:, :], gsc[:, :], gex[:, :]),
                      wafter=[(sV, sV.n)], inc=sV)
                ph.op("vector", lambda v, t=t: v.tensor_scalar(
                    w_all[:, t, :], gsc[:, :], stt[:, 12:13], None, ALU.mult),
                    wafter=[(sV, sV.n)], inc=sC)
            wrdy = (sC, sC.n)
            ph.op("gpsimd", lambda g: g.dma_start(Webf[0][:, :, :], wexp[0, :, :, :]),
                  inc=dG, inc_by=16)
            for e in range(NE):
                wb = Webf[e % 2]
                werdy = (dG, dG.n)
                if e < NE - 1:
                    ph.op("gpsimd", lambda g, e=e: g.dma_start(
                        Webf[(e + 1) % 2][:, :, :], wexp[e + 1, :, :, :]),
                        after=[(sPJ, sPJ.n)], inc=dG, inc_by=16)
                for t in range(NT):
                    bks = (pe_[2 * (t % 2)], pe_[2 * (t % 2) + 1])
                    for c in range(NC):
                        for jj in range(2):
                            ph.op("tensor", lambda tt, t=t, c=c, jj=jj, wb=wb,
                                  bks=bks: tt.matmul(
                                      bks[jj][:, :], y1fm[:, c, 128 * t:128 * t + 128],
                                      wb[:, c, 512 * jj:512 * jj + 512],
                                      start=(c == 0), stop=False),
                                  after=[werdy, (sEV, evcnt.get(id(bks[jj]), 0))])
                    for jj in range(2):
                        ph.op("tensor", lambda tt, e=e, jj=jj, bks=bks: tt.matmul(
                            bks[jj][:, :], ones1[:, :],
                            be_bf[0:1, E * e + 512 * jj:E * e + 512 * jj + 512],
                            start=False, stop=True), after=[berdy], inc=sPJ)
                    pjv = (sPJ, sPJ.n)
                    for jj in range(2):
                        bank = bks[jj]
                        if e == 0:
                            ph.op("vector", lambda v, t=t, jj=jj, bank=bank: v.tensor_scalar(
                                acc[:, t, 512 * jj:512 * jj + 512], bank[:, :],
                                w_all[:, t, 0:1], None, ALU.mult),
                                after=[pjv, wrdy], inc=sEV)
                        else:
                            ph.op("vector", lambda v, t=t, jj=jj, e=e, bank=bank:
                                  v.scalar_tensor_tensor(
                                      acc[:, t, 512 * jj:512 * jj + 512], bank[:, :],
                                      w_all[:, t, e:e + 1],
                                      acc[:, t, 512 * jj:512 * jj + 512],
                                      ALU.mult, ALU.add),
                                  after=[pjv], inc=sEV)
                        evcnt[id(bank)] = sEV.n
            mdone = (sEV, sEV.n)
            emit_drain(ph)
            ph.run()

            with ExitStack() as st6:
                yt = [st6.enter_context(nc.sbuf_tensor(f"p6y{i}", [128, E], F32)) for i in range(2)]
                xt = [st6.enter_context(nc.sbuf_tensor(f"p6x{i}", [128, E], F32)) for i in range(2)]
                t2 = st6.enter_context(nc.sbuf_tensor("p6t2", [128, E], F32))
                f_ = st6.enter_context(nc.sbuf_tensor("p6f", [128, E], F32))
                t3 = t2
                oout = [st6.enter_context(nc.sbuf_tensor(f"p6o{i}", [128, E], F32)) for i in range(2)]
                scr6 = st6.enter_context(nc.sbuf_tensor("p6scr", [128, E], F32))

                ph = Prog(nc)
                tileused = {}
                ofree = {}
                for t in range(NT):
                    ph.op("sync", lambda s, t=t: s.dma_start(
                        yt[t % 2][:, :], y1d[128 * t:128 * t + 128, :]),
                        after=[(sC, tileused.get(t % 2, 0))], inc=dL, inc_by=16)
                    ph.op("sync", lambda s, t=t: s.dma_start(
                        xt[t % 2][:, :], x1d[128 * t:128 * t + 128, :]),
                        inc=dL, inc_by=16)
                    ld = (dL, dL.n)
                    ph.op("vector", lambda v, t=t: v.tensor_add(
                        t2[:, :], yt[t % 2][:, :], acc[:, t, :]), after=[ld])
                    emit_ln(ph, t2[:, :], f_[:, :], scr6, g2[:, :], b2[:, :])
                    ph.op("vector", lambda v, t=t: v.tensor_add(
                        t3[:, :], xt[t % 2][:, :], f_[:, :]), inc=sC)
                    tileused[t % 2] = sC.n
                    emit_ln(ph, t3[:, :], oout[t % 2][:, :], scr6,
                            outwait=(dL, ofree.get(t % 2, 0)), inc=sC)
                    oo_ok = (sC, sC.n)
                    ph.op("sync", lambda s, t=t: s.dma_start(
                        OUT[128 * t:128 * t + 128, :], oout[t % 2][:, :]),
                        after=[oo_ok], inc=dL, inc_by=16)
                    ofree[t % 2] = dL.n
                ph.op("sync", lambda s: s.nop(), after=[(dL, dL.n)])
                ph.op("gpsimd", lambda g: g.nop(), after=[(dG, dG.n)])
                ph.run()

    return nc


_CACHE = {}


def _host_inputs(inputs):
    x0 = np.asarray(inputs["L0_attn_x"], np.float32)
    w_qkv = np.asarray(inputs["L0_attn_w_qkv"], np.float32)
    w_out = np.asarray(inputs["L0_attn_w_out"], np.float32)
    ipw = np.asarray(inputs["L0_ffn_in_proj_w"], np.float32)
    ipb = np.asarray(inputs["L0_ffn_in_proj_b"], np.float32)
    opw = np.asarray(inputs["L0_ffn_out_proj_w"], np.float32)
    opb = np.asarray(inputs["L0_ffn_out_proj_b"], np.float32)
    gw = np.asarray(inputs["L0_ffn_gate_w"], np.float32)
    gb = np.asarray(inputs["L0_ffn_gate_b"], np.float32)
    ew = np.asarray(inputs["L0_ffn_expert_w"], np.float32)
    eb = np.asarray(inputs["L0_ffn_expert_b"], np.float32)
    l1g = np.asarray(inputs["L0_ffn_ln1_w"], np.float32)
    l1b = np.asarray(inputs["L0_ffn_ln1_b"], np.float32)
    l2g = np.asarray(inputs["L0_ffn_ln2_w"], np.float32)
    l2b = np.asarray(inputs["L0_ffn_ln2_b"], np.float32)

    def pairs_layout(Wq, Wk, Wv):
        blocks = []
        for hp in range(8):
            s = slice(128 * hp, 128 * hp + 128)
            blocks.append(np.concatenate([Wq[:, s], Wk[:, s], Wv[:, s]], axis=1))
        arr = np.stack(blocks, axis=0)
        arr = arr.reshape(8, 8, 128, 384).transpose(2, 0, 1, 3)
        return np.ascontiguousarray(arr)

    wqkv1 = pairs_layout(w_qkv[:, 0:E], w_qkv[:, E:2 * E], w_qkv[:, 2 * E:3 * E])
    wqkv2 = pairs_layout(ipw[0:E].T, ipw[E:2 * E].T, ipw[2 * E:3 * E].T)
    wout1 = np.ascontiguousarray(w_out.reshape(H, 64, E).transpose(1, 0, 2))
    wout2 = np.ascontiguousarray(np.ascontiguousarray(opw.T).reshape(H, 64, E).transpose(1, 0, 2))
    wexp = np.ascontiguousarray(
        ew.transpose(0, 2, 1).reshape(NE, 8, 128, E).transpose(0, 2, 1, 3))
    bq = ipb[0:E]; bk = ipb[E:2 * E]; bv = ipb[2 * E:3 * E]
    bqk_h = np.ascontiguousarray(
        np.concatenate([bq.reshape(8, 128).T, bk.reshape(8, 128).T], axis=1))
    gwt_h = np.ascontiguousarray(gw.T.reshape(8, 128, 8).transpose(1, 0, 2))

    inv = (1.0 / 10000.0 ** (np.arange(0, HD, 2, dtype=np.float32) / HD)).astype(np.float32)
    tri = np.triu(np.ones((128, 128), np.float32))

    common = dict(
        wqkv1=wqkv1, wout1=wout1, wqkv2=wqkv2, wout2=wout2, wexp=wexp,
        bqk=bqk_h,
        bvrow=np.ascontiguousarray(bv[None, :]),
        borow=np.ascontiguousarray(opb[None, :]),
        gwt=gwt_h, gbrow=np.ascontiguousarray(gb[None, :]),
        berow=np.ascontiguousarray(eb.reshape(1, NE * E)),
        ln1g=np.ascontiguousarray(l1g[None, :]), ln1b=np.ascontiguousarray(l1b[None, :]),
        ln2g=np.ascontiguousarray(l2g[None, :]), ln2b=np.ascontiguousarray(l2b[None, :]),
        identf=np.eye(128, dtype=np.float32),
    )

    in_maps = []
    for core in range(8):
        b, par = core // 2, core % 2
        xb = x0[b]
        xT = np.ascontiguousarray(xb.T)
        x0f_ = np.ascontiguousarray(xT.reshape(8, 128, S).transpose(1, 0, 2))
        own = xb.reshape(32, 128, E)[par::2]
        x0tm_ = np.ascontiguousarray(own.reshape(TOK, E))
        x0fo_ = np.ascontiguousarray(
            x0f_.reshape(128, 8, 32, 128)[:, :, par::2, :].reshape(128, 8, TOK))
        pos = np.arange(S).reshape(32, 128)[par::2].reshape(TOK).astype(np.float32)
        fr = pos[:, None] * inv[None, :]
        cost_ = np.ascontiguousarray(np.tile(np.cos(fr).astype(np.float32), (1, H)))
        sint_ = np.ascontiguousarray(np.tile(np.sin(fr).astype(np.float32), (1, H)))
        m0_ = tri if par == 0 else np.ones((128, 128), np.float32)
        m1_ = np.zeros((128, 128), np.float32) if par == 0 else tri
        im = dict(common)
        im.update(x0f=x0f_, x0fo=x0fo_, x0tm=x0tm_, cost=cost_, sint=sint_,
                  m0in=m0_, m1in=m1_)
        in_maps.append(im)
    return in_maps


def kernel(**inputs):
    in_maps = _host_inputs(inputs)
    if "nc" not in _CACHE:
        _CACHE["nc"] = build_program()
    nc = _CACHE["nc"]
    res = run_bass_kernel_spmd(nc, in_maps, core_ids=list(range(8)))
    out = np.empty((B, S, E), np.float32)
    for core in range(8):
        b, par = core // 2, core % 2
        o = np.asarray(res.results[core]["OUT"], np.float32).reshape(NT, 128, E)
        out.reshape(B, 32, 128, E)[b, par::2] = o
    return out


# version 05
# speedup vs baseline: 1.4502x; 1.0505x over previous
import numpy as np
from contextlib import ExitStack

import concourse.bass as bass
import concourse.mybir as mybir
from concourse.bass_utils import run_bass_kernel_spmd

F32 = mybir.dt.float32
BF16 = mybir.dt.bfloat16
AF = mybir.ActivationFunctionType
ALU = mybir.AluOpType
AX = mybir.AxisListType
AP = bass.AP

B, S, E, H, HD, NE = 4, 4096, 1024, 16, 64, 8
NT = 16
TOK = NT * 128
NC = 8
LN_EPS = 1e-5

ENGS = ("sync", "gpsimd", "tensor", "vector", "scalar")
DEBUG_TAPS = False


class Sem:
    __slots__ = ("h", "n", "by")

    def __init__(self, st, nc, name):
        self.h = st.enter_context(nc.semaphore(name))
        self.n = 0
        self.by = {e: 0 for e in ENGS}


class Prog:

    def __init__(self, nc):
        self.nc = nc
        self.seq = {e: [] for e in ENGS}

    def op(self, eng, emit, after=(), inc=None, inc_by=1, wafter=()):
        w = []
        for a in after:
            s, v = (a if isinstance(a, tuple) else (a, a.n))
            if v > 0 and s.by[eng] < v:
                w.append((s, v))
        for a in wafter:
            s, v = (a if isinstance(a, tuple) else (a, a.n))
            if v > 0:
                w.append((s, v))
        if inc is not None:
            inc.n += inc_by
            inc.by[eng] += inc_by
        self.seq[eng].append((w, emit, inc, inc_by))

    def run(self):
        with self.nc.Block() as blk:
            for eng in ENGS:
                ops = self.seq[eng]
                if not ops:
                    continue

                def body(engine, ops=ops):
                    seen = {}
                    for (w, emit, inc, inc_by) in ops:
                        for (s, v) in w:
                            if seen.get(id(s), -1) < v:
                                engine.wait_ge(s.h, v)
                                seen[id(s)] = v
                        inst = emit(engine)
                        if inc is not None:
                            inst.then_inc(inc.h, inc_by)
                getattr(blk, eng)(body)


def build_program():
    nc = bass.Bass("TRN2", target_bir_lowering=False,
                   detect_race_conditions=False)

    def I(name, shape, dt=F32):
        return nc.dram_tensor(name, shape, dt, kind="ExternalInput").ap()

    x0f = I("x0f", [128, NC, S])
    x0fo = I("x0fo", [128, NC, TOK])
    x0tm = I("x0tm", [TOK, E])
    wqkv1 = I("wqkv1", [128, 8, NC, 384])
    wout1 = I("wout1", [64, H, E])
    wqkv2 = I("wqkv2", [128, 8, NC, 384])
    wout2 = I("wout2", [64, H, E])
    wexp = I("wexp", [NE, 128, NC, E])
    bqk = I("bqk", [128, 16])
    bvrow = I("bvrow", [1, E])
    borow = I("borow", [1, E])
    gwt = I("gwt", [128, NC, 8])
    gbrow = I("gbrow", [1, 8])
    berow = I("berow", [1, NE * E])
    ln1g = I("ln1g", [1, E]); ln1b = I("ln1b", [1, E])
    ln2g = I("ln2g", [1, E]); ln2b = I("ln2b", [1, E])
    cost = I("cost", [TOK, 512]); sint = I("sint", [TOK, 512])
    m0in = I("m0in", [128, 128]); m1in = I("m1in", [128, 128])
    identf = I("identf", [128, 128])

    OUT = nc.dram_tensor("OUT", [TOK, E], F32, kind="ExternalOutput").ap()

    x1d = nc.dram_tensor("x1d", [TOK, E], F32, kind="Internal").ap()
    y1d = nc.dram_tensor("y1d", [TOK, E], F32, kind="Internal").ap()
    k2loc = nc.dram_tensor("k2loc", [1024, TOK], BF16, kind="Internal")
    v2loc = nc.dram_tensor("v2loc", [TOK, 8 * 130], BF16, kind="Internal")
    k2all = nc.dram_tensor("k2all", [2048, TOK], BF16, kind="Internal")
    v2all = nc.dram_tensor("v2all", [2 * TOK, 8 * 130], BF16, kind="Internal")
    arin = nc.dram_tensor("arin", [1, 1], F32, kind="Internal")
    arout = nc.dram_tensor("arout", [1, 1], F32, kind="Internal")
    RG = [[0, 1], [2, 3], [4, 5], [6, 7]]

    with ExitStack() as st0:
        EE = st0.enter_context
        dL = Sem(st0, nc, "dL")
        dG = Sem(st0, nc, "dG")
        cc1 = Sem(st0, nc, "cc1"); cc2 = Sem(st0, nc, "cc2"); cc3 = Sem(st0, nc, "cc3")
        sPJ = Sem(st0, nc, "sPJ")
        sEV = Sem(st0, nc, "sEV")
        sS = Sem(st0, nc, "sS")
        sE = Sem(st0, nc, "sE")
        sM = Sem(st0, nc, "sM")
        sP = Sem(st0, nc, "sP")
        sDV = Sem(st0, nc, "sDV")
        sA = Sem(st0, nc, "sA")
        sB = Sem(st0, nc, "sB")
        sTR = Sem(st0, nc, "sTR")
        sC = Sem(st0, nc, "sC")
        sV = Sem(st0, nc, "sV")

        ident = EE(nc.sbuf_tensor("ident", [128, 128], BF16))
        m0 = EE(nc.sbuf_tensor("m0", [128, 128], BF16))
        m1 = EE(nc.sbuf_tensor("m1", [128, 128], BF16))
        ones1 = EE(nc.sbuf_tensor("ones1", [1, 128], BF16))
        ones1f = EE(nc.sbuf_tensor("ones1f", [1, 128], F32))
        ones128 = EE(nc.sbuf_tensor("ones128", [128, 1], BF16))
        bqk_t = EE(nc.sbuf_tensor("bqk_t", [128, 16], F32))
        bv_bf = EE(nc.sbuf_tensor("bv_bf", [1, E], BF16))
        bo_bf = EE(nc.sbuf_tensor("bo_bf", [1, E], BF16))
        gb_bf = EE(nc.sbuf_tensor("gb_bf", [1, 8], BF16))
        gwt_bf = EE(nc.sbuf_tensor("gwt_bf", [128, NC, 8], BF16))
        negm = EE(nc.sbuf_tensor("negm", [128, 2], F32))
        negm2 = EE(nc.sbuf_tensor("negm2", [128, 8], F32))
        stt = EE(nc.sbuf_tensor("stt", [128, 16], F32))
        nrm = EE(nc.sbuf_tensor("nrm", [1, 24], F32))
        nrmbf = EE(nc.sbuf_tensor("nrmbf", [1, 2], BF16))
        recb = EE(nc.sbuf_tensor("recb", [128, 8], F32))

        evcnt = {}

        with ExitStack() as st:
            stage = st.enter_context(nc.sbuf_tensor("stage", [128, 128], F32))
            rstage = st.enter_context(nc.sbuf_tensor("rstage", [1, E], F32))
            ph = Prog(nc)
            for (src, dst) in ((identf, ident), (m0in, m0), (m1in, m1)):
                ph.op("sync", lambda s, src=src: s.dma_start(stage[:, :], src),
                      after=[sEV], inc=dL, inc_by=16)
                ph.op("vector", lambda v, dst=dst: v.tensor_copy(dst[:, :], stage[:, :]),
                      after=[dL], inc=sEV)
            ph.op("sync", lambda s: s.dma_start(bqk_t[:, :], bqk), inc=dL, inc_by=16)
            for (src, dst, w_) in ((bvrow, bv_bf, E), (borow, bo_bf, E), (gbrow, gb_bf, 8)):
                ph.op("sync", lambda s, src=src, w_=w_: s.dma_start(rstage[:, 0:w_], src),
                      after=[sEV], inc=dL, inc_by=16)
                ph.op("vector", lambda v, dst=dst, w_=w_: v.tensor_copy(dst[:, 0:w_], rstage[:, 0:w_]),
                      after=[dL], inc=sEV)
            ph.op("gpsimd", lambda g: g.dma_start(gwt_bf[:, :, :], gwt), inc=dG, inc_by=16)
            ph.op("vector", lambda v: v.memset(ones1[:, :], 1.0), after=[dG, dL])
            ph.op("vector", lambda v: v.memset(ones1f[:, :], 1.0))
            ph.op("vector", lambda v: v.memset(ones128[:, :], 1.0))
            ph.run()

        def emit_norm_bound(ph, pk0, src, n512, sqb, acc_col, init):
            first = init
            for j in range(n512):
                ph.op("vector", lambda v, j=j: v.tensor_mul(
                    sqb[:, :], src[:, 512 * j:512 * j + 512],
                    src[:, 512 * j:512 * j + 512]),
                    after=[(sPJ, sPJ.n)], inc=sC)
                ph.op("tensor", lambda t: t.matmul(
                    pk0[0:1, 0:512], ones128[:, :], sqb[:, :], start=True, stop=True),
                    after=[(sC, sC.n), (sEV, evcnt.get(id(pk0), 0))], inc=sPJ)
                ph.op("vector", lambda v: v.reduce_max(
                    nrm[0:1, 2:3], pk0[0:1, 0:512], axis=AX.X), after=[(sPJ, sPJ.n)],
                    inc=sV)
                if first:
                    ph.op("vector", lambda v: v.tensor_copy(
                        nrm[0:1, acc_col:acc_col + 1], nrm[0:1, 2:3]), inc=sV,
                        wafter=[(sV, sV.n)])
                    first = False
                else:
                    ph.op("vector", lambda v: v.tensor_tensor(
                        nrm[0:1, acc_col:acc_col + 1],
                        nrm[0:1, acc_col:acc_col + 1], nrm[0:1, 2:3], ALU.max), inc=sV,
                        wafter=[(sV, sV.n)])
                ph.op("vector", lambda v: v.nop(), wafter=[(sV, sV.n)], inc=sEV)
                evcnt[id(pk0)] = sEV.n

        def emit_make_negm(ph, pk1, qcol, kcol, dst, dst_col):
            ph.op("vector", lambda v: v.tensor_mul(
                nrm[0:1, 4:5], nrm[0:1, qcol:qcol + 1], nrm[0:1, kcol:kcol + 1]),
                wafter=[(sV, sV.n)], inc=sA)
            ph.op("scalar", lambda s: s.activation(nrm[0:1, 5:6], nrm[0:1, 4:5], AF.Ln),
                  after=[(sA, sA.n)])
            ph.op("scalar", lambda s: s.activation(nrm[0:1, 6:7], nrm[0:1, 5:6], AF.Exp,
                                                   scale=0.5), inc=sB)
            ph.op("vector", lambda v: v.tensor_scalar(
                nrm[0:1, 7:8], nrm[0:1, 6:7], -1.0, -1.0, ALU.mult, ALU.add),
                after=[(sB, sB.n)], inc=sV)
            ph.op("vector", lambda v: v.tensor_copy(nrmbf[0:1, 0:1], nrm[0:1, 7:8]),
                  wafter=[(sV, sV.n)], inc=sC)
            ph.op("tensor", lambda t: t.matmul(
                pk1[:, 0:1], ones1[:, :], nrmbf[0:1, 0:1], start=True, stop=True),
                after=[(sC, sC.n), (sEV, evcnt.get(id(pk1), 0))], inc=sPJ)
            ph.op("vector", lambda v: v.tensor_copy(
                dst[:, dst_col:dst_col + 1], pk1[:, 0:1]), after=[(sPJ, sPJ.n)], inc=sEV)
            evcnt[id(pk1)] = sEV.n
            evcnt[("nm", id(dst), dst_col)] = sEV.n

        expcnt = {}

        def emit_attention(ph, is2, hp, Q, K, V, nmten, nmcol, ctx_sb, Sb, ctxf,
                           psb, pk, recb_bf, rbsb, ready):
            for h2 in range(2):
                h = 2 * hp + h2
                for qc in range(2):
                    nkb = 32 if not is2 else (16 if qc == 0 else 32)
                    divwait = (sDV, sDV.n)
                    pend = []

                    def flush_pv():
                        (kb, bi, expv, mwait) = pend.pop(0)
                        for i in range(2):
                            ph.op("tensor", lambda t, kb=kb, bi=bi, i=i, h2=h2,
                                  nkb=nkb: t.matmul(
                                ctxf[0:65, 512 * i:512 * i + 512],
                                V[:, 130 * kb + 65 * h2:130 * kb + 65 * h2 + 65],
                                psb[bi][:, 512 * i:512 * i + 512],
                                start=(kb == 0), stop=(kb == nkb - 1)),
                                after=[(sE, expv), mwait, divwait] + ready,
                                inc=(sP if i == 1 else None))
                    for kb in range(nkb):
                        bi = kb % 2
                        Sx = Sb[bi]
                        scw = [(sE, expcnt.get(id(Sx), 0))] + ready
                        for jj in range(2):
                            ph.op("tensor", lambda t, kb=kb, jj=jj, Sx=Sx,
                                  h2=h2, qc=qc: t.matmul(
                                Sx[:, 512 * jj:512 * jj + 512],
                                K[64 * h2:64 * h2 + 64, 128 * kb:128 * kb + 128],
                                Q[64 * h2:64 * h2 + 64,
                                  1024 * qc + 512 * jj:1024 * qc + 512 * jj + 512],
                                start=True, stop=True),
                                after=scw, inc=sS)
                        ph.op("scalar", lambda s, bi=bi, Sx=Sx: s.activation(
                            psb[bi][:, :], Sx[:, :], AF.Exp,
                            bias=nmten[:, nmcol:nmcol + 1], scale=1.0),
                            after=[(sS, sS.n), (sP, sP.n),
                                   (sEV, evcnt.get(("nm", id(nmten), nmcol), 0))],
                            inc=sE)
                        expcnt[id(Sx)] = sE.n
                        expv = sE.n
                        if is2:
                            j0 = (kb // 2) if qc == 0 else max(0, (kb - 16) // 2)
                            rr = (kb - 2 * j0) if qc == 0 else (kb - 16 - 2 * j0)
                        else:
                            j0, rr = 0, -1
                        if is2 and rr in (0, 1):
                            mk = m0 if rr == 0 else m1
                            ph.op("vector", lambda v, bi=bi, j0=j0, mk=mk: v.tensor_mul(
                                psb[bi][:, 128 * j0:128 * j0 + 128],
                                psb[bi][:, 128 * j0:128 * j0 + 128], mk[:, :]),
                                after=[(sE, expv)], inc=sM)
                            if j0 > 0:
                                ph.op("vector", lambda v, bi=bi, j0=j0: v.memset(
                                    psb[bi][:, 0:128 * j0], 0.0), inc=sM)
                        mwait = (sM, sM.n)
                        pend.append((kb, bi, expv, mwait))
                        if len(pend) > 1:
                            flush_pv()
                    flush_pv()
                    ph.op("scalar", lambda s: s.activation(
                        rbsb[0:1, :], ctxf[64:65, 0:1024], AF.Ln),
                        after=[(sP, sP.n)])
                    ph.op("scalar", lambda s: s.activation(
                        recb_bf[0:1, :], rbsb[0:1, :], AF.Exp, scale=-1.0),
                        inc=sB)
                    recf = (sB, sB.n)
                    for i in range(2):
                        bank = pk[i]
                        ph.op("tensor", lambda t, i=i, bank=bank: t.matmul(
                            bank[0:64, :], ones1[0:1, 0:64],
                            recb_bf[0:1, 512 * i:512 * i + 512],
                            start=True, stop=True),
                            after=[recf, (sEV, evcnt.get(id(bank), 0))], inc=sPJ)
                    bcv = (sPJ, sPJ.n)
                    for i in range(2):
                        bank = pk[i]
                        ph.op("vector", lambda v, i=i, bank=bank: v.tensor_copy(
                            rbsb[0:64, 512 * i:512 * i + 512], bank[0:64, :]),
                            after=[bcv])
                    for i in range(2):
                        ph.op("vector", lambda v, i=i, h=h, qc=qc: v.tensor_mul(
                            ctx_sb[0:64, h, 1024 * qc + 512 * i:1024 * qc + 512 * i + 512],
                            ctxf[0:64, 512 * i:512 * i + 512],
                            rbsb[0:64, 512 * i:512 * i + 512]),
                            inc=(sDV if i == 1 else sEV))
                    evcnt[id(pk[0])] = sEV.n
                    evcnt[id(pk[1])] = sEV.n

        def emit_ln(ph, t_in, t_out, scr, gamma=None, beta=None, outwait=None,
                    inc=None):
            ph.op("vector", lambda v: v.scalar_tensor_tensor(
                scr[:, :], t_in, 1.0, t_in, ALU.mult, ALU.mult, accum_out=stt[:, 1:2]),
                inc=sV)
            ph.op("vector", lambda v: v.reduce_sum(stt[:, 0:1], t_in, axis=AX.X),
                  inc=sV)
            ph.op("vector", lambda v: v.tensor_scalar(stt[:, 2:3], stt[:, 0:1], 1.0 / E, None, ALU.mult),
                  wafter=[(sV, sV.n)], inc=sV)
            ph.op("vector", lambda v: v.tensor_scalar(stt[:, 3:4], stt[:, 1:2], 1.0 / E, None, ALU.mult),
                  inc=sV)
            ph.op("vector", lambda v: v.tensor_mul(stt[:, 4:5], stt[:, 2:3], stt[:, 2:3]),
                  wafter=[(sV, sV.n)], inc=sV)
            ph.op("vector", lambda v: v.tensor_sub(stt[:, 5:6], stt[:, 3:4], stt[:, 4:5]),
                  wafter=[(sV, sV.n)], inc=sV)
            ph.op("vector", lambda v: v.tensor_scalar(stt[:, 6:7], stt[:, 5:6], LN_EPS, None, ALU.add),
                  wafter=[(sV, sV.n)], inc=sA)
            ph.op("scalar", lambda s: s.activation(stt[:, 7:8], stt[:, 6:7], AF.Ln),
                  after=[(sA, sA.n)])
            ph.op("scalar", lambda s: s.activation(stt[:, 8:9], stt[:, 7:8], AF.Exp, scale=-0.5),
                  inc=sB)
            w = [(sB, sB.n)] + ([outwait] if outwait else [])
            lastop = ph.op if gamma is not None else None
            ph.op("vector", lambda v: v.tensor_scalar(
                t_out, t_in, stt[:, 2:3], stt[:, 8:9], ALU.subtract, ALU.mult), after=w,
                inc=(None if gamma is not None else inc))
            if gamma is not None:
                ph.op("vector", lambda v: v.tensor_mul(t_out, t_out, gamma))
                ph.op("vector", lambda v: v.tensor_add(t_out, t_out, beta), inc=inc)

        def emit_drain(ph):
            ph.op("sync", lambda s: s.nop(), after=[(dL, dL.n)])
            ph.op("gpsimd", lambda g: g.nop(), after=[(dG, dG.n)])

        def emit_qkv(ph, is2, hp, wbuf, wq_ready, Q, K, V, pk, qsrc, qwaits,
                     ksrc, vsrc, kwid, kvfree=None, bufwait=None):
            kvw = [kvfree] if kvfree else []
            bw = [bufwait] if bufwait else []
            for qb in range(4):
                bank = pk[qb % 2]
                for c in range(NC):
                    ph.op("tensor", lambda t, c=c, qb=qb, bank=bank: t.matmul(
                        bank[:, :], wbuf[:, c, 0:128],
                        qsrc[:, c, 512 * qb:512 * qb + 512],
                        start=(c == 0), stop=(c == NC - 1)),
                        after=[wq_ready] + qwaits + [(sEV, evcnt.get(id(bank), 0))],
                        inc=(sPJ if c == NC - 1 else None))
                if is2:
                    ph.op("vector", lambda v, qb=qb, bank=bank: v.tensor_scalar(
                        Q[:, 512 * qb:512 * qb + 512], bank[:, :],
                        bqk_t[:, hp:hp + 1], 0.125, ALU.add, ALU.mult),
                        after=[(sPJ, sPJ.n)] + bw, inc=sEV)
                else:
                    ph.op("vector", lambda v, qb=qb, bank=bank: v.tensor_scalar(
                        Q[:, 512 * qb:512 * qb + 512], bank[:, :], 0.125, None, ALU.mult),
                        after=[(sPJ, sPJ.n)] + bw, inc=sEV)
                evcnt[id(bank)] = sEV.n
            for kb in range(kwid // 512):
                bank = pk[kb % 2]
                for c in range(NC):
                    ph.op("tensor", lambda t, c=c, kb=kb, bank=bank: t.matmul(
                        bank[:, :], wbuf[:, c, 128:256],
                        ksrc[:, c, 512 * kb:512 * kb + 512],
                        start=(c == 0), stop=(c == NC - 1)),
                        after=[(sEV, evcnt.get(id(bank), 0))],
                        inc=(sPJ if c == NC - 1 else None))
                if is2:
                    ph.op("vector", lambda v, kb=kb, bank=bank: v.tensor_scalar(
                        K[:, 512 * kb:512 * kb + 512], bank[:, :],
                        bqk_t[:, 8 + hp:9 + hp], None, ALU.add),
                        after=[(sPJ, sPJ.n)] + kvw + bw, inc=sEV)
                else:
                    ph.op("vector", lambda v, kb=kb, bank=bank: v.tensor_copy(
                        K[:, 512 * kb:512 * kb + 512], bank[:, :]),
                        after=[(sPJ, sPJ.n)] + bw, inc=sEV)
                evcnt[id(bank)] = sEV.n
            nvt = kwid // 128
            ph.op("vector", lambda v, nvt=nvt: v.memset(
                AP(V, 64, [[nvt * 130, 128], [130, nvt], [65, 2]]), 1.0),
                after=kvw + bw, inc=sEV)
            vones = sEV.n
            for tt in range(nvt):
                bank = pk[tt % 2]
                for c in range(NC):
                    ph.op("tensor", lambda t, c=c, tt=tt, bank=bank: t.matmul(
                        bank[:, 0:128], vsrc[:, c, 128 * tt:128 * tt + 128],
                        wbuf[:, c, 256:384],
                        start=(c == 0), stop=(False if is2 else c == NC - 1)),
                        after=[(sEV, evcnt.get(id(bank), 0))],
                        inc=(sPJ if (not is2 and c == NC - 1) else None))
                if is2:
                    ph.op("tensor", lambda t, bank=bank: t.matmul(
                        bank[:, 0:128], ones1[:, :],
                        bv_bf[0:1, 128 * hp:128 * hp + 128],
                        start=False, stop=True), inc=sPJ)
                ph.op("vector", lambda v, tt=tt, bank=bank, nvt=nvt: v.tensor_copy(
                    AP(V, 130 * tt, [[nvt * 130, 128], [65, 2], [1, 64]]),
                    AP(bank.ap().tensor, 0, [[512, 128], [64, 2], [1, 64]])),
                    after=[(sPJ, sPJ.n), (sEV, vones)] + bw, inc=sEV)
                evcnt[id(bank)] = sEV.n

        ctx_scope = ExitStack()
        ctx_sb = ctx_scope.enter_context(nc.sbuf_tensor("ctx_sb", [64, H, TOK], BF16))
        recb_bf = ctx_scope.enter_context(nc.sbuf_tensor("recb_bf", [1, 1024], BF16))
        rbsb = ctx_scope.enter_context(nc.sbuf_tensor("rbsb", [64, 1024], F32))

        with ExitStack() as st:
            x0bf = st.enter_context(nc.sbuf_tensor("x0bf", [128, NC, S], BF16))
            x0obf = st.enter_context(nc.sbuf_tensor("x0obf", [128, NC, TOK], BF16))
            wst = st.enter_context(nc.sbuf_tensor("wst", [128, NC, 384], BF16))
            Qp = st.enter_context(nc.sbuf_tensor("Qp", [128, TOK], BF16))
            Kp = st.enter_context(nc.sbuf_tensor("Kp", [128, S], BF16))
            Vp = st.enter_context(nc.sbuf_tensor("Vp", [128, 32 * 130], BF16))
            psb = [st.enter_context(nc.sbuf_tensor(f"psb{i}", [128, 1024], BF16)) for i in range(2)]
            sqb = st.enter_context(nc.sbuf_tensor("sqb", [128, 512], BF16))
            S0 = st.enter_context(nc.psum_tensor("S0", [128, 1024], F32))
            S1 = st.enter_context(nc.psum_tensor("S1", [128, 1024], F32))
            ctxf = st.enter_context(nc.psum_tensor("ctxf", [128, 1024], F32))
            pk = [st.enter_context(nc.psum_tensor(f"pk{i}", [128, 512], F32)) for i in range(2)]

            ph = Prog(nc)
            for c in range(NC):
                ph.op("gpsimd", lambda g, c=c: g.dma_start(x0bf[:, c, :], x0f[:, c, :]),
                      inc=dG, inc_by=16)
                ph.op("gpsimd", lambda g, c=c: g.dma_start(x0obf[:, c, :], x0fo[:, c, :]),
                      inc=dG, inc_by=16)
            x0rdy = (dG, dG.n)
            for hp in range(8):
                ph.op("gpsimd", lambda g, hp=hp: g.dma_start(
                    wst[:, :, :], wqkv1[:, hp, :, :]),
                    after=[(sPJ, sPJ.n)], inc=dG, inc_by=16)
                wb = wst
                wq_ready = (dG, dG.n)
                emit_qkv(ph, False, hp, wb, wq_ready, Qp, Kp,
                         Vp, pk, x0obf, [x0rdy], x0bf, x0bf, S,
                         bufwait=(sDV, sDV.n))
                emit_norm_bound(ph, pk[0], Qp, 4, sqb, 0, True)
                emit_norm_bound(ph, pk[0], Kp, 8, sqb, 3, True)
                emit_make_negm(ph, pk[1], 0, 3, negm, hp % 2)
                ready = [(sEV, sEV.n)]
                emit_attention(ph, False, hp, Qp, Kp, Vp,
                               negm, hp % 2, ctx_sb, [S0, S1], ctxf, psb, pk,
                               recb_bf, rbsb, ready)
            emit_drain(ph)
            ph.run()

        with ExitStack() as st:
            wo1h = st.enter_context(nc.sbuf_tensor("wo1h", [64, H, E], BF16))
            x0t = [st.enter_context(nc.sbuf_tensor(f"x0t{i}", [128, E], F32)) for i in range(2)]
            tres = st.enter_context(nc.sbuf_tensor("tres", [128, E], F32))
            xout = [st.enter_context(nc.sbuf_tensor(f"xout{i}", [128, E], F32)) for i in range(2)]
            scr = st.enter_context(nc.sbuf_tensor("scr", [128, E], F32))
            po = [st.enter_context(nc.psum_tensor(f"po{i}", [128, 512], F32)) for i in range(2)]

            ph = Prog(nc)
            if DEBUG_TAPS:
                dc1 = nc.dram_tensor("dbgctx1", [64, H * TOK], BF16,
                                     kind="ExternalOutput").ap()
                ph.op("sync", lambda s: s.dma_start(
                    dc1, AP(ctx_sb, 0, [[H * TOK, 64], [1, H * TOK]])),
                    after=[(sDV, sDV.n)], inc=dL, inc_by=16)
            ph.op("gpsimd", lambda g: g.dma_start(wo1h[:, :, :], wout1), inc=dG, inc_by=16)
            wordy = (dG, dG.n)
            divdone = (sDV, sDV.n)
            x0used = {}
            xoutfree = {}
            for t in range(NT):
                ph.op("sync", lambda s, t=t: s.dma_start(
                    x0t[t % 2][:, :], x0tm[128 * t:128 * t + 128, :]),
                    after=[(sEV, x0used.get(t % 2, 0))], inc=dL, inc_by=16)
                x0rdy2 = (dL, dL.n)
                for jj in range(2):
                    bank = po[jj]
                    for h in range(H):
                        ph.op("tensor", lambda tt, t=t, h=h, jj=jj, bank=bank: tt.matmul(
                            bank[:, :], ctx_sb[0:64, h, 128 * t:128 * t + 128],
                            wo1h[0:64, h, 512 * jj:512 * jj + 512],
                            start=(h == 0), stop=(h == H - 1)),
                            after=[wordy, divdone, (sEV, evcnt.get(id(bank), 0))],
                            inc=(sPJ if h == H - 1 else None))
                ph.op("vector", lambda v, t=t: v.tensor_add(
                    tres[:, 0:512], po[0][:, :], x0t[t % 2][:, 0:512]),
                    after=[(sPJ, sPJ.n), x0rdy2])
                ph.op("vector", lambda v, t=t: v.tensor_add(
                    tres[:, 512:1024], po[1][:, :], x0t[t % 2][:, 512:1024]), inc=sEV)
                evcnt[id(po[0])] = sEV.n
                evcnt[id(po[1])] = sEV.n
                x0used[t % 2] = sEV.n
                emit_ln(ph, tres[:, :], xout[t % 2][:, :], scr,
                        outwait=(dL, xoutfree.get(t % 2, 0)), inc=sC)
                xo_ok = (sC, sC.n)
                ph.op("sync", lambda s, t=t: s.dma_start(
                    x1d[128 * t:128 * t + 128, :], xout[t % 2][:, :]),
                    after=[xo_ok], inc=dL, inc_by=16)
                xoutfree[t % 2] = dL.n
            emit_drain(ph)
            ph.run()

        midq = ExitStack()
        Q2 = midq.enter_context(nc.sbuf_tensor("Q2", [128, 8, TOK], BF16))

        with ExitStack() as st:
            ropefm = st.enter_context(nc.sbuf_tensor("ropefm", [128, NC, TOK], BF16))
            x1fm = st.enter_context(nc.sbuf_tensor("x1fm", [128, NC, TOK], BF16))
            x1t = [st.enter_context(nc.sbuf_tensor(f"x1t{i}", [128, E], F32)) for i in range(2)]
            cs = st.enter_context(nc.sbuf_tensor("cs", [128, E], F32)); cs = [cs, cs]
            tmp512 = st.enter_context(nc.sbuf_tensor("tmp512", [128, 512], F32))
            rope_tm = st.enter_context(nc.sbuf_tensor("rope_tm", [128, E], F32))
            ropebf = st.enter_context(nc.sbuf_tensor("ropebf", [128, E], BF16))
            x1bf = ropebf
            wst2 = st.enter_context(nc.sbuf_tensor("wst2", [128, NC, 384], BF16))
            K2p = st.enter_context(nc.sbuf_tensor("K2p", [128, TOK], BF16))
            V2p = st.enter_context(nc.sbuf_tensor("V2p", [128, NT * 130], BF16))
            sqb2 = None
            tp = [st.enter_context(nc.psum_tensor(f"tp2{i}", [128, 128], BF16)) for i in range(2)]
            pk = [st.enter_context(nc.psum_tensor(f"pk2{i}", [128, 512], F32)) for i in range(2)]

            ph = Prog(nc)
            tilefree = {}
            for t in range(NT):
                ph.op("sync", lambda s, t=t: s.dma_start(
                    x1t[t % 2][:, :], x1d[128 * t:128 * t + 128, :]),
                    after=[(sC, tilefree.get(t % 2, 0))], inc=dL, inc_by=16)
                ph.op("sync", lambda s, t=t: s.dma_start(
                    cs[t % 2][:, 0:512], cost[128 * t:128 * t + 128, :]),
                    after=[(sC, tilefree.get("cs", 0))], inc=dL, inc_by=16)
                ph.op("sync", lambda s, t=t: s.dma_start(
                    cs[t % 2][:, 512:1024], sint[128 * t:128 * t + 128, :]),
                    inc=dL, inc_by=16)
                ld = (dL, dL.n)
                xt = x1t[t % 2]
                ct = cs[t % 2][:, 0:512]
                sn = cs[t % 2][:, 512:1024]
                xe = AP(xt, 0, [[E, 128], [2, 512]])
                xo = AP(xt, 1, [[E, 128], [2, 512]])
                re_ = AP(rope_tm, 0, [[E, 128], [2, 512]])
                ro_ = AP(rope_tm, 1, [[E, 128], [2, 512]])
                ph.op("vector", lambda v, xe=xe, ct=ct, re_=re_: v.tensor_mul(re_, xe, ct),
                      after=[ld])
                ph.op("vector", lambda v, xo=xo, sn=sn: v.tensor_mul(tmp512[:, :], xo, sn))
                ph.op("vector", lambda v, re_=re_: v.tensor_sub(re_, re_, tmp512[:, :]))
                ph.op("vector", lambda v, xo=xo, ct=ct: v.tensor_mul(tmp512[:, :], xo, ct))
                ph.op("vector", lambda v, xe=xe, sn=sn, ro_=ro_: v.tensor_mul(ro_, xe, sn))
                ph.op("vector", lambda v, ro_=ro_: v.tensor_add(ro_, ro_, tmp512[:, :]))
                ph.op("vector", lambda v: v.tensor_copy(ropebf[:, :], rope_tm[:, :]),
                      inc=sC)
                tilefree["cs"] = sC.n
                cast_ok = (sC, sC.n)
                for c in range(NC):
                    k = (t * NC + c) % 2
                    ph.op("tensor", lambda tt, c=c, k=k: tt.transpose(
                        tp[k][:, :], ropebf[:, 128 * c:128 * c + 128], ident[:, :]),
                        after=[cast_ok, (sEV, evcnt.get(id(tp[k]), 0))], inc=sTR)
                    ph.op("vector", lambda v, t=t, c=c, k=k: v.tensor_copy(
                        ropefm[:, c, 128 * t:128 * t + 128], tp[k][:, :]),
                        after=[(sTR, sTR.n)], inc=sEV)
                    evcnt[id(tp[k])] = sEV.n
                ph.op("vector", lambda v, xt=xt: v.tensor_copy(x1bf[:, :], xt[:, :]),
                      after=[(sTR, sTR.n)], inc=sC)
                tilefree[t % 2] = sC.n
                cast_ok = (sC, sC.n)
                for c in range(NC):
                    k = (t * NC + c) % 2
                    ph.op("tensor", lambda tt, c=c, k=k: tt.transpose(
                        tp[k][:, :], x1bf[:, 128 * c:128 * c + 128], ident[:, :]),
                        after=[cast_ok, (sEV, evcnt.get(id(tp[k]), 0))], inc=sTR)
                    ph.op("vector", lambda v, t=t, c=c, k=k: v.tensor_copy(
                        x1fm[:, c, 128 * t:128 * t + 128], tp[k][:, :]),
                        after=[(sTR, sTR.n)], inc=sEV)
                    evcnt[id(tp[k])] = sEV.n
                ph.op("vector", lambda v: v.nop(), after=[(sTR, sTR.n)], inc=sC)
                tilefree[t % 2] = sC.n
            trdone = (sEV, sEV.n)
            kv2st = 0
            for hp in range(8):
                ph.op("gpsimd", lambda g, hp=hp: g.dma_start(
                    wst2[:, :, :], wqkv2[:, hp, :, :]),
                    after=[(sPJ, sPJ.n)], inc=dG, inc_by=16)
                wb = wst2
                wq_ready = (dG, dG.n)
                Qsl = AP(Q2, hp * TOK, [[8 * TOK, 128], [1, TOK]])
                emit_qkv(ph, True, hp, wb, wq_ready, Qsl, K2p, V2p, pk,
                         ropefm, [trdone], ropefm, x1fm, TOK,
                         kvfree=(dL, kv2st))
                emit_norm_bound(ph, pk[0], Qsl, 4, ropebf[:, 0:512], 8 + hp, True)
                emit_norm_bound(ph, pk[0], K2p, 4, ropebf[:, 0:512], 1, hp == 0)
                ph.op("sync", lambda s, hp=hp: s.dma_start(
                    k2loc.ap()[128 * hp:128 * hp + 128, :], K2p[:, :]),
                    after=[(sEV, sEV.n)], inc=dL, inc_by=16)
                ph.op("sync", lambda s, hp=hp: s.dma_start(
                    v2loc.ap().rearrange("(t p) f -> p t f", p=128)[:, :, 130 * hp:130 * hp + 130],
                    AP(V2p, 0, [[NT * 130, 128], [130, NT], [1, 130]])),
                    after=[(sEV, sEV.n)], inc=dL, inc_by=16)
                kv2st = dL.n
            ph.op("sync", lambda s: s.dma_start(arin.ap(), nrm[0:1, 1:2]),
                  after=[(sEV, sEV.n)], inc=dL, inc_by=16)
            ph.op("gpsimd", lambda g: g.collective_compute(
                "AllGather", ALU.bypass, RG, [k2loc.ap().opt()], [k2all.ap().opt()]),
                after=[(dL, dL.n)], inc=cc1)
            ph.op("gpsimd", lambda g: g.collective_compute(
                "AllGather", ALU.bypass, RG, [v2loc.ap().opt()], [v2all.ap().opt()]),
                inc=cc2)
            ph.op("gpsimd", lambda g: g.collective_compute(
                "AllReduce", ALU.max, RG, [arin.ap().opt()], [arout.ap().opt()]),
                inc=cc3)
            ph.op("sync", lambda s: s.dma_start(nrm[0:1, 1:2], arout.ap()),
                  after=[(cc3, cc3.n)], inc=dL, inc_by=16)
            ph.op("vector", lambda v: v.nop(), after=[(dL, dL.n)], inc=sC)
            for hp in range(8):
                emit_make_negm(ph, pk[1], 8 + hp, 1, negm2, hp)
            emit_drain(ph)
            ph.run()

        with ExitStack() as st:
            Kb = [st.enter_context(nc.sbuf_tensor(f"Kb{i}", [128, S], BF16)) for i in range(2)]
            Vb = [st.enter_context(nc.sbuf_tensor(f"Vb{i}", [128, 32 * 130], BF16)) for i in range(2)]
            psb = [st.enter_context(nc.sbuf_tensor(f"psb2{i}", [128, 1024], BF16)) for i in range(2)]
            S0 = st.enter_context(nc.psum_tensor("S0b", [128, 1024], F32))
            S1 = st.enter_context(nc.psum_tensor("S1b", [128, 1024], F32))
            ctxf = st.enter_context(nc.psum_tensor("ctxf2", [128, 1024], F32))
            pk = [st.enter_context(nc.psum_tensor(f"pk3{i}", [128, 512], F32)) for i in range(2)]

            ph = Prog(nc)
            kread = {}
            for hp in range(8):
                bufK, bufV = Kb[hp % 2], Vb[hp % 2]
                usedone = (sP, kread.get(hp % 2, 0))
                for pbit in range(2):
                    ph.op("sync", lambda s, hp=hp, pbit=pbit, bufK=bufK: s.dma_start(
                        AP(bufK, 128 * pbit, [[S, 128], [256, 16], [1, 128]]),
                        k2all.ap()[1024 * pbit + 128 * hp:1024 * pbit + 128 * hp + 128, :]
                        .rearrange("p (t f) -> p t f", f=128)),
                        after=[(cc1, cc1.n), usedone], inc=dL, inc_by=16)
                    ph.op("sync", lambda s, hp=hp, pbit=pbit, bufV=bufV: s.dma_start(
                        AP(bufV, 130 * pbit, [[32 * 130, 128], [260, 16], [1, 130]]),
                        v2all.ap()[2048 * pbit:2048 * pbit + 2048, 130 * hp:130 * hp + 130]
                        .rearrange("(t p) f -> p t f", p=128)),
                        after=[(cc2, cc2.n), usedone], inc=dL, inc_by=16)
                kv_ok = (dL, dL.n)
                Qsl = AP(Q2, hp * TOK, [[8 * TOK, 128], [1, TOK]])
                emit_attention(ph, True, hp, Qsl, bufK, bufV, negm2, hp,
                               ctx_sb, [S0, S1], ctxf, psb, pk, recb_bf, rbsb,
                               ready=[kv_ok])
                kread[hp % 2] = sP.n
            if DEBUG_TAPS:
                dq = nc.dram_tensor("dbgQ2", [128, 8 * TOK], BF16,
                                    kind="ExternalOutput").ap()
                dc = nc.dram_tensor("dbgctx2", [64, H * TOK], BF16,
                                    kind="ExternalOutput").ap()
                dx1 = nc.dram_tensor("dbgx1", [TOK, E], F32,
                                     kind="ExternalOutput").ap()
                dnm = nc.dram_tensor("dbgnm", [128, 8], F32,
                                     kind="ExternalOutput").ap()
                ph.op("sync", lambda s: s.dma_start(
                    dq, AP(Q2, 0, [[8 * TOK, 128], [1, 8 * TOK]])),
                    after=[(sDV, sDV.n)], inc=dL, inc_by=16)
                ph.op("sync", lambda s: s.dma_start(
                    dc, AP(ctx_sb, 0, [[H * TOK, 64], [1, H * TOK]])),
                    after=[(sDV, sDV.n)], inc=dL, inc_by=16)
                ph.op("sync", lambda s: s.dma_start(dx1, x1d),
                      inc=dL, inc_by=16)
                ph.op("sync", lambda s: s.dma_start(
                    dnm, negm2[:, :]), inc=dL, inc_by=16)
            emit_drain(ph)
            ph.run()
        midq.close()

        p45 = ExitStack()
        y1fm = p45.enter_context(nc.sbuf_tensor("y1fm", [128, NC, TOK], BF16, side="right"))
        g1 = p45.enter_context(nc.sbuf_tensor("g1t", [128, E], F32, side="right"))
        b1 = p45.enter_context(nc.sbuf_tensor("b1t", [128, E], F32, side="right"))
        g2 = p45.enter_context(nc.sbuf_tensor("g2t", [128, E], F32, side="right"))
        b2 = p45.enter_context(nc.sbuf_tensor("b2t", [128, E], F32, side="right"))
        w_all = p45.enter_context(nc.sbuf_tensor("w_all", [128, NT, 8], F32, side="right"))

        bob = p45.enter_context(nc.sbuf_tensor("bob", [128, E], F32, side="right"))
        with ExitStack() as st:
            wo2h = st.enter_context(nc.sbuf_tensor("wo2h", [64, H, E], BF16))
            x1t = [st.enter_context(nc.sbuf_tensor(f"p4x1t{i}", [128, E], F32)) for i in range(2)]
            tres = st.enter_context(nc.sbuf_tensor("p4tres", [128, E], F32))
            yout = [st.enter_context(nc.sbuf_tensor(f"yout{i}", [128, E], F32)) for i in range(2)]
            ybf = st.enter_context(nc.sbuf_tensor("ybf", [128, E], BF16))
            scr = st.enter_context(nc.sbuf_tensor("p4scr", [128, E], F32))
            rstage = st.enter_context(nc.sbuf_tensor("p4rst", [1, E], F32))
            tp = [st.enter_context(nc.psum_tensor(f"tp4{i}", [128, 128], BF16)) for i in range(2)]
            po = [st.enter_context(nc.psum_tensor(f"po4{i}", [128, 512], F32)) for i in range(2)]
            pg = st.enter_context(nc.psum_tensor("pg4", [128, 512], F32))

            ph = Prog(nc)
            ph.op("gpsimd", lambda g: g.dma_start(wo2h[:, :, :], wout2), inc=dG, inc_by=16)
            wordy = (dG, dG.n)
            rowused = 0
            for (row, dst) in ((ln1g, g1), (ln1b, b1), (ln2g, g2), (ln2b, b2), (borow, bob)):
                ph.op("sync", lambda s, row=row: s.dma_start(rstage[:, :], row),
                      after=[(sEV, rowused)], inc=dL, inc_by=16)
                rowrdy = (dL, dL.n)
                for jj in range(2):
                    ph.op("tensor", lambda t, jj=jj: t.matmul(
                        pg[:, :], ones1f[:, :], rstage[0:1, 512 * jj:512 * jj + 512],
                        start=True, stop=True),
                        after=[rowrdy, (sEV, evcnt.get(id(pg), 0))], inc=sPJ)
                    ph.op("vector", lambda v, dst=dst, jj=jj: v.tensor_copy(
                        dst[:, 512 * jj:512 * jj + 512], pg[:, :]),
                        after=[(sPJ, sPJ.n)], inc=sEV)
                    evcnt[id(pg)] = sEV.n
                rowused = sEV.n
            divdone = (sDV, sDV.n)
            x1used = {}
            youtfree = {}
            ybffree = 0
            for t in range(NT):
                ph.op("sync", lambda s, t=t: s.dma_start(
                    x1t[t % 2][:, :], x1d[128 * t:128 * t + 128, :]),
                    after=[(sEV, x1used.get(t % 2, 0))], inc=dL, inc_by=16)
                x1rdy = (dL, dL.n)
                for jj in range(2):
                    bank = po[jj]
                    for h in range(H):
                        ph.op("tensor", lambda tt, t=t, h=h, jj=jj, bank=bank: tt.matmul(
                            bank[:, :], ctx_sb[0:64, h, 128 * t:128 * t + 128],
                            wo2h[0:64, h, 512 * jj:512 * jj + 512],
                            start=(h == 0), stop=(h == H - 1)),
                            after=[wordy, divdone, (sEV, evcnt.get(id(bank), 0))],
                            inc=(sPJ if h == H - 1 else None))
                ph.op("vector", lambda v, t=t: v.tensor_add(
                    tres[:, 0:512], po[0][:, :], x1t[t % 2][:, 0:512]),
                    after=[(sPJ, sPJ.n), x1rdy])
                ph.op("vector", lambda v, t=t: v.tensor_add(
                    tres[:, 512:1024], po[1][:, :], x1t[t % 2][:, 512:1024]))
                ph.op("vector", lambda v: v.tensor_add(
                    tres[:, :], tres[:, :], bob[:, :]), inc=sEV)
                evcnt[id(po[0])] = sEV.n
                evcnt[id(po[1])] = sEV.n
                x1used[t % 2] = sEV.n
                emit_ln(ph, tres[:, :], yout[t % 2][:, :], scr, g1[:, :], b1[:, :],
                        outwait=(dL, youtfree.get(t % 2, 0)))
                ph.op("vector", lambda v, t=t: v.tensor_copy(ybf[:, :], yout[t % 2][:, :]),
                      after=[(sTR, ybffree)], inc=sC)
                ybf_ok = (sC, sC.n)
                ph.op("sync", lambda s, t=t: s.dma_start(
                    y1d[128 * t:128 * t + 128, :], yout[t % 2][:, :]),
                    after=[ybf_ok], inc=dL, inc_by=16)
                youtfree[t % 2] = dL.n
                for c in range(NC):
                    k = (t * NC + c) % 2
                    ph.op("tensor", lambda tt, t=t, c=c, k=k: tt.transpose(
                        tp[k][:, :], ybf[:, 128 * c:128 * c + 128], ident[:, :]),
                        after=[ybf_ok, (sEV, evcnt.get(id(tp[k]), 0))], inc=sTR)
                    ph.op("vector", lambda v, t=t, c=c, k=k: v.tensor_copy(
                        y1fm[:, c, 128 * t:128 * t + 128], tp[k][:, :]),
                        after=[(sTR, sTR.n)], inc=sEV)
                    evcnt[id(tp[k])] = sEV.n
                ybffree = sTR.n
            emit_drain(ph)
            ph.run()
        ctx_scope.close()

        with ExitStack() as st:
            acc = st.enter_context(nc.sbuf_tensor("acc", [128, NT, E], F32))
            Webf = [st.enter_context(nc.sbuf_tensor(f"Webf{i}", [128, NC, E], BF16)) for i in range(2)]
            be_bf = st.enter_context(nc.sbuf_tensor("be_bf", [1, NE * E], BF16))
            gex = st.enter_context(nc.sbuf_tensor("gex", [128, 8], F32))
            gsc = st.enter_context(nc.sbuf_tensor("gsc", [128, 8], F32))
            pgt = st.enter_context(nc.psum_tensor("pgt", [128, 8], F32))
            pe_ = [st.enter_context(nc.psum_tensor(f"pe{i}", [128, 512], F32)) for i in range(4)]

            ph = Prog(nc)
            ph.op("gpsimd", lambda g: g.dma_start(be_bf[:, :], berow), inc=dG, inc_by=16)
            berdy = (dG, dG.n)
            for t in range(NT):
                for c in range(NC):
                    ph.op("tensor", lambda tt, t=t, c=c: tt.matmul(
                        pgt[:, :], y1fm[:, c, 128 * t:128 * t + 128], gwt_bf[:, c, :],
                        start=(c == 0), stop=False),
                        after=[(sEV, evcnt.get(id(pgt), 0))])
                ph.op("tensor", lambda tt: tt.matmul(
                    pgt[:, :], ones1[:, :], gb_bf[0:1, :], start=False, stop=True),
                    inc=sPJ)
                ph.op("vector", lambda v: v.reduce_max(stt[:, 9:10], pgt[:, :], axis=AX.X),
                      after=[(sPJ, sPJ.n)], inc=sV)
                ph.op("vector", lambda v: v.tensor_scalar(
                    stt[:, 10:11], stt[:, 9:10], -1.0, None, ALU.mult), inc=sA,
                    wafter=[(sV, sV.n)])
                ph.op("scalar", lambda s: s.activation(
                    gex[:, :], pgt[:, :], AF.Exp, bias=stt[:, 10:11],
                    accum_out=stt[:, 11:12]), after=[(sA, sA.n)], inc=sB)
                ph.op("vector", lambda v: v.nop(), after=[(sB, sB.n)], inc=sEV)
                evcnt[id(pgt)] = sEV.n
                ph.op("vector", lambda v: v.reciprocal(stt[:, 12:13], stt[:, 11:12]),
                      inc=sV)
                ph.op("vector", lambda v: v.reduce_max(stt[:, 13:14], gex[:, :], axis=AX.X),
                      inc=sV)
                ph.op("vector", lambda v: v.tensor_scalar(
                    gsc[:, :], gex[:, :], stt[:, 13:14], -1e30, ALU.is_equal, ALU.mult),
                    wafter=[(sV, sV.n)], inc=sV)
                ph.op("vector", lambda v: v.tensor_add(gsc[:, :], gsc[:, :], gex[:, :]),
                      wafter=[(sV, sV.n)], inc=sV)
                ph.op("vector", lambda v: v.reduce_max(stt[:, 14:15], gsc[:, :], axis=AX.X),
                      wafter=[(sV, sV.n)], inc=sV)
                ph.op("vector", lambda v: v.tensor_scalar(
                    gsc[:, :], gex[:, :], stt[:, 14:15], None, ALU.is_ge),
                    wafter=[(sV, sV.n)], inc=sV)
                ph.op("vector", lambda v: v.tensor_mul(gsc[:, :], gsc[:, :], gex[:, :]),
                      wafter=[(sV, sV.n)], inc=sV)
                ph.op("vector", lambda v, t=t: v.tensor_scalar(
                    w_all[:, t, :], gsc[:, :], stt[:, 12:13], None, ALU.mult),
                    wafter=[(sV, sV.n)], inc=sC)
            wrdy = (sC, sC.n)
            ph.op("gpsimd", lambda g: g.dma_start(Webf[0][:, :, :], wexp[0, :, :, :]),
                  inc=dG, inc_by=16)
            for e in range(NE):
                wb = Webf[e % 2]
                werdy = (dG, dG.n)
                if e < NE - 1:
                    ph.op("gpsimd", lambda g, e=e: g.dma_start(
                        Webf[(e + 1) % 2][:, :, :], wexp[e + 1, :, :, :]),
                        after=[(sPJ, sPJ.n)], inc=dG, inc_by=16)
                for t in range(NT):
                    bks = (pe_[2 * (t % 2)], pe_[2 * (t % 2) + 1])
                    for c in range(NC):
                        for jj in range(2):
                            ph.op("tensor", lambda tt, t=t, c=c, jj=jj, wb=wb,
                                  bks=bks: tt.matmul(
                                      bks[jj][:, :], y1fm[:, c, 128 * t:128 * t + 128],
                                      wb[:, c, 512 * jj:512 * jj + 512],
                                      start=(c == 0), stop=False),
                                  after=[werdy, (sEV, evcnt.get(id(bks[jj]), 0))])
                    for jj in range(2):
                        ph.op("tensor", lambda tt, e=e, jj=jj, bks=bks: tt.matmul(
                            bks[jj][:, :], ones1[:, :],
                            be_bf[0:1, E * e + 512 * jj:E * e + 512 * jj + 512],
                            start=False, stop=True), after=[berdy], inc=sPJ)
                    pjv = (sPJ, sPJ.n)
                    for jj in range(2):
                        bank = bks[jj]
                        if e == 0:
                            ph.op("vector", lambda v, t=t, jj=jj, bank=bank: v.tensor_scalar(
                                acc[:, t, 512 * jj:512 * jj + 512], bank[:, :],
                                w_all[:, t, 0:1], None, ALU.mult),
                                after=[pjv, wrdy], inc=sEV)
                        else:
                            ph.op("vector", lambda v, t=t, jj=jj, e=e, bank=bank:
                                  v.scalar_tensor_tensor(
                                      acc[:, t, 512 * jj:512 * jj + 512], bank[:, :],
                                      w_all[:, t, e:e + 1],
                                      acc[:, t, 512 * jj:512 * jj + 512],
                                      ALU.mult, ALU.add),
                                  after=[pjv], inc=sEV)
                        evcnt[id(bank)] = sEV.n
            mdone = (sEV, sEV.n)
            emit_drain(ph)
            ph.run()

            with ExitStack() as st6:
                yt = [st6.enter_context(nc.sbuf_tensor(f"p6y{i}", [128, E], F32)) for i in range(2)]
                xt = [st6.enter_context(nc.sbuf_tensor(f"p6x{i}", [128, E], F32)) for i in range(2)]
                t2 = st6.enter_context(nc.sbuf_tensor("p6t2", [128, E], F32))
                f_ = st6.enter_context(nc.sbuf_tensor("p6f", [128, E], F32))
                t3 = t2
                oout = [st6.enter_context(nc.sbuf_tensor(f"p6o{i}", [128, E], F32)) for i in range(2)]
                scr6 = st6.enter_context(nc.sbuf_tensor("p6scr", [128, E], F32))

                ph = Prog(nc)
                tileused = {}
                ofree = {}
                for t in range(NT):
                    ph.op("sync", lambda s, t=t: s.dma_start(
                        yt[t % 2][:, :], y1d[128 * t:128 * t + 128, :]),
                        after=[(sC, tileused.get(t % 2, 0))], inc=dL, inc_by=16)
                    ph.op("sync", lambda s, t=t: s.dma_start(
                        xt[t % 2][:, :], x1d[128 * t:128 * t + 128, :]),
                        inc=dL, inc_by=16)
                    ld = (dL, dL.n)
                    ph.op("vector", lambda v, t=t: v.tensor_add(
                        t2[:, :], yt[t % 2][:, :], acc[:, t, :]), after=[ld])
                    emit_ln(ph, t2[:, :], f_[:, :], scr6, g2[:, :], b2[:, :])
                    ph.op("vector", lambda v, t=t: v.tensor_add(
                        t3[:, :], xt[t % 2][:, :], f_[:, :]), inc=sC)
                    tileused[t % 2] = sC.n
                    emit_ln(ph, t3[:, :], oout[t % 2][:, :], scr6,
                            outwait=(dL, ofree.get(t % 2, 0)), inc=sC)
                    oo_ok = (sC, sC.n)
                    ph.op("sync", lambda s, t=t: s.dma_start(
                        OUT[128 * t:128 * t + 128, :], oout[t % 2][:, :]),
                        after=[oo_ok], inc=dL, inc_by=16)
                    ofree[t % 2] = dL.n
                ph.op("sync", lambda s: s.nop(), after=[(dL, dL.n)])
                ph.op("gpsimd", lambda g: g.nop(), after=[(dG, dG.n)])
                ph.run()

    return nc


_CACHE = {}


def _host_inputs(inputs):
    x0 = np.asarray(inputs["L0_attn_x"], np.float32)
    w_qkv = np.asarray(inputs["L0_attn_w_qkv"], np.float32)
    w_out = np.asarray(inputs["L0_attn_w_out"], np.float32)
    ipw = np.asarray(inputs["L0_ffn_in_proj_w"], np.float32)
    ipb = np.asarray(inputs["L0_ffn_in_proj_b"], np.float32)
    opw = np.asarray(inputs["L0_ffn_out_proj_w"], np.float32)
    opb = np.asarray(inputs["L0_ffn_out_proj_b"], np.float32)
    gw = np.asarray(inputs["L0_ffn_gate_w"], np.float32)
    gb = np.asarray(inputs["L0_ffn_gate_b"], np.float32)
    ew = np.asarray(inputs["L0_ffn_expert_w"], np.float32)
    eb = np.asarray(inputs["L0_ffn_expert_b"], np.float32)
    l1g = np.asarray(inputs["L0_ffn_ln1_w"], np.float32)
    l1b = np.asarray(inputs["L0_ffn_ln1_b"], np.float32)
    l2g = np.asarray(inputs["L0_ffn_ln2_w"], np.float32)
    l2b = np.asarray(inputs["L0_ffn_ln2_b"], np.float32)

    def pairs_layout(Wq, Wk, Wv):
        blocks = []
        for hp in range(8):
            s = slice(128 * hp, 128 * hp + 128)
            blocks.append(np.concatenate([Wq[:, s], Wk[:, s], Wv[:, s]], axis=1))
        arr = np.stack(blocks, axis=0)
        arr = arr.reshape(8, 8, 128, 384).transpose(2, 0, 1, 3)
        return np.ascontiguousarray(arr)

    wqkv1 = pairs_layout(w_qkv[:, 0:E], w_qkv[:, E:2 * E], w_qkv[:, 2 * E:3 * E])
    wqkv2 = pairs_layout(ipw[0:E].T, ipw[E:2 * E].T, ipw[2 * E:3 * E].T)
    wout1 = np.ascontiguousarray(w_out.reshape(H, 64, E).transpose(1, 0, 2))
    wout2 = np.ascontiguousarray(np.ascontiguousarray(opw.T).reshape(H, 64, E).transpose(1, 0, 2))
    wexp = np.ascontiguousarray(
        ew.transpose(0, 2, 1).reshape(NE, 8, 128, E).transpose(0, 2, 1, 3))
    bq = ipb[0:E]; bk = ipb[E:2 * E]; bv = ipb[2 * E:3 * E]
    bqk_h = np.ascontiguousarray(
        np.concatenate([bq.reshape(8, 128).T, bk.reshape(8, 128).T], axis=1))
    gwt_h = np.ascontiguousarray(gw.T.reshape(8, 128, 8).transpose(1, 0, 2))

    inv = (1.0 / 10000.0 ** (np.arange(0, HD, 2, dtype=np.float32) / HD)).astype(np.float32)
    tri = np.triu(np.ones((128, 128), np.float32))

    common = dict(
        wqkv1=wqkv1, wout1=wout1, wqkv2=wqkv2, wout2=wout2, wexp=wexp,
        bqk=bqk_h,
        bvrow=np.ascontiguousarray(bv[None, :]),
        borow=np.ascontiguousarray(opb[None, :]),
        gwt=gwt_h, gbrow=np.ascontiguousarray(gb[None, :]),
        berow=np.ascontiguousarray(eb.reshape(1, NE * E)),
        ln1g=np.ascontiguousarray(l1g[None, :]), ln1b=np.ascontiguousarray(l1b[None, :]),
        ln2g=np.ascontiguousarray(l2g[None, :]), ln2b=np.ascontiguousarray(l2b[None, :]),
        identf=np.eye(128, dtype=np.float32),
    )

    in_maps = []
    for core in range(8):
        b, par = core // 2, core % 2
        xb = x0[b]
        xT = np.ascontiguousarray(xb.T)
        x0f_ = np.ascontiguousarray(xT.reshape(8, 128, S).transpose(1, 0, 2))
        own = xb.reshape(32, 128, E)[par::2]
        x0tm_ = np.ascontiguousarray(own.reshape(TOK, E))
        x0fo_ = np.ascontiguousarray(
            x0f_.reshape(128, 8, 32, 128)[:, :, par::2, :].reshape(128, 8, TOK))
        pos = np.arange(S).reshape(32, 128)[par::2].reshape(TOK).astype(np.float32)
        fr = pos[:, None] * inv[None, :]
        cost_ = np.ascontiguousarray(np.tile(np.cos(fr).astype(np.float32), (1, H)))
        sint_ = np.ascontiguousarray(np.tile(np.sin(fr).astype(np.float32), (1, H)))
        m0_ = tri if par == 0 else np.ones((128, 128), np.float32)
        m1_ = np.zeros((128, 128), np.float32) if par == 0 else tri
        im = dict(common)
        im.update(x0f=x0f_, x0fo=x0fo_, x0tm=x0tm_, cost=cost_, sint=sint_,
                  m0in=m0_, m1in=m1_)
        in_maps.append(im)
    return in_maps


def kernel(**inputs):
    in_maps = _host_inputs(inputs)
    if "nc" not in _CACHE:
        _CACHE["nc"] = build_program()
    nc = _CACHE["nc"]
    res = run_bass_kernel_spmd(nc, in_maps, core_ids=list(range(8)))
    out = np.empty((B, S, E), np.float32)
    for core in range(8):
        b, par = core // 2, core % 2
        o = np.asarray(res.results[core]["OUT"], np.float32).reshape(NT, 128, E)
        out.reshape(B, 32, 128, E)[b, par::2] = o
    return out


# version 07
# speedup vs baseline: 1.8249x; 1.2584x over previous
import numpy as np
from contextlib import ExitStack

PAIR1 = False
PAIR1_N = 8
PMODE = "paired"
PAIR2 = True

import concourse.bass as bass
import concourse.mybir as mybir
from concourse.bass_utils import run_bass_kernel_spmd

F32 = mybir.dt.float32
BF16 = mybir.dt.bfloat16
AF = mybir.ActivationFunctionType
ALU = mybir.AluOpType
AX = mybir.AxisListType
AP = bass.AP

B, S, E, H, HD, NE = 4, 4096, 1024, 16, 64, 8
NT = 16
TOK = NT * 128
NC = 8
LN_EPS = 1e-5

ENGS = ("sync", "gpsimd", "tensor", "vector", "scalar")
DEBUG_TAPS = False


class Sem:
    __slots__ = ("h", "n", "by")

    def __init__(self, st, nc, name):
        self.h = st.enter_context(nc.semaphore(name))
        self.n = 0
        self.by = {e: 0 for e in ENGS}


class Prog:

    def __init__(self, nc):
        self.nc = nc
        self.seq = {e: [] for e in ENGS}

    def op(self, eng, emit, after=(), inc=None, inc_by=1, wafter=()):
        w = []
        for a in after:
            s, v = (a if isinstance(a, tuple) else (a, a.n))
            if v > 0 and s.by[eng] < v:
                w.append((s, v))
        for a in wafter:
            s, v = (a if isinstance(a, tuple) else (a, a.n))
            if v > 0:
                w.append((s, v))
        if inc is not None:
            inc.n += inc_by
            inc.by[eng] += inc_by
        self.seq[eng].append((w, emit, inc, inc_by))

    def run(self):
        with self.nc.Block() as blk:
            for eng in ENGS:
                ops = self.seq[eng]
                if not ops:
                    continue

                def body(engine, ops=ops):
                    seen = {}
                    for (w, emit, inc, inc_by) in ops:
                        for (s, v) in w:
                            if seen.get(id(s), -1) < v:
                                engine.wait_ge(s.h, v)
                                seen[id(s)] = v
                        inst = emit(engine)
                        if inc is not None:
                            inst.then_inc(inc.h, inc_by)
                getattr(blk, eng)(body)


def build_program():
    nc = bass.Bass("TRN2", target_bir_lowering=False,
                   detect_race_conditions=False)

    def I(name, shape, dt=F32):
        return nc.dram_tensor(name, shape, dt, kind="ExternalInput").ap()

    x0f = I("x0f", [128, NC, S])
    x0fo = I("x0fo", [128, NC, TOK])
    x0tm = I("x0tm", [TOK, E])
    wqkv1 = I("wqkv1", [128, 8, NC, 384])
    wout1 = I("wout1", [128, NC, E])
    wqkv2 = I("wqkv2", [128, 8, NC, 384])
    wout2 = I("wout2", [128, NC, E])
    wexp = I("wexp", [NE, 128, NC, E])
    bqk = I("bqk", [128, 16])
    bvrow = I("bvrow", [1, E])
    borow = I("borow", [1, E])
    gwt = I("gwt", [128, NC, 8])
    gbrow = I("gbrow", [1, 8])
    berow = I("berow", [1, NE * E])
    ln1g = I("ln1g", [1, E]); ln1b = I("ln1b", [1, E])
    ln2g = I("ln2g", [1, E]); ln2b = I("ln2b", [1, E])
    cost = I("cost", [TOK, 512]); sint = I("sint", [TOK, 512])
    m0in = I("m0in", [128, 128]); m1in = I("m1in", [128, 128])
    identf = I("identf", [128, 128])

    OUT = nc.dram_tensor("OUT", [TOK, E], F32, kind="ExternalOutput").ap()

    x1d = nc.dram_tensor("x1d", [TOK, E], F32, kind="Internal").ap()
    y1d = nc.dram_tensor("y1d", [TOK, E], F32, kind="Internal").ap()
    k2loc = nc.dram_tensor("k2loc", [1024, TOK], BF16, kind="Internal")
    v2loc = nc.dram_tensor("v2loc", [TOK, 8 * 130], BF16, kind="Internal")
    k2all = nc.dram_tensor("k2all", [2048, TOK], BF16, kind="Internal")
    v2all = nc.dram_tensor("v2all", [2 * TOK, 8 * 130], BF16, kind="Internal")
    arin = nc.dram_tensor("arin", [1, 1], F32, kind="Internal")
    arout = nc.dram_tensor("arout", [1, 1], F32, kind="Internal")
    RG = [[0, 1], [2, 3], [4, 5], [6, 7]]

    with ExitStack() as st0:
        EE = st0.enter_context
        dL = Sem(st0, nc, "dL")
        dG = Sem(st0, nc, "dG")
        cc1 = Sem(st0, nc, "cc1"); cc2 = Sem(st0, nc, "cc2"); cc3 = Sem(st0, nc, "cc3")
        sPJ = Sem(st0, nc, "sPJ")
        sEV = Sem(st0, nc, "sEV")
        sS = Sem(st0, nc, "sS")
        sE = Sem(st0, nc, "sE")
        sM = Sem(st0, nc, "sM")
        sP = Sem(st0, nc, "sP")
        sDV = Sem(st0, nc, "sDV")
        sA = Sem(st0, nc, "sA")
        sB = Sem(st0, nc, "sB")
        sTR = Sem(st0, nc, "sTR")
        sC = Sem(st0, nc, "sC")
        sV = Sem(st0, nc, "sV")

        ident = EE(nc.sbuf_tensor("ident", [128, 128], BF16))
        m0 = EE(nc.sbuf_tensor("m0", [128, 128], BF16))
        m1 = EE(nc.sbuf_tensor("m1", [128, 128], BF16))
        ones1 = EE(nc.sbuf_tensor("ones1", [1, 128], BF16))
        ones1f = EE(nc.sbuf_tensor("ones1f", [1, 128], F32))
        ones128 = EE(nc.sbuf_tensor("ones128", [128, 1], BF16))
        bqk_t = EE(nc.sbuf_tensor("bqk_t", [128, 16], F32))
        bv_bf = EE(nc.sbuf_tensor("bv_bf", [1, E], BF16))
        bo_bf = EE(nc.sbuf_tensor("bo_bf", [1, E], BF16))
        gb_bf = EE(nc.sbuf_tensor("gb_bf", [1, 8], BF16))
        gwt_bf = EE(nc.sbuf_tensor("gwt_bf", [128, NC, 8], BF16))
        negm = EE(nc.sbuf_tensor("negm", [128, 2], F32))
        negm2 = EE(nc.sbuf_tensor("negm2", [128, 8], F32))
        stt = EE(nc.sbuf_tensor("stt", [128, 16], F32))
        nrm = EE(nc.sbuf_tensor("nrm", [1, 24], F32))
        nrmbf = EE(nc.sbuf_tensor("nrmbf", [1, 2], BF16))
        recb = EE(nc.sbuf_tensor("recb", [128, 16], F32))

        evcnt = {}

        with ExitStack() as st:
            stage = st.enter_context(nc.sbuf_tensor("stage", [128, 128], F32))
            rstage = st.enter_context(nc.sbuf_tensor("rstage", [1, E], F32))
            ph = Prog(nc)
            for (src, dst) in ((identf, ident), (m0in, m0), (m1in, m1)):
                ph.op("sync", lambda s, src=src: s.dma_start(stage[:, :], src),
                      after=[sEV], inc=dL, inc_by=16)
                ph.op("vector", lambda v, dst=dst: v.tensor_copy(dst[:, :], stage[:, :]),
                      after=[dL], inc=sEV)
            ph.op("sync", lambda s: s.dma_start(bqk_t[:, :], bqk), inc=dL, inc_by=16)
            for (src, dst, w_) in ((bvrow, bv_bf, E), (borow, bo_bf, E), (gbrow, gb_bf, 8)):
                ph.op("sync", lambda s, src=src, w_=w_: s.dma_start(rstage[:, 0:w_], src),
                      after=[sEV], inc=dL, inc_by=16)
                ph.op("vector", lambda v, dst=dst, w_=w_: v.tensor_copy(dst[:, 0:w_], rstage[:, 0:w_]),
                      after=[dL], inc=sEV)
            ph.op("gpsimd", lambda g: g.dma_start(gwt_bf[:, :, :], gwt), inc=dG, inc_by=16)
            ph.op("vector", lambda v: v.memset(ones1[:, :], 1.0), after=[dG, dL])
            ph.op("vector", lambda v: v.memset(ones1f[:, :], 1.0))
            ph.op("vector", lambda v: v.memset(ones128[:, :], 1.0))
            ph.run()

        def emit_norm_bound(ph, pk0, src, n512, sqb, acc_col, init):
            first = init
            for j in range(n512):
                ph.op("vector", lambda v, j=j: v.tensor_mul(
                    sqb[:, :], src[:, 512 * j:512 * j + 512],
                    src[:, 512 * j:512 * j + 512]),
                    after=[(sPJ, sPJ.n)], inc=sC)
                ph.op("tensor", lambda t: t.matmul(
                    pk0[0:1, 0:512], ones128[:, :], sqb[:, :], start=True, stop=True),
                    after=[(sC, sC.n), (sEV, evcnt.get(id(pk0), 0))], inc=sPJ)
                ph.op("vector", lambda v: v.reduce_max(
                    nrm[0:1, 2:3], pk0[0:1, 0:512], axis=AX.X), after=[(sPJ, sPJ.n)],
                    inc=sV)
                if first:
                    ph.op("vector", lambda v: v.tensor_copy(
                        nrm[0:1, acc_col:acc_col + 1], nrm[0:1, 2:3]), inc=sV,
                        wafter=[(sV, sV.n)])
                    first = False
                else:
                    ph.op("vector", lambda v: v.tensor_tensor(
                        nrm[0:1, acc_col:acc_col + 1],
                        nrm[0:1, acc_col:acc_col + 1], nrm[0:1, 2:3], ALU.max), inc=sV,
                        wafter=[(sV, sV.n)])
                ph.op("vector", lambda v: v.nop(), wafter=[(sV, sV.n)], inc=sEV)
                evcnt[id(pk0)] = sEV.n

        def emit_make_negm(ph, pk1, qcol, kcol, dst, dst_col):
            ph.op("vector", lambda v: v.tensor_mul(
                nrm[0:1, 4:5], nrm[0:1, qcol:qcol + 1], nrm[0:1, kcol:kcol + 1]),
                wafter=[(sV, sV.n)], inc=sA)
            ph.op("scalar", lambda s: s.activation(nrm[0:1, 5:6], nrm[0:1, 4:5], AF.Ln),
                  after=[(sA, sA.n)])
            ph.op("scalar", lambda s: s.activation(nrm[0:1, 6:7], nrm[0:1, 5:6], AF.Exp,
                                                   scale=0.5), inc=sB)
            ph.op("vector", lambda v: v.tensor_scalar(
                nrm[0:1, 7:8], nrm[0:1, 6:7], -1.0, -1.0, ALU.mult, ALU.add),
                after=[(sB, sB.n)], inc=sV)
            ph.op("vector", lambda v: v.tensor_copy(nrmbf[0:1, 0:1], nrm[0:1, 7:8]),
                  wafter=[(sV, sV.n)], inc=sC)
            ph.op("tensor", lambda t: t.matmul(
                pk1[:, 0:1], ones1[:, :], nrmbf[0:1, 0:1], start=True, stop=True),
                after=[(sC, sC.n), (sEV, evcnt.get(id(pk1), 0))], inc=sPJ)
            ph.op("vector", lambda v: v.tensor_copy(
                dst[:, dst_col:dst_col + 1], pk1[:, 0:1]), after=[(sPJ, sPJ.n)], inc=sEV)
            evcnt[id(pk1)] = sEV.n
            evcnt[("nm", id(dst), dst_col)] = sEV.n

        expcnt = {}

        def emit_attention_v1(ph, is2, hp, Q, K, V, nmten, nmcol, ctx_tm, Sb, ctxp,
                           psb, ready):
            for h2 in range(2):
                h = 2 * hp + h2
                hb = 64 * h2
                for qc in range(2):
                    nkb = 32 if not is2 else (16 if qc == 0 else 32)
                    divwait = (sDV, sDV.n)
                    pend = []
                    def flush_pv():
                        (kb, bi, j0, expv, mwait, jstop) = pend.pop(0)
                        js = list(range(j0, 8))
                        for idx, j in enumerate(js):
                            last = (idx == len(js) - 1)
                            stt_ = (kb == 0 and j in (0, 4))
                            stp_ = ((j == 3 and kb == jstop[3]) or
                                    (j == 7 and kb == jstop[7]))
                            ph.op("tensor", lambda t, kb=kb, j=j, bi=bi,
                                  stt_=stt_, stp_=stp_, h2=h2: t.matmul(
                                      ctxp[:, 128 * j:128 * j + 65],
                                      psb[bi][:, 128 * j:128 * j + 128],
                                      V[:, 130 * kb + 65 * h2:130 * kb + 65 * h2 + 65],
                                      start=stt_, stop=stp_),
                                  after=[(sE, expv), mwait, divwait] + ready,
                                  inc=(sP if last else None))
                    for kb in range(nkb):
                        bi = kb % 2
                        Sx = Sb[bi]
                        scw = [(sE, expcnt.get(id(Sx), 0))] + ready
                        for jj in range(2):
                            ph.op("tensor", lambda t, kb=kb, jj=jj, Sx=Sx,
                                  hb=hb, qc=qc: t.matmul(
                                Sx[:, 512 * jj:512 * jj + 512],
                                K[hb:hb + 64, 128 * kb:128 * kb + 128],
                                Q[hb:hb + 64, 1024 * qc + 512 * jj:1024 * qc + 512 * jj + 512],
                                start=True, stop=True),
                                after=scw, inc=sS)
                        ph.op("scalar", lambda s, bi=bi, Sx=Sx: s.activation(
                            psb[bi][:, :], Sx[:, :], AF.Exp,
                            bias=nmten[:, nmcol:nmcol + 1], scale=1.0),
                            after=[(sS, sS.n), (sP, sP.n),
                                   (sEV, evcnt.get(("nm", id(nmten), nmcol), 0))],
                            inc=sE)
                        expcnt[id(Sx)] = sE.n
                        expv = sE.n
                        if is2:
                            j0 = (kb // 2) if qc == 0 else max(0, (kb - 16) // 2)
                            rr = (kb - 2 * j0) if qc == 0 else (kb - 16 - 2 * j0)
                        else:
                            j0, rr = 0, -1
                        if is2 and rr in (0, 1):
                            mk = m0 if rr == 0 else m1
                            ph.op("vector", lambda v, bi=bi, j0=j0, mk=mk: v.tensor_mul(
                                psb[bi][:, 128 * j0:128 * j0 + 128],
                                psb[bi][:, 128 * j0:128 * j0 + 128], mk[:, :]),
                                after=[(sE, expv)], inc=sM)
                        mwait = (sM, sM.n)
                        if is2:
                            jstop = {j: (2 * j + 1 if qc == 0 else 17 + 2 * j)
                                     for j in range(8)}
                        else:
                            jstop = {j: nkb - 1 for j in range(8)}
                        pend.append((kb, bi, j0, expv, mwait, jstop))
                        if len(pend) > 1:
                            flush_pv()
                    flush_pv()
                    ph.op("vector", lambda v: v.reciprocal(
                        recb[:, 0:8], AP(ctxp.ap().tensor, 64, [[1024, 128], [128, 8]])),
                        after=[(sP, sP.n)], inc=sV)
                    recfence = (sV, sV.n)
                    for j in range(8):
                        tloc = qc * 8 + j
                        ph.op("vector", lambda v, j=j, tloc=tloc, h=h: v.tensor_scalar(
                            ctx_tm[:, tloc, 64 * h:64 * h + 64],
                            ctxp[:, 128 * j:128 * j + 64],
                            recb[:, j:j + 1], None, ALU.mult),
                            wafter=[recfence],
                            inc=(sDV if j == 7 else None))

        def emit_attention_p4(ph, is2, hp, Q, K, V, nmten, nmcol, ctx_tm, Sb, ctxp,
                           psb, ready):
            for h2 in range(2):
                h = 2 * hp + h2
                hb = 64 * h2
                for qc in range(2):
                    nkb = 32 if not is2 else (16 if qc == 0 else 32)
                    divwait = (sDV, sDV.n)
                    pend = []
                    def flush_pv():
                        (kb, bi, j0, expv, mwait, jstop) = pend.pop(0)
                        js = list(range(j0, 8))
                        for idx, j in enumerate(js):
                            last = (idx == len(js) - 1)
                            stt_ = (kb == 0 and j in (0, 4))
                            stp_ = ((j == 3 and kb == jstop[3]) or
                                    (j == 7 and kb == jstop[7]))
                            ph.op("tensor", lambda t, kb=kb, j=j, bi=bi,
                                  stt_=stt_, stp_=stp_, h2=h2: t.matmul(
                                      ctxp[:, 128 * j:128 * j + 65],
                                      psb[bi][:, 128 * j:128 * j + 128],
                                      V[:, 130 * kb + 65 * h2:130 * kb + 65 * h2 + 65],
                                      start=stt_, stop=stp_),
                                  after=[(sE, expv), mwait, divwait] + ready,
                                  inc=(sP if last else None))
                    for kb in range(nkb):
                        bi = kb % 2
                        Sx = Sb[bi]
                        pi = 2 * h2 + kb % 2
                        scw = [(sE, expcnt.get(id(Sx), 0))] + ready
                        for jj in range(2):
                            ph.op("tensor", lambda t, kb=kb, jj=jj, Sx=Sx,
                                  hb=hb, qc=qc: t.matmul(
                                Sx[:, 512 * jj:512 * jj + 512],
                                K[hb:hb + 64, 128 * kb:128 * kb + 128],
                                Q[hb:hb + 64, 1024 * qc + 512 * jj:1024 * qc + 512 * jj + 512],
                                start=True, stop=True),
                                after=scw, inc=sS)
                        ph.op("scalar", lambda s, bi=pi, Sx=Sx: s.activation(
                            psb[bi][:, :], Sx[:, :], AF.Exp,
                            bias=nmten[:, nmcol:nmcol + 1], scale=1.0),
                            after=[(sS, sS.n), (sP, sP.n),
                                   (sEV, evcnt.get(("nm", id(nmten), nmcol), 0))],
                            inc=sE)
                        expcnt[id(Sx)] = sE.n
                        expv = sE.n
                        if is2:
                            j0 = (kb // 2) if qc == 0 else max(0, (kb - 16) // 2)
                            rr = (kb - 2 * j0) if qc == 0 else (kb - 16 - 2 * j0)
                        else:
                            j0, rr = 0, -1
                        if is2 and rr in (0, 1):
                            mk = m0 if rr == 0 else m1
                            ph.op("vector", lambda v, bi=pi, j0=j0, mk=mk: v.tensor_mul(
                                psb[bi][:, 128 * j0:128 * j0 + 128],
                                psb[bi][:, 128 * j0:128 * j0 + 128], mk[:, :]),
                                after=[(sE, expv)], inc=sM)
                        mwait = (sM, sM.n)
                        if is2:
                            jstop = {j: (2 * j + 1 if qc == 0 else 17 + 2 * j)
                                     for j in range(8)}
                        else:
                            jstop = {j: nkb - 1 for j in range(8)}
                        pend.append((kb, pi, j0, expv, mwait, jstop))
                        if len(pend) > 1:
                            flush_pv()
                    flush_pv()
                    ph.op("vector", lambda v: v.reciprocal(
                        recb[:, 0:8], AP(ctxp.ap().tensor, 64, [[1024, 128], [128, 8]])),
                        after=[(sP, sP.n)], inc=sV)
                    recfence = (sV, sV.n)
                    for j in range(8):
                        tloc = qc * 8 + j
                        ph.op("vector", lambda v, j=j, tloc=tloc, h=h: v.tensor_scalar(
                            ctx_tm[:, tloc, 64 * h:64 * h + 64],
                            ctxp[:, 128 * j:128 * j + 64],
                            recb[:, j:j + 1], None, ALU.mult),
                            wafter=[recfence],
                            inc=(sDV if j == 7 else None))

        def emit_attention_pc(ph, is2, hp, Q, K, V, nmten, nmcol, ctx_tm, Sb, ctxp,
                           pk, psb, ready):
            ctxsl = {0: [(ctxp, 128 * j) for j in range(8)],
                     1: [(pk[0], 128 * j) for j in range(4)] +
                        [(pk[1], 128 * (j - 4)) for j in range(4, 8)]}
            for h2 in range(2):
                h = 2 * hp + h2
                hb = 64 * h2
                for qc in range(2):
                    nkb = 32 if not is2 else (16 if qc == 0 else 32)
                    divwait = (sDV, sDV.n)
                    pend = []
                    def flush_pv():
                        (kb, bi, j0, expv, mwait, jstop) = pend.pop(0)
                        js = list(range(j0, 8))
                        for idx, j in enumerate(js):
                            last = (idx == len(js) - 1)
                            stt_ = (kb == 0 and j in (0, 4))
                            stp_ = ((j == 3 and kb == jstop[3]) or
                                    (j == 7 and kb == jstop[7]))
                            (ct, off) = ctxsl[h2][j]
                            ph.op("tensor", lambda t, kb=kb, j=j, bi=bi, ct=ct, off=off,
                                  stt_=stt_, stp_=stp_, h2=h2: t.matmul(
                                      ct[:, off:off + 65],
                                      psb[bi][:, 128 * j:128 * j + 128],
                                      V[:, 130 * kb + 65 * h2:130 * kb + 65 * h2 + 65],
                                      start=stt_, stop=stp_),
                                  after=[(sE, expv), mwait, divwait] + ready,
                                  inc=(sP if last else None))
                    for kb in range(nkb):
                        bi = kb % 2
                        Sx = Sb[bi]
                        pi = 2 * h2 + kb % 2
                        scw = [(sE, expcnt.get(id(Sx), 0))] + ready
                        for jj in range(2):
                            ph.op("tensor", lambda t, kb=kb, jj=jj, Sx=Sx,
                                  hb=hb, qc=qc: t.matmul(
                                Sx[:, 512 * jj:512 * jj + 512],
                                K[hb:hb + 64, 128 * kb:128 * kb + 128],
                                Q[hb:hb + 64, 1024 * qc + 512 * jj:1024 * qc + 512 * jj + 512],
                                start=True, stop=True),
                                after=scw, inc=sS)
                        ph.op("scalar", lambda s, bi=pi, Sx=Sx: s.activation(
                            psb[bi][:, :], Sx[:, :], AF.Exp,
                            bias=nmten[:, nmcol:nmcol + 1], scale=1.0),
                            after=[(sS, sS.n), (sP, sP.n),
                                   (sEV, evcnt.get(("nm", id(nmten), nmcol), 0))],
                            inc=sE)
                        expcnt[id(Sx)] = sE.n
                        expv = sE.n
                        if is2:
                            j0 = (kb // 2) if qc == 0 else max(0, (kb - 16) // 2)
                            rr = (kb - 2 * j0) if qc == 0 else (kb - 16 - 2 * j0)
                        else:
                            j0, rr = 0, -1
                        if is2 and rr in (0, 1):
                            mk = m0 if rr == 0 else m1
                            ph.op("vector", lambda v, bi=pi, j0=j0, mk=mk: v.tensor_mul(
                                psb[bi][:, 128 * j0:128 * j0 + 128],
                                psb[bi][:, 128 * j0:128 * j0 + 128], mk[:, :]),
                                after=[(sE, expv)], inc=sM)
                        mwait = (sM, sM.n)
                        if is2:
                            jstop = {j: (2 * j + 1 if qc == 0 else 17 + 2 * j)
                                     for j in range(8)}
                        else:
                            jstop = {j: nkb - 1 for j in range(8)}
                        pend.append((kb, pi, j0, expv, mwait, jstop))
                        if len(pend) > 1:
                            flush_pv()
                    flush_pv()
                    if h2 == 0:
                        ph.op("vector", lambda v: v.reciprocal(
                            recb[:, 0:8], AP(ctxp.ap().tensor, 64, [[1024, 128], [128, 8]])),
                            after=[(sP, sP.n)], inc=sV)
                    else:
                        ph.op("vector", lambda v: v.reciprocal(
                            recb[:, 8:12], AP(pk[0].ap().tensor, 64, [[512, 128], [128, 4]])),
                            after=[(sP, sP.n)], inc=sV)
                        ph.op("vector", lambda v: v.reciprocal(
                            recb[:, 12:16], AP(pk[1].ap().tensor, 64, [[512, 128], [128, 4]])),
                            inc=sV)
                    recfence = (sV, sV.n)
                    for j in range(8):
                        tloc = qc * 8 + j
                        (ct, off) = ctxsl[h2][j]
                        rcol = j if h2 == 0 else 8 + j
                        ph.op("vector", lambda v, j=j, tloc=tloc, h=h, ct=ct,
                              off=off, rcol=rcol: v.tensor_scalar(
                            ctx_tm[:, tloc, 64 * h:64 * h + 64],
                            ct[:, off:off + 64],
                            recb[:, rcol:rcol + 1], None, ALU.mult),
                            wafter=[recfence],
                            inc=(sDV if j == 7 else None))
                    if h2 == 1:
                        ph.op("vector", lambda v: v.nop(), inc=sEV)
                        evcnt[id(pk[0])] = sEV.n
                        evcnt[id(pk[1])] = sEV.n

        def emit_attention(ph, is2, hp, Q, K, V, nmten, nmcol, ctx_tm, Sb, ctxp,
                           pk, psb, ready):
            ctxsl = {0: [(ctxp, 128 * j) for j in range(8)],
                     1: [(pk[0], 128 * j) for j in range(4)] +
                        [(pk[1], 128 * (j - 4)) for j in range(4, 8)]}
            for qc in range(2):
                nkb = 32 if not is2 else (16 if qc == 0 else 32)
                divwait = (sDV, sDV.n)
                pend = []

                def flush_pv():
                    (kb, h2, bi, j0, expv, mwait, jstop) = pend.pop(0)
                    js = list(range(j0, 8))
                    for idx, j in enumerate(js):
                        last = (idx == len(js) - 1)
                        stt_ = (kb == 0 and j in (0, 4))
                        stp_ = ((j == 3 and kb == jstop[3]) or
                                (j == 7 and kb == jstop[7]))
                        (ct, off) = ctxsl[h2][j]
                        ph.op("tensor", lambda t, kb=kb, j=j, bi=bi, h2=h2,
                              stt_=stt_, stp_=stp_, ct=ct, off=off: t.matmul(
                                  ct[:, off:off + 65],
                                  psb[bi][:, 128 * j:128 * j + 128],
                                  V[:, 130 * kb + 65 * h2:130 * kb + 65 * h2 + 65],
                                  start=stt_, stop=stp_),
                              after=[(sE, expv), mwait, divwait] + ready,
                              inc=(sP if last else None))

                for kb in range(nkb):
                    if is2:
                        j0 = (kb // 2) if qc == 0 else max(0, (kb - 16) // 2)
                        rr = (kb - 2 * j0) if qc == 0 else (kb - 16 - 2 * j0)
                    else:
                        j0, rr = 0, -1
                    if is2:
                        jstop = {j: (2 * j + 1 if qc == 0 else 17 + 2 * j)
                                 for j in range(8)}
                    else:
                        jstop = {j: nkb - 1 for j in range(8)}
                    for h2 in range(2):
                        Sx = Sb[h2]
                        scw = [(sE, expcnt.get(id(Sx), 0))] + ready
                        for jj in range(2):
                            ph.op("tensor", lambda t, kb=kb, jj=jj, Sx=Sx,
                                  h2=h2, qc=qc: t.matmul(
                                Sx[:, 512 * jj:512 * jj + 512],
                                K[64 * h2:64 * h2 + 64, 128 * kb:128 * kb + 128],
                                Q[64 * h2:64 * h2 + 64,
                                  1024 * qc + 512 * jj:1024 * qc + 512 * jj + 512],
                                start=True, stop=True),
                                after=scw, inc=sS)
                    for h2 in range(2):
                        Sx = Sb[h2]
                        bi = 2 * h2 + kb % 2
                        ph.op("scalar", lambda s, bi=bi, Sx=Sx: s.activation(
                            psb[bi][:, :], Sx[:, :], AF.Exp,
                            bias=nmten[:, nmcol:nmcol + 1], scale=1.0),
                            after=[(sS, sS.n if h2 == 1 else sS.n - 2),
                                   (sP, sP.n),
                                   (sEV, evcnt.get(("nm", id(nmten), nmcol), 0))],
                            inc=sE)
                        expcnt[id(Sx)] = sE.n
                        expv = sE.n
                        if is2 and rr in (0, 1):
                            mk = m0 if rr == 0 else m1
                            ph.op("vector", lambda v, bi=bi, j0=j0, mk=mk: v.tensor_mul(
                                psb[bi][:, 128 * j0:128 * j0 + 128],
                                psb[bi][:, 128 * j0:128 * j0 + 128], mk[:, :]),
                                after=[(sE, expv)], inc=sM)
                        mwait = (sM, sM.n)
                        pend.append((kb, h2, bi, j0, expv, mwait, jstop))
                    while len(pend) > 2:
                        flush_pv()
                while pend:
                    flush_pv()
                ph.op("vector", lambda v: v.reciprocal(
                    recb[:, 0:8], AP(ctxp.ap().tensor, 64, [[1024, 128], [128, 8]])),
                    after=[(sP, sP.n)], inc=sV)
                ph.op("vector", lambda v: v.reciprocal(
                    recb[:, 8:12], AP(pk[0].ap().tensor, 64, [[512, 128], [128, 4]])),
                    inc=sV)
                ph.op("vector", lambda v: v.reciprocal(
                    recb[:, 12:16], AP(pk[1].ap().tensor, 64, [[512, 128], [128, 4]])),
                    inc=sV)
                recfence = (sV, sV.n)
                for h2 in range(2):
                    h = 2 * hp + h2
                    for j in range(8):
                        tloc = qc * 8 + j
                        (ct, off) = ctxsl[h2][j]
                        rcol = j if h2 == 0 else 8 + j
                        ph.op("vector", lambda v, j=j, tloc=tloc, h=h, ct=ct,
                              off=off, rcol=rcol: v.tensor_scalar(
                            ctx_tm[:, tloc, 64 * h:64 * h + 64],
                            ct[:, off:off + 64],
                            recb[:, rcol:rcol + 1], None, ALU.mult),
                            wafter=[recfence],
                            inc=(sDV if (h2 == 1 and j == 7) else None))
                ph.op("vector", lambda v: v.nop(), inc=sEV)
                evcnt[id(pk[0])] = sEV.n
                evcnt[id(pk[1])] = sEV.n

        def emit_ln(ph, t_in, t_out, scr, gamma=None, beta=None, outwait=None,
                    inc=None):
            ph.op("vector", lambda v: v.scalar_tensor_tensor(
                scr[:, :], t_in, 1.0, t_in, ALU.mult, ALU.mult, accum_out=stt[:, 1:2]),
                inc=sV)
            ph.op("vector", lambda v: v.reduce_sum(stt[:, 0:1], t_in, axis=AX.X),
                  inc=sV)
            ph.op("vector", lambda v: v.tensor_scalar(stt[:, 2:3], stt[:, 0:1], 1.0 / E, None, ALU.mult),
                  wafter=[(sV, sV.n)], inc=sV)
            ph.op("vector", lambda v: v.tensor_scalar(stt[:, 3:4], stt[:, 1:2], 1.0 / E, None, ALU.mult),
                  inc=sV)
            ph.op("vector", lambda v: v.tensor_mul(stt[:, 4:5], stt[:, 2:3], stt[:, 2:3]),
                  wafter=[(sV, sV.n)], inc=sV)
            ph.op("vector", lambda v: v.tensor_sub(stt[:, 5:6], stt[:, 3:4], stt[:, 4:5]),
                  wafter=[(sV, sV.n)], inc=sV)
            ph.op("vector", lambda v: v.tensor_scalar(stt[:, 6:7], stt[:, 5:6], LN_EPS, None, ALU.add),
                  wafter=[(sV, sV.n)], inc=sA)
            ph.op("scalar", lambda s: s.activation(stt[:, 7:8], stt[:, 6:7], AF.Ln),
                  after=[(sA, sA.n)])
            ph.op("scalar", lambda s: s.activation(stt[:, 8:9], stt[:, 7:8], AF.Exp, scale=-0.5),
                  inc=sB)
            w = [(sB, sB.n)] + ([outwait] if outwait else [])
            lastop = ph.op if gamma is not None else None
            ph.op("vector", lambda v: v.tensor_scalar(
                t_out, t_in, stt[:, 2:3], stt[:, 8:9], ALU.subtract, ALU.mult), after=w,
                inc=(None if gamma is not None else inc))
            if gamma is not None:
                ph.op("vector", lambda v: v.tensor_mul(t_out, t_out, gamma))
                ph.op("vector", lambda v: v.tensor_add(t_out, t_out, beta), inc=inc)

        def emit_drain(ph):
            ph.op("sync", lambda s: s.nop(), after=[(dL, dL.n)])
            ph.op("gpsimd", lambda g: g.nop(), after=[(dG, dG.n)])

        def emit_qkv(ph, is2, hp, wbuf, wq_ready, Q, K, V, pk, qsrc, qwaits,
                     ksrc, vsrc, kwid, kvfree=None):
            kvw = [kvfree] if kvfree else []
            for qb in range(4):
                bank = pk[qb % 2]
                for c in range(NC):
                    ph.op("tensor", lambda t, c=c, qb=qb, bank=bank: t.matmul(
                        bank[:, :], wbuf[:, c, 0:128],
                        qsrc[:, c, 512 * qb:512 * qb + 512],
                        start=(c == 0), stop=(c == NC - 1)),
                        after=[wq_ready] + qwaits + [(sEV, evcnt.get(id(bank), 0))],
                        inc=(sPJ if c == NC - 1 else None))
                if is2:
                    ph.op("vector", lambda v, qb=qb, bank=bank: v.tensor_scalar(
                        Q[:, 512 * qb:512 * qb + 512], bank[:, :],
                        bqk_t[:, hp:hp + 1], 0.125, ALU.add, ALU.mult),
                        after=[(sPJ, sPJ.n)], inc=sEV)
                else:
                    ph.op("vector", lambda v, qb=qb, bank=bank: v.tensor_scalar(
                        Q[:, 512 * qb:512 * qb + 512], bank[:, :], 0.125, None, ALU.mult),
                        after=[(sPJ, sPJ.n)], inc=sEV)
                evcnt[id(bank)] = sEV.n
            for kb in range(kwid // 512):
                bank = pk[kb % 2]
                for c in range(NC):
                    ph.op("tensor", lambda t, c=c, kb=kb, bank=bank: t.matmul(
                        bank[:, :], wbuf[:, c, 128:256],
                        ksrc[:, c, 512 * kb:512 * kb + 512],
                        start=(c == 0), stop=(c == NC - 1)),
                        after=[(sEV, evcnt.get(id(bank), 0))],
                        inc=(sPJ if c == NC - 1 else None))
                if is2:
                    ph.op("vector", lambda v, kb=kb, bank=bank: v.tensor_scalar(
                        K[:, 512 * kb:512 * kb + 512], bank[:, :],
                        bqk_t[:, 8 + hp:9 + hp], None, ALU.add),
                        after=[(sPJ, sPJ.n)] + kvw, inc=sEV)
                else:
                    ph.op("vector", lambda v, kb=kb, bank=bank: v.tensor_copy(
                        K[:, 512 * kb:512 * kb + 512], bank[:, :]),
                        after=[(sPJ, sPJ.n)], inc=sEV)
                evcnt[id(bank)] = sEV.n
            nvt = kwid // 128
            ph.op("vector", lambda v, nvt=nvt: v.memset(
                AP(V, 64, [[nvt * 130, 128], [130, nvt], [65, 2]]), 1.0),
                after=kvw, inc=sEV)
            vones = sEV.n
            for tt in range(nvt):
                bank = pk[tt % 2]
                for c in range(NC):
                    ph.op("tensor", lambda t, c=c, tt=tt, bank=bank: t.matmul(
                        bank[:, 0:128], vsrc[:, c, 128 * tt:128 * tt + 128],
                        wbuf[:, c, 256:384],
                        start=(c == 0), stop=(False if is2 else c == NC - 1)),
                        after=[(sEV, evcnt.get(id(bank), 0))],
                        inc=(sPJ if (not is2 and c == NC - 1) else None))
                if is2:
                    ph.op("tensor", lambda t, bank=bank: t.matmul(
                        bank[:, 0:128], ones1[:, :],
                        bv_bf[0:1, 128 * hp:128 * hp + 128],
                        start=False, stop=True), inc=sPJ)
                ph.op("vector", lambda v, tt=tt, bank=bank, nvt=nvt: v.tensor_copy(
                    AP(V, 130 * tt, [[nvt * 130, 128], [65, 2], [1, 64]]),
                    AP(bank.ap().tensor, 0, [[512, 128], [64, 2], [1, 64]])),
                    after=[(sPJ, sPJ.n), (sEV, vones)], inc=sEV)
                evcnt[id(bank)] = sEV.n

        ctx_scope = ExitStack()
        ctx_tm = ctx_scope.enter_context(nc.sbuf_tensor("ctx_tm", [128, NT, E], BF16))

        with ExitStack() as st:
            x0bf = st.enter_context(nc.sbuf_tensor("x0bf", [128, NC, S], BF16))
            x0obf = st.enter_context(nc.sbuf_tensor("x0obf", [128, NC, TOK], BF16))
            wst = [st.enter_context(nc.sbuf_tensor(f"wst{i}", [128, NC, 384], BF16)) for i in range(2)]
            Qp = [st.enter_context(nc.sbuf_tensor(f"Qp{i}", [128, TOK], BF16)) for i in range(2)]
            Kp = [st.enter_context(nc.sbuf_tensor(f"Kp{i}", [128, S], BF16)) for i in range(2)]
            Vp = [st.enter_context(nc.sbuf_tensor(f"Vp{i}", [128, 32 * 130], BF16)) for i in range(2)]
            psb = [st.enter_context(nc.sbuf_tensor(f"psb{i}", [128, 1024], BF16)) for i in range(4)]
            sqb = st.enter_context(nc.sbuf_tensor("sqb", [128, 512], BF16))
            S0 = st.enter_context(nc.psum_tensor("S0", [128, 1024], F32))
            S1 = st.enter_context(nc.psum_tensor("S1", [128, 1024], F32))
            ctxp = st.enter_context(nc.psum_tensor("ctxp", [128, 1024], F32))
            pk = [st.enter_context(nc.psum_tensor(f"pk{i}", [128, 512], F32)) for i in range(2)]

            ph = Prog(nc)
            for c in range(NC):
                ph.op("gpsimd", lambda g, c=c: g.dma_start(x0bf[:, c, :], x0f[:, c, :]),
                      inc=dG, inc_by=16)
                ph.op("gpsimd", lambda g, c=c: g.dma_start(x0obf[:, c, :], x0fo[:, c, :]),
                      inc=dG, inc_by=16)
            x0rdy = (dG, dG.n)
            ph.op("gpsimd", lambda g: g.dma_start(wst[0][:, :, :], wqkv1[:, 0, :, :]),
                  inc=dG, inc_by=16)
            for hp in range(8):
                wb = wst[hp % 2]
                wq_ready = (dG, dG.n)
                if hp < 7:
                    ph.op("gpsimd", lambda g, hp=hp: g.dma_start(
                        wst[(hp + 1) % 2][:, :, :], wqkv1[:, hp + 1, :, :]),
                        after=[(sPJ, sPJ.n)], inc=dG, inc_by=16)
                emit_qkv(ph, False, hp, wb, wq_ready, Qp[hp % 2], Kp[hp % 2],
                         Vp[hp % 2], pk, x0obf, [x0rdy], x0bf, x0bf, S)
                emit_norm_bound(ph, pk[0], Qp[hp % 2], 4, sqb, 0, True)
                emit_norm_bound(ph, pk[0], Kp[hp % 2], 8, sqb, 3, True)
                emit_make_negm(ph, pk[1], 0, 3, negm, hp % 2)
                ready = [(sEV, sEV.n)]
                if PAIR1 and hp < PAIR1_N and PMODE == "paired":
                    emit_attention(ph, False, hp, Qp[hp % 2], Kp[hp % 2], Vp[hp % 2],
                                   negm, hp % 2, ctx_tm, [S0, S1], ctxp, pk, psb, ready)
                elif PAIR1 and hp < PAIR1_N and PMODE == "psb4":
                    emit_attention_p4(ph, False, hp, Qp[hp % 2], Kp[hp % 2], Vp[hp % 2],
                                      negm, hp % 2, ctx_tm, [S0, S1], ctxp, psb, ready)
                elif PAIR1 and hp < PAIR1_N and PMODE == "pkctx":
                    emit_attention_pc(ph, False, hp, Qp[hp % 2], Kp[hp % 2], Vp[hp % 2],
                                      negm, hp % 2, ctx_tm, [S0, S1], ctxp, pk, psb, ready)
                else:
                    emit_attention_v1(ph, False, hp, Qp[hp % 2], Kp[hp % 2], Vp[hp % 2],
                                      negm, hp % 2, ctx_tm, [S0, S1], ctxp, psb, ready)
            emit_drain(ph)
            ph.run()

        with ExitStack() as st:
            ctx_fm = st.enter_context(nc.sbuf_tensor("ctx_fm", [128, NC, TOK], BF16))
            wo1 = st.enter_context(nc.sbuf_tensor("wo1", [128, NC, E], BF16))
            x0t = [st.enter_context(nc.sbuf_tensor(f"x0t{i}", [128, E], F32)) for i in range(2)]
            tres = st.enter_context(nc.sbuf_tensor("tres", [128, E], F32))
            xout = [st.enter_context(nc.sbuf_tensor(f"xout{i}", [128, E], F32)) for i in range(2)]
            scr = st.enter_context(nc.sbuf_tensor("scr", [128, E], F32))
            tp = [st.enter_context(nc.psum_tensor(f"tp{i}", [128, 128], BF16)) for i in range(2)]
            po = [st.enter_context(nc.psum_tensor(f"po{i}", [128, 512], F32)) for i in range(2)]

            ph = Prog(nc)
            if DEBUG_TAPS:
                dc1 = nc.dram_tensor("dbgctx1", [128, NT * E], BF16,
                                     kind="ExternalOutput").ap()
                ph.op("sync", lambda s: s.dma_start(
                    dc1, AP(ctx_tm, 0, [[NT * E, 128], [1, NT * E]])),
                    after=[(sDV, sDV.n)], inc=dL, inc_by=16)
            ph.op("gpsimd", lambda g: g.dma_start(wo1[:, :, :], wout1), inc=dG, inc_by=16)
            wordy = (dG, dG.n)
            divdone = (sDV, sDV.n)
            for t in range(NT):
                for c in range(NC):
                    k = (t * NC + c) % 2
                    ph.op("tensor", lambda tt, t=t, c=c, k=k: tt.transpose(
                        tp[k][:, :], ctx_tm[:, t, 128 * c:128 * c + 128], ident[:, :]),
                        after=[divdone, (sEV, evcnt.get(id(tp[k]), 0))], inc=sTR)
                    ph.op("vector", lambda v, t=t, c=c, k=k: v.tensor_copy(
                        ctx_fm[:, c, 128 * t:128 * t + 128], tp[k][:, :]),
                        after=[(sTR, sTR.n)], inc=sEV)
                    evcnt[id(tp[k])] = sEV.n
            trall = (sEV, sEV.n)
            if DEBUG_TAPS:
                dcf = nc.dram_tensor("dbgctxfm", [128, NC * TOK], BF16,
                                     kind="ExternalOutput").ap()
                ph.op("sync", lambda s: s.dma_start(
                    dcf, AP(ctx_fm, 0, [[NC * TOK, 128], [1, NC * TOK]])),
                    after=[trall], inc=dL, inc_by=16)
            x0used = {}
            xoutfree = {}
            for t in range(NT):
                ph.op("sync", lambda s, t=t: s.dma_start(
                    x0t[t % 2][:, :], x0tm[128 * t:128 * t + 128, :]),
                    after=[(sEV, x0used.get(t % 2, 0))], inc=dL, inc_by=16)
                x0rdy2 = (dL, dL.n)
                for jj in range(2):
                    bank = po[jj]
                    for c in range(NC):
                        ph.op("tensor", lambda tt, t=t, c=c, jj=jj, bank=bank: tt.matmul(
                            bank[:, :], ctx_fm[:, c, 128 * t:128 * t + 128],
                            wo1[:, c, 512 * jj:512 * jj + 512],
                            start=(c == 0), stop=(c == NC - 1)),
                            after=[wordy, trall, (sEV, evcnt.get(id(bank), 0))],
                            inc=(sPJ if c == NC - 1 else None))
                ph.op("vector", lambda v, t=t: v.tensor_add(
                    tres[:, 0:512], po[0][:, :], x0t[t % 2][:, 0:512]),
                    after=[(sPJ, sPJ.n), x0rdy2])
                ph.op("vector", lambda v, t=t: v.tensor_add(
                    tres[:, 512:1024], po[1][:, :], x0t[t % 2][:, 512:1024]), inc=sEV)
                evcnt[id(po[0])] = sEV.n
                evcnt[id(po[1])] = sEV.n
                x0used[t % 2] = sEV.n
                emit_ln(ph, tres[:, :], xout[t % 2][:, :], scr,
                        outwait=(dL, xoutfree.get(t % 2, 0)), inc=sC)
                xo_ok = (sC, sC.n)
                ph.op("sync", lambda s, t=t: s.dma_start(
                    x1d[128 * t:128 * t + 128, :], xout[t % 2][:, :]),
                    after=[xo_ok], inc=dL, inc_by=16)
                xoutfree[t % 2] = dL.n
            emit_drain(ph)
            ph.run()
        ctx_scope.close()

        mid = ExitStack()
        ctx2_tm = mid.enter_context(nc.sbuf_tensor("ctx2_tm", [128, NT, E], BF16))
        midq = ExitStack()
        Q2 = midq.enter_context(nc.sbuf_tensor("Q2", [128, 8, TOK], BF16))

        with ExitStack() as st:
            ropefm = st.enter_context(nc.sbuf_tensor("ropefm", [128, NC, TOK], BF16))
            x1fm = st.enter_context(nc.sbuf_tensor("x1fm", [128, NC, TOK], BF16))
            x1t = [st.enter_context(nc.sbuf_tensor(f"x1t{i}", [128, E], F32)) for i in range(2)]
            cs = [st.enter_context(nc.sbuf_tensor(f"cs{i}", [128, E], F32)) for i in range(2)]
            scrA = st.enter_context(nc.sbuf_tensor("scrA", [128, E], F32))
            rope_tm = st.enter_context(nc.sbuf_tensor("rope_tm", [128, E], F32))
            ropebf = st.enter_context(nc.sbuf_tensor("ropebf", [128, E], BF16))
            x1bf = st.enter_context(nc.sbuf_tensor("x1bf", [128, E], BF16))
            wst2 = [st.enter_context(nc.sbuf_tensor(f"wst2{i}", [128, NC, 384], BF16)) for i in range(2)]
            K2p = st.enter_context(nc.sbuf_tensor("K2p", [128, TOK], BF16))
            V2p = st.enter_context(nc.sbuf_tensor("V2p", [128, NT * 130], BF16))
            sqb2 = st.enter_context(nc.sbuf_tensor("sqb2", [128, 512], BF16))
            tp = [st.enter_context(nc.psum_tensor(f"tp2{i}", [128, 128], BF16)) for i in range(2)]
            pk = [st.enter_context(nc.psum_tensor(f"pk2{i}", [128, 512], F32)) for i in range(2)]

            ph = Prog(nc)
            tilefree = {}
            for t in range(NT):
                ph.op("sync", lambda s, t=t: s.dma_start(
                    x1t[t % 2][:, :], x1d[128 * t:128 * t + 128, :]),
                    after=[(sC, tilefree.get(t % 2, 0))], inc=dL, inc_by=16)
                ph.op("sync", lambda s, t=t: s.dma_start(
                    cs[t % 2][:, 0:512], cost[128 * t:128 * t + 128, :]),
                    inc=dL, inc_by=16)
                ph.op("sync", lambda s, t=t: s.dma_start(
                    cs[t % 2][:, 512:1024], sint[128 * t:128 * t + 128, :]),
                    inc=dL, inc_by=16)
                ld = (dL, dL.n)
                xt = x1t[t % 2]
                ct = cs[t % 2][:, 0:512]
                sn = cs[t % 2][:, 512:1024]
                xe = AP(xt, 0, [[E, 128], [2, 512]])
                xo = AP(xt, 1, [[E, 128], [2, 512]])
                re_ = AP(rope_tm, 0, [[E, 128], [2, 512]])
                ro_ = AP(rope_tm, 1, [[E, 128], [2, 512]])
                ph.op("vector", lambda v, xe=xe, ct=ct: v.tensor_mul(scrA[:, 0:512], xe, ct),
                      after=[ld])
                ph.op("vector", lambda v, xo=xo, sn=sn: v.tensor_mul(scrA[:, 512:1024], xo, sn))
                ph.op("vector", lambda v, re_=re_: v.tensor_sub(re_, scrA[:, 0:512], scrA[:, 512:1024]))
                ph.op("vector", lambda v, xe=xe, sn=sn: v.tensor_mul(scrA[:, 0:512], xe, sn))
                ph.op("vector", lambda v, xo=xo, ct=ct: v.tensor_mul(scrA[:, 512:1024], xo, ct))
                ph.op("vector", lambda v, ro_=ro_: v.tensor_add(ro_, scrA[:, 0:512], scrA[:, 512:1024]))
                ph.op("vector", lambda v: v.tensor_copy(ropebf[:, :], rope_tm[:, :]))
                ph.op("vector", lambda v, xt=xt: v.tensor_copy(x1bf[:, :], xt[:, :]), inc=sC)
                tilefree[t % 2] = sC.n
                cast_ok = (sC, sC.n)
                for c in range(NC):
                    k = (t * NC + c) % 2
                    k2 = (t * NC + c + 1) % 2
                    ph.op("tensor", lambda tt, c=c, k=k: tt.transpose(
                        tp[k][:, :], ropebf[:, 128 * c:128 * c + 128], ident[:, :]),
                        after=[cast_ok, (sEV, evcnt.get(id(tp[k]), 0))], inc=sTR)
                    ph.op("vector", lambda v, t=t, c=c, k=k: v.tensor_copy(
                        ropefm[:, c, 128 * t:128 * t + 128], tp[k][:, :]),
                        after=[(sTR, sTR.n)], inc=sEV)
                    evcnt[id(tp[k])] = sEV.n
                    ph.op("tensor", lambda tt, c=c, k2=k2: tt.transpose(
                        tp[k2][:, :], x1bf[:, 128 * c:128 * c + 128], ident[:, :]),
                        after=[cast_ok, (sEV, evcnt.get(id(tp[k2]), 0))], inc=sTR)
                    ph.op("vector", lambda v, t=t, c=c, k2=k2: v.tensor_copy(
                        x1fm[:, c, 128 * t:128 * t + 128], tp[k2][:, :]),
                        after=[(sTR, sTR.n)], inc=sEV)
                    evcnt[id(tp[k2])] = sEV.n
                ph.op("vector", lambda v: v.nop(), after=[(sTR, sTR.n)], inc=sC)
                tilefree[t % 2] = sC.n
            trdone = (sEV, sEV.n)
            ph.op("gpsimd", lambda g: g.dma_start(wst2[0][:, :, :], wqkv2[:, 0, :, :]),
                  inc=dG, inc_by=16)
            kv2st = 0
            for hp in range(8):
                wb = wst2[hp % 2]
                wq_ready = (dG, dG.n)
                if hp < 7:
                    ph.op("gpsimd", lambda g, hp=hp: g.dma_start(
                        wst2[(hp + 1) % 2][:, :, :], wqkv2[:, hp + 1, :, :]),
                        after=[(sPJ, sPJ.n)], inc=dG, inc_by=16)
                Qsl = AP(Q2, hp * TOK, [[8 * TOK, 128], [1, TOK]])
                emit_qkv(ph, True, hp, wb, wq_ready, Qsl, K2p, V2p, pk,
                         ropefm, [trdone], ropefm, x1fm, TOK,
                         kvfree=(dL, kv2st))
                emit_norm_bound(ph, pk[0], Qsl, 4, sqb2, 8 + hp, True)
                emit_norm_bound(ph, pk[0], K2p, 4, sqb2, 1, hp == 0)
                ph.op("sync", lambda s, hp=hp: s.dma_start(
                    k2loc.ap()[128 * hp:128 * hp + 128, :], K2p[:, :]),
                    after=[(sEV, sEV.n)], inc=dL, inc_by=16)
                ph.op("sync", lambda s, hp=hp: s.dma_start(
                    v2loc.ap().rearrange("(t p) f -> p t f", p=128)[:, :, 130 * hp:130 * hp + 130],
                    AP(V2p, 0, [[NT * 130, 128], [130, NT], [1, 130]])),
                    after=[(sEV, sEV.n)], inc=dL, inc_by=16)
                kv2st = dL.n
            ph.op("sync", lambda s: s.dma_start(arin.ap(), nrm[0:1, 1:2]),
                  after=[(sEV, sEV.n)], inc=dL, inc_by=16)
            ph.op("gpsimd", lambda g: g.collective_compute(
                "AllGather", ALU.bypass, RG, [k2loc.ap().opt()], [k2all.ap().opt()]),
                after=[(dL, dL.n)], inc=cc1)
            ph.op("gpsimd", lambda g: g.collective_compute(
                "AllGather", ALU.bypass, RG, [v2loc.ap().opt()], [v2all.ap().opt()]),
                inc=cc2)
            ph.op("gpsimd", lambda g: g.collective_compute(
                "AllReduce", ALU.max, RG, [arin.ap().opt()], [arout.ap().opt()]),
                inc=cc3)
            ph.op("sync", lambda s: s.dma_start(nrm[0:1, 1:2], arout.ap()),
                  after=[(cc3, cc3.n)], inc=dL, inc_by=16)
            ph.op("vector", lambda v: v.nop(), after=[(dL, dL.n)], inc=sC)
            for hp in range(8):
                emit_make_negm(ph, pk[1], 8 + hp, 1, negm2, hp)
            emit_drain(ph)
            ph.run()

        with ExitStack() as st:
            Kb = [st.enter_context(nc.sbuf_tensor(f"Kb{i}", [128, S], BF16)) for i in range(2)]
            Vb = [st.enter_context(nc.sbuf_tensor(f"Vb{i}", [128, 32 * 130], BF16)) for i in range(2)]
            psb = [st.enter_context(nc.sbuf_tensor(f"psb2{i}", [128, 1024], BF16)) for i in range(4)]
            S0 = st.enter_context(nc.psum_tensor("S0b", [128, 1024], F32))
            S1 = st.enter_context(nc.psum_tensor("S1b", [128, 1024], F32))
            ctxp = st.enter_context(nc.psum_tensor("ctxp2", [128, 1024], F32))
            pk3 = [st.enter_context(nc.psum_tensor(f"pk3{i}", [128, 512], F32)) for i in range(2)]

            ph = Prog(nc)
            kread = {}
            for hp in range(8):
                bufK, bufV = Kb[hp % 2], Vb[hp % 2]
                usedone = (sP, kread.get(hp % 2, 0))
                for pbit in range(2):
                    ph.op("sync", lambda s, hp=hp, pbit=pbit, bufK=bufK: s.dma_start(
                        AP(bufK, 128 * pbit, [[S, 128], [256, 16], [1, 128]]),
                        k2all.ap()[1024 * pbit + 128 * hp:1024 * pbit + 128 * hp + 128, :]
                        .rearrange("p (t f) -> p t f", f=128)),
                        after=[(cc1, cc1.n), usedone], inc=dL, inc_by=16)
                    ph.op("sync", lambda s, hp=hp, pbit=pbit, bufV=bufV: s.dma_start(
                        AP(bufV, 130 * pbit, [[32 * 130, 128], [260, 16], [1, 130]]),
                        v2all.ap()[2048 * pbit:2048 * pbit + 2048, 130 * hp:130 * hp + 130]
                        .rearrange("(t p) f -> p t f", p=128)),
                        after=[(cc2, cc2.n), usedone], inc=dL, inc_by=16)
                kv_ok = (dL, dL.n)
                Qsl = AP(Q2, hp * TOK, [[8 * TOK, 128], [1, TOK]])
                if PAIR2:
                    emit_attention(ph, True, hp, Qsl, bufK, bufV, negm2, hp,
                                   ctx2_tm, [S0, S1], ctxp, pk3, psb, ready=[kv_ok])
                else:
                    emit_attention_v1(ph, True, hp, Qsl, bufK, bufV, negm2, hp,
                                      ctx2_tm, [S0, S1], ctxp, psb, ready=[kv_ok])
                kread[hp % 2] = sP.n
            if DEBUG_TAPS:
                dq = nc.dram_tensor("dbgQ2", [128, 8 * TOK], BF16,
                                    kind="ExternalOutput").ap()
                dc = nc.dram_tensor("dbgctx2", [128, NT * E], BF16,
                                    kind="ExternalOutput").ap()
                dx1 = nc.dram_tensor("dbgx1", [TOK, E], F32,
                                     kind="ExternalOutput").ap()
                dnm = nc.dram_tensor("dbgnm", [128, 8], F32,
                                     kind="ExternalOutput").ap()
                ph.op("sync", lambda s: s.dma_start(
                    dq, AP(Q2, 0, [[8 * TOK, 128], [1, 8 * TOK]])),
                    after=[(sDV, sDV.n)], inc=dL, inc_by=16)
                ph.op("sync", lambda s: s.dma_start(
                    dc, AP(ctx2_tm, 0, [[NT * E, 128], [1, NT * E]])),
                    after=[(sDV, sDV.n)], inc=dL, inc_by=16)
                ph.op("sync", lambda s: s.dma_start(dx1, x1d),
                      inc=dL, inc_by=16)
                ph.op("sync", lambda s: s.dma_start(
                    dnm, negm2[:, :]), inc=dL, inc_by=16)
            emit_drain(ph)
            ph.run()
        midq.close()

        p45 = ExitStack()
        y1fm = p45.enter_context(nc.sbuf_tensor("y1fm", [128, NC, TOK], BF16, side="right"))
        g1 = p45.enter_context(nc.sbuf_tensor("g1t", [128, E], F32, side="right"))
        b1 = p45.enter_context(nc.sbuf_tensor("b1t", [128, E], F32, side="right"))
        g2 = p45.enter_context(nc.sbuf_tensor("g2t", [128, E], F32, side="right"))
        b2 = p45.enter_context(nc.sbuf_tensor("b2t", [128, E], F32, side="right"))
        w_all = p45.enter_context(nc.sbuf_tensor("w_all", [128, NT, 8], F32, side="right"))

        with ExitStack() as st:
            ctx2_fm = st.enter_context(nc.sbuf_tensor("ctx2_fm", [128, NC, TOK], BF16))
            wo2 = st.enter_context(nc.sbuf_tensor("wo2", [128, NC, E], BF16))
            x1t = [st.enter_context(nc.sbuf_tensor(f"p4x1t{i}", [128, E], F32)) for i in range(2)]
            tres = st.enter_context(nc.sbuf_tensor("p4tres", [128, E], F32))
            yout = [st.enter_context(nc.sbuf_tensor(f"yout{i}", [128, E], F32)) for i in range(2)]
            ybf = st.enter_context(nc.sbuf_tensor("ybf", [128, E], BF16))
            scr = st.enter_context(nc.sbuf_tensor("p4scr", [128, E], F32))
            rstage = st.enter_context(nc.sbuf_tensor("p4rst", [1, E], F32))
            tp = [st.enter_context(nc.psum_tensor(f"tp4{i}", [128, 128], BF16)) for i in range(2)]
            po = [st.enter_context(nc.psum_tensor(f"po4{i}", [128, 512], F32)) for i in range(2)]
            pg = st.enter_context(nc.psum_tensor("pg4", [128, 512], F32))

            ph = Prog(nc)
            ph.op("gpsimd", lambda g: g.dma_start(wo2[:, :, :], wout2), inc=dG, inc_by=16)
            wordy = (dG, dG.n)
            rowused = 0
            for (row, dst) in ((ln1g, g1), (ln1b, b1), (ln2g, g2), (ln2b, b2)):
                ph.op("sync", lambda s, row=row: s.dma_start(rstage[:, :], row),
                      after=[(sEV, rowused)], inc=dL, inc_by=16)
                rowrdy = (dL, dL.n)
                for jj in range(2):
                    ph.op("tensor", lambda t, jj=jj: t.matmul(
                        pg[:, :], ones1f[:, :], rstage[0:1, 512 * jj:512 * jj + 512],
                        start=True, stop=True),
                        after=[rowrdy, (sEV, evcnt.get(id(pg), 0))], inc=sPJ)
                    ph.op("vector", lambda v, dst=dst, jj=jj: v.tensor_copy(
                        dst[:, 512 * jj:512 * jj + 512], pg[:, :]),
                        after=[(sPJ, sPJ.n)], inc=sEV)
                    evcnt[id(pg)] = sEV.n
                rowused = sEV.n
            divdone = (sDV, sDV.n)
            for t in range(NT):
                for c in range(NC):
                    k = (t * NC + c) % 2
                    ph.op("tensor", lambda tt, t=t, c=c, k=k: tt.transpose(
                        tp[k][:, :], ctx2_tm[:, t, 128 * c:128 * c + 128], ident[:, :]),
                        after=[divdone, (sEV, evcnt.get(id(tp[k]), 0))], inc=sTR)
                    ph.op("vector", lambda v, t=t, c=c, k=k: v.tensor_copy(
                        ctx2_fm[:, c, 128 * t:128 * t + 128], tp[k][:, :]),
                        after=[(sTR, sTR.n)], inc=sEV)
                    evcnt[id(tp[k])] = sEV.n
            trall = (sEV, sEV.n)
            x1used = {}
            youtfree = {}
            ybffree = 0
            for t in range(NT):
                ph.op("sync", lambda s, t=t: s.dma_start(
                    x1t[t % 2][:, :], x1d[128 * t:128 * t + 128, :]),
                    after=[(sEV, x1used.get(t % 2, 0))], inc=dL, inc_by=16)
                x1rdy = (dL, dL.n)
                for jj in range(2):
                    bank = po[jj]
                    for c in range(NC):
                        ph.op("tensor", lambda tt, t=t, c=c, jj=jj, bank=bank: tt.matmul(
                            bank[:, :], ctx2_fm[:, c, 128 * t:128 * t + 128],
                            wo2[:, c, 512 * jj:512 * jj + 512],
                            start=(c == 0), stop=False),
                            after=[wordy, trall, (sEV, evcnt.get(id(bank), 0))])
                    ph.op("tensor", lambda tt, jj=jj, bank=bank: tt.matmul(
                        bank[:, :], ones1[:, :], bo_bf[0:1, 512 * jj:512 * jj + 512],
                        start=False, stop=True), inc=sPJ)
                ph.op("vector", lambda v, t=t: v.tensor_add(
                    tres[:, 0:512], po[0][:, :], x1t[t % 2][:, 0:512]),
                    after=[(sPJ, sPJ.n), x1rdy])
                ph.op("vector", lambda v, t=t: v.tensor_add(
                    tres[:, 512:1024], po[1][:, :], x1t[t % 2][:, 512:1024]), inc=sEV)
                evcnt[id(po[0])] = sEV.n
                evcnt[id(po[1])] = sEV.n
                x1used[t % 2] = sEV.n
                emit_ln(ph, tres[:, :], yout[t % 2][:, :], scr, g1[:, :], b1[:, :],
                        outwait=(dL, youtfree.get(t % 2, 0)))
                ph.op("vector", lambda v, t=t: v.tensor_copy(ybf[:, :], yout[t % 2][:, :]),
                      after=[(sTR, ybffree)], inc=sC)
                ybf_ok = (sC, sC.n)
                ph.op("sync", lambda s, t=t: s.dma_start(
                    y1d[128 * t:128 * t + 128, :], yout[t % 2][:, :]),
                    after=[ybf_ok], inc=dL, inc_by=16)
                youtfree[t % 2] = dL.n
                for c in range(NC):
                    k = (t * NC + c) % 2
                    ph.op("tensor", lambda tt, t=t, c=c, k=k: tt.transpose(
                        tp[k][:, :], ybf[:, 128 * c:128 * c + 128], ident[:, :]),
                        after=[ybf_ok, (sEV, evcnt.get(id(tp[k]), 0))], inc=sTR)
                    ph.op("vector", lambda v, t=t, c=c, k=k: v.tensor_copy(
                        y1fm[:, c, 128 * t:128 * t + 128], tp[k][:, :]),
                        after=[(sTR, sTR.n)], inc=sEV)
                    evcnt[id(tp[k])] = sEV.n
                ybffree = sTR.n
            emit_drain(ph)
            ph.run()
        mid.close()

        with ExitStack() as st:
            acc = st.enter_context(nc.sbuf_tensor("acc", [128, NT, E], F32))
            Webf = [st.enter_context(nc.sbuf_tensor(f"Webf{i}", [128, NC, E], BF16)) for i in range(2)]
            be_bf = st.enter_context(nc.sbuf_tensor("be_bf", [1, NE * E], BF16))
            gex = st.enter_context(nc.sbuf_tensor("gex", [128, 8], F32))
            gsc = st.enter_context(nc.sbuf_tensor("gsc", [128, 8], F32))
            pgt = st.enter_context(nc.psum_tensor("pgt", [128, 8], F32))
            pe_ = [st.enter_context(nc.psum_tensor(f"pe{i}", [128, 512], F32)) for i in range(4)]

            ph = Prog(nc)
            ph.op("gpsimd", lambda g: g.dma_start(be_bf[:, :], berow), inc=dG, inc_by=16)
            berdy = (dG, dG.n)
            for t in range(NT):
                for c in range(NC):
                    ph.op("tensor", lambda tt, t=t, c=c: tt.matmul(
                        pgt[:, :], y1fm[:, c, 128 * t:128 * t + 128], gwt_bf[:, c, :],
                        start=(c == 0), stop=False),
                        after=[(sEV, evcnt.get(id(pgt), 0))])
                ph.op("tensor", lambda tt: tt.matmul(
                    pgt[:, :], ones1[:, :], gb_bf[0:1, :], start=False, stop=True),
                    inc=sPJ)
                ph.op("vector", lambda v: v.reduce_max(stt[:, 9:10], pgt[:, :], axis=AX.X),
                      after=[(sPJ, sPJ.n)], inc=sV)
                ph.op("vector", lambda v: v.tensor_scalar(
                    stt[:, 10:11], stt[:, 9:10], -1.0, None, ALU.mult), inc=sA,
                    wafter=[(sV, sV.n)])
                ph.op("scalar", lambda s: s.activation(
                    gex[:, :], pgt[:, :], AF.Exp, bias=stt[:, 10:11],
                    accum_out=stt[:, 11:12]), after=[(sA, sA.n)], inc=sB)
                ph.op("vector", lambda v: v.nop(), after=[(sB, sB.n)], inc=sEV)
                evcnt[id(pgt)] = sEV.n
                ph.op("vector", lambda v: v.reciprocal(stt[:, 12:13], stt[:, 11:12]),
                      inc=sV)
                ph.op("vector", lambda v: v.reduce_max(stt[:, 13:14], gex[:, :], axis=AX.X),
                      inc=sV)
                ph.op("vector", lambda v: v.tensor_scalar(
                    gsc[:, :], gex[:, :], stt[:, 13:14], -1e30, ALU.is_equal, ALU.mult),
                    wafter=[(sV, sV.n)], inc=sV)
                ph.op("vector", lambda v: v.tensor_add(gsc[:, :], gsc[:, :], gex[:, :]),
                      wafter=[(sV, sV.n)], inc=sV)
                ph.op("vector", lambda v: v.reduce_max(stt[:, 14:15], gsc[:, :], axis=AX.X),
                      wafter=[(sV, sV.n)], inc=sV)
                ph.op("vector", lambda v: v.tensor_scalar(
                    gsc[:, :], gex[:, :], stt[:, 14:15], None, ALU.is_ge),
                    wafter=[(sV, sV.n)], inc=sV)
                ph.op("vector", lambda v: v.tensor_mul(gsc[:, :], gsc[:, :], gex[:, :]),
                      wafter=[(sV, sV.n)], inc=sV)
                ph.op("vector", lambda v, t=t: v.tensor_scalar(
                    w_all[:, t, :], gsc[:, :], stt[:, 12:13], None, ALU.mult),
                    wafter=[(sV, sV.n)], inc=sC)
            wrdy = (sC, sC.n)
            ph.op("gpsimd", lambda g: g.dma_start(Webf[0][:, :, :], wexp[0, :, :, :]),
                  inc=dG, inc_by=16)
            for e in range(NE):
                wb = Webf[e % 2]
                werdy = (dG, dG.n)
                if e < NE - 1:
                    ph.op("gpsimd", lambda g, e=e: g.dma_start(
                        Webf[(e + 1) % 2][:, :, :], wexp[e + 1, :, :, :]),
                        after=[(sPJ, sPJ.n)], inc=dG, inc_by=16)
                for t in range(NT):
                    bks = (pe_[2 * (t % 2)], pe_[2 * (t % 2) + 1])
                    for c in range(NC):
                        for jj in range(2):
                            ph.op("tensor", lambda tt, t=t, c=c, jj=jj, wb=wb,
                                  bks=bks: tt.matmul(
                                      bks[jj][:, :], y1fm[:, c, 128 * t:128 * t + 128],
                                      wb[:, c, 512 * jj:512 * jj + 512],
                                      start=(c == 0), stop=False),
                                  after=[werdy, (sEV, evcnt.get(id(bks[jj]), 0))])
                    for jj in range(2):
                        ph.op("tensor", lambda tt, e=e, jj=jj, bks=bks: tt.matmul(
                            bks[jj][:, :], ones1[:, :],
                            be_bf[0:1, E * e + 512 * jj:E * e + 512 * jj + 512],
                            start=False, stop=True), after=[berdy], inc=sPJ)
                    pjv = (sPJ, sPJ.n)
                    for jj in range(2):
                        bank = bks[jj]
                        if e == 0:
                            ph.op("vector", lambda v, t=t, jj=jj, bank=bank: v.tensor_scalar(
                                acc[:, t, 512 * jj:512 * jj + 512], bank[:, :],
                                w_all[:, t, 0:1], None, ALU.mult),
                                after=[pjv, wrdy], inc=sEV)
                        else:
                            ph.op("vector", lambda v, t=t, jj=jj, e=e, bank=bank:
                                  v.scalar_tensor_tensor(
                                      acc[:, t, 512 * jj:512 * jj + 512], bank[:, :],
                                      w_all[:, t, e:e + 1],
                                      acc[:, t, 512 * jj:512 * jj + 512],
                                      ALU.mult, ALU.add),
                                  after=[pjv], inc=sEV)
                        evcnt[id(bank)] = sEV.n
            mdone = (sEV, sEV.n)
            emit_drain(ph)
            ph.run()

            with ExitStack() as st6:
                yt = [st6.enter_context(nc.sbuf_tensor(f"p6y{i}", [128, E], F32)) for i in range(2)]
                xt = [st6.enter_context(nc.sbuf_tensor(f"p6x{i}", [128, E], F32)) for i in range(2)]
                t2 = st6.enter_context(nc.sbuf_tensor("p6t2", [128, E], F32))
                f_ = st6.enter_context(nc.sbuf_tensor("p6f", [128, E], F32))
                t3 = st6.enter_context(nc.sbuf_tensor("p6t3", [128, E], F32))
                oout = [st6.enter_context(nc.sbuf_tensor(f"p6o{i}", [128, E], F32)) for i in range(2)]
                scr6 = st6.enter_context(nc.sbuf_tensor("p6scr", [128, E], F32))

                ph = Prog(nc)
                tileused = {}
                ofree = {}
                for t in range(NT):
                    ph.op("sync", lambda s, t=t: s.dma_start(
                        yt[t % 2][:, :], y1d[128 * t:128 * t + 128, :]),
                        after=[(sC, tileused.get(t % 2, 0))], inc=dL, inc_by=16)
                    ph.op("sync", lambda s, t=t: s.dma_start(
                        xt[t % 2][:, :], x1d[128 * t:128 * t + 128, :]),
                        inc=dL, inc_by=16)
                    ld = (dL, dL.n)
                    ph.op("vector", lambda v, t=t: v.tensor_add(
                        t2[:, :], yt[t % 2][:, :], acc[:, t, :]), after=[ld])
                    emit_ln(ph, t2[:, :], f_[:, :], scr6, g2[:, :], b2[:, :])
                    ph.op("vector", lambda v, t=t: v.tensor_add(
                        t3[:, :], xt[t % 2][:, :], f_[:, :]), inc=sC)
                    tileused[t % 2] = sC.n
                    emit_ln(ph, t3[:, :], oout[t % 2][:, :], scr6,
                            outwait=(dL, ofree.get(t % 2, 0)), inc=sC)
                    oo_ok = (sC, sC.n)
                    ph.op("sync", lambda s, t=t: s.dma_start(
                        OUT[128 * t:128 * t + 128, :], oout[t % 2][:, :]),
                        after=[oo_ok], inc=dL, inc_by=16)
                    ofree[t % 2] = dL.n
                ph.op("sync", lambda s: s.nop(), after=[(dL, dL.n)])
                ph.op("gpsimd", lambda g: g.nop(), after=[(dG, dG.n)])
                ph.run()

    return nc


_CACHE = {}


def _host_inputs(inputs):
    x0 = np.asarray(inputs["L0_attn_x"], np.float32)
    w_qkv = np.asarray(inputs["L0_attn_w_qkv"], np.float32)
    w_out = np.asarray(inputs["L0_attn_w_out"], np.float32)
    ipw = np.asarray(inputs["L0_ffn_in_proj_w"], np.float32)
    ipb = np.asarray(inputs["L0_ffn_in_proj_b"], np.float32)
    opw = np.asarray(inputs["L0_ffn_out_proj_w"], np.float32)
    opb = np.asarray(inputs["L0_ffn_out_proj_b"], np.float32)
    gw = np.asarray(inputs["L0_ffn_gate_w"], np.float32)
    gb = np.asarray(inputs["L0_ffn_gate_b"], np.float32)
    ew = np.asarray(inputs["L0_ffn_expert_w"], np.float32)
    eb = np.asarray(inputs["L0_ffn_expert_b"], np.float32)
    l1g = np.asarray(inputs["L0_ffn_ln1_w"], np.float32)
    l1b = np.asarray(inputs["L0_ffn_ln1_b"], np.float32)
    l2g = np.asarray(inputs["L0_ffn_ln2_w"], np.float32)
    l2b = np.asarray(inputs["L0_ffn_ln2_b"], np.float32)

    def pairs_layout(Wq, Wk, Wv):
        blocks = []
        for hp in range(8):
            s = slice(128 * hp, 128 * hp + 128)
            blocks.append(np.concatenate([Wq[:, s], Wk[:, s], Wv[:, s]], axis=1))
        arr = np.stack(blocks, axis=0)
        arr = arr.reshape(8, 8, 128, 384).transpose(2, 0, 1, 3)
        return np.ascontiguousarray(arr)

    wqkv1 = pairs_layout(w_qkv[:, 0:E], w_qkv[:, E:2 * E], w_qkv[:, 2 * E:3 * E])
    wqkv2 = pairs_layout(ipw[0:E].T, ipw[E:2 * E].T, ipw[2 * E:3 * E].T)
    wout1 = np.ascontiguousarray(w_out.reshape(8, 128, E).transpose(1, 0, 2))
    wout2 = np.ascontiguousarray(opw.T.reshape(8, 128, E).transpose(1, 0, 2))
    wexp = np.ascontiguousarray(
        ew.transpose(0, 2, 1).reshape(NE, 8, 128, E).transpose(0, 2, 1, 3))
    bq = ipb[0:E]; bk = ipb[E:2 * E]; bv = ipb[2 * E:3 * E]
    bqk_h = np.ascontiguousarray(
        np.concatenate([bq.reshape(8, 128).T, bk.reshape(8, 128).T], axis=1))
    gwt_h = np.ascontiguousarray(gw.T.reshape(8, 128, 8).transpose(1, 0, 2))

    inv = (1.0 / 10000.0 ** (np.arange(0, HD, 2, dtype=np.float32) / HD)).astype(np.float32)
    tri = np.triu(np.ones((128, 128), np.float32))

    common = dict(
        wqkv1=wqkv1, wout1=wout1, wqkv2=wqkv2, wout2=wout2, wexp=wexp,
        bqk=bqk_h,
        bvrow=np.ascontiguousarray(bv[None, :]),
        borow=np.ascontiguousarray(opb[None, :]),
        gwt=gwt_h, gbrow=np.ascontiguousarray(gb[None, :]),
        berow=np.ascontiguousarray(eb.reshape(1, NE * E)),
        ln1g=np.ascontiguousarray(l1g[None, :]), ln1b=np.ascontiguousarray(l1b[None, :]),
        ln2g=np.ascontiguousarray(l2g[None, :]), ln2b=np.ascontiguousarray(l2b[None, :]),
        identf=np.eye(128, dtype=np.float32),
    )

    in_maps = []
    for core in range(8):
        b, par = core // 2, core % 2
        xb = x0[b]
        xT = np.ascontiguousarray(xb.T)
        x0f_ = np.ascontiguousarray(xT.reshape(8, 128, S).transpose(1, 0, 2))
        own = xb.reshape(32, 128, E)[par::2]
        x0tm_ = np.ascontiguousarray(own.reshape(TOK, E))
        x0fo_ = np.ascontiguousarray(
            x0f_.reshape(128, 8, 32, 128)[:, :, par::2, :].reshape(128, 8, TOK))
        pos = np.arange(S).reshape(32, 128)[par::2].reshape(TOK).astype(np.float32)
        fr = pos[:, None] * inv[None, :]
        cost_ = np.ascontiguousarray(np.tile(np.cos(fr).astype(np.float32), (1, H)))
        sint_ = np.ascontiguousarray(np.tile(np.sin(fr).astype(np.float32), (1, H)))
        m0_ = tri if par == 0 else np.ones((128, 128), np.float32)
        m1_ = np.zeros((128, 128), np.float32) if par == 0 else tri
        im = dict(common)
        im.update(x0f=x0f_, x0fo=x0fo_, x0tm=x0tm_, cost=cost_, sint=sint_,
                  m0in=m0_, m1in=m1_)
        in_maps.append(im)
    return in_maps


def kernel(**inputs):
    in_maps = _host_inputs(inputs)
    if "nc" not in _CACHE:
        _CACHE["nc"] = build_program()
    nc = _CACHE["nc"]
    res = run_bass_kernel_spmd(nc, in_maps, core_ids=list(range(8)))
    out = np.empty((B, S, E), np.float32)
    for core in range(8):
        b, par = core // 2, core % 2
        o = np.asarray(res.results[core]["OUT"], np.float32).reshape(NT, 128, E)
        out.reshape(B, 32, 128, E)[b, par::2] = o
    return out


# version 08
# speedup vs baseline: 1.8899x; 1.0356x over previous
import numpy as np
from contextlib import ExitStack

PAIR1 = True
PAIR1_N = 8
PMODE = "paired"
PAIR2 = True

import concourse.bass as bass
import concourse.mybir as mybir
from concourse.bass_utils import run_bass_kernel_spmd

F32 = mybir.dt.float32
BF16 = mybir.dt.bfloat16
AF = mybir.ActivationFunctionType
ALU = mybir.AluOpType
AX = mybir.AxisListType
AP = bass.AP

B, S, E, H, HD, NE = 4, 4096, 1024, 16, 64, 8
NT = 16
TOK = NT * 128
NC = 8
LN_EPS = 1e-5

ENGS = ("sync", "gpsimd", "tensor", "vector", "scalar")
DEBUG_TAPS = False


class Sem:
    __slots__ = ("h", "n", "by")

    def __init__(self, st, nc, name):
        self.h = st.enter_context(nc.semaphore(name))
        self.n = 0
        self.by = {e: 0 for e in ENGS}


class Prog:

    def __init__(self, nc):
        self.nc = nc
        self.seq = {e: [] for e in ENGS}

    def op(self, eng, emit, after=(), inc=None, inc_by=1, wafter=()):
        w = []
        for a in after:
            s, v = (a if isinstance(a, tuple) else (a, a.n))
            if v > 0 and s.by[eng] < v:
                w.append((s, v))
        for a in wafter:
            s, v = (a if isinstance(a, tuple) else (a, a.n))
            if v > 0:
                w.append((s, v))
        if inc is not None:
            inc.n += inc_by
            inc.by[eng] += inc_by
        self.seq[eng].append((w, emit, inc, inc_by))

    def run(self):
        with self.nc.Block() as blk:
            for eng in ENGS:
                ops = self.seq[eng]
                if not ops:
                    continue

                def body(engine, ops=ops):
                    seen = {}
                    for (w, emit, inc, inc_by) in ops:
                        for (s, v) in w:
                            if seen.get(id(s), -1) < v:
                                engine.wait_ge(s.h, v)
                                seen[id(s)] = v
                        inst = emit(engine)
                        if inc is not None:
                            inst.then_inc(inc.h, inc_by)
                getattr(blk, eng)(body)


def build_program():
    nc = bass.Bass("TRN2", target_bir_lowering=False,
                   detect_race_conditions=False)

    def I(name, shape, dt=F32):
        return nc.dram_tensor(name, shape, dt, kind="ExternalInput").ap()

    x0f = I("x0f", [128, NC, S])
    x0fo = I("x0fo", [128, NC, TOK])
    x0tm = I("x0tm", [TOK, E])
    wqkv1 = I("wqkv1", [128, 8, NC, 384])
    wout1 = I("wout1", [128, NC, E])
    wqkv2 = I("wqkv2", [128, 8, NC, 384])
    wout2 = I("wout2", [128, NC, E])
    wexp = I("wexp", [NE, 128, NC, E])
    bqk = I("bqk", [128, 16])
    bvrow = I("bvrow", [1, E])
    borow = I("borow", [1, E])
    gwt = I("gwt", [128, NC, 8])
    gbrow = I("gbrow", [1, 8])
    berow = I("berow", [1, NE * E])
    ln1g = I("ln1g", [1, E]); ln1b = I("ln1b", [1, E])
    ln2g = I("ln2g", [1, E]); ln2b = I("ln2b", [1, E])
    cost = I("cost", [TOK, 512]); sint = I("sint", [TOK, 512])
    m0in = I("m0in", [128, 128]); m1in = I("m1in", [128, 128])
    identf = I("identf", [128, 128])

    OUT = nc.dram_tensor("OUT", [TOK, E], F32, kind="ExternalOutput").ap()

    x1d = nc.dram_tensor("x1d", [TOK, E], F32, kind="Internal").ap()
    y1d = nc.dram_tensor("y1d", [TOK, E], F32, kind="Internal").ap()
    k2loc = nc.dram_tensor("k2loc", [1024, TOK], BF16, kind="Internal")
    v2loc = nc.dram_tensor("v2loc", [TOK, 8 * 130], BF16, kind="Internal")
    k2all = nc.dram_tensor("k2all", [2048, TOK], BF16, kind="Internal")
    v2all = nc.dram_tensor("v2all", [2 * TOK, 8 * 130], BF16, kind="Internal")
    arin = nc.dram_tensor("arin", [1, 1], F32, kind="Internal")
    arout = nc.dram_tensor("arout", [1, 1], F32, kind="Internal")
    RG = [[0, 1], [2, 3], [4, 5], [6, 7]]

    with ExitStack() as st0:
        EE = st0.enter_context
        dL = Sem(st0, nc, "dL")
        dG = Sem(st0, nc, "dG")
        cc1 = Sem(st0, nc, "cc1"); cc2 = Sem(st0, nc, "cc2"); cc3 = Sem(st0, nc, "cc3")
        sPJ = Sem(st0, nc, "sPJ")
        sEV = Sem(st0, nc, "sEV")
        sS = Sem(st0, nc, "sS")
        sE = Sem(st0, nc, "sE")
        sM = Sem(st0, nc, "sM")
        sP = Sem(st0, nc, "sP")
        sDV = Sem(st0, nc, "sDV")
        sA = Sem(st0, nc, "sA")
        sB = Sem(st0, nc, "sB")
        sTR = Sem(st0, nc, "sTR")
        sC = Sem(st0, nc, "sC")
        sV = Sem(st0, nc, "sV")

        ident = EE(nc.sbuf_tensor("ident", [128, 128], BF16))
        m0 = EE(nc.sbuf_tensor("m0", [128, 128], BF16))
        m1 = EE(nc.sbuf_tensor("m1", [128, 128], BF16))
        ones1 = EE(nc.sbuf_tensor("ones1", [1, 128], BF16))
        ones1f = EE(nc.sbuf_tensor("ones1f", [1, 128], F32))
        ones128 = EE(nc.sbuf_tensor("ones128", [128, 1], BF16))
        bqk_t = EE(nc.sbuf_tensor("bqk_t", [128, 16], F32))
        bv_bf = EE(nc.sbuf_tensor("bv_bf", [1, E], BF16))
        bo_bf = EE(nc.sbuf_tensor("bo_bf", [1, E], BF16))
        gb_bf = EE(nc.sbuf_tensor("gb_bf", [1, 8], BF16))
        gwt_bf = EE(nc.sbuf_tensor("gwt_bf", [128, NC, 8], BF16))
        negm = EE(nc.sbuf_tensor("negm", [128, 2], F32))
        negm2 = EE(nc.sbuf_tensor("negm2", [128, 8], F32))
        stt = EE(nc.sbuf_tensor("stt", [128, 16], F32))
        nrm = EE(nc.sbuf_tensor("nrm", [1, 24], F32))
        nrmbf = EE(nc.sbuf_tensor("nrmbf", [1, 2], BF16))
        recb = EE(nc.sbuf_tensor("recb", [128, 16], F32))

        evcnt = {}

        with ExitStack() as st:
            stage = st.enter_context(nc.sbuf_tensor("stage", [128, 128], F32))
            rstage = st.enter_context(nc.sbuf_tensor("rstage", [1, E], F32))
            ph = Prog(nc)
            for (src, dst) in ((identf, ident), (m0in, m0), (m1in, m1)):
                ph.op("sync", lambda s, src=src: s.dma_start(stage[:, :], src),
                      after=[sEV], inc=dL, inc_by=16)
                ph.op("vector", lambda v, dst=dst: v.tensor_copy(dst[:, :], stage[:, :]),
                      after=[dL], inc=sEV)
            ph.op("sync", lambda s: s.dma_start(bqk_t[:, :], bqk), inc=dL, inc_by=16)
            for (src, dst, w_) in ((bvrow, bv_bf, E), (borow, bo_bf, E), (gbrow, gb_bf, 8)):
                ph.op("sync", lambda s, src=src, w_=w_: s.dma_start(rstage[:, 0:w_], src),
                      after=[sEV], inc=dL, inc_by=16)
                ph.op("vector", lambda v, dst=dst, w_=w_: v.tensor_copy(dst[:, 0:w_], rstage[:, 0:w_]),
                      after=[dL], inc=sEV)
            ph.op("gpsimd", lambda g: g.dma_start(gwt_bf[:, :, :], gwt), inc=dG, inc_by=16)
            ph.op("vector", lambda v: v.memset(ones1[:, :], 1.0), after=[dG, dL])
            ph.op("vector", lambda v: v.memset(ones1f[:, :], 1.0))
            ph.op("vector", lambda v: v.memset(ones128[:, :], 1.0))
            ph.run()

        def emit_norm_bound(ph, pkd, src, n512, sqb, acc_col, init):
            (pk0t, pk0o) = pkd
            """nrm[0:1, acc_col] (+)= max over tokens of 128-partition squared
            norms of bf16 feature-major src [128, 512*n512]."""
            first = init
            for j in range(n512):
                ph.op("vector", lambda v, j=j: v.tensor_mul(
                    sqb[:, :], src[:, 512 * j:512 * j + 512],
                    src[:, 512 * j:512 * j + 512]),
                    after=[(sPJ, sPJ.n)], inc=sC)
                ph.op("tensor", lambda t: t.matmul(
                    pk0t[0:1, pk0o:pk0o + 512], ones128[:, :], sqb[:, :], start=True, stop=True),
                    after=[(sC, sC.n), (sEV, evcnt.get((id(pk0t), pk0o), 0))], inc=sPJ)
                ph.op("vector", lambda v: v.reduce_max(
                    nrm[0:1, 2:3], pk0t[0:1, pk0o:pk0o + 512], axis=AX.X), after=[(sPJ, sPJ.n)],
                    inc=sV)
                if first:
                    ph.op("vector", lambda v: v.tensor_copy(
                        nrm[0:1, acc_col:acc_col + 1], nrm[0:1, 2:3]), inc=sV,
                        wafter=[(sV, sV.n)])
                    first = False
                else:
                    ph.op("vector", lambda v: v.tensor_tensor(
                        nrm[0:1, acc_col:acc_col + 1],
                        nrm[0:1, acc_col:acc_col + 1], nrm[0:1, 2:3], ALU.max), inc=sV,
                        wafter=[(sV, sV.n)])
                ph.op("vector", lambda v: v.nop(), wafter=[(sV, sV.n)], inc=sEV)
                evcnt[(id(pk0t), pk0o)] = sEV.n

        def emit_make_negm(ph, pkd, qcol, kcol, dst, dst_col):
            (pk1t, pk1o) = pkd
            """dst[:, dst_col] = -(sqrt(nrm[qcol]*nrm[kcol]) + 1), broadcast."""
            ph.op("vector", lambda v: v.tensor_mul(
                nrm[0:1, 4:5], nrm[0:1, qcol:qcol + 1], nrm[0:1, kcol:kcol + 1]),
                wafter=[(sV, sV.n)], inc=sA)
            ph.op("scalar", lambda s: s.activation(nrm[0:1, 5:6], nrm[0:1, 4:5], AF.Ln),
                  after=[(sA, sA.n)])
            ph.op("scalar", lambda s: s.activation(nrm[0:1, 6:7], nrm[0:1, 5:6], AF.Exp,
                                                   scale=0.5), inc=sB)
            ph.op("vector", lambda v: v.tensor_scalar(
                nrm[0:1, 7:8], nrm[0:1, 6:7], -1.0, -1.0, ALU.mult, ALU.add),
                after=[(sB, sB.n)], inc=sV)
            ph.op("vector", lambda v: v.tensor_copy(nrmbf[0:1, 0:1], nrm[0:1, 7:8]),
                  wafter=[(sV, sV.n)], inc=sC)
            ph.op("tensor", lambda t: t.matmul(
                pk1t[:, pk1o:pk1o + 1], ones1[:, :], nrmbf[0:1, 0:1], start=True, stop=True),
                after=[(sC, sC.n), (sEV, evcnt.get((id(pk1t), pk1o), 0))], inc=sPJ)
            ph.op("vector", lambda v: v.tensor_copy(
                dst[:, dst_col:dst_col + 1], pk1t[:, pk1o:pk1o + 1]), after=[(sPJ, sPJ.n)], inc=sEV)
            evcnt[(id(pk1t), pk1o)] = sEV.n
            evcnt[("nm", id(dst), dst_col)] = sEV.n

        expcnt = {}

        def emit_attention_v1(ph, is2, hp, Q, K, V, nmten, nmcol, ctx_tm, Sb, ctxp,
                           psb, ready):
            for h2 in range(2):
                h = 2 * hp + h2
                hb = 64 * h2
                for qc in range(2):
                    nkb = 32 if not is2 else (16 if qc == 0 else 32)
                    divwait = (sDV, sDV.n)
                    pend = []
                    def flush_pv():
                        (kb, bi, j0, expv, mwait, jstop) = pend.pop(0)
                        js = list(range(j0, 8))
                        for idx, j in enumerate(js):
                            last = (idx == len(js) - 1)
                            stt_ = (kb == 0 and j in (0, 4))
                            stp_ = ((j == 3 and kb == jstop[3]) or
                                    (j == 7 and kb == jstop[7]))
                            ph.op("tensor", lambda t, kb=kb, j=j, bi=bi,
                                  stt_=stt_, stp_=stp_, h2=h2: t.matmul(
                                      ctxp[:, 128 * j:128 * j + 65],
                                      psb[bi][:, 128 * j:128 * j + 128],
                                      V[:, 130 * kb + 65 * h2:130 * kb + 65 * h2 + 65],
                                      start=stt_, stop=stp_),
                                  after=[(sE, expv), mwait, divwait] + ready,
                                  inc=(sP if last else None))
                    for kb in range(nkb):
                        bi = kb % 2
                        Sx = Sb[bi]
                        scw = [(sE, expcnt.get(id(Sx), 0))] + ready
                        for jj in range(2):
                            ph.op("tensor", lambda t, kb=kb, jj=jj, Sx=Sx,
                                  hb=hb, qc=qc: t.matmul(
                                Sx[:, 512 * jj:512 * jj + 512],
                                K[hb:hb + 64, 128 * kb:128 * kb + 128],
                                Q[hb:hb + 64, 1024 * qc + 512 * jj:1024 * qc + 512 * jj + 512],
                                start=True, stop=True),
                                after=scw, inc=sS)
                        ph.op("scalar", lambda s, bi=bi, Sx=Sx: s.activation(
                            psb[bi][:, :], Sx[:, :], AF.Exp,
                            bias=nmten[:, nmcol:nmcol + 1], scale=1.0),
                            after=[(sS, sS.n), (sP, sP.n),
                                   (sEV, evcnt.get(("nm", id(nmten), nmcol), 0))],
                            inc=sE)
                        expcnt[id(Sx)] = sE.n
                        expv = sE.n
                        if is2:
                            j0 = (kb // 2) if qc == 0 else max(0, (kb - 16) // 2)
                            rr = (kb - 2 * j0) if qc == 0 else (kb - 16 - 2 * j0)
                        else:
                            j0, rr = 0, -1
                        if is2 and rr in (0, 1):
                            mk = m0 if rr == 0 else m1
                            ph.op("vector", lambda v, bi=bi, j0=j0, mk=mk: v.tensor_mul(
                                psb[bi][:, 128 * j0:128 * j0 + 128],
                                psb[bi][:, 128 * j0:128 * j0 + 128], mk[:, :]),
                                after=[(sE, expv)], inc=sM)
                        mwait = (sM, sM.n)
                        if is2:
                            jstop = {j: (2 * j + 1 if qc == 0 else 17 + 2 * j)
                                     for j in range(8)}
                        else:
                            jstop = {j: nkb - 1 for j in range(8)}
                        pend.append((kb, bi, j0, expv, mwait, jstop))
                        if len(pend) > 1:
                            flush_pv()
                    flush_pv()
                    ph.op("vector", lambda v: v.reciprocal(
                        recb[:, 0:8], AP(ctxp.ap().tensor, 64, [[1024, 128], [128, 8]])),
                        after=[(sP, sP.n)], inc=sV)
                    recfence = (sV, sV.n)
                    for j in range(8):
                        tloc = qc * 8 + j
                        ph.op("vector", lambda v, j=j, tloc=tloc, h=h: v.tensor_scalar(
                            ctx_tm[:, tloc, 64 * h:64 * h + 64],
                            ctxp[:, 128 * j:128 * j + 64],
                            recb[:, j:j + 1], None, ALU.mult),
                            wafter=[recfence],
                            inc=(sDV if j == 7 else None))

        def emit_attention_p4(ph, is2, hp, Q, K, V, nmten, nmcol, ctx_tm, Sb, ctxp,
                           psb, ready):
            for h2 in range(2):
                h = 2 * hp + h2
                hb = 64 * h2
                for qc in range(2):
                    nkb = 32 if not is2 else (16 if qc == 0 else 32)
                    divwait = (sDV, sDV.n)
                    pend = []
                    def flush_pv():
                        (kb, bi, j0, expv, mwait, jstop) = pend.pop(0)
                        js = list(range(j0, 8))
                        for idx, j in enumerate(js):
                            last = (idx == len(js) - 1)
                            stt_ = (kb == 0 and j in (0, 4))
                            stp_ = ((j == 3 and kb == jstop[3]) or
                                    (j == 7 and kb == jstop[7]))
                            ph.op("tensor", lambda t, kb=kb, j=j, bi=bi,
                                  stt_=stt_, stp_=stp_, h2=h2: t.matmul(
                                      ctxp[:, 128 * j:128 * j + 65],
                                      psb[bi][:, 128 * j:128 * j + 128],
                                      V[:, 130 * kb + 65 * h2:130 * kb + 65 * h2 + 65],
                                      start=stt_, stop=stp_),
                                  after=[(sE, expv), mwait, divwait] + ready,
                                  inc=(sP if last else None))
                    for kb in range(nkb):
                        bi = kb % 2
                        Sx = Sb[bi]
                        pi = 2 * h2 + kb % 2
                        scw = [(sE, expcnt.get(id(Sx), 0))] + ready
                        for jj in range(2):
                            ph.op("tensor", lambda t, kb=kb, jj=jj, Sx=Sx,
                                  hb=hb, qc=qc: t.matmul(
                                Sx[:, 512 * jj:512 * jj + 512],
                                K[hb:hb + 64, 128 * kb:128 * kb + 128],
                                Q[hb:hb + 64, 1024 * qc + 512 * jj:1024 * qc + 512 * jj + 512],
                                start=True, stop=True),
                                after=scw, inc=sS)
                        ph.op("scalar", lambda s, bi=pi, Sx=Sx: s.activation(
                            psb[bi][:, :], Sx[:, :], AF.Exp,
                            bias=nmten[:, nmcol:nmcol + 1], scale=1.0),
                            after=[(sS, sS.n), (sP, sP.n),
                                   (sEV, evcnt.get(("nm", id(nmten), nmcol), 0))],
                            inc=sE)
                        expcnt[id(Sx)] = sE.n
                        expv = sE.n
                        if is2:
                            j0 = (kb // 2) if qc == 0 else max(0, (kb - 16) // 2)
                            rr = (kb - 2 * j0) if qc == 0 else (kb - 16 - 2 * j0)
                        else:
                            j0, rr = 0, -1
                        if is2 and rr in (0, 1):
                            mk = m0 if rr == 0 else m1
                            ph.op("vector", lambda v, bi=pi, j0=j0, mk=mk: v.tensor_mul(
                                psb[bi][:, 128 * j0:128 * j0 + 128],
                                psb[bi][:, 128 * j0:128 * j0 + 128], mk[:, :]),
                                after=[(sE, expv)], inc=sM)
                        mwait = (sM, sM.n)
                        if is2:
                            jstop = {j: (2 * j + 1 if qc == 0 else 17 + 2 * j)
                                     for j in range(8)}
                        else:
                            jstop = {j: nkb - 1 for j in range(8)}
                        pend.append((kb, pi, j0, expv, mwait, jstop))
                        if len(pend) > 1:
                            flush_pv()
                    flush_pv()
                    ph.op("vector", lambda v: v.reciprocal(
                        recb[:, 0:8], AP(ctxp.ap().tensor, 64, [[1024, 128], [128, 8]])),
                        after=[(sP, sP.n)], inc=sV)
                    recfence = (sV, sV.n)
                    for j in range(8):
                        tloc = qc * 8 + j
                        ph.op("vector", lambda v, j=j, tloc=tloc, h=h: v.tensor_scalar(
                            ctx_tm[:, tloc, 64 * h:64 * h + 64],
                            ctxp[:, 128 * j:128 * j + 64],
                            recb[:, j:j + 1], None, ALU.mult),
                            wafter=[recfence],
                            inc=(sDV if j == 7 else None))

        def emit_attention_pc(ph, is2, hp, Q, K, V, nmten, nmcol, ctx_tm, Sb, ctxp,
                           pk, psb, ready):
            ctxsl = {0: [(ctxp, 128 * j) for j in range(8)],
                     1: [(pk[0], 128 * j) for j in range(4)] +
                        [(pk[1], 128 * (j - 4)) for j in range(4, 8)]}
            for h2 in range(2):
                h = 2 * hp + h2
                hb = 64 * h2
                for qc in range(2):
                    nkb = 32 if not is2 else (16 if qc == 0 else 32)
                    divwait = (sDV, sDV.n)
                    pend = []
                    def flush_pv():
                        (kb, bi, j0, expv, mwait, jstop) = pend.pop(0)
                        js = list(range(j0, 8))
                        for idx, j in enumerate(js):
                            last = (idx == len(js) - 1)
                            stt_ = (kb == 0 and j in (0, 4))
                            stp_ = ((j == 3 and kb == jstop[3]) or
                                    (j == 7 and kb == jstop[7]))
                            (ct, off) = ctxsl[h2][j]
                            ph.op("tensor", lambda t, kb=kb, j=j, bi=bi, ct=ct, off=off,
                                  stt_=stt_, stp_=stp_, h2=h2: t.matmul(
                                      ct[:, off:off + 65],
                                      psb[bi][:, 128 * j:128 * j + 128],
                                      V[:, 130 * kb + 65 * h2:130 * kb + 65 * h2 + 65],
                                      start=stt_, stop=stp_),
                                  after=[(sE, expv), mwait, divwait] + ready,
                                  inc=(sP if last else None))
                    for kb in range(nkb):
                        bi = kb % 2
                        Sx = Sb[bi]
                        pi = 2 * h2 + kb % 2
                        scw = [(sE, expcnt.get(id(Sx), 0))] + ready
                        for jj in range(2):
                            ph.op("tensor", lambda t, kb=kb, jj=jj, Sx=Sx,
                                  hb=hb, qc=qc: t.matmul(
                                Sx[:, 512 * jj:512 * jj + 512],
                                K[hb:hb + 64, 128 * kb:128 * kb + 128],
                                Q[hb:hb + 64, 1024 * qc + 512 * jj:1024 * qc + 512 * jj + 512],
                                start=True, stop=True),
                                after=scw, inc=sS)
                        ph.op("scalar", lambda s, bi=pi, Sx=Sx: s.activation(
                            psb[bi][:, :], Sx[:, :], AF.Exp,
                            bias=nmten[:, nmcol:nmcol + 1], scale=1.0),
                            after=[(sS, sS.n), (sP, sP.n),
                                   (sEV, evcnt.get(("nm", id(nmten), nmcol), 0))],
                            inc=sE)
                        expcnt[id(Sx)] = sE.n
                        expv = sE.n
                        if is2:
                            j0 = (kb // 2) if qc == 0 else max(0, (kb - 16) // 2)
                            rr = (kb - 2 * j0) if qc == 0 else (kb - 16 - 2 * j0)
                        else:
                            j0, rr = 0, -1
                        if is2 and rr in (0, 1):
                            mk = m0 if rr == 0 else m1
                            ph.op("vector", lambda v, bi=pi, j0=j0, mk=mk: v.tensor_mul(
                                psb[bi][:, 128 * j0:128 * j0 + 128],
                                psb[bi][:, 128 * j0:128 * j0 + 128], mk[:, :]),
                                after=[(sE, expv)], inc=sM)
                        mwait = (sM, sM.n)
                        if is2:
                            jstop = {j: (2 * j + 1 if qc == 0 else 17 + 2 * j)
                                     for j in range(8)}
                        else:
                            jstop = {j: nkb - 1 for j in range(8)}
                        pend.append((kb, pi, j0, expv, mwait, jstop))
                        if len(pend) > 1:
                            flush_pv()
                    flush_pv()
                    if h2 == 0:
                        ph.op("vector", lambda v: v.reciprocal(
                            recb[:, 0:8], AP(ctxp.ap().tensor, 64, [[1024, 128], [128, 8]])),
                            after=[(sP, sP.n)], inc=sV)
                    else:
                        ph.op("vector", lambda v: v.reciprocal(
                            recb[:, 8:12], AP(pk[0].ap().tensor, 64, [[512, 128], [128, 4]])),
                            after=[(sP, sP.n)], inc=sV)
                        ph.op("vector", lambda v: v.reciprocal(
                            recb[:, 12:16], AP(pk[1].ap().tensor, 64, [[512, 128], [128, 4]])),
                            inc=sV)
                    recfence = (sV, sV.n)
                    for j in range(8):
                        tloc = qc * 8 + j
                        (ct, off) = ctxsl[h2][j]
                        rcol = j if h2 == 0 else 8 + j
                        ph.op("vector", lambda v, j=j, tloc=tloc, h=h, ct=ct,
                              off=off, rcol=rcol: v.tensor_scalar(
                            ctx_tm[:, tloc, 64 * h:64 * h + 64],
                            ct[:, off:off + 64],
                            recb[:, rcol:rcol + 1], None, ALU.mult),
                            wafter=[recfence],
                            inc=(sDV if j == 7 else None))
                    if h2 == 1:
                        ph.op("vector", lambda v: v.nop(), inc=sEV)
                        evcnt[id(pk[0])] = sEV.n
                        evcnt[id(pk[1])] = sEV.n

        def emit_attention(ph, is2, hp, Q, K, V, nmten, nmcol, ctx_tm, Sb, ctxp,
                           pk, psb, ready):
            ctxsl = {0: [(ctxp, 128 * j) for j in range(8)],
                     1: [(pk[0], 128 * j) for j in range(4)] +
                        [(pk[1], 128 * (j - 4)) for j in range(4, 8)]}
            for qc in range(2):
                nkb = 32 if not is2 else (16 if qc == 0 else 32)
                divwait = (sDV, sDV.n)
                pend = []

                def flush_pv():
                    (kb, h2, bi, j0, expv, mwait, jstop) = pend.pop(0)
                    js = list(range(j0, 8))
                    for idx, j in enumerate(js):
                        last = (idx == len(js) - 1)
                        stt_ = (kb == 0 and j in (0, 4))
                        stp_ = ((j == 3 and kb == jstop[3]) or
                                (j == 7 and kb == jstop[7]))
                        (ct, off) = ctxsl[h2][j]
                        ph.op("tensor", lambda t, kb=kb, j=j, bi=bi, h2=h2,
                              stt_=stt_, stp_=stp_, ct=ct, off=off: t.matmul(
                                  ct[:, off:off + 65],
                                  psb[bi][:, 128 * j:128 * j + 128],
                                  V[:, 130 * kb + 65 * h2:130 * kb + 65 * h2 + 65],
                                  start=stt_, stop=stp_),
                              after=[(sE, expv), mwait, divwait] + ready,
                              inc=(sP if last else None))

                for kb in range(nkb):
                    if is2:
                        j0 = (kb // 2) if qc == 0 else max(0, (kb - 16) // 2)
                        rr = (kb - 2 * j0) if qc == 0 else (kb - 16 - 2 * j0)
                    else:
                        j0, rr = 0, -1
                    if is2:
                        jstop = {j: (2 * j + 1 if qc == 0 else 17 + 2 * j)
                                 for j in range(8)}
                    else:
                        jstop = {j: nkb - 1 for j in range(8)}
                    for h2 in range(2):
                        Sx = Sb[h2]
                        scw = [(sE, expcnt.get(id(Sx), 0))] + ready
                        for jj in range(2):
                            ph.op("tensor", lambda t, kb=kb, jj=jj, Sx=Sx,
                                  h2=h2, qc=qc: t.matmul(
                                Sx[:, 512 * jj:512 * jj + 512],
                                K[64 * h2:64 * h2 + 64, 128 * kb:128 * kb + 128],
                                Q[64 * h2:64 * h2 + 64,
                                  1024 * qc + 512 * jj:1024 * qc + 512 * jj + 512],
                                start=True, stop=True),
                                after=scw, inc=sS)
                    for h2 in range(2):
                        Sx = Sb[h2]
                        bi = 2 * h2 + kb % 2
                        ph.op("scalar", lambda s, bi=bi, Sx=Sx: s.activation(
                            psb[bi][:, :], Sx[:, :], AF.Exp,
                            bias=nmten[:, nmcol:nmcol + 1], scale=1.0),
                            after=[(sS, sS.n if h2 == 1 else sS.n - 2),
                                   (sP, sP.n),
                                   (sEV, evcnt.get(("nm", id(nmten), nmcol), 0))],
                            inc=sE)
                        expcnt[id(Sx)] = sE.n
                        expv = sE.n
                        if is2 and rr in (0, 1):
                            mk = m0 if rr == 0 else m1
                            ph.op("vector", lambda v, bi=bi, j0=j0, mk=mk: v.tensor_mul(
                                psb[bi][:, 128 * j0:128 * j0 + 128],
                                psb[bi][:, 128 * j0:128 * j0 + 128], mk[:, :]),
                                after=[(sE, expv)], inc=sM)
                        mwait = (sM, sM.n)
                        pend.append((kb, h2, bi, j0, expv, mwait, jstop))
                    while len(pend) > 2:
                        flush_pv()
                while pend:
                    flush_pv()
                ph.op("vector", lambda v: v.reciprocal(
                    recb[:, 0:8], AP(ctxp.ap().tensor, 64, [[1024, 128], [128, 8]])),
                    after=[(sP, sP.n)], inc=sV)
                ph.op("vector", lambda v: v.reciprocal(
                    recb[:, 8:12], AP(pk[0].ap().tensor, 64, [[512, 128], [128, 4]])),
                    inc=sV)
                ph.op("vector", lambda v: v.reciprocal(
                    recb[:, 12:16], AP(pk[1].ap().tensor, 64, [[512, 128], [128, 4]])),
                    inc=sV)
                recfence = (sV, sV.n)
                for h2 in range(2):
                    h = 2 * hp + h2
                    for j in range(8):
                        tloc = qc * 8 + j
                        (ct, off) = ctxsl[h2][j]
                        rcol = j if h2 == 0 else 8 + j
                        ph.op("vector", lambda v, j=j, tloc=tloc, h=h, ct=ct,
                              off=off, rcol=rcol: v.tensor_scalar(
                            ctx_tm[:, tloc, 64 * h:64 * h + 64],
                            ct[:, off:off + 64],
                            recb[:, rcol:rcol + 1], None, ALU.mult),
                            wafter=[recfence],
                            inc=(sDV if (h2 == 1 and j == 7) else None))
                ph.op("vector", lambda v: v.nop(), inc=sEV)
                evcnt[(id(ctxp), 0)] = sEV.n
                evcnt[(id(ctxp), 512)] = sEV.n

        def emit_ln(ph, t_in, t_out, scr, gamma=None, beta=None, outwait=None,
                    inc=None):
            ph.op("vector", lambda v: v.scalar_tensor_tensor(
                scr[:, :], t_in, 1.0, t_in, ALU.mult, ALU.mult, accum_out=stt[:, 1:2]),
                inc=sV)
            ph.op("vector", lambda v: v.reduce_sum(stt[:, 0:1], t_in, axis=AX.X),
                  inc=sV)
            ph.op("vector", lambda v: v.tensor_scalar(stt[:, 2:3], stt[:, 0:1], 1.0 / E, None, ALU.mult),
                  wafter=[(sV, sV.n)], inc=sV)
            ph.op("vector", lambda v: v.tensor_scalar(stt[:, 3:4], stt[:, 1:2], 1.0 / E, None, ALU.mult),
                  inc=sV)
            ph.op("vector", lambda v: v.tensor_mul(stt[:, 4:5], stt[:, 2:3], stt[:, 2:3]),
                  wafter=[(sV, sV.n)], inc=sV)
            ph.op("vector", lambda v: v.tensor_sub(stt[:, 5:6], stt[:, 3:4], stt[:, 4:5]),
                  wafter=[(sV, sV.n)], inc=sV)
            ph.op("vector", lambda v: v.tensor_scalar(stt[:, 6:7], stt[:, 5:6], LN_EPS, None, ALU.add),
                  wafter=[(sV, sV.n)], inc=sA)
            ph.op("scalar", lambda s: s.activation(stt[:, 7:8], stt[:, 6:7], AF.Ln),
                  after=[(sA, sA.n)])
            ph.op("scalar", lambda s: s.activation(stt[:, 8:9], stt[:, 7:8], AF.Exp, scale=-0.5),
                  inc=sB)
            w = [(sB, sB.n)] + ([outwait] if outwait else [])
            lastop = ph.op if gamma is not None else None
            ph.op("vector", lambda v: v.tensor_scalar(
                t_out, t_in, stt[:, 2:3], stt[:, 8:9], ALU.subtract, ALU.mult), after=w,
                inc=(None if gamma is not None else inc))
            if gamma is not None:
                ph.op("vector", lambda v: v.tensor_mul(t_out, t_out, gamma))
                ph.op("vector", lambda v: v.tensor_add(t_out, t_out, beta), inc=inc)

        def emit_drain(ph):
            ph.op("sync", lambda s: s.nop(), after=[(dL, dL.n)])
            ph.op("gpsimd", lambda g: g.nop(), after=[(dG, dG.n)])

        def emit_qkv(ph, is2, hp, wbuf, wq_ready, Q, K, V, pk, qsrc, qwaits,
                     ksrc, vsrc, kwid, kvfree=None):
            kvw = [kvfree] if kvfree else []
            for qb in range(4):
                (bt, bo, bpi) = pk[qb % 2]
                for c in range(NC):
                    ph.op("tensor", lambda t, c=c, qb=qb, bt=bt, bo=bo: t.matmul(
                        bt[:, bo:bo + 512], wbuf[:, c, 0:128],
                        qsrc[:, c, 512 * qb:512 * qb + 512],
                        start=(c == 0), stop=(c == NC - 1)),
                        after=[wq_ready] + qwaits + [(sEV, evcnt.get((id(bt), bo), 0))],
                        inc=(sPJ if c == NC - 1 else None))
                if is2:
                    ph.op("vector", lambda v, qb=qb, bt=bt, bo=bo: v.tensor_scalar(
                        Q[:, 512 * qb:512 * qb + 512], bt[:, bo:bo + 512],
                        bqk_t[:, hp:hp + 1], 0.125, ALU.add, ALU.mult),
                        after=[(sPJ, sPJ.n)], inc=sEV)
                else:
                    ph.op("vector", lambda v, qb=qb, bt=bt, bo=bo: v.tensor_scalar(
                        Q[:, 512 * qb:512 * qb + 512], bt[:, bo:bo + 512], 0.125, None, ALU.mult),
                        after=[(sPJ, sPJ.n)], inc=sEV)
                evcnt[(id(bt), bo)] = sEV.n
            for kb in range(kwid // 512):
                (bt, bo, bpi) = pk[kb % 2]
                for c in range(NC):
                    ph.op("tensor", lambda t, c=c, kb=kb, bt=bt, bo=bo: t.matmul(
                        bt[:, bo:bo + 512], wbuf[:, c, 128:256],
                        ksrc[:, c, 512 * kb:512 * kb + 512],
                        start=(c == 0), stop=(c == NC - 1)),
                        after=[(sEV, evcnt.get((id(bt), bo), 0))],
                        inc=(sPJ if c == NC - 1 else None))
                if is2:
                    ph.op("vector", lambda v, kb=kb, bt=bt, bo=bo: v.tensor_scalar(
                        K[:, 512 * kb:512 * kb + 512], bt[:, bo:bo + 512],
                        bqk_t[:, 8 + hp:9 + hp], None, ALU.add),
                        after=[(sPJ, sPJ.n)] + kvw, inc=sEV)
                else:
                    ph.op("vector", lambda v, kb=kb, bt=bt, bo=bo: v.tensor_copy(
                        K[:, 512 * kb:512 * kb + 512], bt[:, bo:bo + 512]),
                        after=[(sPJ, sPJ.n)], inc=sEV)
                evcnt[(id(bt), bo)] = sEV.n
            nvt = kwid // 128
            ph.op("vector", lambda v, nvt=nvt: v.memset(
                AP(V, 64, [[nvt * 130, 128], [130, nvt], [65, 2]]), 1.0),
                after=kvw, inc=sEV)
            vones = sEV.n
            for tt in range(nvt):
                (bt, bo, bpi) = pk[tt % 2]
                for c in range(NC):
                    ph.op("tensor", lambda t, c=c, tt=tt, bt=bt, bo=bo: t.matmul(
                        bt[:, bo:bo + 128], vsrc[:, c, 128 * tt:128 * tt + 128],
                        wbuf[:, c, 256:384],
                        start=(c == 0), stop=(False if is2 else c == NC - 1)),
                        after=[(sEV, evcnt.get((id(bt), bo), 0))],
                        inc=(sPJ if (not is2 and c == NC - 1) else None))
                if is2:
                    ph.op("tensor", lambda t, bt=bt, bo=bo: t.matmul(
                        bt[:, bo:bo + 128], ones1[:, :],
                        bv_bf[0:1, 128 * hp:128 * hp + 128],
                        start=False, stop=True), inc=sPJ)
                ph.op("vector", lambda v, tt=tt, bt=bt, bo=bo, bpi=bpi, nvt=nvt: v.tensor_copy(
                    AP(V, 130 * tt, [[nvt * 130, 128], [65, 2], [1, 64]]),
                    AP(bt.ap().tensor, bo, [[bpi, 128], [64, 2], [1, 64]])),
                    after=[(sPJ, sPJ.n), (sEV, vones)], inc=sEV)
                evcnt[(id(bt), bo)] = sEV.n

        ctx_scope = ExitStack()
        ctx_tm = ctx_scope.enter_context(nc.sbuf_tensor("ctx_tm", [128, NT, E], BF16))

        with ExitStack() as st:
            x0bf = st.enter_context(nc.sbuf_tensor("x0bf", [128, NC, S], BF16))
            x0obf = st.enter_context(nc.sbuf_tensor("x0obf", [128, NC, TOK], BF16))
            wst = [st.enter_context(nc.sbuf_tensor(f"wst{i}", [128, NC, 384], BF16)) for i in range(2)]
            Qp = [st.enter_context(nc.sbuf_tensor(f"Qp{i}", [128, TOK], BF16)) for i in range(2)]
            Kp = [st.enter_context(nc.sbuf_tensor(f"Kp{i}", [128, S], BF16)) for i in range(2)]
            Vp = [st.enter_context(nc.sbuf_tensor(f"Vp{i}", [128, 32 * 130], BF16)) for i in range(2)]
            psb = [st.enter_context(nc.sbuf_tensor(f"psb{i}", [128, 1024], BF16)) for i in range(4)]
            sqb = st.enter_context(nc.sbuf_tensor("sqb", [128, 512], BF16))
            S0 = st.enter_context(nc.psum_tensor("S0", [128, 1024], F32))
            S1 = st.enter_context(nc.psum_tensor("S1", [128, 1024], F32))
            ctxp = st.enter_context(nc.psum_tensor("ctxp", [128, 1024], F32))
            pk = [st.enter_context(nc.psum_tensor(f"pk{i}", [128, 512], F32)) for i in range(2)]

            ph = Prog(nc)
            for c in range(NC):
                ph.op("gpsimd", lambda g, c=c: g.dma_start(x0bf[:, c, :], x0f[:, c, :]),
                      inc=dG, inc_by=16)
                ph.op("gpsimd", lambda g, c=c: g.dma_start(x0obf[:, c, :], x0fo[:, c, :]),
                      inc=dG, inc_by=16)
            x0rdy = (dG, dG.n)
            ph.op("gpsimd", lambda g: g.dma_start(wst[0][:, :, :], wqkv1[:, 0, :, :]),
                  inc=dG, inc_by=16)
            for hp in range(8):
                wb = wst[hp % 2]
                wq_ready = (dG, dG.n)
                if hp < 7:
                    ph.op("gpsimd", lambda g, hp=hp: g.dma_start(
                        wst[(hp + 1) % 2][:, :, :], wqkv1[:, hp + 1, :, :]),
                        after=[(sPJ, sPJ.n)], inc=dG, inc_by=16)
                qb_banks = [(ctxp, 0, 1024), (ctxp, 512, 1024)]
                emit_qkv(ph, False, hp, wb, wq_ready, Qp[hp % 2], Kp[hp % 2],
                         Vp[hp % 2], qb_banks, x0obf, [x0rdy], x0bf, x0bf, S)
                emit_norm_bound(ph, (ctxp, 0), Qp[hp % 2], 4, sqb, 0, True)
                emit_norm_bound(ph, (ctxp, 0), Kp[hp % 2], 8, sqb, 3, True)
                emit_make_negm(ph, (ctxp, 512), 0, 3, negm, hp % 2)
                ready = [(sEV, sEV.n)]
                if PAIR1 and hp < PAIR1_N and PMODE == "paired":
                    emit_attention(ph, False, hp, Qp[hp % 2], Kp[hp % 2], Vp[hp % 2],
                                   negm, hp % 2, ctx_tm, [S0, S1], ctxp, pk, psb, ready)
                elif PAIR1 and hp < PAIR1_N and PMODE == "psb4":
                    emit_attention_p4(ph, False, hp, Qp[hp % 2], Kp[hp % 2], Vp[hp % 2],
                                      negm, hp % 2, ctx_tm, [S0, S1], ctxp, psb, ready)
                elif PAIR1 and hp < PAIR1_N and PMODE == "pkctx":
                    emit_attention_pc(ph, False, hp, Qp[hp % 2], Kp[hp % 2], Vp[hp % 2],
                                      negm, hp % 2, ctx_tm, [S0, S1], ctxp, pk, psb, ready)
                else:
                    emit_attention_v1(ph, False, hp, Qp[hp % 2], Kp[hp % 2], Vp[hp % 2],
                                      negm, hp % 2, ctx_tm, [S0, S1], ctxp, psb, ready)
            emit_drain(ph)
            ph.run()

        with ExitStack() as st:
            ctx_fm = st.enter_context(nc.sbuf_tensor("ctx_fm", [128, NC, TOK], BF16))
            wo1 = st.enter_context(nc.sbuf_tensor("wo1", [128, NC, E], BF16))
            x0t = [st.enter_context(nc.sbuf_tensor(f"x0t{i}", [128, E], F32)) for i in range(2)]
            tres = st.enter_context(nc.sbuf_tensor("tres", [128, E], F32))
            xout = [st.enter_context(nc.sbuf_tensor(f"xout{i}", [128, E], F32)) for i in range(2)]
            scr = st.enter_context(nc.sbuf_tensor("scr", [128, E], F32))
            tp = [st.enter_context(nc.psum_tensor(f"tp{i}", [128, 128], BF16)) for i in range(2)]
            po = [st.enter_context(nc.psum_tensor(f"po{i}", [128, 512], F32)) for i in range(2)]

            ph = Prog(nc)
            if DEBUG_TAPS:
                dc1 = nc.dram_tensor("dbgctx1", [128, NT * E], BF16,
                                     kind="ExternalOutput").ap()
                ph.op("sync", lambda s: s.dma_start(
                    dc1, AP(ctx_tm, 0, [[NT * E, 128], [1, NT * E]])),
                    after=[(sDV, sDV.n)], inc=dL, inc_by=16)
            ph.op("gpsimd", lambda g: g.dma_start(wo1[:, :, :], wout1), inc=dG, inc_by=16)
            wordy = (dG, dG.n)
            divdone = (sDV, sDV.n)
            for t in range(NT):
                for c in range(NC):
                    k = (t * NC + c) % 2
                    ph.op("tensor", lambda tt, t=t, c=c, k=k: tt.transpose(
                        tp[k][:, :], ctx_tm[:, t, 128 * c:128 * c + 128], ident[:, :]),
                        after=[divdone, (sEV, evcnt.get(id(tp[k]), 0))], inc=sTR)
                    ph.op("vector", lambda v, t=t, c=c, k=k: v.tensor_copy(
                        ctx_fm[:, c, 128 * t:128 * t + 128], tp[k][:, :]),
                        after=[(sTR, sTR.n)], inc=sEV)
                    evcnt[id(tp[k])] = sEV.n
            trall = (sEV, sEV.n)
            if DEBUG_TAPS:
                dcf = nc.dram_tensor("dbgctxfm", [128, NC * TOK], BF16,
                                     kind="ExternalOutput").ap()
                ph.op("sync", lambda s: s.dma_start(
                    dcf, AP(ctx_fm, 0, [[NC * TOK, 128], [1, NC * TOK]])),
                    after=[trall], inc=dL, inc_by=16)
            x0used = {}
            xoutfree = {}
            for t in range(NT):
                ph.op("sync", lambda s, t=t: s.dma_start(
                    x0t[t % 2][:, :], x0tm[128 * t:128 * t + 128, :]),
                    after=[(sEV, x0used.get(t % 2, 0))], inc=dL, inc_by=16)
                x0rdy2 = (dL, dL.n)
                for jj in range(2):
                    bank = po[jj]
                    for c in range(NC):
                        ph.op("tensor", lambda tt, t=t, c=c, jj=jj, bank=bank: tt.matmul(
                            bank[:, :], ctx_fm[:, c, 128 * t:128 * t + 128],
                            wo1[:, c, 512 * jj:512 * jj + 512],
                            start=(c == 0), stop=(c == NC - 1)),
                            after=[wordy, trall, (sEV, evcnt.get(id(bank), 0))],
                            inc=(sPJ if c == NC - 1 else None))
                ph.op("vector", lambda v, t=t: v.tensor_add(
                    tres[:, 0:512], po[0][:, :], x0t[t % 2][:, 0:512]),
                    after=[(sPJ, sPJ.n), x0rdy2])
                ph.op("vector", lambda v, t=t: v.tensor_add(
                    tres[:, 512:1024], po[1][:, :], x0t[t % 2][:, 512:1024]), inc=sEV)
                evcnt[id(po[0])] = sEV.n
                evcnt[id(po[1])] = sEV.n
                x0used[t % 2] = sEV.n
                emit_ln(ph, tres[:, :], xout[t % 2][:, :], scr,
                        outwait=(dL, xoutfree.get(t % 2, 0)), inc=sC)
                xo_ok = (sC, sC.n)
                ph.op("sync", lambda s, t=t: s.dma_start(
                    x1d[128 * t:128 * t + 128, :], xout[t % 2][:, :]),
                    after=[xo_ok], inc=dL, inc_by=16)
                xoutfree[t % 2] = dL.n
            emit_drain(ph)
            ph.run()
        ctx_scope.close()

        mid = ExitStack()
        ctx2_tm = mid.enter_context(nc.sbuf_tensor("ctx2_tm", [128, NT, E], BF16))
        midq = ExitStack()
        Q2 = midq.enter_context(nc.sbuf_tensor("Q2", [128, 8, TOK], BF16))

        with ExitStack() as st:
            ropefm = st.enter_context(nc.sbuf_tensor("ropefm", [128, NC, TOK], BF16))
            x1fm = st.enter_context(nc.sbuf_tensor("x1fm", [128, NC, TOK], BF16))
            x1t = [st.enter_context(nc.sbuf_tensor(f"x1t{i}", [128, E], F32)) for i in range(2)]
            cs = [st.enter_context(nc.sbuf_tensor(f"cs{i}", [128, E], F32)) for i in range(2)]
            scrA = st.enter_context(nc.sbuf_tensor("scrA", [128, E], F32))
            rope_tm = st.enter_context(nc.sbuf_tensor("rope_tm", [128, E], F32))
            ropebf = st.enter_context(nc.sbuf_tensor("ropebf", [128, E], BF16))
            x1bf = st.enter_context(nc.sbuf_tensor("x1bf", [128, E], BF16))
            wst2 = [st.enter_context(nc.sbuf_tensor(f"wst2{i}", [128, NC, 384], BF16)) for i in range(2)]
            K2p = st.enter_context(nc.sbuf_tensor("K2p", [128, TOK], BF16))
            V2p = st.enter_context(nc.sbuf_tensor("V2p", [128, NT * 130], BF16))
            sqb2 = st.enter_context(nc.sbuf_tensor("sqb2", [128, 512], BF16))
            tp = [st.enter_context(nc.psum_tensor(f"tp2{i}", [128, 128], BF16)) for i in range(2)]
            pk = [st.enter_context(nc.psum_tensor(f"pk2{i}", [128, 512], F32)) for i in range(2)]

            ph = Prog(nc)
            tilefree = {}
            for t in range(NT):
                ph.op("sync", lambda s, t=t: s.dma_start(
                    x1t[t % 2][:, :], x1d[128 * t:128 * t + 128, :]),
                    after=[(sC, tilefree.get(t % 2, 0))], inc=dL, inc_by=16)
                ph.op("sync", lambda s, t=t: s.dma_start(
                    cs[t % 2][:, 0:512], cost[128 * t:128 * t + 128, :]),
                    inc=dL, inc_by=16)
                ph.op("sync", lambda s, t=t: s.dma_start(
                    cs[t % 2][:, 512:1024], sint[128 * t:128 * t + 128, :]),
                    inc=dL, inc_by=16)
                ld = (dL, dL.n)
                xt = x1t[t % 2]
                ct = cs[t % 2][:, 0:512]
                sn = cs[t % 2][:, 512:1024]
                xe = AP(xt, 0, [[E, 128], [2, 512]])
                xo = AP(xt, 1, [[E, 128], [2, 512]])
                re_ = AP(rope_tm, 0, [[E, 128], [2, 512]])
                ro_ = AP(rope_tm, 1, [[E, 128], [2, 512]])
                ph.op("vector", lambda v, xe=xe, ct=ct: v.tensor_mul(scrA[:, 0:512], xe, ct),
                      after=[ld])
                ph.op("vector", lambda v, xo=xo, sn=sn: v.tensor_mul(scrA[:, 512:1024], xo, sn))
                ph.op("vector", lambda v, re_=re_: v.tensor_sub(re_, scrA[:, 0:512], scrA[:, 512:1024]))
                ph.op("vector", lambda v, xe=xe, sn=sn: v.tensor_mul(scrA[:, 0:512], xe, sn))
                ph.op("vector", lambda v, xo=xo, ct=ct: v.tensor_mul(scrA[:, 512:1024], xo, ct))
                ph.op("vector", lambda v, ro_=ro_: v.tensor_add(ro_, scrA[:, 0:512], scrA[:, 512:1024]))
                ph.op("vector", lambda v: v.tensor_copy(ropebf[:, :], rope_tm[:, :]))
                ph.op("vector", lambda v, xt=xt: v.tensor_copy(x1bf[:, :], xt[:, :]), inc=sC)
                tilefree[t % 2] = sC.n
                cast_ok = (sC, sC.n)
                for c in range(NC):
                    k = (t * NC + c) % 2
                    k2 = (t * NC + c + 1) % 2
                    ph.op("tensor", lambda tt, c=c, k=k: tt.transpose(
                        tp[k][:, :], ropebf[:, 128 * c:128 * c + 128], ident[:, :]),
                        after=[cast_ok, (sEV, evcnt.get(id(tp[k]), 0))], inc=sTR)
                    ph.op("vector", lambda v, t=t, c=c, k=k: v.tensor_copy(
                        ropefm[:, c, 128 * t:128 * t + 128], tp[k][:, :]),
                        after=[(sTR, sTR.n)], inc=sEV)
                    evcnt[id(tp[k])] = sEV.n
                    ph.op("tensor", lambda tt, c=c, k2=k2: tt.transpose(
                        tp[k2][:, :], x1bf[:, 128 * c:128 * c + 128], ident[:, :]),
                        after=[cast_ok, (sEV, evcnt.get(id(tp[k2]), 0))], inc=sTR)
                    ph.op("vector", lambda v, t=t, c=c, k2=k2: v.tensor_copy(
                        x1fm[:, c, 128 * t:128 * t + 128], tp[k2][:, :]),
                        after=[(sTR, sTR.n)], inc=sEV)
                    evcnt[id(tp[k2])] = sEV.n
                ph.op("vector", lambda v: v.nop(), after=[(sTR, sTR.n)], inc=sC)
                tilefree[t % 2] = sC.n
            trdone = (sEV, sEV.n)
            ph.op("gpsimd", lambda g: g.dma_start(wst2[0][:, :, :], wqkv2[:, 0, :, :]),
                  inc=dG, inc_by=16)
            kv2st = 0
            for hp in range(8):
                wb = wst2[hp % 2]
                wq_ready = (dG, dG.n)
                if hp < 7:
                    ph.op("gpsimd", lambda g, hp=hp: g.dma_start(
                        wst2[(hp + 1) % 2][:, :, :], wqkv2[:, hp + 1, :, :]),
                        after=[(sPJ, sPJ.n)], inc=dG, inc_by=16)
                Qsl = AP(Q2, hp * TOK, [[8 * TOK, 128], [1, TOK]])
                emit_qkv(ph, True, hp, wb, wq_ready, Qsl, K2p, V2p,
                         [(pk[0], 0, 512), (pk[1], 0, 512)],
                         ropefm, [trdone], ropefm, x1fm, TOK,
                         kvfree=(dL, kv2st))
                emit_norm_bound(ph, (pk[0], 0), Qsl, 4, sqb2, 8 + hp, True)
                emit_norm_bound(ph, (pk[0], 0), K2p, 4, sqb2, 1, hp == 0)
                ph.op("sync", lambda s, hp=hp: s.dma_start(
                    k2loc.ap()[128 * hp:128 * hp + 128, :], K2p[:, :]),
                    after=[(sEV, sEV.n)], inc=dL, inc_by=16)
                ph.op("sync", lambda s, hp=hp: s.dma_start(
                    v2loc.ap().rearrange("(t p) f -> p t f", p=128)[:, :, 130 * hp:130 * hp + 130],
                    AP(V2p, 0, [[NT * 130, 128], [130, NT], [1, 130]])),
                    after=[(sEV, sEV.n)], inc=dL, inc_by=16)
                kv2st = dL.n
            ph.op("sync", lambda s: s.dma_start(arin.ap(), nrm[0:1, 1:2]),
                  after=[(sEV, sEV.n)], inc=dL, inc_by=16)
            ph.op("gpsimd", lambda g: g.collective_compute(
                "AllGather", ALU.bypass, RG, [k2loc.ap().opt()], [k2all.ap().opt()]),
                after=[(dL, dL.n)], inc=cc1)
            ph.op("gpsimd", lambda g: g.collective_compute(
                "AllGather", ALU.bypass, RG, [v2loc.ap().opt()], [v2all.ap().opt()]),
                inc=cc2)
            ph.op("gpsimd", lambda g: g.collective_compute(
                "AllReduce", ALU.max, RG, [arin.ap().opt()], [arout.ap().opt()]),
                inc=cc3)
            ph.op("sync", lambda s: s.dma_start(nrm[0:1, 1:2], arout.ap()),
                  after=[(cc3, cc3.n)], inc=dL, inc_by=16)
            ph.op("vector", lambda v: v.nop(), after=[(dL, dL.n)], inc=sC)
            for hp in range(8):
                emit_make_negm(ph, (pk[1], 0), 8 + hp, 1, negm2, hp)
            emit_drain(ph)
            ph.run()

        with ExitStack() as st:
            Kb = [st.enter_context(nc.sbuf_tensor(f"Kb{i}", [128, S], BF16)) for i in range(2)]
            Vb = [st.enter_context(nc.sbuf_tensor(f"Vb{i}", [128, 32 * 130], BF16)) for i in range(2)]
            psb = [st.enter_context(nc.sbuf_tensor(f"psb2{i}", [128, 1024], BF16)) for i in range(4)]
            S0 = st.enter_context(nc.psum_tensor("S0b", [128, 1024], F32))
            S1 = st.enter_context(nc.psum_tensor("S1b", [128, 1024], F32))
            ctxp = st.enter_context(nc.psum_tensor("ctxp2", [128, 1024], F32))
            pk3 = [st.enter_context(nc.psum_tensor(f"pk3{i}", [128, 512], F32)) for i in range(2)]

            ph = Prog(nc)
            kread = {}
            for hp in range(8):
                bufK, bufV = Kb[hp % 2], Vb[hp % 2]
                usedone = (sP, kread.get(hp % 2, 0))
                for pbit in range(2):
                    ph.op("sync", lambda s, hp=hp, pbit=pbit, bufK=bufK: s.dma_start(
                        AP(bufK, 128 * pbit, [[S, 128], [256, 16], [1, 128]]),
                        k2all.ap()[1024 * pbit + 128 * hp:1024 * pbit + 128 * hp + 128, :]
                        .rearrange("p (t f) -> p t f", f=128)),
                        after=[(cc1, cc1.n), usedone], inc=dL, inc_by=16)
                    ph.op("sync", lambda s, hp=hp, pbit=pbit, bufV=bufV: s.dma_start(
                        AP(bufV, 130 * pbit, [[32 * 130, 128], [260, 16], [1, 130]]),
                        v2all.ap()[2048 * pbit:2048 * pbit + 2048, 130 * hp:130 * hp + 130]
                        .rearrange("(t p) f -> p t f", p=128)),
                        after=[(cc2, cc2.n), usedone], inc=dL, inc_by=16)
                kv_ok = (dL, dL.n)
                Qsl = AP(Q2, hp * TOK, [[8 * TOK, 128], [1, TOK]])
                if PAIR2:
                    emit_attention(ph, True, hp, Qsl, bufK, bufV, negm2, hp,
                                   ctx2_tm, [S0, S1], ctxp, pk3, psb, ready=[kv_ok])
                else:
                    emit_attention_v1(ph, True, hp, Qsl, bufK, bufV, negm2, hp,
                                      ctx2_tm, [S0, S1], ctxp, psb, ready=[kv_ok])
                kread[hp % 2] = sP.n
            if DEBUG_TAPS:
                dq = nc.dram_tensor("dbgQ2", [128, 8 * TOK], BF16,
                                    kind="ExternalOutput").ap()
                dc = nc.dram_tensor("dbgctx2", [128, NT * E], BF16,
                                    kind="ExternalOutput").ap()
                dx1 = nc.dram_tensor("dbgx1", [TOK, E], F32,
                                     kind="ExternalOutput").ap()
                dnm = nc.dram_tensor("dbgnm", [128, 8], F32,
                                     kind="ExternalOutput").ap()
                ph.op("sync", lambda s: s.dma_start(
                    dq, AP(Q2, 0, [[8 * TOK, 128], [1, 8 * TOK]])),
                    after=[(sDV, sDV.n)], inc=dL, inc_by=16)
                ph.op("sync", lambda s: s.dma_start(
                    dc, AP(ctx2_tm, 0, [[NT * E, 128], [1, NT * E]])),
                    after=[(sDV, sDV.n)], inc=dL, inc_by=16)
                ph.op("sync", lambda s: s.dma_start(dx1, x1d),
                      inc=dL, inc_by=16)
                ph.op("sync", lambda s: s.dma_start(
                    dnm, negm2[:, :]), inc=dL, inc_by=16)
            emit_drain(ph)
            ph.run()
        midq.close()

        p45 = ExitStack()
        y1fm = p45.enter_context(nc.sbuf_tensor("y1fm", [128, NC, TOK], BF16, side="right"))
        g1 = p45.enter_context(nc.sbuf_tensor("g1t", [128, E], F32, side="right"))
        b1 = p45.enter_context(nc.sbuf_tensor("b1t", [128, E], F32, side="right"))
        g2 = p45.enter_context(nc.sbuf_tensor("g2t", [128, E], F32, side="right"))
        b2 = p45.enter_context(nc.sbuf_tensor("b2t", [128, E], F32, side="right"))
        w_all = p45.enter_context(nc.sbuf_tensor("w_all", [128, NT, 8], F32, side="right"))

        with ExitStack() as st:
            ctx2_fm = st.enter_context(nc.sbuf_tensor("ctx2_fm", [128, NC, TOK], BF16))
            wo2 = st.enter_context(nc.sbuf_tensor("wo2", [128, NC, E], BF16))
            x1t = [st.enter_context(nc.sbuf_tensor(f"p4x1t{i}", [128, E], F32)) for i in range(2)]
            tres = st.enter_context(nc.sbuf_tensor("p4tres", [128, E], F32))
            yout = [st.enter_context(nc.sbuf_tensor(f"yout{i}", [128, E], F32)) for i in range(2)]
            ybf = st.enter_context(nc.sbuf_tensor("ybf", [128, E], BF16))
            scr = st.enter_context(nc.sbuf_tensor("p4scr", [128, E], F32))
            rstage = st.enter_context(nc.sbuf_tensor("p4rst", [1, E], F32))
            tp = [st.enter_context(nc.psum_tensor(f"tp4{i}", [128, 128], BF16)) for i in range(2)]
            po = [st.enter_context(nc.psum_tensor(f"po4{i}", [128, 512], F32)) for i in range(2)]
            pg = st.enter_context(nc.psum_tensor("pg4", [128, 512], F32))

            ph = Prog(nc)
            ph.op("gpsimd", lambda g: g.dma_start(wo2[:, :, :], wout2), inc=dG, inc_by=16)
            wordy = (dG, dG.n)
            rowused = 0
            for (row, dst) in ((ln1g, g1), (ln1b, b1), (ln2g, g2), (ln2b, b2)):
                ph.op("sync", lambda s, row=row: s.dma_start(rstage[:, :], row),
                      after=[(sEV, rowused)], inc=dL, inc_by=16)
                rowrdy = (dL, dL.n)
                for jj in range(2):
                    ph.op("tensor", lambda t, jj=jj: t.matmul(
                        pg[:, :], ones1f[:, :], rstage[0:1, 512 * jj:512 * jj + 512],
                        start=True, stop=True),
                        after=[rowrdy, (sEV, evcnt.get(id(pg), 0))], inc=sPJ)
                    ph.op("vector", lambda v, dst=dst, jj=jj: v.tensor_copy(
                        dst[:, 512 * jj:512 * jj + 512], pg[:, :]),
                        after=[(sPJ, sPJ.n)], inc=sEV)
                    evcnt[id(pg)] = sEV.n
                rowused = sEV.n
            divdone = (sDV, sDV.n)
            for t in range(NT):
                for c in range(NC):
                    k = (t * NC + c) % 2
                    ph.op("tensor", lambda tt, t=t, c=c, k=k: tt.transpose(
                        tp[k][:, :], ctx2_tm[:, t, 128 * c:128 * c + 128], ident[:, :]),
                        after=[divdone, (sEV, evcnt.get(id(tp[k]), 0))], inc=sTR)
                    ph.op("vector", lambda v, t=t, c=c, k=k: v.tensor_copy(
                        ctx2_fm[:, c, 128 * t:128 * t + 128], tp[k][:, :]),
                        after=[(sTR, sTR.n)], inc=sEV)
                    evcnt[id(tp[k])] = sEV.n
            trall = (sEV, sEV.n)
            x1used = {}
            youtfree = {}
            ybffree = 0
            for t in range(NT):
                ph.op("sync", lambda s, t=t: s.dma_start(
                    x1t[t % 2][:, :], x1d[128 * t:128 * t + 128, :]),
                    after=[(sEV, x1used.get(t % 2, 0))], inc=dL, inc_by=16)
                x1rdy = (dL, dL.n)
                for jj in range(2):
                    bank = po[jj]
                    for c in range(NC):
                        ph.op("tensor", lambda tt, t=t, c=c, jj=jj, bank=bank: tt.matmul(
                            bank[:, :], ctx2_fm[:, c, 128 * t:128 * t + 128],
                            wo2[:, c, 512 * jj:512 * jj + 512],
                            start=(c == 0), stop=False),
                            after=[wordy, trall, (sEV, evcnt.get(id(bank), 0))])
                    ph.op("tensor", lambda tt, jj=jj, bank=bank: tt.matmul(
                        bank[:, :], ones1[:, :], bo_bf[0:1, 512 * jj:512 * jj + 512],
                        start=False, stop=True), inc=sPJ)
                ph.op("vector", lambda v, t=t: v.tensor_add(
                    tres[:, 0:512], po[0][:, :], x1t[t % 2][:, 0:512]),
                    after=[(sPJ, sPJ.n), x1rdy])
                ph.op("vector", lambda v, t=t: v.tensor_add(
                    tres[:, 512:1024], po[1][:, :], x1t[t % 2][:, 512:1024]), inc=sEV)
                evcnt[id(po[0])] = sEV.n
                evcnt[id(po[1])] = sEV.n
                x1used[t % 2] = sEV.n
                emit_ln(ph, tres[:, :], yout[t % 2][:, :], scr, g1[:, :], b1[:, :],
                        outwait=(dL, youtfree.get(t % 2, 0)))
                ph.op("vector", lambda v, t=t: v.tensor_copy(ybf[:, :], yout[t % 2][:, :]),
                      after=[(sTR, ybffree)], inc=sC)
                ybf_ok = (sC, sC.n)
                ph.op("sync", lambda s, t=t: s.dma_start(
                    y1d[128 * t:128 * t + 128, :], yout[t % 2][:, :]),
                    after=[ybf_ok], inc=dL, inc_by=16)
                youtfree[t % 2] = dL.n
                for c in range(NC):
                    k = (t * NC + c) % 2
                    ph.op("tensor", lambda tt, t=t, c=c, k=k: tt.transpose(
                        tp[k][:, :], ybf[:, 128 * c:128 * c + 128], ident[:, :]),
                        after=[ybf_ok, (sEV, evcnt.get(id(tp[k]), 0))], inc=sTR)
                    ph.op("vector", lambda v, t=t, c=c, k=k: v.tensor_copy(
                        y1fm[:, c, 128 * t:128 * t + 128], tp[k][:, :]),
                        after=[(sTR, sTR.n)], inc=sEV)
                    evcnt[id(tp[k])] = sEV.n
                ybffree = sTR.n
            emit_drain(ph)
            ph.run()
        mid.close()

        with ExitStack() as st:
            acc = st.enter_context(nc.sbuf_tensor("acc", [128, NT, E], F32))
            Webf = [st.enter_context(nc.sbuf_tensor(f"Webf{i}", [128, NC, E], BF16)) for i in range(2)]
            be_bf = st.enter_context(nc.sbuf_tensor("be_bf", [1, NE * E], BF16))
            gex = st.enter_context(nc.sbuf_tensor("gex", [128, 8], F32))
            gsc = st.enter_context(nc.sbuf_tensor("gsc", [128, 8], F32))
            pgt = st.enter_context(nc.psum_tensor("pgt", [128, 8], F32))
            pe_ = [st.enter_context(nc.psum_tensor(f"pe{i}", [128, 512], F32)) for i in range(4)]

            ph = Prog(nc)
            ph.op("gpsimd", lambda g: g.dma_start(be_bf[:, :], berow), inc=dG, inc_by=16)
            berdy = (dG, dG.n)
            for t in range(NT):
                for c in range(NC):
                    ph.op("tensor", lambda tt, t=t, c=c: tt.matmul(
                        pgt[:, :], y1fm[:, c, 128 * t:128 * t + 128], gwt_bf[:, c, :],
                        start=(c == 0), stop=False),
                        after=[(sEV, evcnt.get(id(pgt), 0))])
                ph.op("tensor", lambda tt: tt.matmul(
                    pgt[:, :], ones1[:, :], gb_bf[0:1, :], start=False, stop=True),
                    inc=sPJ)
                ph.op("vector", lambda v: v.reduce_max(stt[:, 9:10], pgt[:, :], axis=AX.X),
                      after=[(sPJ, sPJ.n)], inc=sV)
                ph.op("vector", lambda v: v.tensor_scalar(
                    stt[:, 10:11], stt[:, 9:10], -1.0, None, ALU.mult), inc=sA,
                    wafter=[(sV, sV.n)])
                ph.op("scalar", lambda s: s.activation(
                    gex[:, :], pgt[:, :], AF.Exp, bias=stt[:, 10:11],
                    accum_out=stt[:, 11:12]), after=[(sA, sA.n)], inc=sB)
                ph.op("vector", lambda v: v.nop(), after=[(sB, sB.n)], inc=sEV)
                evcnt[id(pgt)] = sEV.n
                ph.op("vector", lambda v: v.reciprocal(stt[:, 12:13], stt[:, 11:12]),
                      inc=sV)
                ph.op("vector", lambda v: v.reduce_max(stt[:, 13:14], gex[:, :], axis=AX.X),
                      inc=sV)
                ph.op("vector", lambda v: v.tensor_scalar(
                    gsc[:, :], gex[:, :], stt[:, 13:14], -1e30, ALU.is_equal, ALU.mult),
                    wafter=[(sV, sV.n)], inc=sV)
                ph.op("vector", lambda v: v.tensor_add(gsc[:, :], gsc[:, :], gex[:, :]),
                      wafter=[(sV, sV.n)], inc=sV)
                ph.op("vector", lambda v: v.reduce_max(stt[:, 14:15], gsc[:, :], axis=AX.X),
                      wafter=[(sV, sV.n)], inc=sV)
                ph.op("vector", lambda v: v.tensor_scalar(
                    gsc[:, :], gex[:, :], stt[:, 14:15], None, ALU.is_ge),
                    wafter=[(sV, sV.n)], inc=sV)
                ph.op("vector", lambda v: v.tensor_mul(gsc[:, :], gsc[:, :], gex[:, :]),
                      wafter=[(sV, sV.n)], inc=sV)
                ph.op("vector", lambda v, t=t: v.tensor_scalar(
                    w_all[:, t, :], gsc[:, :], stt[:, 12:13], None, ALU.mult),
                    wafter=[(sV, sV.n)], inc=sC)
            wrdy = (sC, sC.n)
            ph.op("gpsimd", lambda g: g.dma_start(Webf[0][:, :, :], wexp[0, :, :, :]),
                  inc=dG, inc_by=16)
            for e in range(NE):
                wb = Webf[e % 2]
                werdy = (dG, dG.n)
                if e < NE - 1:
                    ph.op("gpsimd", lambda g, e=e: g.dma_start(
                        Webf[(e + 1) % 2][:, :, :], wexp[e + 1, :, :, :]),
                        after=[(sPJ, sPJ.n)], inc=dG, inc_by=16)
                for t in range(NT):
                    bks = (pe_[2 * (t % 2)], pe_[2 * (t % 2) + 1])
                    for c in range(NC):
                        for jj in range(2):
                            ph.op("tensor", lambda tt, t=t, c=c, jj=jj, wb=wb,
                                  bks=bks: tt.matmul(
                                      bks[jj][:, :], y1fm[:, c, 128 * t:128 * t + 128],
                                      wb[:, c, 512 * jj:512 * jj + 512],
                                      start=(c == 0), stop=False),
                                  after=[werdy, (sEV, evcnt.get(id(bks[jj]), 0))])
                    for jj in range(2):
                        ph.op("tensor", lambda tt, e=e, jj=jj, bks=bks: tt.matmul(
                            bks[jj][:, :], ones1[:, :],
                            be_bf[0:1, E * e + 512 * jj:E * e + 512 * jj + 512],
                            start=False, stop=True), after=[berdy], inc=sPJ)
                    pjv = (sPJ, sPJ.n)
                    for jj in range(2):
                        bank = bks[jj]
                        if e == 0:
                            ph.op("vector", lambda v, t=t, jj=jj, bank=bank: v.tensor_scalar(
                                acc[:, t, 512 * jj:512 * jj + 512], bank[:, :],
                                w_all[:, t, 0:1], None, ALU.mult),
                                after=[pjv, wrdy], inc=sEV)
                        else:
                            ph.op("vector", lambda v, t=t, jj=jj, e=e, bank=bank:
                                  v.scalar_tensor_tensor(
                                      acc[:, t, 512 * jj:512 * jj + 512], bank[:, :],
                                      w_all[:, t, e:e + 1],
                                      acc[:, t, 512 * jj:512 * jj + 512],
                                      ALU.mult, ALU.add),
                                  after=[pjv], inc=sEV)
                        evcnt[id(bank)] = sEV.n
            mdone = (sEV, sEV.n)
            emit_drain(ph)
            ph.run()

            with ExitStack() as st6:
                yt = [st6.enter_context(nc.sbuf_tensor(f"p6y{i}", [128, E], F32)) for i in range(2)]
                xt = [st6.enter_context(nc.sbuf_tensor(f"p6x{i}", [128, E], F32)) for i in range(2)]
                t2 = st6.enter_context(nc.sbuf_tensor("p6t2", [128, E], F32))
                f_ = st6.enter_context(nc.sbuf_tensor("p6f", [128, E], F32))
                t3 = st6.enter_context(nc.sbuf_tensor("p6t3", [128, E], F32))
                oout = [st6.enter_context(nc.sbuf_tensor(f"p6o{i}", [128, E], F32)) for i in range(2)]
                scr6 = st6.enter_context(nc.sbuf_tensor("p6scr", [128, E], F32))

                ph = Prog(nc)
                tileused = {}
                ofree = {}
                for t in range(NT):
                    ph.op("sync", lambda s, t=t: s.dma_start(
                        yt[t % 2][:, :], y1d[128 * t:128 * t + 128, :]),
                        after=[(sC, tileused.get(t % 2, 0))], inc=dL, inc_by=16)
                    ph.op("sync", lambda s, t=t: s.dma_start(
                        xt[t % 2][:, :], x1d[128 * t:128 * t + 128, :]),
                        inc=dL, inc_by=16)
                    ld = (dL, dL.n)
                    ph.op("vector", lambda v, t=t: v.tensor_add(
                        t2[:, :], yt[t % 2][:, :], acc[:, t, :]), after=[ld])
                    emit_ln(ph, t2[:, :], f_[:, :], scr6, g2[:, :], b2[:, :])
                    ph.op("vector", lambda v, t=t: v.tensor_add(
                        t3[:, :], xt[t % 2][:, :], f_[:, :]), inc=sC)
                    tileused[t % 2] = sC.n
                    emit_ln(ph, t3[:, :], oout[t % 2][:, :], scr6,
                            outwait=(dL, ofree.get(t % 2, 0)), inc=sC)
                    oo_ok = (sC, sC.n)
                    ph.op("sync", lambda s, t=t: s.dma_start(
                        OUT[128 * t:128 * t + 128, :], oout[t % 2][:, :]),
                        after=[oo_ok], inc=dL, inc_by=16)
                    ofree[t % 2] = dL.n
                ph.op("sync", lambda s: s.nop(), after=[(dL, dL.n)])
                ph.op("gpsimd", lambda g: g.nop(), after=[(dG, dG.n)])
                ph.run()

    return nc


_CACHE = {}


def _host_inputs(inputs):
    x0 = np.asarray(inputs["L0_attn_x"], np.float32)
    w_qkv = np.asarray(inputs["L0_attn_w_qkv"], np.float32)
    w_out = np.asarray(inputs["L0_attn_w_out"], np.float32)
    ipw = np.asarray(inputs["L0_ffn_in_proj_w"], np.float32)
    ipb = np.asarray(inputs["L0_ffn_in_proj_b"], np.float32)
    opw = np.asarray(inputs["L0_ffn_out_proj_w"], np.float32)
    opb = np.asarray(inputs["L0_ffn_out_proj_b"], np.float32)
    gw = np.asarray(inputs["L0_ffn_gate_w"], np.float32)
    gb = np.asarray(inputs["L0_ffn_gate_b"], np.float32)
    ew = np.asarray(inputs["L0_ffn_expert_w"], np.float32)
    eb = np.asarray(inputs["L0_ffn_expert_b"], np.float32)
    l1g = np.asarray(inputs["L0_ffn_ln1_w"], np.float32)
    l1b = np.asarray(inputs["L0_ffn_ln1_b"], np.float32)
    l2g = np.asarray(inputs["L0_ffn_ln2_w"], np.float32)
    l2b = np.asarray(inputs["L0_ffn_ln2_b"], np.float32)

    def pairs_layout(Wq, Wk, Wv):
        blocks = []
        for hp in range(8):
            s = slice(128 * hp, 128 * hp + 128)
            blocks.append(np.concatenate([Wq[:, s], Wk[:, s], Wv[:, s]], axis=1))
        arr = np.stack(blocks, axis=0)
        arr = arr.reshape(8, 8, 128, 384).transpose(2, 0, 1, 3)
        return np.ascontiguousarray(arr)

    wqkv1 = pairs_layout(w_qkv[:, 0:E], w_qkv[:, E:2 * E], w_qkv[:, 2 * E:3 * E])
    wqkv2 = pairs_layout(ipw[0:E].T, ipw[E:2 * E].T, ipw[2 * E:3 * E].T)
    wout1 = np.ascontiguousarray(w_out.reshape(8, 128, E).transpose(1, 0, 2))
    wout2 = np.ascontiguousarray(opw.T.reshape(8, 128, E).transpose(1, 0, 2))
    wexp = np.ascontiguousarray(
        ew.transpose(0, 2, 1).reshape(NE, 8, 128, E).transpose(0, 2, 1, 3))
    bq = ipb[0:E]; bk = ipb[E:2 * E]; bv = ipb[2 * E:3 * E]
    bqk_h = np.ascontiguousarray(
        np.concatenate([bq.reshape(8, 128).T, bk.reshape(8, 128).T], axis=1))
    gwt_h = np.ascontiguousarray(gw.T.reshape(8, 128, 8).transpose(1, 0, 2))

    inv = (1.0 / 10000.0 ** (np.arange(0, HD, 2, dtype=np.float32) / HD)).astype(np.float32)
    tri = np.triu(np.ones((128, 128), np.float32))

    common = dict(
        wqkv1=wqkv1, wout1=wout1, wqkv2=wqkv2, wout2=wout2, wexp=wexp,
        bqk=bqk_h,
        bvrow=np.ascontiguousarray(bv[None, :]),
        borow=np.ascontiguousarray(opb[None, :]),
        gwt=gwt_h, gbrow=np.ascontiguousarray(gb[None, :]),
        berow=np.ascontiguousarray(eb.reshape(1, NE * E)),
        ln1g=np.ascontiguousarray(l1g[None, :]), ln1b=np.ascontiguousarray(l1b[None, :]),
        ln2g=np.ascontiguousarray(l2g[None, :]), ln2b=np.ascontiguousarray(l2b[None, :]),
        identf=np.eye(128, dtype=np.float32),
    )

    in_maps = []
    for core in range(8):
        b, par = core // 2, core % 2
        xb = x0[b]
        xT = np.ascontiguousarray(xb.T)
        x0f_ = np.ascontiguousarray(xT.reshape(8, 128, S).transpose(1, 0, 2))
        own = xb.reshape(32, 128, E)[par::2]
        x0tm_ = np.ascontiguousarray(own.reshape(TOK, E))
        x0fo_ = np.ascontiguousarray(
            x0f_.reshape(128, 8, 32, 128)[:, :, par::2, :].reshape(128, 8, TOK))
        pos = np.arange(S).reshape(32, 128)[par::2].reshape(TOK).astype(np.float32)
        fr = pos[:, None] * inv[None, :]
        cost_ = np.ascontiguousarray(np.tile(np.cos(fr).astype(np.float32), (1, H)))
        sint_ = np.ascontiguousarray(np.tile(np.sin(fr).astype(np.float32), (1, H)))
        m0_ = tri if par == 0 else np.ones((128, 128), np.float32)
        m1_ = np.zeros((128, 128), np.float32) if par == 0 else tri
        im = dict(common)
        im.update(x0f=x0f_, x0fo=x0fo_, x0tm=x0tm_, cost=cost_, sint=sint_,
                  m0in=m0_, m1in=m1_)
        in_maps.append(im)
    return in_maps


def kernel(**inputs):
    in_maps = _host_inputs(inputs)
    if "nc" not in _CACHE:
        _CACHE["nc"] = build_program()
    nc = _CACHE["nc"]
    res = run_bass_kernel_spmd(nc, in_maps, core_ids=list(range(8)))
    out = np.empty((B, S, E), np.float32)
    for core in range(8):
        b, par = core // 2, core % 2
        o = np.asarray(res.results[core]["OUT"], np.float32).reshape(NT, 128, E)
        out.reshape(B, 32, 128, E)[b, par::2] = o
    return out


# version 09
# speedup vs baseline: 1.9482x; 1.0308x over previous
import numpy as np
from contextlib import ExitStack

PAIR1 = True
PAIR1_N = 8
PMODE = "paired"
PAIR2 = True

import concourse.bass as bass
import concourse.mybir as mybir
from concourse.bass_utils import run_bass_kernel_spmd

F32 = mybir.dt.float32
BF16 = mybir.dt.bfloat16
AF = mybir.ActivationFunctionType
ALU = mybir.AluOpType
AX = mybir.AxisListType
AP = bass.AP

B, S, E, H, HD, NE = 4, 4096, 1024, 16, 64, 8
NT = 16
TOK = NT * 128
NC = 8
LN_EPS = 1e-5

ENGS = ("sync", "gpsimd", "tensor", "vector", "scalar")
DEBUG_TAPS = False


class Sem:
    __slots__ = ("h", "n", "by")

    def __init__(self, st, nc, name):
        self.h = st.enter_context(nc.semaphore(name))
        self.n = 0
        self.by = {e: 0 for e in ENGS}


class Prog:

    def __init__(self, nc):
        self.nc = nc
        self.seq = {e: [] for e in ENGS}

    def op(self, eng, emit, after=(), inc=None, inc_by=1, wafter=()):
        w = []
        for a in after:
            s, v = (a if isinstance(a, tuple) else (a, a.n))
            if v > 0 and s.by[eng] < v:
                w.append((s, v))
        for a in wafter:
            s, v = (a if isinstance(a, tuple) else (a, a.n))
            if v > 0:
                w.append((s, v))
        if inc is not None:
            inc.n += inc_by
            inc.by[eng] += inc_by
        self.seq[eng].append((w, emit, inc, inc_by))

    def run(self):
        with self.nc.Block() as blk:
            for eng in ENGS:
                ops = self.seq[eng]
                if not ops:
                    continue

                def body(engine, ops=ops):
                    seen = {}
                    for (w, emit, inc, inc_by) in ops:
                        for (s, v) in w:
                            if seen.get(id(s), -1) < v:
                                engine.wait_ge(s.h, v)
                                seen[id(s)] = v
                        inst = emit(engine)
                        if inc is not None:
                            inst.then_inc(inc.h, inc_by)
                getattr(blk, eng)(body)


def build_program():
    nc = bass.Bass("TRN2", target_bir_lowering=False,
                   detect_race_conditions=False)

    def I(name, shape, dt=F32):
        return nc.dram_tensor(name, shape, dt, kind="ExternalInput").ap()

    x0f = I("x0f", [128, NC, S])
    x0fo = I("x0fo", [128, NC, TOK])
    x0tm = I("x0tm", [TOK, E])
    wqkv1 = I("wqkv1", [128, 8, NC, 384])
    wout1 = I("wout1", [128, NC, E])
    wqkv2 = I("wqkv2", [128, 8, NC, 384])
    wout2 = I("wout2", [128, NC, E])
    wexp = I("wexp", [NE, 128, NC, E])
    bqk = I("bqk", [128, 16])
    bvrow = I("bvrow", [1, E])
    borow = I("borow", [1, E])
    gwt = I("gwt", [128, NC, 8])
    gbrow = I("gbrow", [1, 8])
    berow = I("berow", [1, NE * E])
    ln1g = I("ln1g", [1, E]); ln1b = I("ln1b", [1, E])
    ln2g = I("ln2g", [1, E]); ln2b = I("ln2b", [1, E])
    cost = I("cost", [TOK, 512]); sint = I("sint", [TOK, 512])
    m0in = I("m0in", [128, 128]); m1in = I("m1in", [128, 128])
    identf = I("identf", [128, 128])

    OUT = nc.dram_tensor("OUT", [TOK, E], F32, kind="ExternalOutput").ap()

    x1d = nc.dram_tensor("x1d", [TOK, E], F32, kind="Internal").ap()
    y1d = nc.dram_tensor("y1d", [TOK, E], F32, kind="Internal").ap()
    k2loc = nc.dram_tensor("k2loc", [1024, TOK], BF16, kind="Internal")
    v2loc = nc.dram_tensor("v2loc", [TOK, 8 * 130], BF16, kind="Internal")
    k2all = nc.dram_tensor("k2all", [2048, TOK], BF16, kind="Internal")
    v2all = nc.dram_tensor("v2all", [2 * TOK, 8 * 130], BF16, kind="Internal")
    arin = nc.dram_tensor("arin", [1, 1], F32, kind="Internal")
    arout = nc.dram_tensor("arout", [1, 1], F32, kind="Internal")
    RG = [[0, 1], [2, 3], [4, 5], [6, 7]]

    with ExitStack() as st0:
        EE = st0.enter_context
        dL = Sem(st0, nc, "dL")
        dG = Sem(st0, nc, "dG")
        cc1 = Sem(st0, nc, "cc1"); cc2 = Sem(st0, nc, "cc2"); cc3 = Sem(st0, nc, "cc3")
        sPJ = Sem(st0, nc, "sPJ")
        sEV = Sem(st0, nc, "sEV")
        sS = Sem(st0, nc, "sS")
        sE = Sem(st0, nc, "sE")
        sM = Sem(st0, nc, "sM")
        sP = Sem(st0, nc, "sP")
        sDV = Sem(st0, nc, "sDV")
        sA = Sem(st0, nc, "sA")
        sB = Sem(st0, nc, "sB")
        sTR = Sem(st0, nc, "sTR")
        sC = Sem(st0, nc, "sC")
        sV = Sem(st0, nc, "sV")

        ident = EE(nc.sbuf_tensor("ident", [128, 128], BF16))
        m0 = EE(nc.sbuf_tensor("m0", [128, 128], BF16))
        m1 = EE(nc.sbuf_tensor("m1", [128, 128], BF16))
        ones1 = EE(nc.sbuf_tensor("ones1", [1, 128], BF16))
        ones1f = EE(nc.sbuf_tensor("ones1f", [1, 128], F32))
        ones128 = EE(nc.sbuf_tensor("ones128", [128, 1], BF16))
        bqk_t = EE(nc.sbuf_tensor("bqk_t", [128, 16], F32))
        bv_bf = EE(nc.sbuf_tensor("bv_bf", [1, E], BF16))
        bo_bf = EE(nc.sbuf_tensor("bo_bf", [1, E], BF16))
        gb_bf = EE(nc.sbuf_tensor("gb_bf", [1, 8], BF16))
        gwt_bf = EE(nc.sbuf_tensor("gwt_bf", [128, NC, 8], BF16))
        negm = EE(nc.sbuf_tensor("negm", [128, 2], F32))
        negm2 = EE(nc.sbuf_tensor("negm2", [128, 8], F32))
        stt = EE(nc.sbuf_tensor("stt", [128, 16], F32))
        nrm = EE(nc.sbuf_tensor("nrm", [1, 24], F32))
        nrmbf = EE(nc.sbuf_tensor("nrmbf", [1, 2], BF16))
        recb = EE(nc.sbuf_tensor("recb", [128, 16], F32))

        evcnt = {}

        with ExitStack() as st:
            stage = st.enter_context(nc.sbuf_tensor("stage", [128, 128], F32))
            rstage = st.enter_context(nc.sbuf_tensor("rstage", [1, E], F32))
            ph = Prog(nc)
            for (src, dst) in ((identf, ident), (m0in, m0), (m1in, m1)):
                ph.op("sync", lambda s, src=src: s.dma_start(stage[:, :], src),
                      after=[sEV], inc=dL, inc_by=16)
                ph.op("vector", lambda v, dst=dst: v.tensor_copy(dst[:, :], stage[:, :]),
                      after=[dL], inc=sEV)
            ph.op("sync", lambda s: s.dma_start(bqk_t[:, :], bqk), inc=dL, inc_by=16)
            for (src, dst, w_) in ((bvrow, bv_bf, E), (borow, bo_bf, E), (gbrow, gb_bf, 8)):
                ph.op("sync", lambda s, src=src, w_=w_: s.dma_start(rstage[:, 0:w_], src),
                      after=[sEV], inc=dL, inc_by=16)
                ph.op("vector", lambda v, dst=dst, w_=w_: v.tensor_copy(dst[:, 0:w_], rstage[:, 0:w_]),
                      after=[dL], inc=sEV)
            ph.op("gpsimd", lambda g: g.dma_start(gwt_bf[:, :, :], gwt), inc=dG, inc_by=16)
            ph.op("vector", lambda v: v.memset(ones1[:, :], 1.0), after=[dG, dL])
            ph.op("vector", lambda v: v.memset(ones1f[:, :], 1.0))
            ph.op("vector", lambda v: v.memset(ones128[:, :], 1.0))
            ph.run()

        def emit_norm_bound(ph, pkd, src, n512, sqb, acc_col, init):
            (pk0t, pk0o) = pkd
            """nrm[0:1, acc_col] (+)= max over tokens of 128-partition squared
            norms of bf16 feature-major src [128, 512*n512]."""
            first = init
            for j in range(n512):
                ph.op("vector", lambda v, j=j: v.tensor_mul(
                    sqb[:, :], src[:, 512 * j:512 * j + 512],
                    src[:, 512 * j:512 * j + 512]),
                    after=[(sPJ, sPJ.n)], inc=sC)
                ph.op("tensor", lambda t: t.matmul(
                    pk0t[0:1, pk0o:pk0o + 512], ones128[:, :], sqb[:, :], start=True, stop=True),
                    after=[(sC, sC.n), (sEV, evcnt.get((id(pk0t), pk0o), 0))], inc=sPJ)
                ph.op("vector", lambda v: v.reduce_max(
                    nrm[0:1, 2:3], pk0t[0:1, pk0o:pk0o + 512], axis=AX.X), after=[(sPJ, sPJ.n)],
                    inc=sV)
                if first:
                    ph.op("vector", lambda v: v.tensor_copy(
                        nrm[0:1, acc_col:acc_col + 1], nrm[0:1, 2:3]), inc=sV,
                        wafter=[(sV, sV.n)])
                    first = False
                else:
                    ph.op("vector", lambda v: v.tensor_tensor(
                        nrm[0:1, acc_col:acc_col + 1],
                        nrm[0:1, acc_col:acc_col + 1], nrm[0:1, 2:3], ALU.max), inc=sV,
                        wafter=[(sV, sV.n)])
                ph.op("vector", lambda v: v.nop(), wafter=[(sV, sV.n)], inc=sEV)
                evcnt[(id(pk0t), pk0o)] = sEV.n

        def emit_make_negm(ph, pkd, qcol, kcol, dst, dst_col):
            (pk1t, pk1o) = pkd
            """dst[:, dst_col] = -(sqrt(nrm[qcol]*nrm[kcol]) + 1), broadcast."""
            ph.op("vector", lambda v: v.tensor_mul(
                nrm[0:1, 4:5], nrm[0:1, qcol:qcol + 1], nrm[0:1, kcol:kcol + 1]),
                wafter=[(sV, sV.n)], inc=sA)
            ph.op("scalar", lambda s: s.activation(nrm[0:1, 5:6], nrm[0:1, 4:5], AF.Ln),
                  after=[(sA, sA.n)])
            ph.op("scalar", lambda s: s.activation(nrm[0:1, 6:7], nrm[0:1, 5:6], AF.Exp,
                                                   scale=0.5), inc=sB)
            ph.op("vector", lambda v: v.tensor_scalar(
                nrm[0:1, 7:8], nrm[0:1, 6:7], -1.0, -1.0, ALU.mult, ALU.add),
                after=[(sB, sB.n)], inc=sV)
            ph.op("vector", lambda v: v.tensor_copy(nrmbf[0:1, 0:1], nrm[0:1, 7:8]),
                  wafter=[(sV, sV.n)], inc=sC)
            ph.op("tensor", lambda t: t.matmul(
                pk1t[:, pk1o:pk1o + 1], ones1[:, :], nrmbf[0:1, 0:1], start=True, stop=True),
                after=[(sC, sC.n), (sEV, evcnt.get((id(pk1t), pk1o), 0))], inc=sPJ)
            ph.op("vector", lambda v: v.tensor_copy(
                dst[:, dst_col:dst_col + 1], pk1t[:, pk1o:pk1o + 1]), after=[(sPJ, sPJ.n)], inc=sEV)
            evcnt[(id(pk1t), pk1o)] = sEV.n
            evcnt[("nm", id(dst), dst_col)] = sEV.n

        expcnt = {}

        def emit_attention_v1(ph, is2, hp, Q, K, V, nmten, nmcol, ctx_tm, Sb, ctxp,
                           psb, ready):
            for h2 in range(2):
                h = 2 * hp + h2
                hb = 64 * h2
                for qc in range(2):
                    nkb = 32 if not is2 else (16 if qc == 0 else 32)
                    divwait = (sDV, sDV.n)
                    pend = []
                    def flush_pv():
                        (kb, bi, j0, expv, mwait, jstop) = pend.pop(0)
                        js = list(range(j0, 8))
                        for idx, j in enumerate(js):
                            last = (idx == len(js) - 1)
                            stt_ = (kb == 0 and j in (0, 4))
                            stp_ = ((j == 3 and kb == jstop[3]) or
                                    (j == 7 and kb == jstop[7]))
                            ph.op("tensor", lambda t, kb=kb, j=j, bi=bi,
                                  stt_=stt_, stp_=stp_, h2=h2: t.matmul(
                                      ctxp[:, 128 * j:128 * j + 65],
                                      psb[bi][:, 128 * j:128 * j + 128],
                                      V[:, 130 * kb + 65 * h2:130 * kb + 65 * h2 + 65],
                                      start=stt_, stop=stp_),
                                  after=[(sE, expv), mwait, divwait] + ready,
                                  inc=(sP if last else None))
                    for kb in range(nkb):
                        bi = kb % 2
                        Sx = Sb[bi]
                        scw = [(sE, expcnt.get(id(Sx), 0))] + ready
                        for jj in range(2):
                            ph.op("tensor", lambda t, kb=kb, jj=jj, Sx=Sx,
                                  hb=hb, qc=qc: t.matmul(
                                Sx[:, 512 * jj:512 * jj + 512],
                                K[hb:hb + 64, 128 * kb:128 * kb + 128],
                                Q[hb:hb + 64, 1024 * qc + 512 * jj:1024 * qc + 512 * jj + 512],
                                start=True, stop=True),
                                after=scw, inc=sS)
                        ph.op("scalar", lambda s, bi=bi, Sx=Sx: s.activation(
                            psb[bi][:, :], Sx[:, :], AF.Exp,
                            bias=nmten[:, nmcol:nmcol + 1], scale=1.0),
                            after=[(sS, sS.n), (sP, sP.n),
                                   (sEV, evcnt.get(("nm", id(nmten), nmcol), 0))],
                            inc=sE)
                        expcnt[id(Sx)] = sE.n
                        expv = sE.n
                        if is2:
                            j0 = (kb // 2) if qc == 0 else max(0, (kb - 16) // 2)
                            rr = (kb - 2 * j0) if qc == 0 else (kb - 16 - 2 * j0)
                        else:
                            j0, rr = 0, -1
                        if is2 and rr in (0, 1):
                            mk = m0 if rr == 0 else m1
                            ph.op("vector", lambda v, bi=bi, j0=j0, mk=mk: v.tensor_mul(
                                psb[bi][:, 128 * j0:128 * j0 + 128],
                                psb[bi][:, 128 * j0:128 * j0 + 128], mk[:, :]),
                                after=[(sE, expv)], inc=sM)
                        mwait = (sM, sM.n)
                        if is2:
                            jstop = {j: (2 * j + 1 if qc == 0 else 17 + 2 * j)
                                     for j in range(8)}
                        else:
                            jstop = {j: nkb - 1 for j in range(8)}
                        pend.append((kb, bi, j0, expv, mwait, jstop))
                        if len(pend) > 1:
                            flush_pv()
                    flush_pv()
                    ph.op("vector", lambda v: v.reciprocal(
                        recb[:, 0:8], AP(ctxp.ap().tensor, 64, [[1024, 128], [128, 8]])),
                        after=[(sP, sP.n)], inc=sV)
                    recfence = (sV, sV.n)
                    for j in range(8):
                        tloc = qc * 8 + j
                        ph.op("vector", lambda v, j=j, tloc=tloc, h=h: v.tensor_scalar(
                            ctx_tm[:, tloc, 64 * h:64 * h + 64],
                            ctxp[:, 128 * j:128 * j + 64],
                            recb[:, j:j + 1], None, ALU.mult),
                            wafter=[recfence],
                            inc=(sDV if j == 7 else None))

        def emit_attention_p4(ph, is2, hp, Q, K, V, nmten, nmcol, ctx_tm, Sb, ctxp,
                           psb, ready):
            for h2 in range(2):
                h = 2 * hp + h2
                hb = 64 * h2
                for qc in range(2):
                    nkb = 32 if not is2 else (16 if qc == 0 else 32)
                    divwait = (sDV, sDV.n)
                    pend = []
                    def flush_pv():
                        (kb, bi, j0, expv, mwait, jstop) = pend.pop(0)
                        js = list(range(j0, 8))
                        for idx, j in enumerate(js):
                            last = (idx == len(js) - 1)
                            stt_ = (kb == 0 and j in (0, 4))
                            stp_ = ((j == 3 and kb == jstop[3]) or
                                    (j == 7 and kb == jstop[7]))
                            ph.op("tensor", lambda t, kb=kb, j=j, bi=bi,
                                  stt_=stt_, stp_=stp_, h2=h2: t.matmul(
                                      ctxp[:, 128 * j:128 * j + 65],
                                      psb[bi][:, 128 * j:128 * j + 128],
                                      V[:, 130 * kb + 65 * h2:130 * kb + 65 * h2 + 65],
                                      start=stt_, stop=stp_),
                                  after=[(sE, expv), mwait, divwait] + ready,
                                  inc=(sP if last else None))
                    for kb in range(nkb):
                        bi = kb % 2
                        Sx = Sb[bi]
                        pi = 2 * h2 + kb % 2
                        scw = [(sE, expcnt.get(id(Sx), 0))] + ready
                        for jj in range(2):
                            ph.op("tensor", lambda t, kb=kb, jj=jj, Sx=Sx,
                                  hb=hb, qc=qc: t.matmul(
                                Sx[:, 512 * jj:512 * jj + 512],
                                K[hb:hb + 64, 128 * kb:128 * kb + 128],
                                Q[hb:hb + 64, 1024 * qc + 512 * jj:1024 * qc + 512 * jj + 512],
                                start=True, stop=True),
                                after=scw, inc=sS)
                        ph.op("scalar", lambda s, bi=pi, Sx=Sx: s.activation(
                            psb[bi][:, :], Sx[:, :], AF.Exp,
                            bias=nmten[:, nmcol:nmcol + 1], scale=1.0),
                            after=[(sS, sS.n), (sP, sP.n),
                                   (sEV, evcnt.get(("nm", id(nmten), nmcol), 0))],
                            inc=sE)
                        expcnt[id(Sx)] = sE.n
                        expv = sE.n
                        if is2:
                            j0 = (kb // 2) if qc == 0 else max(0, (kb - 16) // 2)
                            rr = (kb - 2 * j0) if qc == 0 else (kb - 16 - 2 * j0)
                        else:
                            j0, rr = 0, -1
                        if is2 and rr in (0, 1):
                            mk = m0 if rr == 0 else m1
                            ph.op("vector", lambda v, bi=pi, j0=j0, mk=mk: v.tensor_mul(
                                psb[bi][:, 128 * j0:128 * j0 + 128],
                                psb[bi][:, 128 * j0:128 * j0 + 128], mk[:, :]),
                                after=[(sE, expv)], inc=sM)
                        mwait = (sM, sM.n)
                        if is2:
                            jstop = {j: (2 * j + 1 if qc == 0 else 17 + 2 * j)
                                     for j in range(8)}
                        else:
                            jstop = {j: nkb - 1 for j in range(8)}
                        pend.append((kb, pi, j0, expv, mwait, jstop))
                        if len(pend) > 1:
                            flush_pv()
                    flush_pv()
                    ph.op("vector", lambda v: v.reciprocal(
                        recb[:, 0:8], AP(ctxp.ap().tensor, 64, [[1024, 128], [128, 8]])),
                        after=[(sP, sP.n)], inc=sV)
                    recfence = (sV, sV.n)
                    for j in range(8):
                        tloc = qc * 8 + j
                        ph.op("vector", lambda v, j=j, tloc=tloc, h=h: v.tensor_scalar(
                            ctx_tm[:, tloc, 64 * h:64 * h + 64],
                            ctxp[:, 128 * j:128 * j + 64],
                            recb[:, j:j + 1], None, ALU.mult),
                            wafter=[recfence],
                            inc=(sDV if j == 7 else None))

        def emit_attention_pc(ph, is2, hp, Q, K, V, nmten, nmcol, ctx_tm, Sb, ctxp,
                           pk, psb, ready):
            ctxsl = {0: [(ctxp, 128 * j) for j in range(8)],
                     1: [(pk[0], 128 * j) for j in range(4)] +
                        [(pk[1], 128 * (j - 4)) for j in range(4, 8)]}
            for h2 in range(2):
                h = 2 * hp + h2
                hb = 64 * h2
                for qc in range(2):
                    nkb = 32 if not is2 else (16 if qc == 0 else 32)
                    divwait = (sDV, sDV.n)
                    pend = []
                    def flush_pv():
                        (kb, bi, j0, expv, mwait, jstop) = pend.pop(0)
                        js = list(range(j0, 8))
                        for idx, j in enumerate(js):
                            last = (idx == len(js) - 1)
                            stt_ = (kb == 0 and j in (0, 4))
                            stp_ = ((j == 3 and kb == jstop[3]) or
                                    (j == 7 and kb == jstop[7]))
                            (ct, off) = ctxsl[h2][j]
                            ph.op("tensor", lambda t, kb=kb, j=j, bi=bi, ct=ct, off=off,
                                  stt_=stt_, stp_=stp_, h2=h2: t.matmul(
                                      ct[:, off:off + 65],
                                      psb[bi][:, 128 * j:128 * j + 128],
                                      V[:, 130 * kb + 65 * h2:130 * kb + 65 * h2 + 65],
                                      start=stt_, stop=stp_),
                                  after=[(sE, expv), mwait, divwait] + ready,
                                  inc=(sP if last else None))
                    for kb in range(nkb):
                        bi = kb % 2
                        Sx = Sb[bi]
                        pi = 2 * h2 + kb % 2
                        scw = [(sE, expcnt.get(id(Sx), 0))] + ready
                        for jj in range(2):
                            ph.op("tensor", lambda t, kb=kb, jj=jj, Sx=Sx,
                                  hb=hb, qc=qc: t.matmul(
                                Sx[:, 512 * jj:512 * jj + 512],
                                K[hb:hb + 64, 128 * kb:128 * kb + 128],
                                Q[hb:hb + 64, 1024 * qc + 512 * jj:1024 * qc + 512 * jj + 512],
                                start=True, stop=True),
                                after=scw, inc=sS)
                        ph.op("scalar", lambda s, bi=pi, Sx=Sx: s.activation(
                            psb[bi][:, :], Sx[:, :], AF.Exp,
                            bias=nmten[:, nmcol:nmcol + 1], scale=1.0),
                            after=[(sS, sS.n), (sP, sP.n),
                                   (sEV, evcnt.get(("nm", id(nmten), nmcol), 0))],
                            inc=sE)
                        expcnt[id(Sx)] = sE.n
                        expv = sE.n
                        if is2:
                            j0 = (kb // 2) if qc == 0 else max(0, (kb - 16) // 2)
                            rr = (kb - 2 * j0) if qc == 0 else (kb - 16 - 2 * j0)
                        else:
                            j0, rr = 0, -1
                        if is2 and rr in (0, 1):
                            mk = m0 if rr == 0 else m1
                            ph.op("vector", lambda v, bi=pi, j0=j0, mk=mk: v.tensor_mul(
                                psb[bi][:, 128 * j0:128 * j0 + 128],
                                psb[bi][:, 128 * j0:128 * j0 + 128], mk[:, :]),
                                after=[(sE, expv)], inc=sM)
                        mwait = (sM, sM.n)
                        if is2:
                            jstop = {j: (2 * j + 1 if qc == 0 else 17 + 2 * j)
                                     for j in range(8)}
                        else:
                            jstop = {j: nkb - 1 for j in range(8)}
                        pend.append((kb, pi, j0, expv, mwait, jstop))
                        if len(pend) > 1:
                            flush_pv()
                    flush_pv()
                    if h2 == 0:
                        ph.op("vector", lambda v: v.reciprocal(
                            recb[:, 0:8], AP(ctxp.ap().tensor, 64, [[1024, 128], [128, 8]])),
                            after=[(sP, sP.n)], inc=sV)
                    else:
                        ph.op("vector", lambda v: v.reciprocal(
                            recb[:, 8:12], AP(pk[0].ap().tensor, 64, [[512, 128], [128, 4]])),
                            after=[(sP, sP.n)], inc=sV)
                        ph.op("vector", lambda v: v.reciprocal(
                            recb[:, 12:16], AP(pk[1].ap().tensor, 64, [[512, 128], [128, 4]])),
                            inc=sV)
                    recfence = (sV, sV.n)
                    for j in range(8):
                        tloc = qc * 8 + j
                        (ct, off) = ctxsl[h2][j]
                        rcol = j if h2 == 0 else 8 + j
                        ph.op("vector", lambda v, j=j, tloc=tloc, h=h, ct=ct,
                              off=off, rcol=rcol: v.tensor_scalar(
                            ctx_tm[:, tloc, 64 * h:64 * h + 64],
                            ct[:, off:off + 64],
                            recb[:, rcol:rcol + 1], None, ALU.mult),
                            wafter=[recfence],
                            inc=(sDV if j == 7 else None))
                    if h2 == 1:
                        ph.op("vector", lambda v: v.nop(), inc=sEV)
                        evcnt[id(pk[0])] = sEV.n
                        evcnt[id(pk[1])] = sEV.n

        def emit_attention(ph, is2, hp, Q, K, V, nmten, nmcol, ctx_tm, Sb, ctxp,
                           pk, psb, ready):
            ctxsl = {0: [(ctxp, 128 * j) for j in range(8)],
                     1: [(pk[0], 128 * j) for j in range(4)] +
                        [(pk[1], 128 * (j - 4)) for j in range(4, 8)]}
            for qc in range(2):
                nkb = 32 if not is2 else (16 if qc == 0 else 32)
                divwait = (sDV, sDV.n)
                pend = []

                def flush_pv():
                    (kb, h2, bi, j0, expv, mwait, jstop) = pend.pop(0)
                    js = list(range(j0, 8))
                    for idx, j in enumerate(js):
                        last = (idx == len(js) - 1)
                        stt_ = (kb == 0 and j in (0, 4))
                        stp_ = ((j == 3 and kb == jstop[3]) or
                                (j == 7 and kb == jstop[7]))
                        (ct, off) = ctxsl[h2][j]
                        ph.op("tensor", lambda t, kb=kb, j=j, bi=bi, h2=h2,
                              stt_=stt_, stp_=stp_, ct=ct, off=off: t.matmul(
                                  ct[:, off:off + 65],
                                  psb[bi][:, 128 * j:128 * j + 128],
                                  V[:, 130 * kb + 65 * h2:130 * kb + 65 * h2 + 65],
                                  start=stt_, stop=stp_),
                              after=[(sE, expv), mwait, divwait] + ready,
                              inc=(sP if last else None))

                for kb in range(nkb):
                    if is2:
                        j0 = (kb // 2) if qc == 0 else max(0, (kb - 16) // 2)
                        rr = (kb - 2 * j0) if qc == 0 else (kb - 16 - 2 * j0)
                    else:
                        j0, rr = 0, -1
                    if is2:
                        jstop = {j: (2 * j + 1 if qc == 0 else 17 + 2 * j)
                                 for j in range(8)}
                    else:
                        jstop = {j: nkb - 1 for j in range(8)}
                    eoff = 128 * j0 if is2 else 0
                    scv = [0, 0]
                    for h2 in range(2):
                        Sx = Sb[h2]
                        scw = [(sE, expcnt.get(id(Sx), 0))] + ready
                        for jj in range(2):
                            if 512 * jj + 512 <= eoff:
                                continue
                            ph.op("tensor", lambda t, kb=kb, jj=jj, Sx=Sx,
                                  h2=h2, qc=qc: t.matmul(
                                Sx[:, 512 * jj:512 * jj + 512],
                                K[64 * h2:64 * h2 + 64, 128 * kb:128 * kb + 128],
                                Q[64 * h2:64 * h2 + 64,
                                  1024 * qc + 512 * jj:1024 * qc + 512 * jj + 512],
                                start=True, stop=True),
                                after=scw, inc=sS)
                        scv[h2] = sS.n
                    for h2 in range(2):
                        Sx = Sb[h2]
                        bi = 2 * h2 + kb % 2
                        ph.op("scalar", lambda s, bi=bi, Sx=Sx, eoff=eoff: s.activation(
                            psb[bi][:, eoff:1024], Sx[:, eoff:1024], AF.Exp,
                            bias=nmten[:, nmcol:nmcol + 1], scale=1.0),
                            after=[(sS, scv[h2]),
                                   (sP, sP.n),
                                   (sEV, evcnt.get(("nm", id(nmten), nmcol), 0))],
                            inc=sE)
                        expcnt[id(Sx)] = sE.n
                        expv = sE.n
                        if is2 and rr in (0, 1):
                            mk = m0 if rr == 0 else m1
                            ph.op("vector", lambda v, bi=bi, j0=j0, mk=mk: v.tensor_mul(
                                psb[bi][:, 128 * j0:128 * j0 + 128],
                                psb[bi][:, 128 * j0:128 * j0 + 128], mk[:, :]),
                                after=[(sE, expv)], inc=sM)
                        mwait = (sM, sM.n)
                        pend.append((kb, h2, bi, j0, expv, mwait, jstop))
                    while len(pend) > 2:
                        flush_pv()
                while pend:
                    flush_pv()
                ph.op("vector", lambda v: v.reciprocal(
                    recb[:, 0:8], AP(ctxp.ap().tensor, 64, [[1024, 128], [128, 8]])),
                    after=[(sP, sP.n)], inc=sV)
                ph.op("vector", lambda v: v.reciprocal(
                    recb[:, 8:12], AP(pk[0].ap().tensor, 64, [[512, 128], [128, 4]])),
                    inc=sV)
                ph.op("vector", lambda v: v.reciprocal(
                    recb[:, 12:16], AP(pk[1].ap().tensor, 64, [[512, 128], [128, 4]])),
                    inc=sV)
                recfence = (sV, sV.n)
                for h2 in range(2):
                    h = 2 * hp + h2
                    for j in range(8):
                        tloc = qc * 8 + j
                        (ct, off) = ctxsl[h2][j]
                        rcol = j if h2 == 0 else 8 + j
                        ph.op("vector", lambda v, j=j, tloc=tloc, h=h, ct=ct,
                              off=off, rcol=rcol: v.tensor_scalar(
                            ctx_tm[:, tloc, 64 * h:64 * h + 64],
                            ct[:, off:off + 64],
                            recb[:, rcol:rcol + 1], None, ALU.mult),
                            wafter=[recfence],
                            inc=(sDV if (h2 == 1 and j == 7) else None))
                ph.op("vector", lambda v: v.nop(), inc=sEV)
                evcnt[(id(ctxp), 0)] = sEV.n
                evcnt[(id(ctxp), 512)] = sEV.n

        def emit_ln(ph, t_in, t_out, scr, gamma=None, beta=None, outwait=None,
                    inc=None):
            ph.op("vector", lambda v: v.scalar_tensor_tensor(
                scr[:, :], t_in, 1.0, t_in, ALU.mult, ALU.mult, accum_out=stt[:, 1:2]),
                inc=sV)
            ph.op("vector", lambda v: v.reduce_sum(stt[:, 0:1], t_in, axis=AX.X),
                  inc=sV)
            ph.op("vector", lambda v: v.tensor_scalar(stt[:, 2:3], stt[:, 0:1], 1.0 / E, None, ALU.mult),
                  wafter=[(sV, sV.n)], inc=sV)
            ph.op("vector", lambda v: v.tensor_scalar(stt[:, 3:4], stt[:, 1:2], 1.0 / E, None, ALU.mult),
                  inc=sV)
            ph.op("vector", lambda v: v.tensor_mul(stt[:, 4:5], stt[:, 2:3], stt[:, 2:3]),
                  wafter=[(sV, sV.n)], inc=sV)
            ph.op("vector", lambda v: v.tensor_sub(stt[:, 5:6], stt[:, 3:4], stt[:, 4:5]),
                  wafter=[(sV, sV.n)], inc=sV)
            ph.op("vector", lambda v: v.tensor_scalar(stt[:, 6:7], stt[:, 5:6], LN_EPS, None, ALU.add),
                  wafter=[(sV, sV.n)], inc=sA)
            ph.op("scalar", lambda s: s.activation(stt[:, 7:8], stt[:, 6:7], AF.Ln),
                  after=[(sA, sA.n)])
            ph.op("scalar", lambda s: s.activation(stt[:, 8:9], stt[:, 7:8], AF.Exp, scale=-0.5),
                  inc=sB)
            w = [(sB, sB.n)] + ([outwait] if outwait else [])
            lastop = ph.op if gamma is not None else None
            ph.op("vector", lambda v: v.tensor_scalar(
                t_out, t_in, stt[:, 2:3], stt[:, 8:9], ALU.subtract, ALU.mult), after=w,
                inc=(None if gamma is not None else inc))
            if gamma is not None:
                ph.op("vector", lambda v: v.tensor_mul(t_out, t_out, gamma))
                ph.op("vector", lambda v: v.tensor_add(t_out, t_out, beta), inc=inc)

        def emit_drain(ph):
            ph.op("sync", lambda s: s.nop(), after=[(dL, dL.n)])
            ph.op("gpsimd", lambda g: g.nop(), after=[(dG, dG.n)])

        def emit_qkv(ph, is2, hp, wbuf, wq_ready, Q, K, V, pk, qsrc, qwaits,
                     ksrc, vsrc, kwid, kvfree=None):
            kvw = [kvfree] if kvfree else []
            for qb in range(4):
                (bt, bo, bpi) = pk[qb % 2]
                for c in range(NC):
                    ph.op("tensor", lambda t, c=c, qb=qb, bt=bt, bo=bo: t.matmul(
                        bt[:, bo:bo + 512], wbuf[:, c, 0:128],
                        qsrc[:, c, 512 * qb:512 * qb + 512],
                        start=(c == 0), stop=(c == NC - 1)),
                        after=[wq_ready] + qwaits + [(sEV, evcnt.get((id(bt), bo), 0))],
                        inc=(sPJ if c == NC - 1 else None))
                if is2:
                    ph.op("vector", lambda v, qb=qb, bt=bt, bo=bo: v.tensor_scalar(
                        Q[:, 512 * qb:512 * qb + 512], bt[:, bo:bo + 512],
                        bqk_t[:, hp:hp + 1], 0.125, ALU.add, ALU.mult),
                        after=[(sPJ, sPJ.n)], inc=sEV)
                else:
                    ph.op("vector", lambda v, qb=qb, bt=bt, bo=bo: v.tensor_scalar(
                        Q[:, 512 * qb:512 * qb + 512], bt[:, bo:bo + 512], 0.125, None, ALU.mult),
                        after=[(sPJ, sPJ.n)], inc=sEV)
                evcnt[(id(bt), bo)] = sEV.n
            for kb in range(kwid // 512):
                (bt, bo, bpi) = pk[kb % 2]
                for c in range(NC):
                    ph.op("tensor", lambda t, c=c, kb=kb, bt=bt, bo=bo: t.matmul(
                        bt[:, bo:bo + 512], wbuf[:, c, 128:256],
                        ksrc[:, c, 512 * kb:512 * kb + 512],
                        start=(c == 0), stop=(c == NC - 1)),
                        after=[(sEV, evcnt.get((id(bt), bo), 0))],
                        inc=(sPJ if c == NC - 1 else None))
                if is2:
                    ph.op("vector", lambda v, kb=kb, bt=bt, bo=bo: v.tensor_scalar(
                        K[:, 512 * kb:512 * kb + 512], bt[:, bo:bo + 512],
                        bqk_t[:, 8 + hp:9 + hp], None, ALU.add),
                        after=[(sPJ, sPJ.n)] + kvw, inc=sEV)
                else:
                    ph.op("vector", lambda v, kb=kb, bt=bt, bo=bo: v.tensor_copy(
                        K[:, 512 * kb:512 * kb + 512], bt[:, bo:bo + 512]),
                        after=[(sPJ, sPJ.n)], inc=sEV)
                evcnt[(id(bt), bo)] = sEV.n
            nvt = kwid // 128
            ph.op("vector", lambda v, nvt=nvt: v.memset(
                AP(V, 64, [[nvt * 130, 128], [130, nvt], [65, 2]]), 1.0),
                after=kvw, inc=sEV)
            vones = sEV.n
            for tt in range(nvt):
                (bt, bo, bpi) = pk[tt % 2]
                for c in range(NC):
                    ph.op("tensor", lambda t, c=c, tt=tt, bt=bt, bo=bo: t.matmul(
                        bt[:, bo:bo + 128], vsrc[:, c, 128 * tt:128 * tt + 128],
                        wbuf[:, c, 256:384],
                        start=(c == 0), stop=(False if is2 else c == NC - 1)),
                        after=[(sEV, evcnt.get((id(bt), bo), 0))],
                        inc=(sPJ if (not is2 and c == NC - 1) else None))
                if is2:
                    ph.op("tensor", lambda t, bt=bt, bo=bo: t.matmul(
                        bt[:, bo:bo + 128], ones1[:, :],
                        bv_bf[0:1, 128 * hp:128 * hp + 128],
                        start=False, stop=True), inc=sPJ)
                ph.op("vector", lambda v, tt=tt, bt=bt, bo=bo, bpi=bpi, nvt=nvt: v.tensor_copy(
                    AP(V, 130 * tt, [[nvt * 130, 128], [65, 2], [1, 64]]),
                    AP(bt.ap().tensor, bo, [[bpi, 128], [64, 2], [1, 64]])),
                    after=[(sPJ, sPJ.n), (sEV, vones)], inc=sEV)
                evcnt[(id(bt), bo)] = sEV.n

        ctx_scope = ExitStack()
        ctx_tm = ctx_scope.enter_context(nc.sbuf_tensor("ctx_tm", [128, NT, E], BF16))

        with ExitStack() as st:
            x0bf = st.enter_context(nc.sbuf_tensor("x0bf", [128, NC, S], BF16))
            x0obf = st.enter_context(nc.sbuf_tensor("x0obf", [128, NC, TOK], BF16))
            wst = [st.enter_context(nc.sbuf_tensor(f"wst{i}", [128, NC, 384], BF16)) for i in range(2)]
            Qp = [st.enter_context(nc.sbuf_tensor(f"Qp{i}", [128, TOK], BF16)) for i in range(2)]
            Kp = [st.enter_context(nc.sbuf_tensor(f"Kp{i}", [128, S], BF16)) for i in range(2)]
            Vp = [st.enter_context(nc.sbuf_tensor(f"Vp{i}", [128, 32 * 130], BF16)) for i in range(2)]
            psb = [st.enter_context(nc.sbuf_tensor(f"psb{i}", [128, 1024], BF16)) for i in range(4)]
            sqb = st.enter_context(nc.sbuf_tensor("sqb", [128, 512], BF16))
            S0 = st.enter_context(nc.psum_tensor("S0", [128, 1024], F32))
            S1 = st.enter_context(nc.psum_tensor("S1", [128, 1024], F32))
            ctxp = st.enter_context(nc.psum_tensor("ctxp", [128, 1024], F32))
            pk = [st.enter_context(nc.psum_tensor(f"pk{i}", [128, 512], F32)) for i in range(2)]

            ph = Prog(nc)
            for c in range(NC):
                ph.op("gpsimd", lambda g, c=c: g.dma_start(x0bf[:, c, :], x0f[:, c, :]),
                      inc=dG, inc_by=16)
                ph.op("gpsimd", lambda g, c=c: g.dma_start(x0obf[:, c, :], x0fo[:, c, :]),
                      inc=dG, inc_by=16)
            x0rdy = (dG, dG.n)
            ph.op("gpsimd", lambda g: g.dma_start(wst[0][:, :, :], wqkv1[:, 0, :, :]),
                  inc=dG, inc_by=16)
            for hp in range(8):
                wb = wst[hp % 2]
                wq_ready = (dG, dG.n)
                if hp < 7:
                    ph.op("gpsimd", lambda g, hp=hp: g.dma_start(
                        wst[(hp + 1) % 2][:, :, :], wqkv1[:, hp + 1, :, :]),
                        after=[(sPJ, sPJ.n)], inc=dG, inc_by=16)
                qb_banks = [(ctxp, 0, 1024), (ctxp, 512, 1024)]
                emit_qkv(ph, False, hp, wb, wq_ready, Qp[hp % 2], Kp[hp % 2],
                         Vp[hp % 2], qb_banks, x0obf, [x0rdy], x0bf, x0bf, S)
                emit_norm_bound(ph, (ctxp, 0), Qp[hp % 2], 4, sqb, 0, True)
                emit_norm_bound(ph, (ctxp, 0), Kp[hp % 2], 8, sqb, 3, True)
                emit_make_negm(ph, (ctxp, 512), 0, 3, negm, hp % 2)
                ready = [(sEV, sEV.n)]
                if PAIR1 and hp < PAIR1_N and PMODE == "paired":
                    emit_attention(ph, False, hp, Qp[hp % 2], Kp[hp % 2], Vp[hp % 2],
                                   negm, hp % 2, ctx_tm, [S0, S1], ctxp, pk, psb, ready)
                elif PAIR1 and hp < PAIR1_N and PMODE == "psb4":
                    emit_attention_p4(ph, False, hp, Qp[hp % 2], Kp[hp % 2], Vp[hp % 2],
                                      negm, hp % 2, ctx_tm, [S0, S1], ctxp, psb, ready)
                elif PAIR1 and hp < PAIR1_N and PMODE == "pkctx":
                    emit_attention_pc(ph, False, hp, Qp[hp % 2], Kp[hp % 2], Vp[hp % 2],
                                      negm, hp % 2, ctx_tm, [S0, S1], ctxp, pk, psb, ready)
                else:
                    emit_attention_v1(ph, False, hp, Qp[hp % 2], Kp[hp % 2], Vp[hp % 2],
                                      negm, hp % 2, ctx_tm, [S0, S1], ctxp, psb, ready)
            emit_drain(ph)
            ph.run()

        with ExitStack() as st:
            ctx_fm = st.enter_context(nc.sbuf_tensor("ctx_fm", [128, NC, TOK], BF16))
            wo1 = st.enter_context(nc.sbuf_tensor("wo1", [128, NC, E], BF16))
            x0t = [st.enter_context(nc.sbuf_tensor(f"x0t{i}", [128, E], F32)) for i in range(2)]
            tres = st.enter_context(nc.sbuf_tensor("tres", [128, E], F32))
            xout = [st.enter_context(nc.sbuf_tensor(f"xout{i}", [128, E], F32)) for i in range(2)]
            scr = st.enter_context(nc.sbuf_tensor("scr", [128, E], F32))
            tp = [st.enter_context(nc.psum_tensor(f"tp{i}", [128, 128], BF16)) for i in range(2)]
            po = [st.enter_context(nc.psum_tensor(f"po{i}", [128, 512], F32)) for i in range(2)]

            ph = Prog(nc)
            if DEBUG_TAPS:
                dc1 = nc.dram_tensor("dbgctx1", [128, NT * E], BF16,
                                     kind="ExternalOutput").ap()
                ph.op("sync", lambda s: s.dma_start(
                    dc1, AP(ctx_tm, 0, [[NT * E, 128], [1, NT * E]])),
                    after=[(sDV, sDV.n)], inc=dL, inc_by=16)
            ph.op("gpsimd", lambda g: g.dma_start(wo1[:, :, :], wout1), inc=dG, inc_by=16)
            wordy = (dG, dG.n)
            divdone = (sDV, sDV.n)
            for t in range(NT):
                for c in range(NC):
                    k = (t * NC + c) % 2
                    ph.op("tensor", lambda tt, t=t, c=c, k=k: tt.transpose(
                        tp[k][:, :], ctx_tm[:, t, 128 * c:128 * c + 128], ident[:, :]),
                        after=[divdone, (sEV, evcnt.get(id(tp[k]), 0))], inc=sTR)
                    ph.op("vector", lambda v, t=t, c=c, k=k: v.tensor_copy(
                        ctx_fm[:, c, 128 * t:128 * t + 128], tp[k][:, :]),
                        after=[(sTR, sTR.n)], inc=sEV)
                    evcnt[id(tp[k])] = sEV.n
            trall = (sEV, sEV.n)
            if DEBUG_TAPS:
                dcf = nc.dram_tensor("dbgctxfm", [128, NC * TOK], BF16,
                                     kind="ExternalOutput").ap()
                ph.op("sync", lambda s: s.dma_start(
                    dcf, AP(ctx_fm, 0, [[NC * TOK, 128], [1, NC * TOK]])),
                    after=[trall], inc=dL, inc_by=16)
            x0used = {}
            xoutfree = {}
            for t in range(NT):
                ph.op("sync", lambda s, t=t: s.dma_start(
                    x0t[t % 2][:, :], x0tm[128 * t:128 * t + 128, :]),
                    after=[(sEV, x0used.get(t % 2, 0))], inc=dL, inc_by=16)
                x0rdy2 = (dL, dL.n)
                for jj in range(2):
                    bank = po[jj]
                    for c in range(NC):
                        ph.op("tensor", lambda tt, t=t, c=c, jj=jj, bank=bank: tt.matmul(
                            bank[:, :], ctx_fm[:, c, 128 * t:128 * t + 128],
                            wo1[:, c, 512 * jj:512 * jj + 512],
                            start=(c == 0), stop=(c == NC - 1)),
                            after=[wordy, trall, (sEV, evcnt.get(id(bank), 0))],
                            inc=(sPJ if c == NC - 1 else None))
                ph.op("vector", lambda v, t=t: v.tensor_add(
                    tres[:, 0:512], po[0][:, :], x0t[t % 2][:, 0:512]),
                    after=[(sPJ, sPJ.n), x0rdy2])
                ph.op("vector", lambda v, t=t: v.tensor_add(
                    tres[:, 512:1024], po[1][:, :], x0t[t % 2][:, 512:1024]), inc=sEV)
                evcnt[id(po[0])] = sEV.n
                evcnt[id(po[1])] = sEV.n
                x0used[t % 2] = sEV.n
                emit_ln(ph, tres[:, :], xout[t % 2][:, :], scr,
                        outwait=(dL, xoutfree.get(t % 2, 0)), inc=sC)
                xo_ok = (sC, sC.n)
                ph.op("sync", lambda s, t=t: s.dma_start(
                    x1d[128 * t:128 * t + 128, :], xout[t % 2][:, :]),
                    after=[xo_ok], inc=dL, inc_by=16)
                xoutfree[t % 2] = dL.n
            emit_drain(ph)
            ph.run()
        ctx_scope.close()

        mid = ExitStack()
        ctx2_tm = mid.enter_context(nc.sbuf_tensor("ctx2_tm", [128, NT, E], BF16))
        midq = ExitStack()
        Q2 = midq.enter_context(nc.sbuf_tensor("Q2", [128, 8, TOK], BF16))

        with ExitStack() as st:
            ropefm = st.enter_context(nc.sbuf_tensor("ropefm", [128, NC, TOK], BF16))
            x1fm = st.enter_context(nc.sbuf_tensor("x1fm", [128, NC, TOK], BF16))
            x1t = [st.enter_context(nc.sbuf_tensor(f"x1t{i}", [128, E], F32)) for i in range(2)]
            cs = [st.enter_context(nc.sbuf_tensor(f"cs{i}", [128, E], F32)) for i in range(2)]
            scrA = st.enter_context(nc.sbuf_tensor("scrA", [128, E], F32))
            rope_tm = st.enter_context(nc.sbuf_tensor("rope_tm", [128, E], F32))
            ropebf = st.enter_context(nc.sbuf_tensor("ropebf", [128, E], BF16))
            x1bf = st.enter_context(nc.sbuf_tensor("x1bf", [128, E], BF16))
            wst2 = [st.enter_context(nc.sbuf_tensor(f"wst2{i}", [128, NC, 384], BF16)) for i in range(2)]
            K2p = st.enter_context(nc.sbuf_tensor("K2p", [128, TOK], BF16))
            V2p = st.enter_context(nc.sbuf_tensor("V2p", [128, NT * 130], BF16))
            sqb2 = st.enter_context(nc.sbuf_tensor("sqb2", [128, 512], BF16))
            tp = [st.enter_context(nc.psum_tensor(f"tp2{i}", [128, 128], BF16)) for i in range(2)]
            pk = [st.enter_context(nc.psum_tensor(f"pk2{i}", [128, 512], F32)) for i in range(2)]

            ph = Prog(nc)
            tilefree = {}
            for t in range(NT):
                ph.op("sync", lambda s, t=t: s.dma_start(
                    x1t[t % 2][:, :], x1d[128 * t:128 * t + 128, :]),
                    after=[(sC, tilefree.get(t % 2, 0))], inc=dL, inc_by=16)
                ph.op("sync", lambda s, t=t: s.dma_start(
                    cs[t % 2][:, 0:512], cost[128 * t:128 * t + 128, :]),
                    inc=dL, inc_by=16)
                ph.op("sync", lambda s, t=t: s.dma_start(
                    cs[t % 2][:, 512:1024], sint[128 * t:128 * t + 128, :]),
                    inc=dL, inc_by=16)
                ld = (dL, dL.n)
                xt = x1t[t % 2]
                ct = cs[t % 2][:, 0:512]
                sn = cs[t % 2][:, 512:1024]
                xe = AP(xt, 0, [[E, 128], [2, 512]])
                xo = AP(xt, 1, [[E, 128], [2, 512]])
                re_ = AP(rope_tm, 0, [[E, 128], [2, 512]])
                ro_ = AP(rope_tm, 1, [[E, 128], [2, 512]])
                ph.op("vector", lambda v, xe=xe, ct=ct: v.tensor_mul(scrA[:, 0:512], xe, ct),
                      after=[ld])
                ph.op("vector", lambda v, xo=xo, sn=sn: v.tensor_mul(scrA[:, 512:1024], xo, sn))
                ph.op("vector", lambda v, re_=re_: v.tensor_sub(re_, scrA[:, 0:512], scrA[:, 512:1024]))
                ph.op("vector", lambda v, xe=xe, sn=sn: v.tensor_mul(scrA[:, 0:512], xe, sn))
                ph.op("vector", lambda v, xo=xo, ct=ct: v.tensor_mul(scrA[:, 512:1024], xo, ct))
                ph.op("vector", lambda v, ro_=ro_: v.tensor_add(ro_, scrA[:, 0:512], scrA[:, 512:1024]))
                ph.op("vector", lambda v: v.tensor_copy(ropebf[:, :], rope_tm[:, :]))
                ph.op("vector", lambda v, xt=xt: v.tensor_copy(x1bf[:, :], xt[:, :]), inc=sC)
                tilefree[t % 2] = sC.n
                cast_ok = (sC, sC.n)
                for c in range(NC):
                    k = (t * NC + c) % 2
                    k2 = (t * NC + c + 1) % 2
                    ph.op("tensor", lambda tt, c=c, k=k: tt.transpose(
                        tp[k][:, :], ropebf[:, 128 * c:128 * c + 128], ident[:, :]),
                        after=[cast_ok, (sEV, evcnt.get(id(tp[k]), 0))], inc=sTR)
                    ph.op("vector", lambda v, t=t, c=c, k=k: v.tensor_copy(
                        ropefm[:, c, 128 * t:128 * t + 128], tp[k][:, :]),
                        after=[(sTR, sTR.n)], inc=sEV)
                    evcnt[id(tp[k])] = sEV.n
                    ph.op("tensor", lambda tt, c=c, k2=k2: tt.transpose(
                        tp[k2][:, :], x1bf[:, 128 * c:128 * c + 128], ident[:, :]),
                        after=[cast_ok, (sEV, evcnt.get(id(tp[k2]), 0))], inc=sTR)
                    ph.op("vector", lambda v, t=t, c=c, k2=k2: v.tensor_copy(
                        x1fm[:, c, 128 * t:128 * t + 128], tp[k2][:, :]),
                        after=[(sTR, sTR.n)], inc=sEV)
                    evcnt[id(tp[k2])] = sEV.n
                ph.op("vector", lambda v: v.nop(), after=[(sTR, sTR.n)], inc=sC)
                tilefree[t % 2] = sC.n
            trdone = (sEV, sEV.n)
            ph.op("gpsimd", lambda g: g.dma_start(wst2[0][:, :, :], wqkv2[:, 0, :, :]),
                  inc=dG, inc_by=16)
            kv2st = 0
            for hp in range(8):
                wb = wst2[hp % 2]
                wq_ready = (dG, dG.n)
                if hp < 7:
                    ph.op("gpsimd", lambda g, hp=hp: g.dma_start(
                        wst2[(hp + 1) % 2][:, :, :], wqkv2[:, hp + 1, :, :]),
                        after=[(sPJ, sPJ.n)], inc=dG, inc_by=16)
                Qsl = AP(Q2, hp * TOK, [[8 * TOK, 128], [1, TOK]])
                emit_qkv(ph, True, hp, wb, wq_ready, Qsl, K2p, V2p,
                         [(pk[0], 0, 512), (pk[1], 0, 512)],
                         ropefm, [trdone], ropefm, x1fm, TOK,
                         kvfree=(dL, kv2st))
                emit_norm_bound(ph, (pk[0], 0), Qsl, 4, sqb2, 8 + hp, True)
                emit_norm_bound(ph, (pk[0], 0), K2p, 4, sqb2, 1, hp == 0)
                ph.op("sync", lambda s, hp=hp: s.dma_start(
                    k2loc.ap()[128 * hp:128 * hp + 128, :], K2p[:, :]),
                    after=[(sEV, sEV.n)], inc=dL, inc_by=16)
                ph.op("sync", lambda s, hp=hp: s.dma_start(
                    v2loc.ap().rearrange("(t p) f -> p t f", p=128)[:, :, 130 * hp:130 * hp + 130],
                    AP(V2p, 0, [[NT * 130, 128], [130, NT], [1, 130]])),
                    after=[(sEV, sEV.n)], inc=dL, inc_by=16)
                kv2st = dL.n
            ph.op("sync", lambda s: s.dma_start(arin.ap(), nrm[0:1, 1:2]),
                  after=[(sEV, sEV.n)], inc=dL, inc_by=16)
            ph.op("gpsimd", lambda g: g.collective_compute(
                "AllGather", ALU.bypass, RG, [k2loc.ap().opt()], [k2all.ap().opt()]),
                after=[(dL, dL.n)], inc=cc1)
            ph.op("gpsimd", lambda g: g.collective_compute(
                "AllGather", ALU.bypass, RG, [v2loc.ap().opt()], [v2all.ap().opt()]),
                inc=cc2)
            ph.op("gpsimd", lambda g: g.collective_compute(
                "AllReduce", ALU.max, RG, [arin.ap().opt()], [arout.ap().opt()]),
                inc=cc3)
            ph.op("sync", lambda s: s.dma_start(nrm[0:1, 1:2], arout.ap()),
                  after=[(cc3, cc3.n)], inc=dL, inc_by=16)
            ph.op("vector", lambda v: v.nop(), after=[(dL, dL.n)], inc=sC)
            for hp in range(8):
                emit_make_negm(ph, (pk[1], 0), 8 + hp, 1, negm2, hp)
            emit_drain(ph)
            ph.run()

        with ExitStack() as st:
            Kb = [st.enter_context(nc.sbuf_tensor(f"Kb{i}", [128, S], BF16)) for i in range(2)]
            Vb = [st.enter_context(nc.sbuf_tensor(f"Vb{i}", [128, 32 * 130], BF16)) for i in range(2)]
            psb = [st.enter_context(nc.sbuf_tensor(f"psb2{i}", [128, 1024], BF16)) for i in range(4)]
            S0 = st.enter_context(nc.psum_tensor("S0b", [128, 1024], F32))
            S1 = st.enter_context(nc.psum_tensor("S1b", [128, 1024], F32))
            ctxp = st.enter_context(nc.psum_tensor("ctxp2", [128, 1024], F32))
            pk3 = [st.enter_context(nc.psum_tensor(f"pk3{i}", [128, 512], F32)) for i in range(2)]

            ph = Prog(nc)
            kread = {}
            for hp in range(8):
                bufK, bufV = Kb[hp % 2], Vb[hp % 2]
                usedone = (sP, kread.get(hp % 2, 0))
                for pbit in range(2):
                    ph.op("sync", lambda s, hp=hp, pbit=pbit, bufK=bufK: s.dma_start(
                        AP(bufK, 128 * pbit, [[S, 128], [256, 16], [1, 128]]),
                        k2all.ap()[1024 * pbit + 128 * hp:1024 * pbit + 128 * hp + 128, :]
                        .rearrange("p (t f) -> p t f", f=128)),
                        after=[(cc1, cc1.n), usedone], inc=dL, inc_by=16)
                    ph.op("sync", lambda s, hp=hp, pbit=pbit, bufV=bufV: s.dma_start(
                        AP(bufV, 130 * pbit, [[32 * 130, 128], [260, 16], [1, 130]]),
                        v2all.ap()[2048 * pbit:2048 * pbit + 2048, 130 * hp:130 * hp + 130]
                        .rearrange("(t p) f -> p t f", p=128)),
                        after=[(cc2, cc2.n), usedone], inc=dL, inc_by=16)
                kv_ok = (dL, dL.n)
                Qsl = AP(Q2, hp * TOK, [[8 * TOK, 128], [1, TOK]])
                if PAIR2:
                    emit_attention(ph, True, hp, Qsl, bufK, bufV, negm2, hp,
                                   ctx2_tm, [S0, S1], ctxp, pk3, psb, ready=[kv_ok])
                else:
                    emit_attention_v1(ph, True, hp, Qsl, bufK, bufV, negm2, hp,
                                      ctx2_tm, [S0, S1], ctxp, psb, ready=[kv_ok])
                kread[hp % 2] = sP.n
            if DEBUG_TAPS:
                dq = nc.dram_tensor("dbgQ2", [128, 8 * TOK], BF16,
                                    kind="ExternalOutput").ap()
                dc = nc.dram_tensor("dbgctx2", [128, NT * E], BF16,
                                    kind="ExternalOutput").ap()
                dx1 = nc.dram_tensor("dbgx1", [TOK, E], F32,
                                     kind="ExternalOutput").ap()
                dnm = nc.dram_tensor("dbgnm", [128, 8], F32,
                                     kind="ExternalOutput").ap()
                ph.op("sync", lambda s: s.dma_start(
                    dq, AP(Q2, 0, [[8 * TOK, 128], [1, 8 * TOK]])),
                    after=[(sDV, sDV.n)], inc=dL, inc_by=16)
                ph.op("sync", lambda s: s.dma_start(
                    dc, AP(ctx2_tm, 0, [[NT * E, 128], [1, NT * E]])),
                    after=[(sDV, sDV.n)], inc=dL, inc_by=16)
                ph.op("sync", lambda s: s.dma_start(dx1, x1d),
                      inc=dL, inc_by=16)
                ph.op("sync", lambda s: s.dma_start(
                    dnm, negm2[:, :]), inc=dL, inc_by=16)
            emit_drain(ph)
            ph.run()
        midq.close()

        p45 = ExitStack()
        y1fm = p45.enter_context(nc.sbuf_tensor("y1fm", [128, NC, TOK], BF16, side="right"))
        g1 = p45.enter_context(nc.sbuf_tensor("g1t", [128, E], F32, side="right"))
        b1 = p45.enter_context(nc.sbuf_tensor("b1t", [128, E], F32, side="right"))
        g2 = p45.enter_context(nc.sbuf_tensor("g2t", [128, E], F32, side="right"))
        b2 = p45.enter_context(nc.sbuf_tensor("b2t", [128, E], F32, side="right"))
        w_all = p45.enter_context(nc.sbuf_tensor("w_all", [128, NT, 8], F32, side="right"))

        with ExitStack() as st:
            ctx2_fm = st.enter_context(nc.sbuf_tensor("ctx2_fm", [128, NC, TOK], BF16))
            wo2 = st.enter_context(nc.sbuf_tensor("wo2", [128, NC, E], BF16))
            x1t = [st.enter_context(nc.sbuf_tensor(f"p4x1t{i}", [128, E], F32)) for i in range(2)]
            tres = st.enter_context(nc.sbuf_tensor("p4tres", [128, E], F32))
            yout = [st.enter_context(nc.sbuf_tensor(f"yout{i}", [128, E], F32)) for i in range(2)]
            ybf = st.enter_context(nc.sbuf_tensor("ybf", [128, E], BF16))
            scr = st.enter_context(nc.sbuf_tensor("p4scr", [128, E], F32))
            rstage = st.enter_context(nc.sbuf_tensor("p4rst", [1, E], F32))
            tp = [st.enter_context(nc.psum_tensor(f"tp4{i}", [128, 128], BF16)) for i in range(2)]
            po = [st.enter_context(nc.psum_tensor(f"po4{i}", [128, 512], F32)) for i in range(2)]
            pg = st.enter_context(nc.psum_tensor("pg4", [128, 512], F32))

            ph = Prog(nc)
            ph.op("gpsimd", lambda g: g.dma_start(wo2[:, :, :], wout2), inc=dG, inc_by=16)
            wordy = (dG, dG.n)
            rowused = 0
            for (row, dst) in ((ln1g, g1), (ln1b, b1), (ln2g, g2), (ln2b, b2)):
                ph.op("sync", lambda s, row=row: s.dma_start(rstage[:, :], row),
                      after=[(sEV, rowused)], inc=dL, inc_by=16)
                rowrdy = (dL, dL.n)
                for jj in range(2):
                    ph.op("tensor", lambda t, jj=jj: t.matmul(
                        pg[:, :], ones1f[:, :], rstage[0:1, 512 * jj:512 * jj + 512],
                        start=True, stop=True),
                        after=[rowrdy, (sEV, evcnt.get(id(pg), 0))], inc=sPJ)
                    ph.op("vector", lambda v, dst=dst, jj=jj: v.tensor_copy(
                        dst[:, 512 * jj:512 * jj + 512], pg[:, :]),
                        after=[(sPJ, sPJ.n)], inc=sEV)
                    evcnt[id(pg)] = sEV.n
                rowused = sEV.n
            divdone = (sDV, sDV.n)
            for t in range(NT):
                for c in range(NC):
                    k = (t * NC + c) % 2
                    ph.op("tensor", lambda tt, t=t, c=c, k=k: tt.transpose(
                        tp[k][:, :], ctx2_tm[:, t, 128 * c:128 * c + 128], ident[:, :]),
                        after=[divdone, (sEV, evcnt.get(id(tp[k]), 0))], inc=sTR)
                    ph.op("vector", lambda v, t=t, c=c, k=k: v.tensor_copy(
                        ctx2_fm[:, c, 128 * t:128 * t + 128], tp[k][:, :]),
                        after=[(sTR, sTR.n)], inc=sEV)
                    evcnt[id(tp[k])] = sEV.n
            trall = (sEV, sEV.n)
            x1used = {}
            youtfree = {}
            ybffree = 0
            for t in range(NT):
                ph.op("sync", lambda s, t=t: s.dma_start(
                    x1t[t % 2][:, :], x1d[128 * t:128 * t + 128, :]),
                    after=[(sEV, x1used.get(t % 2, 0))], inc=dL, inc_by=16)
                x1rdy = (dL, dL.n)
                for jj in range(2):
                    bank = po[jj]
                    for c in range(NC):
                        ph.op("tensor", lambda tt, t=t, c=c, jj=jj, bank=bank: tt.matmul(
                            bank[:, :], ctx2_fm[:, c, 128 * t:128 * t + 128],
                            wo2[:, c, 512 * jj:512 * jj + 512],
                            start=(c == 0), stop=False),
                            after=[wordy, trall, (sEV, evcnt.get(id(bank), 0))])
                    ph.op("tensor", lambda tt, jj=jj, bank=bank: tt.matmul(
                        bank[:, :], ones1[:, :], bo_bf[0:1, 512 * jj:512 * jj + 512],
                        start=False, stop=True), inc=sPJ)
                ph.op("vector", lambda v, t=t: v.tensor_add(
                    tres[:, 0:512], po[0][:, :], x1t[t % 2][:, 0:512]),
                    after=[(sPJ, sPJ.n), x1rdy])
                ph.op("vector", lambda v, t=t: v.tensor_add(
                    tres[:, 512:1024], po[1][:, :], x1t[t % 2][:, 512:1024]), inc=sEV)
                evcnt[id(po[0])] = sEV.n
                evcnt[id(po[1])] = sEV.n
                x1used[t % 2] = sEV.n
                emit_ln(ph, tres[:, :], yout[t % 2][:, :], scr, g1[:, :], b1[:, :],
                        outwait=(dL, youtfree.get(t % 2, 0)))
                ph.op("vector", lambda v, t=t: v.tensor_copy(ybf[:, :], yout[t % 2][:, :]),
                      after=[(sTR, ybffree)], inc=sC)
                ybf_ok = (sC, sC.n)
                ph.op("sync", lambda s, t=t: s.dma_start(
                    y1d[128 * t:128 * t + 128, :], yout[t % 2][:, :]),
                    after=[ybf_ok], inc=dL, inc_by=16)
                youtfree[t % 2] = dL.n
                for c in range(NC):
                    k = (t * NC + c) % 2
                    ph.op("tensor", lambda tt, t=t, c=c, k=k: tt.transpose(
                        tp[k][:, :], ybf[:, 128 * c:128 * c + 128], ident[:, :]),
                        after=[ybf_ok, (sEV, evcnt.get(id(tp[k]), 0))], inc=sTR)
                    ph.op("vector", lambda v, t=t, c=c, k=k: v.tensor_copy(
                        y1fm[:, c, 128 * t:128 * t + 128], tp[k][:, :]),
                        after=[(sTR, sTR.n)], inc=sEV)
                    evcnt[id(tp[k])] = sEV.n
                ybffree = sTR.n
            emit_drain(ph)
            ph.run()
        mid.close()

        with ExitStack() as st:
            acc = st.enter_context(nc.sbuf_tensor("acc", [128, NT, E], F32))
            Webf = [st.enter_context(nc.sbuf_tensor(f"Webf{i}", [128, NC, E], BF16)) for i in range(2)]
            be_bf = st.enter_context(nc.sbuf_tensor("be_bf", [1, NE * E], BF16))
            gex = st.enter_context(nc.sbuf_tensor("gex", [128, 8], F32))
            gsc = st.enter_context(nc.sbuf_tensor("gsc", [128, 8], F32))
            pgt = st.enter_context(nc.psum_tensor("pgt", [128, 8], F32))
            pe_ = [st.enter_context(nc.psum_tensor(f"pe{i}", [128, 512], F32)) for i in range(4)]

            ph = Prog(nc)
            ph.op("gpsimd", lambda g: g.dma_start(be_bf[:, :], berow), inc=dG, inc_by=16)
            berdy = (dG, dG.n)
            for t in range(NT):
                for c in range(NC):
                    ph.op("tensor", lambda tt, t=t, c=c: tt.matmul(
                        pgt[:, :], y1fm[:, c, 128 * t:128 * t + 128], gwt_bf[:, c, :],
                        start=(c == 0), stop=False),
                        after=[(sEV, evcnt.get(id(pgt), 0))])
                ph.op("tensor", lambda tt: tt.matmul(
                    pgt[:, :], ones1[:, :], gb_bf[0:1, :], start=False, stop=True),
                    inc=sPJ)
                ph.op("vector", lambda v: v.reduce_max(stt[:, 9:10], pgt[:, :], axis=AX.X),
                      after=[(sPJ, sPJ.n)], inc=sV)
                ph.op("vector", lambda v: v.tensor_scalar(
                    stt[:, 10:11], stt[:, 9:10], -1.0, None, ALU.mult), inc=sA,
                    wafter=[(sV, sV.n)])
                ph.op("scalar", lambda s: s.activation(
                    gex[:, :], pgt[:, :], AF.Exp, bias=stt[:, 10:11],
                    accum_out=stt[:, 11:12]), after=[(sA, sA.n)], inc=sB)
                ph.op("vector", lambda v: v.nop(), after=[(sB, sB.n)], inc=sEV)
                evcnt[id(pgt)] = sEV.n
                ph.op("vector", lambda v: v.reciprocal(stt[:, 12:13], stt[:, 11:12]),
                      inc=sV)
                ph.op("vector", lambda v: v.reduce_max(stt[:, 13:14], gex[:, :], axis=AX.X),
                      inc=sV)
                ph.op("vector", lambda v: v.tensor_scalar(
                    gsc[:, :], gex[:, :], stt[:, 13:14], -1e30, ALU.is_equal, ALU.mult),
                    wafter=[(sV, sV.n)], inc=sV)
                ph.op("vector", lambda v: v.tensor_add(gsc[:, :], gsc[:, :], gex[:, :]),
                      wafter=[(sV, sV.n)], inc=sV)
                ph.op("vector", lambda v: v.reduce_max(stt[:, 14:15], gsc[:, :], axis=AX.X),
                      wafter=[(sV, sV.n)], inc=sV)
                ph.op("vector", lambda v: v.tensor_scalar(
                    gsc[:, :], gex[:, :], stt[:, 14:15], None, ALU.is_ge),
                    wafter=[(sV, sV.n)], inc=sV)
                ph.op("vector", lambda v: v.tensor_mul(gsc[:, :], gsc[:, :], gex[:, :]),
                      wafter=[(sV, sV.n)], inc=sV)
                ph.op("vector", lambda v, t=t: v.tensor_scalar(
                    w_all[:, t, :], gsc[:, :], stt[:, 12:13], None, ALU.mult),
                    wafter=[(sV, sV.n)], inc=sC)
            wrdy = (sC, sC.n)
            ph.op("gpsimd", lambda g: g.dma_start(Webf[0][:, :, :], wexp[0, :, :, :]),
                  inc=dG, inc_by=16)
            for e in range(NE):
                wb = Webf[e % 2]
                werdy = (dG, dG.n)
                if e < NE - 1:
                    ph.op("gpsimd", lambda g, e=e: g.dma_start(
                        Webf[(e + 1) % 2][:, :, :], wexp[e + 1, :, :, :]),
                        after=[(sPJ, sPJ.n)], inc=dG, inc_by=16)
                for t in range(NT):
                    bks = (pe_[2 * (t % 2)], pe_[2 * (t % 2) + 1])
                    for c in range(NC):
                        for jj in range(2):
                            ph.op("tensor", lambda tt, t=t, c=c, jj=jj, wb=wb,
                                  bks=bks: tt.matmul(
                                      bks[jj][:, :], y1fm[:, c, 128 * t:128 * t + 128],
                                      wb[:, c, 512 * jj:512 * jj + 512],
                                      start=(c == 0), stop=False),
                                  after=[werdy, (sEV, evcnt.get(id(bks[jj]), 0))])
                    for jj in range(2):
                        ph.op("tensor", lambda tt, e=e, jj=jj, bks=bks: tt.matmul(
                            bks[jj][:, :], ones1[:, :],
                            be_bf[0:1, E * e + 512 * jj:E * e + 512 * jj + 512],
                            start=False, stop=True), after=[berdy], inc=sPJ)
                    pjv = (sPJ, sPJ.n)
                    for jj in range(2):
                        bank = bks[jj]
                        if e == 0:
                            ph.op("vector", lambda v, t=t, jj=jj, bank=bank: v.tensor_scalar(
                                acc[:, t, 512 * jj:512 * jj + 512], bank[:, :],
                                w_all[:, t, 0:1], None, ALU.mult),
                                after=[pjv, wrdy], inc=sEV)
                        else:
                            ph.op("vector", lambda v, t=t, jj=jj, e=e, bank=bank:
                                  v.scalar_tensor_tensor(
                                      acc[:, t, 512 * jj:512 * jj + 512], bank[:, :],
                                      w_all[:, t, e:e + 1],
                                      acc[:, t, 512 * jj:512 * jj + 512],
                                      ALU.mult, ALU.add),
                                  after=[pjv], inc=sEV)
                        evcnt[id(bank)] = sEV.n
            mdone = (sEV, sEV.n)
            emit_drain(ph)
            ph.run()

            with ExitStack() as st6:
                yt = [st6.enter_context(nc.sbuf_tensor(f"p6y{i}", [128, E], F32)) for i in range(2)]
                xt = [st6.enter_context(nc.sbuf_tensor(f"p6x{i}", [128, E], F32)) for i in range(2)]
                t2 = st6.enter_context(nc.sbuf_tensor("p6t2", [128, E], F32))
                f_ = st6.enter_context(nc.sbuf_tensor("p6f", [128, E], F32))
                t3 = st6.enter_context(nc.sbuf_tensor("p6t3", [128, E], F32))
                oout = [st6.enter_context(nc.sbuf_tensor(f"p6o{i}", [128, E], F32)) for i in range(2)]
                scr6 = st6.enter_context(nc.sbuf_tensor("p6scr", [128, E], F32))

                ph = Prog(nc)
                tileused = {}
                ofree = {}
                for t in range(NT):
                    ph.op("sync", lambda s, t=t: s.dma_start(
                        yt[t % 2][:, :], y1d[128 * t:128 * t + 128, :]),
                        after=[(sC, tileused.get(t % 2, 0))], inc=dL, inc_by=16)
                    ph.op("sync", lambda s, t=t: s.dma_start(
                        xt[t % 2][:, :], x1d[128 * t:128 * t + 128, :]),
                        inc=dL, inc_by=16)
                    ld = (dL, dL.n)
                    ph.op("vector", lambda v, t=t: v.tensor_add(
                        t2[:, :], yt[t % 2][:, :], acc[:, t, :]), after=[ld])
                    emit_ln(ph, t2[:, :], f_[:, :], scr6, g2[:, :], b2[:, :])
                    ph.op("vector", lambda v, t=t: v.tensor_add(
                        t3[:, :], xt[t % 2][:, :], f_[:, :]), inc=sC)
                    tileused[t % 2] = sC.n
                    emit_ln(ph, t3[:, :], oout[t % 2][:, :], scr6,
                            outwait=(dL, ofree.get(t % 2, 0)), inc=sC)
                    oo_ok = (sC, sC.n)
                    ph.op("sync", lambda s, t=t: s.dma_start(
                        OUT[128 * t:128 * t + 128, :], oout[t % 2][:, :]),
                        after=[oo_ok], inc=dL, inc_by=16)
                    ofree[t % 2] = dL.n
                ph.op("sync", lambda s: s.nop(), after=[(dL, dL.n)])
                ph.op("gpsimd", lambda g: g.nop(), after=[(dG, dG.n)])
                ph.run()

    return nc


_CACHE = {}


def _host_inputs(inputs):
    x0 = np.asarray(inputs["L0_attn_x"], np.float32)
    w_qkv = np.asarray(inputs["L0_attn_w_qkv"], np.float32)
    w_out = np.asarray(inputs["L0_attn_w_out"], np.float32)
    ipw = np.asarray(inputs["L0_ffn_in_proj_w"], np.float32)
    ipb = np.asarray(inputs["L0_ffn_in_proj_b"], np.float32)
    opw = np.asarray(inputs["L0_ffn_out_proj_w"], np.float32)
    opb = np.asarray(inputs["L0_ffn_out_proj_b"], np.float32)
    gw = np.asarray(inputs["L0_ffn_gate_w"], np.float32)
    gb = np.asarray(inputs["L0_ffn_gate_b"], np.float32)
    ew = np.asarray(inputs["L0_ffn_expert_w"], np.float32)
    eb = np.asarray(inputs["L0_ffn_expert_b"], np.float32)
    l1g = np.asarray(inputs["L0_ffn_ln1_w"], np.float32)
    l1b = np.asarray(inputs["L0_ffn_ln1_b"], np.float32)
    l2g = np.asarray(inputs["L0_ffn_ln2_w"], np.float32)
    l2b = np.asarray(inputs["L0_ffn_ln2_b"], np.float32)

    def pairs_layout(Wq, Wk, Wv):
        blocks = []
        for hp in range(8):
            s = slice(128 * hp, 128 * hp + 128)
            blocks.append(np.concatenate([Wq[:, s], Wk[:, s], Wv[:, s]], axis=1))
        arr = np.stack(blocks, axis=0)
        arr = arr.reshape(8, 8, 128, 384).transpose(2, 0, 1, 3)
        return np.ascontiguousarray(arr)

    wqkv1 = pairs_layout(w_qkv[:, 0:E], w_qkv[:, E:2 * E], w_qkv[:, 2 * E:3 * E])
    wqkv2 = pairs_layout(ipw[0:E].T, ipw[E:2 * E].T, ipw[2 * E:3 * E].T)
    wout1 = np.ascontiguousarray(w_out.reshape(8, 128, E).transpose(1, 0, 2))
    wout2 = np.ascontiguousarray(opw.T.reshape(8, 128, E).transpose(1, 0, 2))
    wexp = np.ascontiguousarray(
        ew.transpose(0, 2, 1).reshape(NE, 8, 128, E).transpose(0, 2, 1, 3))
    bq = ipb[0:E]; bk = ipb[E:2 * E]; bv = ipb[2 * E:3 * E]
    bqk_h = np.ascontiguousarray(
        np.concatenate([bq.reshape(8, 128).T, bk.reshape(8, 128).T], axis=1))
    gwt_h = np.ascontiguousarray(gw.T.reshape(8, 128, 8).transpose(1, 0, 2))

    inv = (1.0 / 10000.0 ** (np.arange(0, HD, 2, dtype=np.float32) / HD)).astype(np.float32)
    tri = np.triu(np.ones((128, 128), np.float32))

    common = dict(
        wqkv1=wqkv1, wout1=wout1, wqkv2=wqkv2, wout2=wout2, wexp=wexp,
        bqk=bqk_h,
        bvrow=np.ascontiguousarray(bv[None, :]),
        borow=np.ascontiguousarray(opb[None, :]),
        gwt=gwt_h, gbrow=np.ascontiguousarray(gb[None, :]),
        berow=np.ascontiguousarray(eb.reshape(1, NE * E)),
        ln1g=np.ascontiguousarray(l1g[None, :]), ln1b=np.ascontiguousarray(l1b[None, :]),
        ln2g=np.ascontiguousarray(l2g[None, :]), ln2b=np.ascontiguousarray(l2b[None, :]),
        identf=np.eye(128, dtype=np.float32),
    )

    in_maps = []
    for core in range(8):
        b, par = core // 2, core % 2
        xb = x0[b]
        xT = np.ascontiguousarray(xb.T)
        x0f_ = np.ascontiguousarray(xT.reshape(8, 128, S).transpose(1, 0, 2))
        own = xb.reshape(32, 128, E)[par::2]
        x0tm_ = np.ascontiguousarray(own.reshape(TOK, E))
        x0fo_ = np.ascontiguousarray(
            x0f_.reshape(128, 8, 32, 128)[:, :, par::2, :].reshape(128, 8, TOK))
        pos = np.arange(S).reshape(32, 128)[par::2].reshape(TOK).astype(np.float32)
        fr = pos[:, None] * inv[None, :]
        cost_ = np.ascontiguousarray(np.tile(np.cos(fr).astype(np.float32), (1, H)))
        sint_ = np.ascontiguousarray(np.tile(np.sin(fr).astype(np.float32), (1, H)))
        m0_ = tri if par == 0 else np.ones((128, 128), np.float32)
        m1_ = np.zeros((128, 128), np.float32) if par == 0 else tri
        im = dict(common)
        im.update(x0f=x0f_, x0fo=x0fo_, x0tm=x0tm_, cost=cost_, sint=sint_,
                  m0in=m0_, m1in=m1_)
        in_maps.append(im)
    return in_maps


def kernel(**inputs):
    in_maps = _host_inputs(inputs)
    if "nc" not in _CACHE:
        _CACHE["nc"] = build_program()
    nc = _CACHE["nc"]
    res = run_bass_kernel_spmd(nc, in_maps, core_ids=list(range(8)))
    out = np.empty((B, S, E), np.float32)
    for core in range(8):
        b, par = core // 2, core % 2
        o = np.asarray(res.results[core]["OUT"], np.float32).reshape(NT, 128, E)
        out.reshape(B, 32, 128, E)[b, par::2] = o
    return out
